# version 20
# baseline (speedup 1.0000x reference)
import numpy as np
import ml_dtypes
from contextlib import ExitStack

import concourse.bass as bass
import concourse.mybir as mybir
from concourse.bass_utils import run_bass_kernel_spmd

F32 = mybir.dt.float32
BF16 = mybir.dt.bfloat16
I32 = mybir.dt.int32
AF = mybir.ActivationFunctionType
ALU = mybir.AluOpType
AX = mybir.AxisListType

T = 4096
D = 1024
NT = 8
BLK = 512
NBLK = T // BLK
LN7 = float(np.log(np.float32(1e-7)))
L1M = float(np.log1p(np.float64(-1e-5)))


class Sched:

    def __init__(self):
        self.rec = {}
        self.pass2 = False

    def begin(self, pass2):
        self.pass2 = pass2
        self.cnt = {}
        self.sems = {}
        self.pfx = ""

    def reg(self, key, handle):
        self.sems[self.pfx + key] = handle

    def inc(self, instr, key, n=1, label=None):
        key = self.pfx + key
        self.cnt[key] = self.cnt.get(key, 0) + n
        if instr is not None:
            instr.then_inc(self.sems[key], n)
        if label is not None:
            if self.pass2:
                assert self.rec.get(label) == (key, self.cnt[key]), \
                    f"schedule drift at {label}"
            else:
                self.rec[label] = (key, self.cnt[key])
        return self.cnt[key]

    def chain(self, instr, key, label=None):
        key = self.pfx + key
        prev = self.cnt.get(key, 0)
        if self.pass2 and prev > 0:
            instr._wait_ge(self.sems[key], prev)
        self.cnt[key] = prev + 1
        instr.then_inc(self.sems[key], 1)
        if label is not None:
            if self.pass2:
                assert self.rec.get(label) == (key, self.cnt[key]), \
                    f"schedule drift at {label}"
            else:
                self.rec[label] = (key, self.cnt[key])
        return self.cnt[key]

    def label(self, label, key):
        key = self.pfx + key
        if self.pass2:
            assert self.rec.get(label) == (key, self.cnt.get(key, 0)), \
                f"schedule drift at {label}"
        else:
            self.rec[label] = (key, self.cnt.get(key, 0))

    def wait(self, eng, label):
        if not self.pass2:
            return
        key, val = self.rec[label]
        if val > 0:
            eng.wait_ge(self.sems[key], val)

    def wait_cnt(self, eng, key, val):
        if self.pass2 and val > 0:
            eng.wait_ge(self.sems[self.pfx + key], val)

    def sem(self, key):
        return self.sems[self.pfx + key]


def _ios(nc, name, shape, dtype, kind):
    return nc.dram_tensor(name, shape, dtype, kind=kind).ap()


IO_SPECS = {
    "s1": {"in": [("x", [T, D], F32), ("w_in", [8, 128, 128], F32),
                  ("w_a", [8, 128, 128], F32), ("b_inT", [128, 8], F32),
                  ("b_aT", [128, 8], F32), ("a_paramT", [128, 8], F32)],
           "out": [("xT", [D, T], F32), ("xnbf", [D, T], BF16),
                   ("s0T", [D, T], F32)]},
    "s2": {"in": [("W1T", [1024, 4096], F32), ("W2T", [4096, 1024], F32),
                  ("b1c", [128, 32], F32), ("xnbf", [D, T], BF16)],
           "out": [("h1", [4096, T], BF16), ("W2t", [4096, 1024], BF16)]},
    "s3": {"in": [("W2t", [4096, 1024], BF16), ("b2c", [128, 8], F32),
                  ("h1", [4096, T], BF16), ("xT", [D, T], F32),
                  ("s0T", [D, T], F32)],
           "out": [("x1T", [D, T], F32)]},
    "s4": {"in": [("x1T", [D, T], F32), ("WinT", [1024, 4232], F32),
                  ("bz_c", [128, 16], F32), ("bxs_r", [1, 2048], F32),
                  ("bbc_c", [128, 1], F32), ("bbc_r", [1, 128], F32),
                  ("bdt_c", [8, 1], F32), ("dtb_c", [8, 1], F32),
                  ("Alog_c", [8, 1], F32), ("Dsk_r", [1, 8], F32),
                  ("gam_c", [128, 8], F32), ("bet_c", [128, 8], F32)],
           "out": [("xn1T", [D, T], F32), ("yfinT", [2048, T], BF16)]},
    "s5": {"in": [("Wo", [2048, 1024], F32), ("bo_c", [128, 8], F32),
                  ("yfinT", [2048, T], BF16), ("xn1T", [D, T], F32)],
           "out": [("s1T", [D, T], F32)]},
    "s6": {"in": [("wgT", [1024, 3072], I32), ("wuT", [1024, 3072], I32),
                  ("xn1T", [D, T], F32)],
           "out": [("hgT", [3072, T], BF16)]},
    "s7": {"in": [("wdT", [3072, 1024], I32), ("hgT", [3072, T], BF16),
                  ("x1T", [D, T], F32), ("s1T", [D, T], F32)],
           "out": [("out", [T, D], F32)]},
}


class Env:
    def __init__(self, nc, io, outer, pfx=""):
        self.nc = nc
        self.io = io
        self.outer = outer
        self.pfx = pfx


def _standalone_env(stage, outer):
    nc = bass.Bass("TRN2", target_bir_lowering=False)
    io = {}
    for name, shape, dt_ in IO_SPECS[stage]["in"]:
        io[name] = _ios(nc, name, shape, dt_, "ExternalInput")
    for name, shape, dt_ in IO_SPECS[stage]["out"]:
        io[name] = _ios(nc, name, shape, dt_, "ExternalOutput")
    return Env(nc, io, outer)


_STAGES = ["s1", "s2", "s3", "s4", "s5", "s6", "s7"]
_ENGS = ["gpsimd", "sync", "tensor", "vector", "scalar"]


def _barrier_block(nc, bar, wait_val, do_inc):
    with nc.Block() as bb:
        def mk():
            def body(eng):
                if do_inc:
                    eng.nop().then_inc(bar, 1)
                if wait_val > 0:
                    eng.wait_ge(bar, wait_val)
            return body
        for engname in _ENGS:
            getattr(bb, engname)(mk())


def emit_s1(S: Sched, env):
    nc = env.nc
    x = env.io["x"]
    w_in = env.io["w_in"]
    w_a = env.io["w_a"]
    b_inT = env.io["b_inT"]
    b_aT = env.io["b_aT"]
    a_paramT = env.io["a_paramT"]
    xT_d = env.io["xT"]
    xnbf_d = env.io["xnbf"]
    s0_d = env.io["s0T"]

    with ExitStack() as ctx:
        sb = lambda n, s, d: ctx.enter_context(nc.sbuf_tensor(env.pfx + n, s, d))
        psx = lambda n, s, d: ctx.enter_context(nc.psum_tensor(env.pfx + n, s, d))
        sem = lambda n: env.outer.enter_context(nc.semaphore(env.pfx + n))

        xtok = [sb("xtok0", [128, 4096], F32), sb("xtok1", [128, 4096], F32)]
        cN = sb("cN", [128, 4096], F32)
        dX = sb("dX", [128, 4096], F32)
        eG = sb("eG", [128, 4096], F32)
        fA = sb("fA", [128, 4096], F32)
        gM = sb("gM", [128, 4096], F32)
        laX = sb("laX", [128, 8 * 513], F32)
        iL = sb("iL", [128, 4096], F32)
        jS = sb("jS", [128, 4096], F32)
        xnT = sb("xnT", [128, 4096], F32)
        xnbfT = sb("xnbfT", [128, 4096], BF16)
        sqt = sb("sqt", [128, 1024], F32)
        mu_c = sb("mu_c", [128, 4], F32)
        sq_c = sb("sq_c", [128, 4], F32)
        rs_c = sb("rs_c", [128, 4], F32)
        Lcar = sb("Lcar", [128, 8], F32)
        Scar = sb("Scar", [128, 8], F32)
        wbf = sb("wbf", [128, 2048], BF16)
        bcols = sb("bcols", [128, 16], F32)
        negc = sb("negc", [128, 8], F32)
        apc = sb("apc", [128, 8], F32)
        tmp8 = sb("tmp8", [128, 8], F32)
        idf = sb("idf", [128, 128], F32)
        zeros = sb("zeros", [128, 512], F32)

        ps_tr = [psx("ps_tr0", [128, 512], F32), psx("ps_tr1", [128, 512], F32)]
        ps_g = [psx("ps_g0", [128, 512], F32), psx("ps_g1", [128, 512], F32)]

        for k in ["d", "w", "t", "v", "a", "p", "o1", "o2", "o3"]:
            S.reg(k, sem("sem_" + k))
        block = ctx.enter_context(nc.Block())

        @block.gpsimd
        def _(g):
            wbf3 = wbf.rearrange("p (g h j) -> p g h j", g=2, j=128)
            S.inc(g.dma_start(out=wbf3[:, 0], in_=w_in.rearrange("h i j -> i h j")),
                  "w", 16)
            S.inc(g.dma_start(out=wbf3[:, 1], in_=w_a.rearrange("h i j -> i h j")),
                  "w", 16)
            S.inc(g.dma_start(out=bcols[:, 0:8], in_=b_inT), "w", 16)
            S.inc(g.dma_start(out=bcols[:, 8:16], in_=b_aT), "w", 16)
            S.inc(g.dma_start(out=apc[:], in_=a_paramT), "w", 16)
            g.wait_ge(S.sem("w"), 80)
            S.chain(g.memset(idf[:], 1.0), "p")
            S.chain(g.affine_select(idf[:], idf[:], pattern=[[1, 128]], base=0,
                                    channel_multiplier=-1,
                                    compare_op=ALU.is_equal, fill=0.0), "p")
            S.chain(g.memset(zeros[:], 0.0), "p")
            S.chain(g.memset(Lcar[:], 0.0), "p")
            S.chain(g.memset(Scar[:], 0.0), "p")
            S.chain(g.memset(laX[:], 0.0), "p", label="s1_const")

        @block.sync
        def _(sy):
            for b in range(NBLK):
                if b >= 2:
                    S.wait(sy, f"s1_trx_{b - 2}_3_1")
                    S.wait(sy, f"s1_sq_{b - 2}")
                    S.wait(sy, f"s1_ln_{b - 2}_3")
                for tt in range(4):
                    S.inc(sy.dma_start(
                        out=xtok[b % 2][:, tt * 1024:(tt + 1) * 1024],
                        in_=x[(b * 4 + tt) * 128:(b * 4 + tt + 1) * 128, :]),
                        "d", 16)
                S.wait(sy, f"s1_xT_done_{b}")
                for k in range(NT):
                    S.inc(sy.dma_start(
                        out=xT_d[k * 128:(k + 1) * 128, b * 512:(b + 1) * 512],
                        in_=dX[:, k * 512:(k + 1) * 512]), "o1", 16,
                        label=(f"s1_xTst_{b}" if k == NT - 1 else None))
                S.wait(sy, f"s1_xnbf_done_{b}")
                for k in range(NT):
                    S.inc(sy.dma_start(
                        out=xnbf_d[k * 128:(k + 1) * 128, b * 512:(b + 1) * 512],
                        in_=xnbfT[:, k * 512:(k + 1) * 512]), "o2", 16,
                        label=(f"s1_xnst_{b}" if k == NT - 1 else None))
                S.wait(sy, f"s1_h_done_{b}")
                for k in range(NT):
                    S.inc(sy.dma_start(
                        out=s0_d[k * 128:(k + 1) * 128, b * 512:(b + 1) * 512],
                        in_=iL[:, k * 512:(k + 1) * 512]), "o3", 16,
                        label=(f"s1_out_{b}" if k == NT - 1 else None))
            sy.wait_ge(S.sem("o1"), 16 * 8 * NBLK)
            sy.wait_ge(S.sem("o2"), 16 * 8 * NBLK)
            sy.wait_ge(S.sem("o3"), 16 * 8 * NBLK)

        @block.tensor
        def _(t):
            S.wait(t, "s1_const")
            S.wait_cnt(t, "w", 80)
            for b in range(NBLK):
                for tt in range(4):
                    S.wait_cnt(t, "d", 64 * (b + 1))
                    for half in range(2):
                        if tt > 0:
                            S.wait(t, f"s1_trxe_{b}_{tt - 1}_{half}")
                        elif b > 0:
                            S.wait(t, f"s1_trne_{b - 1}_3_{half}")
                        for j in range(4):
                            jj = half * 4 + j
                            ti = t.transpose(ps_tr[half][:, j * 128:(j + 1) * 128],
                                             xtok[b % 2][:, tt * 1024 + jj * 128:
                                                         tt * 1024 + (jj + 1) * 128],
                                             idf[:])
                            S.inc(ti, "t",
                                  label=(f"s1_trx_{b}_{tt}_{half}" if j == 3
                                         else None))
                for tt in range(4):
                    S.wait(t, f"s1_ln_{b}_{tt}")
                    for half in range(2):
                        if tt > 0:
                            S.wait(t, f"s1_trne_{b}_{tt - 1}_{half}")
                        else:
                            S.wait(t, f"s1_trxe_{b}_3_{half}")
                        for j in range(4):
                            jj = half * 4 + j
                            ti = t.transpose(ps_tr[half][:, j * 128:(j + 1) * 128],
                                             cN[:, tt * 1024 + jj * 128:
                                                tt * 1024 + (jj + 1) * 128],
                                             idf[:])
                            S.inc(ti, "t",
                                  label=(f"s1_trn_{b}_{tt}_{half}" if j == 3
                                         else None))
                S.wait(t, f"s1_xnbf_done_{b}")
                for h in range(8):
                    for gi in range(2):
                        if h > 0:
                            S.wait(t, f"s1_sgm_{b}_{h - 1}_{gi}")
                        elif b > 0:
                            S.wait(t, f"s1_sgm_{b - 1}_7_{gi}")
                        S.inc(t.matmul(ps_g[gi][:],
                                       wbf[:, gi * 1024 + h * 128:
                                           gi * 1024 + (h + 1) * 128],
                                       xnbfT[:, h * 512:(h + 1) * 512],
                                       start=True, stop=True),
                              "t", label=f"s1_gate_{b}_{h}_{gi}")

        @block.scalar
        def _(s):
            S.wait_cnt(s, "w", 80)
            S.wait(s, "s1_const")
            S.chain(s.activation(tmp8[:], apc[:], AF.Abs), "a")
            S.chain(s.activation(tmp8[:], tmp8[:], AF.Exp, scale=-1.0), "a")
            S.chain(s.activation(tmp8[:], tmp8[:], AF.Ln, bias=1.0), "a",
                    label="s1_sp1")
            for b in range(NBLK):
                for tt in range(4):
                    S.wait_cnt(s, "d", 64 * (b + 1))
                    S.chain(s.activation(sqt[:],
                                         xtok[b % 2][:, tt * 1024:(tt + 1) * 1024],
                                         AF.Square, accum_out=sq_c[:, tt:tt + 1]),
                            "a", label=(f"s1_sq_{b}" if tt == 3 else None))
                S.wait(s, f"s1_var_{b}")
                S.chain(s.activation(rs_c[:], rs_c[:], AF.Ln), "a")
                S.chain(s.activation(rs_c[:], rs_c[:], AF.Exp, scale=-0.5), "a",
                        label=f"s1_rstd_{b}")
                if b > 0:
                    S.wait(s, f"s1_h_done_{b - 1}")
                for h in range(8):
                    for gi in range(2):
                        S.wait(s, f"s1_gate_{b}_{h}_{gi}")
                        dst = eG if gi == 0 else fA
                        S.chain(s.activation(dst[:, h * 512:(h + 1) * 512],
                                             ps_g[gi][:], AF.Sigmoid,
                                             bias=bcols[:, 8 * gi + h:
                                                        8 * gi + h + 1]),
                                "a", label=f"s1_sgm_{b}_{h}_{gi}")
                S.wait(s, f"s1_la_{b}")
                for k in range(NT):
                    S.chain(s.activation(fA[:, k * 512:(k + 1) * 512],
                                         laX[:, k * 513 + 1:k * 513 + 513],
                                         AF.Exp, scale=2.0), "a",
                            label=(f"s1_a2_{b}" if k == NT - 1 else None))
                S.wait(s, f"s1_m2_{b}")
                S.chain(s.activation(gM[:], gM[:], AF.Ln), "a")
                S.chain(s.activation(gM[:], gM[:], AF.Exp, scale=0.5), "a",
                        label=f"s1_mult_{b}")
                S.wait(s, f"s1_lc_{b}")
                S.wait(s, f"s1_xgm_{b}")
                S.chain(s.activation(gM[:], iL[:], AF.Exp, scale=-1.0), "a",
                        label=f"s1_pinv_{b}")
                S.wait(s, f"s1_w_{b}")
                S.chain(s.activation(eG[:], iL[:], AF.Exp), "a",
                        label=f"s1_p_{b}")

        @block.vector
        def _(v):
            S.wait(v, "s1_sp1")
            S.chain(v.tensor_scalar_max(negc[:], apc[:], 0.0), "v")
            S.chain(v.tensor_add(negc[:], negc[:], tmp8[:]), "v")
            S.chain(v.tensor_scalar(negc[:], negc[:], 1e-3, -1.0,
                                    op0=ALU.add, op1=ALU.mult), "v",
                    label="s1_negc")
            for b in range(NBLK):
                for tt in range(4):
                    S.wait_cnt(v, "d", 64 * (b + 1))
                    S.chain(v.tensor_reduce(mu_c[:, tt:tt + 1],
                                            xtok[b % 2][:, tt * 1024:(tt + 1) * 1024],
                                            axis=AX.X, op=ALU.add), "v")
                S.wait(v, f"s1_sq_{b}")
                S.chain(v.tensor_scalar_mul(mu_c[:], mu_c[:], 1.0 / D), "v")
                S.chain(v.tensor_mul(rs_c[:], mu_c[:], mu_c[:]), "v")
                S.chain(v.scalar_tensor_tensor(rs_c[:], sq_c[:], 1.0 / D, rs_c[:],
                                               op0=ALU.mult, op1=ALU.subtract), "v")
                S.chain(v.tensor_scalar_add(rs_c[:], rs_c[:], 1e-5), "v",
                        label=f"s1_var_{b}")
                S.wait(v, f"s1_rstd_{b}")
                for tt in range(4):
                    S.chain(v.tensor_scalar(cN[:, tt * 1024:(tt + 1) * 1024],
                                            xtok[b % 2][:, tt * 1024:(tt + 1) * 1024],
                                            mu_c[:, tt:tt + 1], rs_c[:, tt:tt + 1],
                                            op0=ALU.subtract, op1=ALU.mult),
                            "v", label=f"s1_ln_{b}_{tt}")
                if b > 0:
                    S.wait(v, f"s1_xTst_{b - 1}")
                for tt in range(4):
                    for half in range(2):
                        S.wait(v, f"s1_trx_{b}_{tt}_{half}")
                        dst = dX.rearrange("p (k c) -> p k c", c=512)
                        S.chain(v.tensor_copy(
                            dst[:, half * 4:half * 4 + 4, tt * 128:(tt + 1) * 128],
                            ps_tr[half].rearrange("p (k c) -> p k c", c=128)),
                            "v", label=f"s1_trxe_{b}_{tt}_{half}")
                S.label(f"s1_xT_done_{b}", "v")
                if b > 0:
                    S.wait(v, f"s1_xnst_{b - 1}")
                    S.wait(v, f"s1_gate_{b - 1}_7_1")
                for tt in range(4):
                    for half in range(2):
                        S.wait(v, f"s1_trn_{b}_{tt}_{half}")
                        dstn = xnT.rearrange("p (k c) -> p k c", c=512)
                        S.chain(v.tensor_copy(
                            dstn[:, half * 4:half * 4 + 4, tt * 128:(tt + 1) * 128],
                            ps_tr[half].rearrange("p (k c) -> p k c", c=128)),
                            "v", label=f"s1_trne_{b}_{tt}_{half}")
                S.chain(v.tensor_copy(xnbfT[:], xnT[:]), "v",
                        label=f"s1_xnbf_done_{b}")
                if b == 0:
                    S.wait(v, "s1_negc")
                else:
                    S.wait(v, f"s1_a2_{b - 1}")
                for k in range(NT):
                    S.wait(v, f"s1_sgm_{b}_{k}_1")
                    S.chain(v.tensor_scalar(laX[:, k * 513 + 1:k * 513 + 513],
                                            fA[:, k * 512:(k + 1) * 512],
                                            negc[:, k:k + 1], L1M,
                                            op0=ALU.mult, op1=ALU.min),
                            "v", label=(f"s1_la_{b}" if k == NT - 1 else None))
                if b > 0:
                    S.wait(v, f"s1_out_{b - 1}")
                for k in range(NT):
                    S.chain(v.tensor_tensor_scan(iL[:, k * 512:(k + 1) * 512],
                                                 laX[:, k * 513:k * 513 + 512],
                                                 zeros[:, 0:512],
                                                 Lcar[:, k:k + 1],
                                                 op0=ALU.add, op1=ALU.add), "v")
                S.chain(v.tensor_copy(
                    Lcar.rearrange("p (k o) -> p k o", o=1),
                    iL.rearrange("p (k c) -> p k c", c=512)[:, :, 511:512]), "v")
                S.chain(v.tensor_copy(
                    laX.rearrange("p (k c) -> p k c", c=513)[:, :, 0:1],
                    laX.rearrange("p (k c) -> p k c", c=513)[:, :, 512:513]), "v")
                S.chain(v.tensor_scalar_max(iL[:], iL[:], LN7), "v",
                        label=f"s1_lc_{b}")
                S.wait(v, f"s1_a2_{b}")
                S.chain(v.tensor_scalar(gM[:], fA[:], -1.0, 1.0,
                                        op0=ALU.mult, op1=ALU.add), "v")
                S.chain(v.tensor_scalar_max(gM[:], gM[:], 1e-5), "v",
                        label=f"s1_m2_{b}")
                S.wait(v, f"s1_sgm_{b}_7_0")
                S.chain(v.tensor_mul(cN[:], xnT[:], eG[:]), "v",
                        label=f"s1_xg_{b}")
                S.wait(v, f"s1_mult_{b}")
                S.chain(v.tensor_mul(eG[:], cN[:], gM[:]), "v",
                        label=f"s1_xgm_{b}")
                S.wait(v, f"s1_pinv_{b}")
                S.chain(v.scalar_tensor_tensor(cN[:], eG[:], 1e-7, gM[:],
                                               op0=ALU.add, op1=ALU.mult),
                        "v", label=f"s1_w_{b}")
                for k in range(NT):
                    S.chain(v.tensor_tensor_scan(jS[:, k * 512:(k + 1) * 512],
                                                 cN[:, k * 512:(k + 1) * 512],
                                                 zeros[:, 0:512],
                                                 Scar[:, k:k + 1],
                                                 op0=ALU.add, op1=ALU.add), "v")
                S.chain(v.tensor_copy(
                    Scar.rearrange("p (k o) -> p k o", o=1),
                    jS.rearrange("p (k c) -> p k c", c=512)[:, :, 511:512]), "v",
                    label=f"s1_scum_{b}")
                S.wait(v, f"s1_p_{b}")
                S.chain(v.tensor_mul(iL[:], eG[:], jS[:]), "v",
                        label=f"s1_h_done_{b}")
    return nc




def emit_s2(S: Sched, env):
    nc = env.nc
    W1T = env.io["W1T"]
    W2T = env.io["W2T"]
    b1c = env.io["b1c"]
    xnbf_d = env.io["xnbf"]
    h1_d = env.io["h1"]
    W2t_d = env.io["W2t"]

    with ExitStack() as ctx:
        sb = lambda n, s_, d_: ctx.enter_context(nc.sbuf_tensor(env.pfx + n, s_, d_))
        psx = lambda n, s_, d_: ctx.enter_context(nc.psum_tensor(env.pfx + n, s_, d_))
        sem = lambda n: env.outer.enter_context(nc.semaphore(env.pfx + n))

        wf = [sb(f"wf{i}", [128, 1024], F32) for i in range(4)]
        wi = sb("wi", [128, 1024], F32)
        sc2 = sb("sc2", [1, 2], F32)
        invc2 = sb("invc2", [128, 2], F32)
        w2s = [sb(f"w2s{i}", [128, 1024], BF16) for i in range(2)]
        wT = [sb(f"wT{i}", [128, 4096], BF16) for i in range(8)]
        xnb = [sb("xnb0", [128, 8 * 512], BF16), sb("xnb1", [128, 8 * 512], BF16)]
        h1s = sb("h1s", [128, 32 * 512], BF16)
        b1s = sb("b1s", [128, 32], F32)
        acc = sb("acc", [128, 1], F32)
        red = sb("red", [128, 1], F32)
        onec = sb("onec", [128, 1], F32)
        onerow = sb("onerow", [1, 128], F32)
        sc1 = sb("sc1", [1, 1], F32)
        invc = sb("invc", [128, 1], F32)
        ps_r = [psx("ps_r0", [128, 512], F32), psx("ps_r1", [128, 512], F32),
                psx("ps_r2", [128, 512], F32)]
        ps_s = psx("ps_s", [1, 1], F32)
        ps_c = psx("ps_c", [128, 2], F32)

        for k in ["d0", "d1", "w", "w0", "w1", "w2", "w3", "t", "v", "a", "p",
                  "o1", "o20", "o21"]:
            S.reg(k, sem("sem_" + k))
        block = ctx.enter_context(nc.Block())

        @block.gpsimd
        def _(g):
            S.inc(g.dma_start(out=b1s[:], in_=b1c), "w", 16)
            def _w1src(k):
                fg, dc = k // 8, k % 8
                return W1T[dc * 128:(dc + 1) * 128,
                           fg * 1024:(fg + 1) * 1024]
            for k in range(32):
                if k >= 4:
                    S.wait(g, f"s2_absred_{k - 4}")
                S.inc(g.dma_start(out=wf[k % 4][:], in_=_w1src(k)),
                      f"w{k % 4}", 16, label=f"s2_wfA_{k}")
            for k in range(32):
                S.wait(g, f"s2_w1t_{k - 4}" if k >= 4
                       else f"s2_absred_{28 + k}")
                S.inc(g.dma_start(out=wf[k % 4][:], in_=_w1src(k)),
                      f"w{k % 4}", 16, label=f"s2_wfB_{k}")
            for dc in range(32):
                if dc >= 4:
                    S.wait(g, f"s2_w2red_{dc - 4}")
                else:
                    S.wait(g, f"s2_w1t_{28 + dc}")
                S.inc(g.dma_start(out=wf[dc % 4][:],
                                  in_=W2T[dc * 128:(dc + 1) * 128, :]),
                      f"w{dc % 4}", 16, label=f"s2_w2A_{dc}")
            for dc in range(32):
                S.wait(g, f"s2_w2t_{dc - 4}" if dc >= 4 else "s2_scale2")
                S.inc(g.dma_start(out=wf[dc % 4][:],
                                  in_=W2T[dc * 128:(dc + 1) * 128, :]),
                      f"w{dc % 4}", 16, label=f"s2_w2B_{dc}")
                if dc >= 2:
                    S.wait(g, f"s2_w2t_{dc - 2}")
                    S.inc(g.dma_start(
                        out=W2t_d[(dc - 2) * 128:(dc - 1) * 128, :],
                        in_=w2s[dc % 2][:]), f"o2{dc % 2}", 16,
                        label=f"s2_w2st_{dc - 2}")
            for dc in (30, 31):
                S.wait(g, f"s2_w2t_{dc}")
                S.inc(g.dma_start(out=W2t_d[dc * 128:(dc + 1) * 128, :],
                                  in_=w2s[dc % 2][:]), f"o2{dc % 2}", 16,
                      label=f"s2_w2st_{dc}")

        @block.sync
        def _(sy):
            for b in range(NBLK):
                if b >= 2:
                    S.wait(sy, f"s2_mmdone_{b - 2}")
                S.inc(sy.dma_start(
                    out=xnb[b % 2].rearrange("p (k c) -> p k c", c=512),
                    in_=xnbf_d.rearrange("(k p) t -> p k t", p=128)[:, :, b * 512:(b + 1) * 512]),
                    f"d{b % 2}", 16, label=f"s2_xnb_{b}")
                if b >= 1:
                    S.wait(sy, f"s2_h1done_{b - 1}")
                    S.inc(sy.dma_start(
                        out=h1_d.rearrange("(f p) t -> p f t", p=128)[:, :, (b - 1) * 512:b * 512],
                        in_=h1s.rearrange("p (f c) -> p f c", c=512)),
                        "o1", 16, label=f"s2_h1st_{b - 1}")
            S.wait(sy, f"s2_h1done_{NBLK - 1}")
            S.inc(sy.dma_start(
                out=h1_d.rearrange("(f p) t -> p f t", p=128)[:, :, (NBLK - 1) * 512:NBLK * 512],
                in_=h1s.rearrange("p (f c) -> p f c", c=512)),
                "o1", 16, label=f"s2_h1st_{NBLK - 1}")
            sy.wait_ge(S.sem("o1"), 16 * NBLK)
            sy.wait_ge(S.sem("o20"), 16 * 16)
            sy.wait_ge(S.sem("o21"), 16 * 16)

        @block.vector
        def _(v):
            S.chain(v.memset(acc[:], 0.0), "v")
            S.chain(v.memset(onec[:], 1.0), "v")
            S.chain(v.memset(onerow[:], 1.0), "v", label="s2_ones")
            for k in range(32):
                S.wait(v, f"s2_wfA_{k}")
                S.chain(v.tensor_reduce(red[:], wf[k % 4][:], axis=AX.X,
                                        op=ALU.add, apply_absolute_value=True), "v")
                S.chain(v.tensor_add(acc[:], acc[:], red[:]), "v",
                        label=f"s2_absred_{k}")
            S.wait(v, "s2_total")
            S.chain(v.tensor_copy(sc2[0:1, 0:1], ps_s[:]), "v")
            S.chain(v.tensor_scalar(sc2[0:1, 0:1], sc2[0:1, 0:1],
                                    0.5 / (4096 * 1024), 0.5e-8,
                                    op0=ALU.mult, op1=ALU.max), "v")
            S.chain(v.tensor_scalar_mul(sc2[0:1, 1:2], sc2[0:1, 0:1],
                                        -1.0), "v", label="s2_inv")
            S.wait(v, "s2_invb")
            S.chain(v.tensor_copy(invc2[:], ps_c[:, 0:2]), "v",
                    label="s2_scale")
            for k in range(32):
                fg, dc = k // 8, k % 8
                S.wait(v, f"s2_wfB_{k}")
                S.chain(v.tensor_scalar(wi[:], wf[k % 4][:],
                                        invc2[:, 1:2], None,
                                        op0=ALU.is_lt), "v")
                S.chain(v.tensor_scalar(wf[k % 4][:], wf[k % 4][:],
                                        invc2[:, 0:1], None,
                                        op0=ALU.is_gt), "v")
                S.chain(v.tensor_sub(wT[dc][:, fg * 1024:(fg + 1) * 1024],
                                     wf[k % 4][:], wi[:]), "v",
                        label=f"s2_w1t_{k}")
            S.chain(v.memset(acc[:], 0.0), "v")
            for dc in range(32):
                S.wait(v, f"s2_w2A_{dc}")
                S.chain(v.tensor_reduce(red[:], wf[dc % 4][:], axis=AX.X,
                                        op=ALU.add, apply_absolute_value=True), "v")
                S.chain(v.tensor_add(acc[:], acc[:], red[:]), "v",
                        label=f"s2_w2red_{dc}")
            S.wait(v, "s2_total2")
            S.chain(v.tensor_copy(sc1[:], ps_s[:]), "v")
            S.chain(v.tensor_scalar(sc1[:], sc1[:], 1.0 / (4096 * 1024), 1e-8,
                                    op0=ALU.mult, op1=ALU.max), "v")
            S.chain(v.reciprocal(sc1[:], sc1[:]), "v", label="s2_inv2")
            S.wait(v, "s2_invb2")
            S.chain(v.tensor_copy(invc[:], ps_c[:, 0:1]), "v",
                    label="s2_scale2")
            for dc in range(32):
                S.wait(v, f"s2_w2B_{dc}")
                if dc >= 2:
                    S.wait(v, f"s2_w2st_{dc - 2}")
                S.chain(v.tensor_scalar_mul(wf[dc % 4][:],
                                            wf[dc % 4][:],
                                            invc[:, 0:1]), "v")
                S.chain(v.tensor_scalar(wi[:], wf[dc % 4][:],
                                        -0.5, None, op0=ALU.is_lt), "v")
                S.chain(v.tensor_scalar(wf[dc % 4][:],
                                        wf[dc % 4][:], 0.5, None,
                                        op0=ALU.is_gt), "v")
                S.chain(v.tensor_sub(w2s[dc % 2][:], wf[dc % 4][:],
                                     wi[:]), "v",
                        label=f"s2_w2t_{dc}")

        @block.tensor
        def _(t):
            S.wait(t, "s2_absred_31")
            S.inc(t.matmul(ps_s[:], onec[:], acc[:], start=True, stop=True),
                  "t", label="s2_total")
            S.wait(t, "s2_inv")
            S.wait(t, "s2_ones")
            S.inc(t.matmul(ps_c[:, 0:2], onerow[:], sc2[:], start=True,
                           stop=True), "t", label="s2_invb")
            S.wait_cnt(t, "w", 16)
            for b in range(NBLK):
                if b == 3:
                    S.wait(t, "s2_w2red_31")
                    S.wait(t, "s2_scale")
                    S.inc(t.matmul(ps_s[:], onec[:], acc[:], start=True,
                                   stop=True), "t", label="s2_total2")
                if b == 4:
                    S.wait(t, "s2_inv2")
                    S.inc(t.matmul(ps_c[:, 0:1], onerow[:], sc1[:], start=True,
                                   stop=True), "t", label="s2_invb2")
                S.wait(t, f"s2_xnb_{b}")
                for ft in range(32):
                    if b == 0 and ft % 8 == 0:
                        S.wait(t, f"s2_w1t_{(ft // 8) * 8 + 7}")
                    ps = ps_r[ft % 3]
                    if ft >= 3:
                        S.wait(t, f"s2_gelu_{b}_{ft - 3}")
                    elif b > 0:
                        prevu = max(x for x in range(32) if x % 3 == ft % 3)
                        S.wait(t, f"s2_gelu_{b - 1}_{prevu}")
                    for dc in range(8):
                        S.inc(t.matmul(ps[:], wT[dc][:, ft * 128:(ft + 1) * 128],
                                       xnb[b % 2][:, dc * 512:(dc + 1) * 512],
                                       start=(dc == 0), stop=(dc == 7)),
                              "t", label=(f"s2_mm_{b}_{ft}" if dc == 7 else None))
                S.label(f"s2_mmdone_{b}", "t")

        @block.scalar
        def _(s):
            for b in range(NBLK):
                if b > 0:
                    S.wait(s, f"s2_h1st_{b - 1}")
                for ft in range(32):
                    ps = ps_r[ft % 3]
                    S.wait(s, f"s2_mm_{b}_{ft}")
                    S.chain(s.activation(h1s[:, ft * 512:(ft + 1) * 512], ps[:],
                                         AF.Gelu, bias=b1s[:, ft:ft + 1]),
                            "a", label=f"s2_gelu_{b}_{ft}")
                S.label(f"s2_h1done_{b}", "a")
    return nc


def build_s2():
    S = Sched()
    nc = None
    for p2 in (False, True):
        S.begin(p2)
        with ExitStack() as outer:
            env = _standalone_env("s2", outer)
            emit_s2(S, env)
            nc = env.nc
    return nc


def emit_s3(S: Sched, env):
    nc = env.nc
    W2t_d = env.io["W2t"]
    b2c = env.io["b2c"]
    h1_d = env.io["h1"]
    xT_d = env.io["xT"]
    s0_d = env.io["s0T"]
    x1_d = env.io["x1T"]

    with ExitStack() as ctx:
        sb = lambda n, s_, d_: ctx.enter_context(nc.sbuf_tensor(env.pfx + n, s_, d_))
        psx = lambda n, s_, d_: ctx.enter_context(nc.psum_tensor(env.pfx + n, s_, d_))
        sem = lambda n: env.outer.enter_context(nc.semaphore(env.pfx + n))

        wT = [sb(f"wT{i}", [128, 1024], BF16) for i in range(32)]
        h1b = [sb("h1b0", [128, 32 * 512], BF16), sb("h1b1", [128, 32 * 512], BF16)]
        xb = [sb("xb0", [128, 8 * 512], F32), sb("xb1", [128, 8 * 512], F32)]
        s0b = [sb("s0b0", [128, 8 * 512], F32), sb("s0b1", [128, 8 * 512], F32)]
        gb = sb("gb", [128, 2 * 512], F32)
        b2s = sb("b2s", [128, 8], F32)
        ps_r = [psx("ps_r0", [128, 512], F32), psx("ps_r1", [128, 512], F32),
                psx("ps_r2", [128, 512], F32), psx("ps_r3", [128, 512], F32)]

        for k in ["d0", "d1", "d20", "d21", "w", "w0", "w1", "w2", "w3", "t",
                  "v", "a", "p", "o10", "o11"]:
            S.reg(k, sem("sem_" + k))
        block = ctx.enter_context(nc.Block())

        @block.gpsimd
        def _(g):
            for dc in range(32):
                S.inc(g.dma_start(out=wT[dc][:], in_=W2t_d[dc * 128:(dc + 1) * 128, :]),
                      "w0", 16)
            S.inc(g.dma_start(out=b2s[:], in_=b2c), "w", 16)
            for b in range(NBLK):
                if b >= 2:
                    S.wait(g, f"s3_comb_{b - 2}")
                    S.wait(g, f"s3_x1st_{b - 2}")
                S.inc(g.dma_start(
                    out=xb[b % 2].rearrange("p (k c) -> p k c", c=512),
                    in_=xT_d.rearrange("(k p) t -> p k t", p=128)[:, :, b * 512:(b + 1) * 512]),
                    f"d2{b % 2}", 16)
                S.inc(g.dma_start(
                    out=s0b[b % 2].rearrange("p (k c) -> p k c", c=512),
                    in_=s0_d.rearrange("(k p) t -> p k t", p=128)[:, :, b * 512:(b + 1) * 512]),
                    f"d2{b % 2}", 16, label=f"s3_xs_{b}")

        @block.sync
        def _(sy):
            for b in range(NBLK):
                if b >= 2:
                    S.wait(sy, f"s3_mmdone_{b - 2}")
                S.inc(sy.dma_start(
                    out=h1b[b % 2].rearrange("p (f c) -> p f c", c=512),
                    in_=h1_d.rearrange("(f p) t -> p f t", p=128)[:, :, b * 512:(b + 1) * 512]),
                    f"d{b % 2}", 16, label=f"s3_h1in_{b}")
                if b >= 1:
                    S.wait(sy, f"s3_comb_{b - 1}")
                    S.inc(sy.dma_start(
                        out=x1_d.rearrange("(k p) t -> p k t", p=128)[:, :, (b - 1) * 512:b * 512],
                        in_=xb[(b - 1) % 2].rearrange("p (k c) -> p k c", c=512)),
                        f"o1{(b - 1) % 2}", 16, label=f"s3_x1st_{b - 1}")
            S.wait(sy, f"s3_comb_{NBLK - 1}")
            S.inc(sy.dma_start(
                out=x1_d.rearrange("(k p) t -> p k t", p=128)[:, :, (NBLK - 1) * 512:NBLK * 512],
                in_=xb[(NBLK - 1) % 2].rearrange("p (k c) -> p k c", c=512)),
                f"o1{(NBLK - 1) % 2}", 16, label=f"s3_x1st_{NBLK - 1}")
            sy.wait_ge(S.sem("o10"), 16 * (NBLK // 2))
            sy.wait_ge(S.sem("o11"), 16 * (NBLK // 2))

        @block.vector
        def _(v):
            for b in range(NBLK):
                S.wait(v, f"s3_xs_{b}")
                for do in range(8):
                    S.wait(v, f"s3_silu_{b}_{do}")
                    sl = slice(do * 512, (do + 1) * 512)
                    gsl = slice((do % 2) * 512, (do % 2 + 1) * 512)
                    S.chain(v.tensor_mul(gb[:, gsl], s0b[b % 2][:, sl],
                                         gb[:, gsl]), "v")
                    S.chain(v.tensor_add(xb[b % 2][:, sl], xb[b % 2][:, sl],
                                         gb[:, gsl]), "v",
                            label=(f"s3_comb_{b}" if do == 7 else
                                   f"s3_combp_{b}_{do}"))

        @block.tensor
        def _(t):
            S.wait_cnt(t, "w0", 16 * 32)
            for b in range(NBLK):
                S.wait(t, f"s3_h1in_{b}")
                for do in range(8):
                    ps = ps_r[do % 4]
                    prev = do - 4 if do >= 4 else (4 + do if b > 0 else -1)
                    if prev >= 0:
                        S.wait(t, f"s3_silu_{b if do >= 4 else b - 1}_{prev}")
                    for fc in range(32):
                        S.inc(t.matmul(ps[:], wT[fc][:, do * 128:(do + 1) * 128],
                                       h1b[b % 2][:, fc * 512:(fc + 1) * 512],
                                       start=(fc == 0), stop=(fc == 31)),
                              "t", label=(f"s3_mm_{b}_{do}" if fc == 31 else None))
                S.label(f"s3_mmdone_{b}", "t")

        @block.scalar
        def _(s):
            S.wait_cnt(s, "w", 16)
            for b in range(NBLK):
                if b > 0:
                    S.wait(s, f"s3_comb_{b - 1}")
                for do in range(8):
                    ps = ps_r[do % 4]
                    S.wait(s, f"s3_mm_{b}_{do}")
                    if do >= 2:
                        S.wait(s, f"s3_combp_{b}_{do - 2}")
                    elif b > 0:
                        S.wait(s, f"s3_combp_{b - 1}_{6 + do}" if do == 0
                               else f"s3_comb_{b - 1}")
                    S.chain(s.activation(gb[:, (do % 2) * 512:(do % 2 + 1) * 512],
                                         ps[:], AF.Silu, bias=b2s[:, do:do + 1]),
                            "a", label=f"s3_silu_{b}_{do}")
    return nc


def build_s3():
    S = Sched()
    nc = None
    for p2 in (False, True):
        S.begin(p2)
        with ExitStack() as outer:
            env = _standalone_env("s3", outer)
            emit_s3(S, env)
            nc = env.nc
    return nc


def emit_s4(S: Sched, env):
    nc = env.nc
    x1_d = env.io["x1T"]
    WinT = env.io["WinT"]
    bz_c = env.io["bz_c"]
    bxs_r = env.io["bxs_r"]
    bbc_c = env.io["bbc_c"]
    bbc_r = env.io["bbc_r"]
    bdt_c = env.io["bdt_c"]
    dtb_c = env.io["dtb_c"]
    Alog_c = env.io["Alog_c"]
    Dsk_r = env.io["Dsk_r"]
    gam_c = env.io["gam_c"]
    bet_c = env.io["bet_c"]
    xn1_d = env.io["xn1T"]
    yf_d = env.io["yfinT"]

    with ExitStack() as ctx:
        sb = lambda n, s_, d_: ctx.enter_context(nc.sbuf_tensor(env.pfx + n, s_, d_))
        psx = lambda n, s_, d_: ctx.enter_context(nc.psum_tensor(env.pfx + n, s_, d_))
        sem = lambda n: env.outer.enter_context(nc.semaphore(env.pfx + n))

        winb = [sb(f"winb{i}", [128, 4232], BF16) for i in range(8)]
        _x1b0 = sb("x1b0", [128, 4096], F32)
        x1b = [_x1b0, _x1b0]
        xnT = _x1b0
        ubf = sb("ubf", [128, 4096], BF16)
        zsil = sb("zsil", [128, 16 * 512], BF16)
        onec_bf = sb("onec_bf", [128, 1], BF16)
        stg_x = sb("stg_x", [128, 1024], BF16)
        stg_q = sb("stg_q", [128, 1024], BF16)
        xst = sb("xst", [128, 4 * 2048], BF16)
        dtxt = sb("dtxt", [128, 4 * 2048], BF16)
        yfin = sb("yfin", [128, 16 * 512], BF16)
        Bf = sb("Bf", [64, 512], BF16)
        Cf = sb("Cf", [64, 512], BF16)
        Cx8 = sb("Cx8", [64, 8 * 128], BF16)
        Btok = sb("Btok", [128, 4 * 64], BF16)
        Bw8 = sb("Bw8", [128, 8 * 64], BF16)
        Msb8 = sb("Msb8", [128, 8 * 128], BF16)
        Mt = sb("Mt", [128, 128], F32)
        BCm = sb("BCm", [128, 128], BF16)
        sct8 = sb("sct8", [128, 8 * 128], BF16)
        dtr = sb("dtr", [8, 512], F32)
        t8a = sb("t8a", [8, 512], F32)
        eLe = t8a
        Lrow = sb("Lrow", [8, 512], F32)
        eLp = sb("eLp", [8, 512], F32)
        smallTT = sb("smallTT", [128, 4 * 24], F32)
        erow = sb("erow", [1, 4 * 8], F32)
        dsk_s = sb("dsk_s", [1, 8], F32)
        dcol = sb("dcol", [64, 4 * 8], F32)
        stf = sb("stf", [64, 2048], F32)
        stb = sb("stb", [64, 2048], BF16)
        IDh = sb("IDh", [128, 8 * 128], BF16)
        idf = sb("idf", [128, 128], F32)
        idb = sb("idb", [128, 128], BF16)
        trib = sb("trib", [128, 128], BF16)
        oneh8 = sb("oneh8", [8, 8 * 128], F32)
        onec = sb("onec", [128, 1], F32)
        onerow = sb("onerow", [1, 128], F32)
        onerow_bf = sb("onerow_bf", [1, 128], BF16)
        bxs_bf = sb("bxs_bf", [1, 2048], BF16)
        bbc_rb = sb("bbc_rb", [1, 128], BF16)
        murow = sb("murow", [1, 512], F32)
        vrow = sb("vrow", [1, 512], F32)
        mrow2 = sb("mrow2", [1, 512], F32)
        bsum = sb("bsum", [8, 1], F32)
        dtbs = sb("dtbs", [8, 1], F32)
        negA = sb("negA", [8, 1], F32)
        Dcols = sb("Dcols", [128, 8], F32)
        bzs = sb("bzs", [128, 16], F32)
        bbcs = sb("bbcs", [128, 1], F32)
        gams = sb("gams", [128, 8], F32)
        bets = sb("bets", [128, 8], F32)

        ps_io = [psx("ps_io0", [128, 512], F32), psx("ps_io1", [128, 512], F32)]
        ps_stat = psx("ps_stat", [128, 512], F32)
        ps_bc2 = psx("ps_bc2", [128, 512], F32)
        ps_sc = psx("ps_sc", [128, 512], F32)
        ps_y = psx("ps_y", [128, 512], F32)
        ps_stu = psx("ps_stu", [64, 512], F32)
        ps_mis = psx("ps_mis", [128, 512], F32)

        for k in ["dx", "w", "wc", "t", "v", "a", "p", "o1", "o2"]:
            S.reg(k, sem("sem_" + k))
        block = ctx.enter_context(nc.Block())
        NCH = 4

        @block.gpsimd
        def _(g):
            for dc in range(8):
                S.inc(g.dma_start(out=winb[dc][:],
                                  in_=WinT[dc * 128:(dc + 1) * 128, :]), "w", 16)
            S.inc(g.dma_start(out=bzs[:], in_=bz_c), "wc", 16)
            S.inc(g.dma_start(out=bxs_bf[:], in_=bxs_r), "wc", 16)
            S.inc(g.dma_start(out=bbcs[:], in_=bbc_c), "wc", 16)
            S.inc(g.dma_start(out=bbc_rb[:], in_=bbc_r), "wc", 16)
            S.inc(g.dma_start(out=bsum[:], in_=bdt_c), "wc", 16)
            S.inc(g.dma_start(out=dtbs[:], in_=dtb_c), "wc", 16)
            S.inc(g.dma_start(out=negA[:], in_=Alog_c), "wc", 16)
            S.inc(g.dma_start(out=dsk_s[:], in_=Dsk_r), "wc", 16)
            S.inc(g.dma_start(out=gams[:], in_=gam_c), "wc", 16)
            S.inc(g.dma_start(out=bets[:], in_=bet_c), "wc", 16)
            g.wait_ge(S.sem("wc"), 16 * 10)
            S.chain(g.memset(idf[:], 1.0), "p")
            S.chain(g.affine_select(idf[:], idf[:], pattern=[[1, 128]], base=0,
                                    channel_multiplier=-1,
                                    compare_op=ALU.is_equal, fill=0.0), "p")
            S.chain(g.tensor_copy(idb[:], idf[:]), "p")
            S.chain(g.memset(trib[:], 1.0), "p")
            S.chain(g.affine_select(trib[:], trib[:], pattern=[[1, 128]], base=0,
                                    channel_multiplier=-1,
                                    compare_op=ALU.is_ge, fill=0.0), "p")
            S.chain(g.memset(oneh8[:], 1.0), "p")
            S.chain(g.affine_select(oneh8[:], oneh8[:], pattern=[[1, 8], [0, 128]],
                                    base=0, channel_multiplier=-1,
                                    compare_op=ALU.is_equal, fill=0.0), "p")
            S.chain(g.memset(onec[:], 1.0), "p")
            S.chain(g.memset(onec_bf[:], 1.0), "p")
            S.chain(g.memset(onerow[:], 1.0), "p")
            S.chain(g.memset(onerow_bf[:], 1.0), "p")
            S.chain(g.memset(stf[:], 0.0), "p")
            S.chain(g.memset(stb[:], 0.0), "p", label="s4_const")

        @block.sync
        def _(sy):
            for b in range(NBLK):
                if b >= 1:
                    S.wait(sy, f"s4_xnst_{b - 1}")
                    S.wait(sy, f"s4_uc_{b - 1}_{NT - 1}")
                    S.wait(sy, f"s4_sq2_{b - 1}_{NT - 1}")
                S.inc(sy.dma_start(
                    out=x1b[b % 2].rearrange("p (k c) -> p k c", c=512),
                    in_=x1_d.rearrange("(k p) t -> p k t", p=128)[:, :, b * 512:(b + 1) * 512]),
                    "dx", 16, label=f"s4_x1_{b}")
                if b >= 1:
                    S.wait(sy, f"s4_yf_done_{b - 1}")
                    S.inc(sy.dma_start(
                        out=yf_d.rearrange("(f p) t -> p f t", p=128)[:, :, (b - 1) * 512:b * 512],
                        in_=yfin.rearrange("p (f c) -> p f c", c=512)),
                        "o2", 16, label=f"s4_yfst_{b - 1}")
                S.wait(sy, f"s4_xn_done_{b}")
                S.inc(sy.dma_start(
                    out=xn1_d.rearrange("(k p) t -> p k t", p=128)[:, :, b * 512:(b + 1) * 512],
                    in_=xnT.rearrange("p (k c) -> p k c", c=512)),
                    "o1", 16, label=f"s4_xnst_{b}")
            S.wait(sy, f"s4_yf_done_{NBLK - 1}")
            S.inc(sy.dma_start(
                out=yf_d.rearrange("(f p) t -> p f t", p=128)[:, :, (NBLK - 1) * 512:NBLK * 512],
                in_=yfin.rearrange("p (f c) -> p f c", c=512)),
                "o2", 16, label=f"s4_yfst_{NBLK - 1}")
            sy.wait_ge(S.sem("o1"), 16 * NBLK)
            sy.wait_ge(S.sem("o2"), 16 * NBLK)

        @block.tensor
        def _(t):
            S.wait(t, "s4_const")
            S.wait_cnt(t, "w", 16 * 8)
            S.inc(t.matmul(ps_mis[:, 0:8], onerow[:], dsk_s[:],
                           start=True, stop=True), "t", label="s4_dskb")
            for b in range(NBLK):
                S.wait(t, f"s4_x1_{b}")
                if b > 0:
                    S.wait(t, f"s4_uc_{b - 1}_{NT - 1}")
                    S.wait(t, f"s4_m_{b - 1}_{NCH - 1}_7")
                for dc in range(NT):
                    S.wait(t, f"s4_cp1_{b}_{dc}")
                    S.inc(t.matmul(ps_stat[0:1, :], onec_bf[:],
                                   stg_x[:, (dc % 2) * 512:(dc % 2 + 1) * 512],
                                   start=(dc == 0), stop=(dc == NT - 1)),
                          "t", label=f"s4_summm1_{b}_{dc}")
                for dc in range(NT):
                    S.wait(t, f"s4_sq1_{b}_{dc}")
                    S.inc(t.matmul(ps_stat[32:33, :], onec_bf[:],
                                   stg_q[:, (dc % 2) * 512:(dc % 2 + 1) * 512],
                                   start=(dc == 0), stop=(dc == NT - 1)),
                          "t", label=f"s4_sqmm1_{b}_{dc}")
                S.wait(t, f"s4_rows1v_{b}")
                S.inc(t.matmul(ps_stat[:, :], onerow[:], murow[:],
                               start=True, stop=True), "t", label=f"s4_mub_{b}")
                S.wait(t, f"s4_rows1a_{b}")
                S.inc(t.matmul(ps_bc2[:, :], onerow[:], mrow2[:],
                               start=True, stop=True), "t", label=f"s4_rb_{b}")
                S.wait(t, f"s4_xn_done_{b}")
                for dc in range(NT):
                    S.wait(t, f"s4_cp2_{b}_{dc}")
                    S.inc(t.matmul(ps_stat[0:1, :], onec_bf[:],
                                   stg_x[:, (dc % 2) * 512:(dc % 2 + 1) * 512],
                                   start=(dc == 0), stop=(dc == NT - 1)),
                          "t", label=f"s4_summm2_{b}_{dc}")
                for dc in range(NT):
                    S.wait(t, f"s4_sq2_{b}_{dc}")
                    S.inc(t.matmul(ps_stat[32:33, :], onec_bf[:],
                                   stg_q[:, (dc % 2) * 512:(dc % 2 + 1) * 512],
                                   start=(dc == 0), stop=(dc == NT - 1)),
                          "t", label=f"s4_sqmm2_{b}_{dc}")
                S.label(f"s4_sum2_{b}", "t")
                S.wait(t, f"s4_rows2v_{b}")
                S.inc(t.matmul(ps_stat[:, :], onerow[:], murow[:],
                               start=True, stop=True), "t", label=f"s4_mub2_{b}")
                S.wait(t, f"s4_rows2a_{b}")
                S.inc(t.matmul(ps_bc2[:, :], onerow[:], mrow2[:],
                               start=True, stop=True), "t", label=f"s4_rb2_{b}")
                if b > 0:
                    S.wait(t, f"s4_stup_{b - 1}_{NCH - 1}_7")
                    S.wait(t, f"s4_ye_{b - 1}_{NCH - 1}_7")
                    S.wait(t, f"s4_dte_{b - 1}")
                    S.wait(t, f"s4_bctok_{b - 1}_3")
                    S.wait(t, f"s4_sct8_{b - 1}_3")
                for dc in range(NT):
                    S.wait(t, f"s4_ubf_{b}_{dc}")
                    S.inc(t.matmul(ps_io[1][0:8, 0:512],
                                   winb[dc][:, 4224:4232],
                                   ubf[:, dc * 512:(dc + 1) * 512],
                                   start=(dc == 0), stop=(dc == NT - 1)),
                          "t", label=(f"s4_dtmm_{b}" if dc == NT - 1 else None))
                for dc in range(NT):
                    S.inc(t.matmul(ps_sc[:, 0:512],
                                   winb[dc][:, 4096:4224],
                                   ubf[:, dc * 512:(dc + 1) * 512],
                                   start=(dc == 0), stop=(dc == NT - 1)),
                          "t", label=(f"s4_bcmm_{b}" if dc == NT - 1 else None))
                for tt in range(4):
                    if tt >= 2:
                        S.wait(t, f"s4_bctok_{b}_{tt - 2}")
                    elif tt == 1:
                        S.wait(t, f"s4_dte_{b}")
                    for dc in range(NT):
                        S.inc(t.matmul(
                            ps_io[tt % 2][:, 0:128],
                            ubf[:, dc * 512 + tt * 128:dc * 512 + (tt + 1) * 128],
                            winb[dc][:, 4096:4224],
                            start=(dc == 0), stop=False), "t")
                    S.inc(t.matmul(ps_io[tt % 2][:, 0:128], onerow_bf[:],
                                   bbc_rb[:], start=False, stop=True), "t",
                          label=f"s4_bctokmm_{b}_{tt}")
                for ft in range(16):
                    if ft >= 2:
                        S.wait(t, f"s4_zsil_{b}_{ft - 2}")
                    elif ft == 0:
                        S.wait(t, f"s4_bctok_{b}_2")
                    else:
                        S.wait(t, f"s4_bctok_{b}_3")
                    for dc in range(NT):
                        S.inc(t.matmul(ps_io[ft % 2][:],
                                       winb[dc][:, ft * 128:(ft + 1) * 128],
                                       ubf[:, dc * 512:(dc + 1) * 512],
                                       start=(dc == 0), stop=(dc == NT - 1)),
                              "t", label=(f"s4_zmm_{b}_{ft}" if dc == NT - 1
                                          else None))
                for tt in range(4):
                    for fb in range(4):
                        par = (tt * 4 + fb) % 2
                        pidx = tt * 4 + fb
                        if pidx >= 2:
                            pt, pf = divmod(pidx - 2, 4)
                            S.wait(t, f"s4_xse_{b}_{pt}_{pf}")
                        else:
                            S.wait(t, f"s4_zsil_{b}_{14 + pidx}")
                        for dc in range(NT):
                            S.inc(t.matmul(
                                ps_io[par][:, 0:512],
                                ubf[:, dc * 512 + tt * 128:dc * 512 + (tt + 1) * 128],
                                winb[dc][:, 2048 + fb * 512:2048 + (fb + 1) * 512],
                                start=(dc == 0), stop=False), "t")
                        S.inc(t.matmul(ps_io[par][:, 0:512], onerow_bf[:],
                                       bxs_bf[0:1, fb * 512:(fb + 1) * 512],
                                       start=False, stop=True), "t",
                              label=(f"s4_xsmm_{b}_{tt}_{fb}"))
                S.label(f"s4_inproj_{b}", "t")
                S.wait(t, f"s4_eL_{b}")
                for ch in range(NCH):
                    if ch > 0:
                        S.wait(t, f"s4_smallevac_{b}_{ch - 1}")
                    elif b > 0:
                        S.wait(t, f"s4_smallevac_{b - 1}_{NCH - 1}")
                        S.wait(t, f"s4_cxg_{b - 1}_{NCH - 1}_1")
                    else:
                        S.wait(t, "s4_idh")
                    S.inc(t.transpose(ps_mis[:, 0:8],
                                      dtr[:, ch * 128:(ch + 1) * 128],
                                      idf[0:8, 0:8]), "t")
                    S.inc(t.transpose(ps_mis[:, 8:16],
                                      Lrow[:, ch * 128:(ch + 1) * 128],
                                      idf[0:8, 0:8]), "t")
                    S.inc(t.transpose(ps_mis[:, 16:24],
                                      eLe[:, ch * 128:(ch + 1) * 128],
                                      idf[0:8, 0:8]), "t")
                    S.inc(t.transpose(ps_mis[0:1, 256:264],
                                      eLp[:, ch * 128 + 127:ch * 128 + 128],
                                      idf[0:8, 0:8]),
                          "t", label=f"s4_str_{b}_{ch}")
                for ch in range(NCH):
                    S.wait(t, f"s4_smallevac_{b}_{ch}")
                    if ch > 0:
                        S.wait(t, f"s4_dcole_{b}_{ch - 1}")
                    elif b > 0:
                        S.wait(t, f"s4_stup_{b - 1}_{NCH - 1}_7")
                    S.inc(t.matmul(ps_stu[:, 256:264], onerow[0:1, 0:64],
                                   erow[0:1, ch * 8:(ch + 1) * 8],
                                   start=True, stop=True), "t",
                          label=f"s4_dcb_{b}_{ch}")
                for ch in range(NCH):
                    S.wait(t, f"s4_bce_{b}")
                    if ch > 0:
                        S.wait(t, f"s4_bcmsk_{b}_{ch - 1}")
                    S.inc(t.matmul(ps_sc[:, 0:128], Bf[:, ch * 128:(ch + 1) * 128],
                                   Cf[:, ch * 128:(ch + 1) * 128],
                                   start=True, stop=True), "t",
                          label=f"s4_scmm_{b}_{ch}")
                    for h in range(8):
                        bank = ps_bc2 if h % 2 == 0 else ps_stat
                        prevm = (f"s4_m_{b}_{ch}_{h - 2}" if h >= 2 else
                                 (f"s4_m_{b}_{ch - 1}_{6 + h}" if ch > 0 else
                                  (f"s4_m_{b - 1}_{NCH - 1}_{6 + h}" if b > 0
                                   else None)))
                        if prevm:
                            S.wait(t, prevm)
                        S.inc(t.matmul(bank[:, 0:128],
                                       oneh8[:, h * 128:(h + 1) * 128],
                                       Lrow[:, ch * 128:(ch + 1) * 128],
                                       start=True, stop=True), "t",
                              label=f"s4_lb_{b}_{ch}_{h}")
                    for grp in range(2):
                        if ch == 0 and grp == 0:
                            S.wait(t, f"s4_smallevac_{b}_{NCH - 1}")
                        prevcx = (f"s4_cxg_{b}_{ch}_{grp - 1}" if grp > 0 else
                                  (f"s4_cxg_{b}_{ch - 1}_1" if ch > 0 else
                                   (f"s4_cxg_{b - 1}_{NCH - 1}_1" if b > 0 else None)))
                        if prevcx:
                            S.wait(t, prevcx)
                        for hh in range(4):
                            h = grp * 4 + hh
                            S.inc(t.matmul(ps_mis[:, hh * 128:(hh + 1) * 128],
                                           oneh8[:, h * 128:(h + 1) * 128],
                                           eLp[:, ch * 128:(ch + 1) * 128],
                                           start=True, stop=True), "t",
                                  label=(f"s4_epbg_{b}_{ch}_{grp}" if hh == 3
                                         else None))
                    S.wait(t, f"s4_sct8_{b}_{ch}")
                    S.wait(t, f"s4_cxg_{b}_{ch}_1")
                    S.wait(t, f"s4_bw8_{b}_{ch}")
                    for h in range(8):
                        ybank = ps_y if h % 2 == 0 else ps_io[0]
                        prevye = (f"s4_ye_{b}_{ch}_{h - 2}" if h > 1 else
                                  (f"s4_ye_{b}_{ch - 1}_{6 + h}" if ch > 0 else
                                   (f"s4_ye_{b - 1}_{NCH - 1}_{6 + h}" if b > 0
                                    else None)))
                        if prevye:
                            S.wait(t, prevye)
                        if ch > 0:
                            S.wait(t, f"s4_stup_{b}_{ch - 1}_{h}")
                        elif b > 0:
                            S.wait(t, f"s4_stup_{b - 1}_{NCH - 1}_{h}")
                        for half in range(2):
                            off = h * 256 + half * 128
                            S.inc(t.matmul(ybank[:, half * 128:(half + 1) * 128],
                                           dtxt[:, ch * 2048 + off:ch * 2048 + off + 128],
                                           sct8[:, h * 128:(h + 1) * 128],
                                           start=True, stop=False), "t")
                            S.inc(t.matmul(ybank[:, half * 128:(half + 1) * 128],
                                           xst[:, ch * 2048 + off:ch * 2048 + off + 128],
                                           IDh[:, h * 128:(h + 1) * 128],
                                           start=False, stop=False), "t")
                            S.inc(t.matmul(ybank[:, half * 128:(half + 1) * 128],
                                           stb[0:64, off:off + 128],
                                           Cx8[:, h * 128:(h + 1) * 128],
                                           start=False, stop=True),
                                  "t", label=(f"s4_ymm_{b}_{ch}_{h}"
                                              if half == 1 else None))
                        stbank = ps_stu if h % 2 == 0 else ps_io[1]
                        prevst = (f"s4_stup_{b}_{ch}_{h - 2}" if h > 1 else
                                  (f"s4_stup_{b}_{ch - 1}_{6 + h}" if ch > 0 else
                                   (f"s4_stup_{b - 1}_{NCH - 1}_{6 + h}" if b > 0
                                    else None)))
                        if prevst:
                            S.wait(t, prevst)
                        S.inc(t.matmul(stbank[0:64, 0:256],
                                       Bw8[:, h * 64:(h + 1) * 64],
                                       dtxt[:, ch * 2048 + h * 256:
                                            ch * 2048 + (h + 1) * 256],
                                       start=True, stop=True), "t",
                              label=f"s4_stmm_{b}_{ch}_{h}")

        @block.scalar
        def _(s):
            S.wait(s, "s4_const")
            S.chain(s.activation(negA[:], negA[:], AF.Exp), "a",
                    label="s4_negA1")
            for b in range(NBLK):
                def _stage1(dc):
                    if dc == 0:
                        S.wait(s, f"s4_x1_{b}")
                    if dc > 1:
                        S.wait(s, f"s4_summm1_{b}_{dc - 2}")
                    S.chain(s.activation(stg_x[:, (dc % 2) * 512:(dc % 2 + 1) * 512],
                                         x1b[b % 2][:, dc * 512:(dc + 1) * 512],
                                         AF.Copy), "a", label=f"s4_cp1_{b}_{dc}")
                def _stage1q(dc):
                    if dc > 1:
                        S.wait(s, f"s4_sqmm1_{b}_{dc - 2}")
                    S.chain(s.activation(stg_q[:, (dc % 2) * 512:(dc % 2 + 1) * 512],
                                         x1b[b % 2][:, dc * 512:(dc + 1) * 512],
                                         AF.Square), "a", label=f"s4_sq1_{b}_{dc}")
                def _mbuild(pb, ch, h):
                    S.wait(s, f"s4_lb_{pb}_{ch}_{h}")
                    S.wait(s, f"s4_smallevac_{pb}_{ch}")
                    if h == 0:
                        prevsct8 = (f"s4_sct8_{pb}_{ch - 1}" if ch > 0 else
                                    (f"s4_sct8_{pb - 1}_{NCH - 1}" if pb > 0
                                     else None))
                        if prevsct8:
                            S.wait(s, prevsct8)
                    bank = ps_bc2 if h % 2 == 0 else ps_stat
                    S.chain(s.activation(
                        Mt[:], bank[:, 0:128], AF.Relu, scale=-1.0,
                        bias=smallTT[:, ch * 24 + 8 + h:ch * 24 + 8 + h + 1]),
                        "a")
                    S.chain(s.activation(Msb8[:, h * 128:(h + 1) * 128],
                                         Mt[:], AF.Exp, scale=-1.0), "a",
                            label=f"s4_m_{pb}_{ch}_{h}")
                if b > 0:
                    free_ops = [("x", 0), ("x", 1), ("q", 0), ("q", 1)]
                    for ch in range(NCH):
                        for h in range(8):
                            _mbuild(b - 1, ch, h)
                        if free_ops:
                            kind, dc = free_ops.pop(0)
                            (_stage1 if kind == "x" else _stage1q)(dc)
                    for dc in range(2, NT):
                        _stage1(dc)
                    for dc in range(2, NT):
                        _stage1q(dc)
                else:
                    for dc in range(NT):
                        _stage1(dc)
                    for dc in range(NT):
                        _stage1q(dc)
                S.wait(s, f"s4_rows1q_{b}")
                S.chain(s.activation(mrow2[:], mrow2[:], AF.Ln), "a")
                S.chain(s.activation(mrow2[:], mrow2[:], AF.Exp, scale=-0.5), "a",
                        label=f"s4_rows1a_{b}")
                for dc in range(NT):
                    S.wait(s, f"s4_xnc_{b}_{dc}")
                    if dc > 1:
                        S.wait(s, f"s4_summm2_{b}_{dc - 2}")
                    else:
                        S.wait(s, f"s4_summm1_{b}_{NT - 1}")
                    S.chain(s.activation(stg_x[:, (dc % 2) * 512:(dc % 2 + 1) * 512],
                                         xnT[:, dc * 512:(dc + 1) * 512],
                                         AF.Copy), "a", label=f"s4_cp2_{b}_{dc}")
                for dc in range(NT):
                    if dc > 1:
                        S.wait(s, f"s4_sqmm2_{b}_{dc - 2}")
                    else:
                        S.wait(s, f"s4_sqmm1_{b}_{NT - 1}")
                    S.chain(s.activation(stg_q[:, (dc % 2) * 512:(dc % 2 + 1) * 512],
                                         xnT[:, dc * 512:(dc + 1) * 512],
                                         AF.Square), "a", label=f"s4_sq2_{b}_{dc}")
                S.wait(s, f"s4_rows2q_{b}")
                S.chain(s.activation(mrow2[:], mrow2[:], AF.Ln), "a")
                S.chain(s.activation(mrow2[:], mrow2[:], AF.Exp, scale=-0.5), "a",
                        label=f"s4_rows2a_{b}")
                for dc in range(NT):
                    S.wait(s, f"s4_uc_{b}_{dc}")
                    S.chain(s.activation(ubf[:, dc * 512:(dc + 1) * 512],
                                         ubf[:, dc * 512:(dc + 1) * 512],
                                         AF.Identity, bias=bets[:, dc:dc + 1],
                                         scale=gams[:, dc:dc + 1]), "a",
                            label=f"s4_ubf_{b}_{dc}")
                S.wait(s, f"s4_dtmm_{b}")
                S.wait(s, "s4_bsum")
                if b > 0:
                    S.wait(s, f"s4_dtla_{b - 1}")
                S.chain(s.activation(dtr[:], ps_io[1][0:8, 0:512], AF.Identity,
                                     bias=bsum[:]), "a", label=f"s4_dte_{b}")
                S.chain(s.activation(t8a[:], dtr[:], AF.Abs), "a")
                S.chain(s.activation(t8a[:], t8a[:], AF.Exp, scale=-1.0), "a")
                S.chain(s.activation(t8a[:], t8a[:], AF.Ln, bias=1.0), "a",
                        label=f"s4_sp_{b}")
                S.wait(s, f"s4_bcmm_{b}")
                if b > 0:
                    S.wait(s, f"s4_scmm_{b - 1}_{NCH - 1}")
                    S.wait(s, f"s4_cxg_{b - 1}_{NCH - 1}_1")
                S.chain(s.activation(Bf[:], ps_sc[0:64, 0:512], AF.Identity,
                                     bias=bbcs[0:64, 0:1]), "a")
                S.chain(s.activation(Cf[:], ps_sc[64:128, 0:512], AF.Identity,
                                     bias=bbcs[64:128, 0:1]), "a",
                        label=f"s4_bce_{b}")
                for tt in range(4):
                    S.wait(s, f"s4_bctokmm_{b}_{tt}")
                    if b > 0 and tt == 0:
                        S.wait(s, f"s4_bw8_{b - 1}_{NCH - 1}")
                    S.chain(s.activation(Btok[:, tt * 64:(tt + 1) * 64],
                                         ps_io[tt % 2][:, 0:64], AF.Copy), "a",
                            label=f"s4_bctok_{b}_{tt}")
                for ft in range(16):
                    S.wait(s, f"s4_zmm_{b}_{ft}")
                    if b > 0 and ft == 0:
                        S.wait(s, f"s4_ye_{b - 1}_{NCH - 1}_7")
                    S.chain(s.activation(zsil[:, ft * 512:(ft + 1) * 512],
                                         ps_io[ft % 2][:], AF.Silu,
                                         bias=bzs[:, ft:ft + 1]), "a",
                            label=f"s4_zsil_{b}_{ft}")
                for tt in range(4):
                    for fb in range(4):
                        S.wait(s, f"s4_xsmm_{b}_{tt}_{fb}")
                        if b > 0 and tt == 0 and fb == 0:
                            S.wait(s, f"s4_stmm_{b - 1}_{NCH - 1}_7")
                            S.wait(s, f"s4_dtx_{b - 1}")
                        S.chain(s.activation(
                            xst[:, tt * 2048 + fb * 512:tt * 2048 + (fb + 1) * 512],
                            ps_io[(tt * 4 + fb) % 2][:], AF.Copy), "a",
                            label=f"s4_xse_{b}_{tt}_{fb}")
                S.wait(s, f"s4_L_{b}")
                for ch in range(NCH):
                    S.chain(s.activation(eLe[:, ch * 128:(ch + 1) * 128],
                                         Lrow[:, ch * 128:(ch + 1) * 128],
                                         AF.Exp, scale=-1.0,
                                         bias=Lrow[:, ch * 128 + 127:ch * 128 + 128]),
                            "a")
                S.chain(s.activation(eLp[:], Lrow[:], AF.Exp), "a",
                        label=f"s4_eL_{b}")
                if b == NBLK - 1:
                    for ch in range(NCH):
                        for h in range(8):
                            _mbuild(b, ch, h)
        @block.vector
        def _(v):
            S.wait_cnt(v, "wc", 160)
            S.chain(v.tensor_add(bsum[:], bsum[:], dtbs[:]), "v",
                    label="s4_bsum")
            S.wait(v, "s4_negA1")
            S.chain(v.tensor_scalar_mul(negA[:], negA[:], -1.0), "v",
                    label="s4_negA")
            S.wait(v, "s4_dskb")
            S.wait(v, "s4_const")
            S.chain(v.tensor_copy(Dcols[:, 0:8], ps_mis[:, 0:8]), "v")
            for h in range(8):
                S.chain(v.tensor_scalar_mul(IDh[:, h * 128:(h + 1) * 128],
                                            idb[:], Dcols[:, h:h + 1]), "v",
                        label=("s4_idh" if h == 7 else None))
            for b in range(NBLK):
                S.wait(v, f"s4_sqmm1_{b}_{NT - 1}")
                S.chain(v.tensor_scalar_mul(murow[:], ps_stat[0:1, :], 1.0 / D), "v")
                S.chain(v.tensor_scalar_mul(vrow[:], ps_stat[32:33, :], 1.0 / D), "v",
                        label=f"s4_rows1v_{b}")
                S.chain(v.tensor_mul(mrow2[:], murow[:], murow[:]), "v")
                S.chain(v.tensor_sub(mrow2[:], vrow[:], mrow2[:]), "v")
                S.chain(v.tensor_scalar_add(mrow2[:], mrow2[:], 1e-5), "v",
                        label=f"s4_rows1q_{b}")
                S.wait(v, f"s4_mub_{b}")
                S.wait(v, f"s4_rb_{b}")
                if b > 0:
                    S.wait(v, f"s4_xnst_{b - 1}")
                for dc in range(NT):
                    S.chain(v.tensor_sub(xnT[:, dc * 512:(dc + 1) * 512],
                                         x1b[b % 2][:, dc * 512:(dc + 1) * 512],
                                         ps_stat[:, :]), "v")
                    S.chain(v.tensor_mul(xnT[:, dc * 512:(dc + 1) * 512],
                                         xnT[:, dc * 512:(dc + 1) * 512],
                                         ps_bc2[:, :]), "v",
                            label=f"s4_xnc_{b}_{dc}")
                S.label(f"s4_xn_done_{b}", "v")
                S.wait(v, f"s4_sum2_{b}")
                S.wait(v, f"s4_sqmm2_{b}_{NT - 1}")
                S.chain(v.tensor_scalar_mul(murow[:], ps_stat[0:1, :], 1.0 / D), "v")
                S.chain(v.tensor_scalar_mul(vrow[:], ps_stat[32:33, :], 1.0 / D), "v",
                        label=f"s4_rows2v_{b}")
                S.chain(v.tensor_mul(mrow2[:], murow[:], murow[:]), "v")
                S.chain(v.tensor_sub(mrow2[:], vrow[:], mrow2[:]), "v")
                S.chain(v.tensor_scalar_add(mrow2[:], mrow2[:], 1e-5), "v",
                        label=f"s4_rows2q_{b}")
                S.wait(v, f"s4_mub2_{b}")
                S.wait(v, f"s4_rb2_{b}")
                S.wait(v, f"s4_sqmm2_{b}_{NT - 1}")
                for dc in range(NT):
                    S.chain(v.tensor_sub(ubf[:, dc * 512:(dc + 1) * 512],
                                         xnT[:, dc * 512:(dc + 1) * 512],
                                         ps_stat[:, :]), "v")
                    S.chain(v.tensor_mul(ubf[:, dc * 512:(dc + 1) * 512],
                                         ubf[:, dc * 512:(dc + 1) * 512],
                                         ps_bc2[:, :]), "v",
                            label=f"s4_uc_{b}_{dc}")
                S.wait(v, f"s4_sp_{b}")
                S.chain(v.tensor_scalar_max(dtr[:], dtr[:], 0.0), "v")
                S.chain(v.tensor_add(dtr[:], dtr[:], t8a[:]), "v")
                S.wait(v, "s4_negA")
                S.chain(v.tensor_scalar_mul(t8a[:], dtr[:], negA[:, 0:1]), "v",
                        label=f"s4_dtla_{b}")
                if b > 0:
                    S.wait(v, f"s4_lb_{b - 1}_{NCH - 1}_7")
                    S.wait(v, f"s4_epbg_{b - 1}_{NCH - 1}_1")
                S.chain(v.memset(Mt[0:8, 0:128], 0.0), "v")
                for ch in range(NCH):
                    S.chain(v.tensor_tensor_scan(
                        Lrow[:, ch * 128:(ch + 1) * 128],
                        t8a[:, ch * 128:(ch + 1) * 128],
                        Mt[0:8, 0:128], 0.0, op0=ALU.add, op1=ALU.add), "v",
                        label=(f"s4_L_{b}" if ch == NCH - 1 else None))
                for ch in range(NCH):
                    S.wait(v, f"s4_str_{b}_{ch}")
                    S.chain(v.tensor_copy(smallTT[:, ch * 24:(ch + 1) * 24],
                                          ps_mis[:, 0:24]), "v")
                    S.chain(v.tensor_copy(erow[0:1, ch * 8:(ch + 1) * 8],
                                          ps_mis[0:1, 256:264]), "v",
                            label=f"s4_smallevac_{b}_{ch}")
                for ch in range(NCH):
                    S.wait(v, f"s4_dcb_{b}_{ch}")
                    S.chain(v.tensor_copy(dcol[:, ch * 8:(ch + 1) * 8],
                                          ps_stu[:, 256:264]), "v",
                            label=f"s4_dcole_{b}_{ch}")
                S.wait(v, f"s4_xse_{b}_3_3")
                if b > 0:
                    S.wait(v, f"s4_stmm_{b - 1}_{NCH - 1}_7")
                for tt in range(4):
                    for h in range(8):
                        S.chain(v.tensor_scalar_mul(
                            dtxt[:, tt * 2048 + h * 256:tt * 2048 + (h + 1) * 256],
                            xst[:, tt * 2048 + h * 256:tt * 2048 + (h + 1) * 256],
                            smallTT[:, tt * 24 + h:tt * 24 + h + 1]), "v",
                            label=(f"s4_dtx_{b}" if tt == 3 and h == 7 else None))
                for ch in range(NCH):
                    S.wait(v, f"s4_scmm_{b}_{ch}")
                    S.chain(v.tensor_mul(BCm[:], ps_sc[:, 0:128], trib[:]), "v",
                            label=f"s4_bcmsk_{b}_{ch}")
                    for grp in range(2):
                        S.wait(v, f"s4_epbg_{b}_{ch}_{grp}")
                        if grp == 0 and (ch > 0 or b > 0):
                            pb, pch = (b, ch - 1) if ch > 0 else (b - 1, NCH - 1)
                            S.wait(v, f"s4_ymm_{pb}_{pch}_7")
                        cfb = bass.AP(Cf[:].tensor, ch * 128,
                                      [[512, 64], [0, 4], [1, 128]])
                        S.chain(v.tensor_mul(
                            Cx8.rearrange("p (h c) -> p h c", c=128)[
                                :, grp * 4:(grp + 1) * 4, :],
                            ps_mis.rearrange("p (h c) -> p h c", c=128)[
                                0:64, 0:4, :],
                            cfb), "v", label=f"s4_cxg_{b}_{ch}_{grp}")
                    S.wait(v, f"s4_m_{b}_{ch}_7")
                    bcb = bass.AP(BCm[:].tensor, 0, [[128, 128], [0, 8], [1, 128]])
                    S.chain(v.tensor_mul(
                        sct8.rearrange("p (h c) -> p h c", c=128),
                        bcb,
                        Msb8.rearrange("p (h c) -> p h c", c=128)), "v",
                        label=f"s4_sct8_{b}_{ch}")
                    if ch > 0:
                        S.wait(v, f"s4_stmm_{b}_{ch - 1}_7")
                    elif b > 0:
                        S.wait(v, f"s4_stmm_{b - 1}_{NCH - 1}_7")
                    for h in range(8):
                        S.chain(v.tensor_scalar_mul(
                            Bw8[:, h * 64:(h + 1) * 64],
                            Btok[:, ch * 64:(ch + 1) * 64],
                            smallTT[:, ch * 24 + 16 + h:ch * 24 + 16 + h + 1]),
                            "v", label=(f"s4_bw8_{b}_{ch}" if h == 7 else None))
                    for h in range(8):
                        ybank = ps_y if h % 2 == 0 else ps_io[0]
                        stbank = ps_stu if h % 2 == 0 else ps_io[1]
                        S.wait(v, f"s4_ymm_{b}_{ch}_{h}")
                        if b > 0 and ch == 0 and h == 0:
                            S.wait(v, f"s4_yfst_{b - 1}")
                        for half in range(2):
                            fcol = (2 * h + half) * 512 + ch * 128
                            S.chain(v.tensor_mul(
                                yfin[:, fcol:fcol + 128],
                                ybank[:, half * 128:(half + 1) * 128],
                                zsil[:, fcol:fcol + 128]), "v",
                                label=(f"s4_ye_{b}_{ch}_{h}" if half == 1
                                       else None))
                        S.wait(v, f"s4_stmm_{b}_{ch}_{h}")
                        if h == 0:
                            S.wait(v, f"s4_dcole_{b}_{ch}")
                        S.chain(v.scalar_tensor_tensor(
                            stf[:, h * 256:(h + 1) * 256],
                            stf[:, h * 256:(h + 1) * 256],
                            dcol[:, ch * 8 + h:ch * 8 + h + 1],
                            stbank[0:64, 0:256], op0=ALU.mult, op1=ALU.add), "v")
                        S.chain(v.tensor_copy(stb[:, h * 256:(h + 1) * 256],
                                              stf[:, h * 256:(h + 1) * 256]),
                                "v", label=f"s4_stup_{b}_{ch}_{h}")
                S.label(f"s4_yf_done_{b}", "v")
    return nc


def build_s4():
    S = Sched()
    nc = None
    for p2 in (False, True):
        S.begin(p2)
        with ExitStack() as outer:
            env = _standalone_env("s4", outer)
            emit_s4(S, env)
            nc = env.nc
    return nc


def emit_s5(S: Sched, env):
    nc = env.nc
    Wo = env.io["Wo"]
    bo_c = env.io["bo_c"]
    yf_d = env.io["yfinT"]
    xn1_d = env.io["xn1T"]
    s1_d = env.io["s1T"]

    with ExitStack() as ctx:
        sb = lambda n, s_, d_: ctx.enter_context(nc.sbuf_tensor(env.pfx + n, s_, d_))
        psx = lambda n, s_, d_: ctx.enter_context(nc.psum_tensor(env.pfx + n, s_, d_))
        sem = lambda n: env.outer.enter_context(nc.semaphore(env.pfx + n))
        wT = [sb(f"wT{i}", [128, 1024], BF16) for i in range(16)]
        yb = [sb("yb0", [128, 16 * 512], BF16), sb("yb1", [128, 16 * 512], BF16)]
        xnb = [sb("xnb0", [128, 8 * 512], F32), sb("xnb1", [128, 8 * 512], F32)]
        s1b = sb("s1b", [128, 8 * 512], F32)
        bos = sb("bos", [128, 8], F32)
        ps_r = [psx("ps_r0", [128, 512], F32), psx("ps_r1", [128, 512], F32),
                psx("ps_r2", [128, 512], F32), psx("ps_r3", [128, 512], F32)]
        for k in ["d0", "d1", "w", "t", "v", "a", "p", "o1"]:
            S.reg(k, sem("sem_" + k))
        block = ctx.enter_context(nc.Block())

        @block.gpsimd
        def _(g):
            for fc in range(16):
                S.inc(g.dma_start(out=wT[fc][:], in_=Wo[fc * 128:(fc + 1) * 128, :]),
                      "w", 16)
            S.inc(g.dma_start(out=bos[:], in_=bo_c), "w", 16)

        @block.sync
        def _(sy):
            for b in range(NBLK):
                if b >= 2:
                    S.wait(sy, f"s5_mmdone_{b - 2}")
                    S.wait(sy, f"s5_comb_{b - 2}")
                S.inc(sy.dma_start(
                    out=yb[b % 2].rearrange("p (f c) -> p f c", c=512),
                    in_=yf_d.rearrange("(f p) t -> p f t", p=128)[:, :, b * 512:(b + 1) * 512]),
                    f"d{b % 2}", 16)
                S.inc(sy.dma_start(
                    out=xnb[b % 2].rearrange("p (k c) -> p k c", c=512),
                    in_=xn1_d.rearrange("(k p) t -> p k t", p=128)[:, :, b * 512:(b + 1) * 512]),
                    f"d{b % 2}", 16, label=f"s5_in_{b}")
                if b >= 1:
                    S.wait(sy, f"s5_comb_{b - 1}")
                    S.inc(sy.dma_start(
                        out=s1_d.rearrange("(k p) t -> p k t", p=128)[:, :, (b - 1) * 512:b * 512],
                        in_=s1b.rearrange("p (k c) -> p k c", c=512)),
                        "o1", 16, label=f"s5_st_{b - 1}")
            S.wait(sy, f"s5_comb_{NBLK - 1}")
            S.inc(sy.dma_start(
                out=s1_d.rearrange("(k p) t -> p k t", p=128)[:, :, (NBLK - 1) * 512:NBLK * 512],
                in_=s1b.rearrange("p (k c) -> p k c", c=512)),
                "o1", 16, label=f"s5_st_{NBLK - 1}")
            sy.wait_ge(S.sem("o1"), 16 * NBLK)

        @block.tensor
        def _(t):
            S.wait_cnt(t, "w", 16 * 17)
            for b in range(NBLK):
                S.wait(t, f"s5_in_{b}")
                for do in range(8):
                    if do >= 4:
                        S.wait(t, f"s5_ev_{b}_{do - 4}")
                    elif b > 0:
                        S.wait(t, f"s5_ev_{b - 1}_{4 + do}")
                    ps = ps_r[do % 4]
                    for fc in range(16):
                        S.inc(t.matmul(ps[:], wT[fc][:, do * 128:(do + 1) * 128],
                                       yb[b % 2][:, fc * 512:(fc + 1) * 512],
                                       start=(fc == 0), stop=(fc == 15)),
                              "t", label=(f"s5_mm_{b}_{do}" if fc == 15 else None))
                S.label(f"s5_mmdone_{b}", "t")

        @block.vector
        def _(v):
            for b in range(NBLK):
                if b > 0:
                    S.wait(v, f"s5_st_{b - 1}")
                for do in range(8):
                    ps = ps_r[do % 4]
                    S.wait(v, f"s5_mm_{b}_{do}")
                    S.chain(v.scalar_tensor_tensor(
                        s1b[:, do * 512:(do + 1) * 512], ps[:],
                        bos[:, do:do + 1],
                        xnb[b % 2][:, do * 512:(do + 1) * 512],
                        op0=ALU.add, op1=ALU.add), "v",
                        label=f"s5_ev_{b}_{do}")
                S.label(f"s5_comb_{b}", "v")
    return nc


def build_s5():
    S = Sched()
    nc = None
    for p2 in (False, True):
        S.begin(p2)
        with ExitStack() as outer:
            env = _standalone_env("s5", outer)
            emit_s5(S, env)
            nc = env.nc
    return nc


def emit_s6(S: Sched, env):
    nc = env.nc
    wgT = env.io["wgT"]
    wuT = env.io["wuT"]
    xn1_d = env.io["xn1T"]
    hg_d = env.io["hgT"]

    with ExitStack() as ctx:
        sb = lambda n, s_, d_: ctx.enter_context(nc.sbuf_tensor(env.pfx + n, s_, d_))
        psx = lambda n, s_, d_: ctx.enter_context(nc.psum_tensor(env.pfx + n, s_, d_))
        sem = lambda n: env.outer.enter_context(nc.semaphore(env.pfx + n))
        wg = [sb(f"wg{i}", [128, 3072], BF16) for i in range(8)]
        wu = [sb(f"wu{i}", [128, 3072], BF16) for i in range(8)]
        xf = [sb("xf0", [128, 8 * 512], F32), sb("xf1", [128, 8 * 512], F32)]
        xb = sb("xb", [128, 8 * 512], BF16)
        hg = sb("hg", [128, 24 * 512], BF16)
        sg = [sb("sg0", [128, 512], BF16), sb("sg1", [128, 512], BF16)]
        ps_g = [psx(f"ps_g{i}", [128, 512], F32) for i in range(3)]
        ps_u = [psx(f"ps_u{i}", [128, 512], F32) for i in range(3)]
        for k in ["d0", "d1", "w", "t", "v", "a", "p", "o1"]:
            S.reg(k, sem("sem_" + k))
        block = ctx.enter_context(nc.Block())

        @block.gpsimd
        def _(g):
            for dc in range(8):
                S.inc(g.dma_start(out=wg[dc][:], in_=wgT[dc * 128:(dc + 1) * 128, :]),
                      "w", 16)
                S.inc(g.dma_start(out=wu[dc][:], in_=wuT[dc * 128:(dc + 1) * 128, :]),
                      "w", 16)

        @block.sync
        def _(sy):
            for b in range(NBLK):
                if b >= 2:
                    S.wait(sy, f"s6_xcast_{b - 2}")
                S.inc(sy.dma_start(
                    out=xf[b % 2].rearrange("p (k c) -> p k c", c=512),
                    in_=xn1_d.rearrange("(k p) t -> p k t", p=128)[:, :, b * 512:(b + 1) * 512]),
                    f"d{b % 2}", 16, label=f"s6_in_{b}")
                if b >= 1:
                    S.wait(sy, f"s6_hgdone_{b - 1}")
                    S.inc(sy.dma_start(
                        out=hg_d.rearrange("(f p) t -> p f t", p=128)[:, :, (b - 1) * 512:b * 512],
                        in_=hg.rearrange("p (f c) -> p f c", c=512)),
                        "o1", 16, label=f"s6_st_{b - 1}")
            S.wait(sy, f"s6_hgdone_{NBLK - 1}")
            S.inc(sy.dma_start(
                out=hg_d.rearrange("(f p) t -> p f t", p=128)[:, :, (NBLK - 1) * 512:NBLK * 512],
                in_=hg.rearrange("p (f c) -> p f c", c=512)),
                "o1", 16, label=f"s6_st_{NBLK - 1}")
            sy.wait_ge(S.sem("o1"), 16 * NBLK)

        @block.vector
        def _(v):
            for b in range(NBLK):
                S.wait(v, f"s6_in_{b}")
                if b > 0:
                    S.wait(v, f"s6_mmdone_{b - 1}")
                S.chain(v.tensor_scalar_mul(xb[:], xf[b % 2][:], 1.0 / 127.0),
                        "v", label=f"s6_xcast_{b}")
                if b > 0:
                    S.wait(v, f"s6_st_{b - 1}")
                for ft in range(24):
                    S.wait(v, f"s6_sg_{b}_{ft}")
                    S.wait(v, f"s6_umm_{b}_{ft}")
                    S.chain(v.tensor_mul(hg[:, ft * 512:(ft + 1) * 512],
                                         sg[ft % 2][:], ps_u[ft % 3][:]), "v",
                            label=f"s6_hge_{b}_{ft}")
                S.label(f"s6_hgdone_{b}", "v")

        @block.tensor
        def _(t):
            S.wait_cnt(t, "w", 16 * 16)
            for b in range(NBLK):
                S.wait(t, f"s6_xcast_{b}")
                for ft in range(24):
                    if ft >= 3:
                        S.wait(t, f"s6_sg_{b}_{ft - 3}")
                        S.wait(t, f"s6_hge_{b}_{ft - 3}")
                    elif b > 0:
                        S.wait(t, f"s6_sg_{b - 1}_{21 + ft}")
                        S.wait(t, f"s6_hge_{b - 1}_{21 + ft}")
                    for dc in range(8):
                        S.inc(t.matmul(ps_g[ft % 3][:],
                                       wg[dc][:, ft * 128:(ft + 1) * 128],
                                       xb[:, dc * 512:(dc + 1) * 512],
                                       start=(dc == 0), stop=(dc == 7)),
                              "t", label=(f"s6_gmm_{b}_{ft}" if dc == 7 else None))
                    for dc in range(8):
                        S.inc(t.matmul(ps_u[ft % 3][:],
                                       wu[dc][:, ft * 128:(ft + 1) * 128],
                                       xb[:, dc * 512:(dc + 1) * 512],
                                       start=(dc == 0), stop=(dc == 7)),
                              "t", label=(f"s6_umm_{b}_{ft}" if dc == 7 else None))
                S.label(f"s6_mmdone_{b}", "t")

        @block.scalar
        def _(s):
            for b in range(NBLK):
                for ft in range(24):
                    S.wait(s, f"s6_gmm_{b}_{ft}")
                    if ft > 1:
                        S.wait(s, f"s6_hge_{b}_{ft - 2}")
                    elif b > 0:
                        S.wait(s, f"s6_hge_{b - 1}_{22 + ft}")
                    S.chain(s.activation(sg[ft % 2][:], ps_g[ft % 3][:], AF.Silu),
                            "a", label=f"s6_sg_{b}_{ft}")
    return nc


def build_s6():
    S = Sched()
    nc = None
    for p2 in (False, True):
        S.begin(p2)
        with ExitStack() as outer:
            env = _standalone_env("s6", outer)
            emit_s6(S, env)
            nc = env.nc
    return nc


def emit_s7(S: Sched, env):
    nc = env.nc
    wdT = env.io["wdT"]
    hg_d = env.io["hgT"]
    x1_d = env.io["x1T"]
    s1_d = env.io["s1T"]
    out_d = env.io["out"]

    with ExitStack() as ctx:
        sb = lambda n, s_, d_: ctx.enter_context(nc.sbuf_tensor(env.pfx + n, s_, d_))
        psx = lambda n, s_, d_: ctx.enter_context(nc.psum_tensor(env.pfx + n, s_, d_))
        sem = lambda n: env.outer.enter_context(nc.semaphore(env.pfx + n))
        wT = [sb(f"wT{i}", [128, 1024], BF16) for i in range(24)]
        hgb = [sb("hgb0", [128, 24 * 512], BF16), sb("hgb1", [128, 24 * 512], BF16)]
        x1b = [sb("x1b0", [128, 8 * 512], F32), sb("x1b1", [128, 8 * 512], F32)]
        s1b = [sb("s1b0", [128, 8 * 512], F32), sb("s1b1", [128, 8 * 512], F32)]
        gb = sb("gb", [128, 8 * 512], F32)
        otok = sb("otok", [128, 4 * 1024], F32)
        idf = sb("idf", [128, 128], F32)
        ps_r = [psx(f"ps_r{i}", [128, 512], F32) for i in range(4)]
        ps_t = [psx("ps_t0", [128, 512], F32), psx("ps_t1", [128, 512], F32)]
        for k in ["d0", "d1", "d20", "d21", "w", "t", "v", "a", "p", "o1"]:
            S.reg(k, sem("sem_" + k))
        block = ctx.enter_context(nc.Block())

        @block.gpsimd
        def _(g):
            for fc in range(24):
                S.inc(g.dma_start(out=wT[fc][:], in_=wdT[fc * 128:(fc + 1) * 128, :]),
                      "w", 16)
            S.chain(g.memset(idf[:], 1.0), "p")
            S.chain(g.affine_select(idf[:], idf[:], pattern=[[1, 128]], base=0,
                                    channel_multiplier=-1,
                                    compare_op=ALU.is_equal, fill=0.0), "p",
                    label="s7_const")
            for b in range(NBLK):
                if b >= 2:
                    S.wait(g, f"s7_comb_{b - 2}")
                S.inc(g.dma_start(
                    out=x1b[b % 2].rearrange("p (k c) -> p k c", c=512),
                    in_=x1_d.rearrange("(k p) t -> p k t", p=128)[:, :, b * 512:(b + 1) * 512]),
                    f"d2{b % 2}", 16)
                S.inc(g.dma_start(
                    out=s1b[b % 2].rearrange("p (k c) -> p k c", c=512),
                    in_=s1_d.rearrange("(k p) t -> p k t", p=128)[:, :, b * 512:(b + 1) * 512]),
                    f"d2{b % 2}", 16, label=f"s7_xs_{b}")

        @block.sync
        def _(sy):
            for b in range(NBLK):
                if b >= 2:
                    S.wait(sy, f"s7_mmdone_{b - 2}")
                S.inc(sy.dma_start(
                    out=hgb[b % 2].rearrange("p (f c) -> p f c", c=512),
                    in_=hg_d.rearrange("(f p) t -> p f t", p=128)[:, :, b * 512:(b + 1) * 512]),
                    f"d{b % 2}", 16, label=f"s7_in_{b}")
                if b >= 1:
                    S.wait(sy, f"s7_otok_{b - 1}")
                    for tt in range(4):
                        S.inc(sy.dma_start(
                            out=out_d[((b - 1) * 4 + tt) * 128:
                                      ((b - 1) * 4 + tt + 1) * 128, :],
                            in_=otok[:, tt * 1024:(tt + 1) * 1024]),
                            "o1", 16, label=(f"s7_st_{b - 1}" if tt == 3 else None))
            S.wait(sy, f"s7_otok_{NBLK - 1}")
            for tt in range(4):
                S.inc(sy.dma_start(
                    out=out_d[((NBLK - 1) * 4 + tt) * 128:
                              ((NBLK - 1) * 4 + tt + 1) * 128, :],
                    in_=otok[:, tt * 1024:(tt + 1) * 1024]),
                    "o1", 16, label=(f"s7_st_{NBLK - 1}" if tt == 3 else None))
            sy.wait_ge(S.sem("o1"), 16 * 4 * NBLK)

        @block.tensor
        def _(t):
            S.wait_cnt(t, "w", 16 * 24)
            S.wait(t, "s7_const")
            for b in range(NBLK):
                S.wait(t, f"s7_in_{b}")
                for do in range(8):
                    if do >= 4:
                        S.wait(t, f"s7_g_{b}_{do - 4}")
                    elif b > 0:
                        S.wait(t, f"s7_g_{b - 1}_{4 + do}")
                    ps = ps_r[do % 4]
                    for fc in range(24):
                        S.inc(t.matmul(ps[:], wT[fc][:, do * 128:(do + 1) * 128],
                                       hgb[b % 2][:, fc * 512:(fc + 1) * 512],
                                       start=(fc == 0), stop=(fc == 23)),
                              "t", label=(f"s7_mm_{b}_{do}" if fc == 23 else None))
                S.label(f"s7_mmdone_{b}", "t")
                S.wait(t, f"s7_comb_{b}")
                for tt in range(4):
                    for half in range(2):
                        if tt > 0 or b > 0:
                            ptt, phalf = (tt, half) if tt > 0 else (4, half)
                            S.wait(t, f"s7_te_{b if tt > 0 else b - 1}_"
                                      f"{tt - 1 if tt > 0 else 3}_{half}")
                        for j in range(4):
                            jj = half * 4 + j
                            S.inc(t.transpose(
                                ps_t[half][:, j * 128:(j + 1) * 128],
                                gb[:, jj * 512 + tt * 128:jj * 512 + (tt + 1) * 128],
                                idf[:]), "t",
                                label=(f"s7_tr_{b}_{tt}_{half}" if j == 3
                                       else None))

        @block.scalar
        def _(s):
            for b in range(NBLK):
                for do in range(8):
                    ps = ps_r[do % 4]
                    S.wait(s, f"s7_mm_{b}_{do}")
                    if b > 0 and do == 0:
                        S.wait(s, f"s7_tr_{b - 1}_3_1")
                    S.chain(s.activation(gb[:, do * 512:(do + 1) * 512], ps[:],
                                         AF.Silu, scale=1.0 / 127.0), "a",
                            label=f"s7_g_{b}_{do}")

        @block.vector
        def _(v):
            for b in range(NBLK):
                S.wait(v, f"s7_xs_{b}")
                for do in range(8):
                    S.wait(v, f"s7_g_{b}_{do}")
                    sl = slice(do * 512, (do + 1) * 512)
                    S.chain(v.tensor_mul(gb[:, sl], s1b[b % 2][:, sl],
                                         gb[:, sl]), "v")
                    S.chain(v.tensor_add(gb[:, sl], gb[:, sl],
                                         x1b[b % 2][:, sl]), "v",
                            label=(f"s7_comb_{b}" if do == 7 else None))
                if b > 0:
                    S.wait(v, f"s7_st_{b - 1}")
                for tt in range(4):
                    for half in range(2):
                        S.wait(v, f"s7_tr_{b}_{tt}_{half}")
                        dst = otok.rearrange("p (k c) -> p k c", c=1024)
                        S.chain(v.tensor_copy(
                            dst[:, tt:tt + 1, half * 512:(half + 1) * 512].rearrange(
                                "p a (j c) -> p (a j) c", c=128),
                            ps_t[half].rearrange("p (j c) -> p j c", c=128)),
                            "v", label=f"s7_te_{b}_{tt}_{half}")
                S.label(f"s7_otok_{b}", "v")
    return nc


def build_s7():
    S = Sched()
    nc = None
    for p2 in (False, True):
        S.begin(p2)
        with ExitStack() as outer:
            env = _standalone_env("s7", outer)
            emit_s7(S, env)
            nc = env.nc
    return nc


def build_s1():
    S = Sched()
    nc = None
    for p2 in (False, True):
        S.begin(p2)
        with ExitStack() as outer:
            env = _standalone_env("s1", outer)
            emit_s1(S, env)
            nc = env.nc
    return nc


def _shard_layouts(inputs):
    i = inputs
    c = {}
    c["s1"] = {
        "w_in": i["L0_attn_w_in"],
        "w_a": i["L0_attn_w_a"],
        "b_inT": np.ascontiguousarray(i["L0_attn_b_in"].T),
        "b_aT": np.ascontiguousarray(i["L0_attn_b_a"].T),
        "a_paramT": np.ascontiguousarray(i["L0_attn_a_param"].reshape(8, 128).T),
    }
    c["s2"] = {
        "W1T": np.ascontiguousarray(i["L0_ffn_W1"].T),
        "W2T": np.ascontiguousarray(i["L0_ffn_W2"].T),
        "b1c": np.ascontiguousarray(i["L0_ffn_b1"].reshape(32, 128).T),
    }
    c["s3"] = {
        "b2c": np.ascontiguousarray(i["L0_ffn_b2"].reshape(8, 128).T),
    }
    b_in = i["L1_attn_b_in"]
    c["s4"] = {
        "WinT": i["L1_attn_W_in"],
        "bz_c": np.ascontiguousarray(b_in[0:2048].reshape(16, 128).T),
        "bxs_r": np.ascontiguousarray(b_in[2048:4096].reshape(1, 2048)),
        "bbc_c": np.ascontiguousarray(b_in[4096:4224].reshape(128, 1)),
        "bbc_r": np.ascontiguousarray(b_in[4096:4224].reshape(1, 128)),
        "bdt_c": np.ascontiguousarray(b_in[4224:4232].reshape(8, 1)),
        "dtb_c": np.ascontiguousarray(i["L1_attn_dt_bias"].reshape(8, 1)),
        "Alog_c": np.ascontiguousarray(i["L1_attn_A_log"].reshape(8, 1)),
        "Dsk_r": np.ascontiguousarray(i["L1_attn_D_skip"].reshape(1, 8)),
        "gam_c": np.ascontiguousarray(i["L1_attn_ln_gamma"].reshape(8, 128).T),
        "bet_c": np.ascontiguousarray(i["L1_attn_ln_beta"].reshape(8, 128).T),
    }
    c["s5"] = {
        "Wo": i["L1_attn_W_out"],
        "bo_c": np.ascontiguousarray(i["L1_attn_b_out"].reshape(8, 128).T),
    }
    c["s6"] = {
        "wgT": np.ascontiguousarray(i["L1_ffn_w_gate"].T),
        "wuT": np.ascontiguousarray(i["L1_ffn_w_up"].T),
    }
    c["s7"] = {
        "wdT": np.ascontiguousarray(i["L1_ffn_w_down"].T),
    }
    return c


def build_fused():
    emitters = {"s1": emit_s1, "s2": emit_s2, "s3": emit_s3, "s4": emit_s4,
                "s5": emit_s5, "s6": emit_s6, "s7": emit_s7}
    S = Sched()
    nc = None
    for p2 in (False, True):
        S.begin(p2)
        with ExitStack() as outer:
            nc = bass.Bass("TRN2", target_bir_lowering=False)
            outs_of = set()
            for st in _STAGES:
                for name, _sh, _dt in IO_SPECS[st]["out"]:
                    outs_of.add(name)
            kinds = {}
            for st in _STAGES:
                for name, sh, dt_ in IO_SPECS[st]["in"]:
                    kinds.setdefault(name, ("Internal" if name in outs_of
                                            else "ExternalInput", sh, dt_))
            for st in _STAGES:
                for name, sh, dt_ in IO_SPECS[st]["out"]:
                    kinds.setdefault(name, ("ExternalOutput" if name == "out"
                                            else "Internal", sh, dt_))
            io = {}
            for name, (kind, sh, dt_) in kinds.items():
                io[name] = _ios(nc, name, sh, dt_, kind)
            bar = outer.enter_context(nc.semaphore("stage_bar"))
            for k, st in enumerate(_STAGES):
                if k > 0:
                    _barrier_block(nc, bar, 5 * k, True)
                S.pfx = st + "_"
                env = Env(nc, io, outer, pfx=st + "_")
                emitters[st](S, env)
            S.pfx = ""
    return nc


def kernel(**inputs):
    NC = 8
    x = inputs["L0_attn_x"]
    c = _shard_layouts(inputs)
    common = {}
    for d in c.values():
        common.update(d)
    nc = build_fused()
    in_maps = [{**common, "x": np.ascontiguousarray(x[b])} for b in range(NC)]
    res = run_bass_kernel_spmd(nc, in_maps, core_ids=list(range(NC)))
    return np.stack([res.results[b]["out"] for b in range(NC)], axis=0)


def kernel_unfused(**inputs):
    NC = 8
    cores = list(range(NC))
    x = inputs["L0_attn_x"]
    c = _shard_layouts(inputs)

    r1 = run_bass_kernel_spmd(build_s1(), [
        {**c["s1"], "x": np.ascontiguousarray(x[b])} for b in range(NC)],
        core_ids=cores)
    r2 = run_bass_kernel_spmd(build_s2(), [
        {**c["s2"], "xnbf": r1.results[b]["xnbf"]} for b in range(NC)],
        core_ids=cores)
    r3 = run_bass_kernel_spmd(build_s3(), [
        {**c["s3"], "h1": r2.results[b]["h1"], "W2t": r2.results[b]["W2t"],
         "xT": r1.results[b]["xT"], "s0T": r1.results[b]["s0T"]}
        for b in range(NC)], core_ids=cores)
    r4 = run_bass_kernel_spmd(build_s4(), [
        {**c["s4"], "x1T": r3.results[b]["x1T"]} for b in range(NC)],
        core_ids=cores)
    r5 = run_bass_kernel_spmd(build_s5(), [
        {**c["s5"], "yfinT": r4.results[b]["yfinT"],
         "xn1T": r4.results[b]["xn1T"]} for b in range(NC)],
        core_ids=cores)
    r6 = run_bass_kernel_spmd(build_s6(), [
        {**c["s6"], "xn1T": r4.results[b]["xn1T"]} for b in range(NC)],
        core_ids=cores)
    r7 = run_bass_kernel_spmd(build_s7(), [
        {**c["s7"], "hgT": r6.results[b]["hgT"], "x1T": r3.results[b]["x1T"],
         "s1T": r5.results[b]["s1T"]} for b in range(NC)],
        core_ids=cores)
    return np.stack([r7.results[b]["out"] for b in range(NC)], axis=0)


# version 21
# speedup vs baseline: 1.0482x; 1.0482x over previous
import numpy as np
import ml_dtypes
from contextlib import ExitStack

import concourse.bass as bass
import concourse.mybir as mybir
from concourse.bass_utils import run_bass_kernel_spmd

F32 = mybir.dt.float32
BF16 = mybir.dt.bfloat16
I32 = mybir.dt.int32
AF = mybir.ActivationFunctionType
ALU = mybir.AluOpType
AX = mybir.AxisListType

T = 4096
D = 1024
NT = 8
BLK = 512
NBLK = T // BLK
LN7 = float(np.log(np.float32(1e-7)))
L1M = float(np.log1p(np.float64(-1e-5)))


class Sched:

    def __init__(self):
        self.rec = {}
        self.pass2 = False

    def begin(self, pass2):
        self.pass2 = pass2
        self.cnt = {}
        self.sems = {}
        self.pfx = ""

    def reg(self, key, handle):
        self.sems[self.pfx + key] = handle

    def inc(self, instr, key, n=1, label=None):
        key = self.pfx + key
        self.cnt[key] = self.cnt.get(key, 0) + n
        if instr is not None:
            instr.then_inc(self.sems[key], n)
        if label is not None:
            if self.pass2:
                assert self.rec.get(label) == (key, self.cnt[key]), \
                    f"schedule drift at {label}"
            else:
                self.rec[label] = (key, self.cnt[key])
        return self.cnt[key]

    def chain(self, instr, key, label=None):
        key = self.pfx + key
        prev = self.cnt.get(key, 0)
        if self.pass2 and prev > 0:
            instr._wait_ge(self.sems[key], prev)
        self.cnt[key] = prev + 1
        instr.then_inc(self.sems[key], 1)
        if label is not None:
            if self.pass2:
                assert self.rec.get(label) == (key, self.cnt[key]), \
                    f"schedule drift at {label}"
            else:
                self.rec[label] = (key, self.cnt[key])
        return self.cnt[key]

    def label(self, label, key):
        key = self.pfx + key
        if self.pass2:
            assert self.rec.get(label) == (key, self.cnt.get(key, 0)), \
                f"schedule drift at {label}"
        else:
            self.rec[label] = (key, self.cnt.get(key, 0))

    def wait(self, eng, label):
        if not self.pass2:
            return
        key, val = self.rec[label]
        if val > 0:
            eng.wait_ge(self.sems[key], val)

    def wait_cnt(self, eng, key, val):
        if self.pass2 and val > 0:
            eng.wait_ge(self.sems[self.pfx + key], val)

    def sem(self, key):
        return self.sems[self.pfx + key]


def _ios(nc, name, shape, dtype, kind):
    return nc.dram_tensor(name, shape, dtype, kind=kind).ap()


IO_SPECS = {
    "s1": {"in": [("x", [T, D], F32), ("w_in", [8, 128, 128], F32),
                  ("w_a", [8, 128, 128], F32), ("b_inT", [128, 8], F32),
                  ("b_aT", [128, 8], F32), ("a_paramT", [128, 8], F32)],
           "out": [("xT", [D, T], F32), ("xnbf", [D, T], BF16),
                   ("s0T", [D, T], F32)]},
    "s2": {"in": [("W1T", [1024, 4096], F32), ("W2T", [4096, 1024], F32),
                  ("b1c", [128, 32], F32), ("xnbf", [D, T], BF16)],
           "out": [("h1", [4096, T], BF16), ("W2t", [4096, 1024], BF16)]},
    "s3": {"in": [("W2t", [4096, 1024], BF16), ("b2c", [128, 8], F32),
                  ("h1", [4096, T], BF16), ("xT", [D, T], F32),
                  ("s0T", [D, T], F32)],
           "out": [("x1T", [D, T], F32)]},
    "s4": {"in": [("x1T", [D, T], F32), ("WinT", [1024, 4232], F32),
                  ("bz_c", [128, 16], F32), ("bxs_r", [1, 2048], F32),
                  ("bbc_c", [128, 1], F32), ("bbc_r", [1, 128], F32),
                  ("bdt_c", [8, 1], F32), ("dtb_c", [8, 1], F32),
                  ("Alog_c", [8, 1], F32), ("Dsk_r", [1, 8], F32),
                  ("gam_c", [128, 8], F32), ("bet_c", [128, 8], F32)],
           "out": [("xn1T", [D, T], F32), ("yfinT", [2048, T], BF16)]},
    "s5": {"in": [("Wo", [2048, 1024], F32), ("bo_c", [128, 8], F32),
                  ("yfinT", [2048, T], BF16), ("xn1T", [D, T], F32)],
           "out": [("s1T", [D, T], F32)]},
    "s6": {"in": [("wgT", [1024, 3072], I32), ("wuT", [1024, 3072], I32),
                  ("xn1T", [D, T], F32)],
           "out": [("hgT", [3072, T], BF16)]},
    "s7": {"in": [("wdT", [3072, 1024], I32), ("hgT", [3072, T], BF16),
                  ("x1T", [D, T], F32), ("s1T", [D, T], F32)],
           "out": [("out", [T, D], F32)]},
}


class Env:
    def __init__(self, nc, io, outer, pfx=""):
        self.nc = nc
        self.io = io
        self.outer = outer
        self.pfx = pfx


def _standalone_env(stage, outer):
    nc = bass.Bass("TRN2", target_bir_lowering=False)
    io = {}
    for name, shape, dt_ in IO_SPECS[stage]["in"]:
        io[name] = _ios(nc, name, shape, dt_, "ExternalInput")
    for name, shape, dt_ in IO_SPECS[stage]["out"]:
        io[name] = _ios(nc, name, shape, dt_, "ExternalOutput")
    return Env(nc, io, outer)


_STAGES = ["s1", "s2", "s3", "s4", "s5", "s6", "s7"]
_ENGS = ["gpsimd", "sync", "tensor", "vector", "scalar"]


def _barrier_block(nc, bar, wait_val, do_inc):
    with nc.Block() as bb:
        def mk():
            def body(eng):
                if do_inc:
                    eng.nop().then_inc(bar, 1)
                if wait_val > 0:
                    eng.wait_ge(bar, wait_val)
            return body
        for engname in _ENGS:
            getattr(bb, engname)(mk())


def emit_s1(S: Sched, env):
    nc = env.nc
    x = env.io["x"]
    w_in = env.io["w_in"]
    w_a = env.io["w_a"]
    b_inT = env.io["b_inT"]
    b_aT = env.io["b_aT"]
    a_paramT = env.io["a_paramT"]
    xT_d = env.io["xT"]
    xnbf_d = env.io["xnbf"]
    s0_d = env.io["s0T"]

    with ExitStack() as ctx:
        sb = lambda n, s, d: ctx.enter_context(nc.sbuf_tensor(env.pfx + n, s, d))
        psx = lambda n, s, d: ctx.enter_context(nc.psum_tensor(env.pfx + n, s, d))
        sem = lambda n: env.outer.enter_context(nc.semaphore(env.pfx + n))

        xtok = [sb("xtok0", [128, 4096], F32), sb("xtok1", [128, 4096], F32)]
        cN = sb("cN", [128, 4096], F32)
        dX = sb("dX", [128, 4096], F32)
        eG = sb("eG", [128, 4096], F32)
        fA = sb("fA", [128, 4096], F32)
        gM = sb("gM", [128, 4096], F32)
        laX = sb("laX", [128, 8 * 513], F32)
        iL = sb("iL", [128, 4096], F32)
        jS = sb("jS", [128, 4096], F32)
        xnT = sb("xnT", [128, 4096], F32)
        xnbfT = sb("xnbfT", [128, 4096], BF16)
        sqt = sb("sqt", [128, 1024], F32)
        mu_c = sb("mu_c", [128, 4], F32)
        sq_c = sb("sq_c", [128, 4], F32)
        rs_c = sb("rs_c", [128, 4], F32)
        Lcar = sb("Lcar", [128, 8], F32)
        Scar = sb("Scar", [128, 8], F32)
        wbf = sb("wbf", [128, 2048], BF16)
        bcols = sb("bcols", [128, 16], F32)
        negc = sb("negc", [128, 8], F32)
        apc = sb("apc", [128, 8], F32)
        tmp8 = sb("tmp8", [128, 8], F32)
        idf = sb("idf", [128, 128], F32)
        zeros = sb("zeros", [128, 512], F32)

        ps_tr = [psx("ps_tr0", [128, 512], F32), psx("ps_tr1", [128, 512], F32)]
        ps_g = [psx("ps_g0", [128, 512], F32), psx("ps_g1", [128, 512], F32)]

        for k in ["d", "w", "t", "v", "a", "p", "o1", "o2", "o3"]:
            S.reg(k, sem("sem_" + k))
        block = ctx.enter_context(nc.Block())

        @block.gpsimd
        def _(g):
            wbf3 = wbf.rearrange("p (g h j) -> p g h j", g=2, j=128)
            S.inc(g.dma_start(out=wbf3[:, 0], in_=w_in.rearrange("h i j -> i h j")),
                  "w", 16)
            S.inc(g.dma_start(out=wbf3[:, 1], in_=w_a.rearrange("h i j -> i h j")),
                  "w", 16)
            S.inc(g.dma_start(out=bcols[:, 0:8], in_=b_inT), "w", 16)
            S.inc(g.dma_start(out=bcols[:, 8:16], in_=b_aT), "w", 16)
            S.inc(g.dma_start(out=apc[:], in_=a_paramT), "w", 16)
            g.wait_ge(S.sem("w"), 80)
            S.chain(g.memset(idf[:], 1.0), "p")
            S.chain(g.affine_select(idf[:], idf[:], pattern=[[1, 128]], base=0,
                                    channel_multiplier=-1,
                                    compare_op=ALU.is_equal, fill=0.0), "p")
            S.chain(g.memset(zeros[:], 0.0), "p")
            S.chain(g.memset(Lcar[:], 0.0), "p")
            S.chain(g.memset(Scar[:], 0.0), "p")
            S.chain(g.memset(laX[:], 0.0), "p", label="s1_const")

        @block.sync
        def _(sy):
            def _stores(b):
                S.wait(sy, f"s1_xT_done_{b}")
                for k in range(NT):
                    S.inc(sy.dma_start(
                        out=xT_d[k * 128:(k + 1) * 128, b * 512:(b + 1) * 512],
                        in_=dX[:, k * 512:(k + 1) * 512]), "o1", 16,
                        label=(f"s1_xTst_{b}" if k == NT - 1 else None))
                S.wait(sy, f"s1_xnbf_done_{b}")
                for k in range(NT):
                    S.inc(sy.dma_start(
                        out=xnbf_d[k * 128:(k + 1) * 128, b * 512:(b + 1) * 512],
                        in_=xnbfT[:, k * 512:(k + 1) * 512]), "o2", 16,
                        label=(f"s1_xnst_{b}" if k == NT - 1 else None))
                S.wait(sy, f"s1_h_done_{b}")
                for k in range(NT):
                    S.inc(sy.dma_start(
                        out=s0_d[k * 128:(k + 1) * 128, b * 512:(b + 1) * 512],
                        in_=iL[:, k * 512:(k + 1) * 512]), "o3", 16,
                        label=(f"s1_out_{b}" if k == NT - 1 else None))
            for b in range(NBLK):
                if b >= 2:
                    S.wait(sy, f"s1_trx_{b - 2}_3_1")
                    S.wait(sy, f"s1_sq_{b - 2}")
                    S.wait(sy, f"s1_ln_{b - 2}_3")
                if b >= 1:
                    S.wait(sy, f"s1_xld_{b - 1}")
                for tt in range(4):
                    S.inc(sy.dma_start(
                        out=xtok[b % 2][:, tt * 1024:(tt + 1) * 1024],
                        in_=x[(b * 4 + tt) * 128:(b * 4 + tt + 1) * 128, :]),
                        "d", 16, label=(f"s1_xld_{b}" if tt == 3 else None))
                if b >= 1:
                    _stores(b - 1)
            _stores(NBLK - 1)
            sy.wait_ge(S.sem("o1"), 16 * 8 * NBLK)
            sy.wait_ge(S.sem("o2"), 16 * 8 * NBLK)
            sy.wait_ge(S.sem("o3"), 16 * 8 * NBLK)

        @block.tensor
        def _(t):
            S.wait(t, "s1_const")
            S.wait_cnt(t, "w", 80)
            for b in range(NBLK):
                for tt in range(4):
                    S.wait_cnt(t, "d", 64 * (b + 1))
                    for half in range(2):
                        if tt > 0:
                            S.wait(t, f"s1_trxe_{b}_{tt - 1}_{half}")
                        elif b > 0:
                            S.wait(t, f"s1_trne_{b - 1}_3_{half}")
                        for j in range(4):
                            jj = half * 4 + j
                            ti = t.transpose(ps_tr[half][:, j * 128:(j + 1) * 128],
                                             xtok[b % 2][:, tt * 1024 + jj * 128:
                                                         tt * 1024 + (jj + 1) * 128],
                                             idf[:])
                            S.inc(ti, "t",
                                  label=(f"s1_trx_{b}_{tt}_{half}" if j == 3
                                         else None))
                for tt in range(4):
                    S.wait(t, f"s1_ln_{b}_{tt}")
                    for half in range(2):
                        if tt > 0:
                            S.wait(t, f"s1_trne_{b}_{tt - 1}_{half}")
                        else:
                            S.wait(t, f"s1_trxe_{b}_3_{half}")
                        for j in range(4):
                            jj = half * 4 + j
                            ti = t.transpose(ps_tr[half][:, j * 128:(j + 1) * 128],
                                             cN[:, tt * 1024 + jj * 128:
                                                tt * 1024 + (jj + 1) * 128],
                                             idf[:])
                            S.inc(ti, "t",
                                  label=(f"s1_trn_{b}_{tt}_{half}" if j == 3
                                         else None))
                S.wait(t, f"s1_xnbf_done_{b}")
                for h in range(8):
                    for gi in range(2):
                        if h > 0:
                            S.wait(t, f"s1_sgm_{b}_{h - 1}_{gi}")
                        elif b > 0:
                            S.wait(t, f"s1_sgm_{b - 1}_7_{gi}")
                        S.inc(t.matmul(ps_g[gi][:],
                                       wbf[:, gi * 1024 + h * 128:
                                           gi * 1024 + (h + 1) * 128],
                                       xnbfT[:, h * 512:(h + 1) * 512],
                                       start=True, stop=True),
                              "t", label=f"s1_gate_{b}_{h}_{gi}")

        @block.scalar
        def _(s):
            S.wait_cnt(s, "w", 80)
            S.wait(s, "s1_const")
            S.chain(s.activation(tmp8[:], apc[:], AF.Abs), "a")
            S.chain(s.activation(tmp8[:], tmp8[:], AF.Exp, scale=-1.0), "a")
            S.chain(s.activation(tmp8[:], tmp8[:], AF.Ln, bias=1.0), "a",
                    label="s1_sp1")
            for b in range(NBLK):
                for tt in range(4):
                    S.wait_cnt(s, "d", 64 * (b + 1))
                    S.chain(s.activation(sqt[:],
                                         xtok[b % 2][:, tt * 1024:(tt + 1) * 1024],
                                         AF.Square, accum_out=sq_c[:, tt:tt + 1]),
                            "a", label=(f"s1_sq_{b}" if tt == 3 else None))
                S.wait(s, f"s1_var_{b}")
                S.chain(s.activation(rs_c[:], rs_c[:], AF.Ln), "a")
                S.chain(s.activation(rs_c[:], rs_c[:], AF.Exp, scale=-0.5), "a",
                        label=f"s1_rstd_{b}")
                if b > 0:
                    S.wait(s, f"s1_h_done_{b - 1}")
                for h in range(8):
                    for gi in range(2):
                        S.wait(s, f"s1_gate_{b}_{h}_{gi}")
                        dst = eG if gi == 0 else fA
                        S.chain(s.activation(dst[:, h * 512:(h + 1) * 512],
                                             ps_g[gi][:], AF.Sigmoid,
                                             bias=bcols[:, 8 * gi + h:
                                                        8 * gi + h + 1]),
                                "a", label=f"s1_sgm_{b}_{h}_{gi}")
                S.wait(s, f"s1_la_{b}")
                for k in range(NT):
                    S.chain(s.activation(fA[:, k * 512:(k + 1) * 512],
                                         laX[:, k * 513 + 1:k * 513 + 513],
                                         AF.Exp, scale=2.0), "a",
                            label=(f"s1_a2_{b}" if k == NT - 1 else None))
                S.wait(s, f"s1_m2_{b}")
                S.chain(s.activation(gM[:], gM[:], AF.Ln), "a")
                S.chain(s.activation(gM[:], gM[:], AF.Exp, scale=0.5), "a",
                        label=f"s1_mult_{b}")
                S.wait(s, f"s1_lc_{b}")
                S.wait(s, f"s1_xgm_{b}")
                S.chain(s.activation(gM[:], iL[:], AF.Exp, scale=-1.0), "a",
                        label=f"s1_pinv_{b}")
                S.wait(s, f"s1_w_{b}")
                S.chain(s.activation(eG[:], iL[:], AF.Exp), "a",
                        label=f"s1_p_{b}")

        @block.vector
        def _(v):
            S.wait(v, "s1_sp1")
            S.chain(v.tensor_scalar_max(negc[:], apc[:], 0.0), "v")
            S.chain(v.tensor_add(negc[:], negc[:], tmp8[:]), "v")
            S.chain(v.tensor_scalar(negc[:], negc[:], 1e-3, -1.0,
                                    op0=ALU.add, op1=ALU.mult), "v",
                    label="s1_negc")
            for b in range(NBLK):
                for tt in range(4):
                    S.wait_cnt(v, "d", 64 * (b + 1))
                    S.chain(v.tensor_reduce(mu_c[:, tt:tt + 1],
                                            xtok[b % 2][:, tt * 1024:(tt + 1) * 1024],
                                            axis=AX.X, op=ALU.add), "v")
                S.wait(v, f"s1_sq_{b}")
                S.chain(v.tensor_scalar_mul(mu_c[:], mu_c[:], 1.0 / D), "v")
                S.chain(v.tensor_mul(rs_c[:], mu_c[:], mu_c[:]), "v")
                S.chain(v.scalar_tensor_tensor(rs_c[:], sq_c[:], 1.0 / D, rs_c[:],
                                               op0=ALU.mult, op1=ALU.subtract), "v")
                S.chain(v.tensor_scalar_add(rs_c[:], rs_c[:], 1e-5), "v",
                        label=f"s1_var_{b}")
                S.wait(v, f"s1_rstd_{b}")
                for tt in range(4):
                    S.chain(v.tensor_scalar(cN[:, tt * 1024:(tt + 1) * 1024],
                                            xtok[b % 2][:, tt * 1024:(tt + 1) * 1024],
                                            mu_c[:, tt:tt + 1], rs_c[:, tt:tt + 1],
                                            op0=ALU.subtract, op1=ALU.mult),
                            "v", label=f"s1_ln_{b}_{tt}")
                if b > 0:
                    S.wait(v, f"s1_xTst_{b - 1}")
                for tt in range(4):
                    for half in range(2):
                        S.wait(v, f"s1_trx_{b}_{tt}_{half}")
                        dst = dX.rearrange("p (k c) -> p k c", c=512)
                        S.chain(v.tensor_copy(
                            dst[:, half * 4:half * 4 + 4, tt * 128:(tt + 1) * 128],
                            ps_tr[half].rearrange("p (k c) -> p k c", c=128)),
                            "v", label=f"s1_trxe_{b}_{tt}_{half}")
                S.label(f"s1_xT_done_{b}", "v")
                if b > 0:
                    S.wait(v, f"s1_xnst_{b - 1}")
                    S.wait(v, f"s1_gate_{b - 1}_7_1")
                for tt in range(4):
                    for half in range(2):
                        S.wait(v, f"s1_trn_{b}_{tt}_{half}")
                        dstn = xnT.rearrange("p (k c) -> p k c", c=512)
                        S.chain(v.tensor_copy(
                            dstn[:, half * 4:half * 4 + 4, tt * 128:(tt + 1) * 128],
                            ps_tr[half].rearrange("p (k c) -> p k c", c=128)),
                            "v", label=f"s1_trne_{b}_{tt}_{half}")
                S.chain(v.tensor_copy(xnbfT[:], xnT[:]), "v",
                        label=f"s1_xnbf_done_{b}")
                if b == 0:
                    S.wait(v, "s1_negc")
                else:
                    S.wait(v, f"s1_a2_{b - 1}")
                for k in range(NT):
                    S.wait(v, f"s1_sgm_{b}_{k}_1")
                    S.chain(v.tensor_scalar(laX[:, k * 513 + 1:k * 513 + 513],
                                            fA[:, k * 512:(k + 1) * 512],
                                            negc[:, k:k + 1], L1M,
                                            op0=ALU.mult, op1=ALU.min),
                            "v", label=(f"s1_la_{b}" if k == NT - 1 else None))
                if b > 0:
                    S.wait(v, f"s1_out_{b - 1}")
                for k in range(NT):
                    S.chain(v.tensor_tensor_scan(iL[:, k * 512:(k + 1) * 512],
                                                 laX[:, k * 513:k * 513 + 512],
                                                 zeros[:, 0:512],
                                                 Lcar[:, k:k + 1],
                                                 op0=ALU.add, op1=ALU.add), "v")
                S.chain(v.tensor_copy(
                    Lcar.rearrange("p (k o) -> p k o", o=1),
                    iL.rearrange("p (k c) -> p k c", c=512)[:, :, 511:512]), "v")
                S.chain(v.tensor_copy(
                    laX.rearrange("p (k c) -> p k c", c=513)[:, :, 0:1],
                    laX.rearrange("p (k c) -> p k c", c=513)[:, :, 512:513]), "v")
                S.chain(v.tensor_scalar_max(iL[:], iL[:], LN7), "v",
                        label=f"s1_lc_{b}")
                S.wait(v, f"s1_a2_{b}")
                S.chain(v.tensor_scalar(gM[:], fA[:], -1.0, 1.0,
                                        op0=ALU.mult, op1=ALU.add), "v")
                S.chain(v.tensor_scalar_max(gM[:], gM[:], 1e-5), "v",
                        label=f"s1_m2_{b}")
                S.wait(v, f"s1_sgm_{b}_7_0")
                S.chain(v.tensor_mul(cN[:], xnT[:], eG[:]), "v",
                        label=f"s1_xg_{b}")
                S.wait(v, f"s1_mult_{b}")
                S.chain(v.tensor_mul(eG[:], cN[:], gM[:]), "v",
                        label=f"s1_xgm_{b}")
                S.wait(v, f"s1_pinv_{b}")
                S.chain(v.scalar_tensor_tensor(cN[:], eG[:], 1e-7, gM[:],
                                               op0=ALU.add, op1=ALU.mult),
                        "v", label=f"s1_w_{b}")
                for k in range(NT):
                    S.chain(v.tensor_tensor_scan(jS[:, k * 512:(k + 1) * 512],
                                                 cN[:, k * 512:(k + 1) * 512],
                                                 zeros[:, 0:512],
                                                 Scar[:, k:k + 1],
                                                 op0=ALU.add, op1=ALU.add), "v")
                S.chain(v.tensor_copy(
                    Scar.rearrange("p (k o) -> p k o", o=1),
                    jS.rearrange("p (k c) -> p k c", c=512)[:, :, 511:512]), "v",
                    label=f"s1_scum_{b}")
                S.wait(v, f"s1_p_{b}")
                S.chain(v.tensor_mul(iL[:], eG[:], jS[:]), "v",
                        label=f"s1_h_done_{b}")
    return nc




def emit_s2(S: Sched, env):
    nc = env.nc
    W1T = env.io["W1T"]
    W2T = env.io["W2T"]
    b1c = env.io["b1c"]
    xnbf_d = env.io["xnbf"]
    h1_d = env.io["h1"]
    W2t_d = env.io["W2t"]

    with ExitStack() as ctx:
        sb = lambda n, s_, d_: ctx.enter_context(nc.sbuf_tensor(env.pfx + n, s_, d_))
        psx = lambda n, s_, d_: ctx.enter_context(nc.psum_tensor(env.pfx + n, s_, d_))
        sem = lambda n: env.outer.enter_context(nc.semaphore(env.pfx + n))

        wf = [sb(f"wf{i}", [128, 1024], F32) for i in range(4)]
        wi = sb("wi", [128, 1024], F32)
        sc2 = sb("sc2", [1, 2], F32)
        invc2 = sb("invc2", [128, 2], F32)
        w2s = [sb(f"w2s{i}", [128, 1024], BF16) for i in range(2)]
        wT = [sb(f"wT{i}", [128, 4096], BF16) for i in range(8)]
        xnb = [sb("xnb0", [128, 8 * 512], BF16), sb("xnb1", [128, 8 * 512], BF16)]
        h1s = sb("h1s", [128, 32 * 512], BF16)
        b1s = sb("b1s", [128, 32], F32)
        acc = sb("acc", [128, 1], F32)
        red = sb("red", [128, 1], F32)
        onec = sb("onec", [128, 1], F32)
        onerow = sb("onerow", [1, 128], F32)
        sc1 = sb("sc1", [1, 1], F32)
        invc = sb("invc", [128, 1], F32)
        ps_r = [psx("ps_r0", [128, 512], F32), psx("ps_r1", [128, 512], F32),
                psx("ps_r2", [128, 512], F32)]
        ps_s = psx("ps_s", [1, 1], F32)
        ps_c = psx("ps_c", [128, 2], F32)

        for k in ["d0", "d1", "w", "w0", "w1", "w2", "w3", "t", "v", "a", "p",
                  "o1", "o20", "o21"]:
            S.reg(k, sem("sem_" + k))
        block = ctx.enter_context(nc.Block())

        @block.gpsimd
        def _(g):
            S.inc(g.dma_start(out=b1s[:], in_=b1c), "w", 16)
            def _w1src(k):
                fg, dc = k // 8, k % 8
                return W1T[dc * 128:(dc + 1) * 128,
                           fg * 1024:(fg + 1) * 1024]
            for k in range(32):
                if k >= 4:
                    S.wait(g, f"s2_absred_{k - 4}")
                S.inc(g.dma_start(out=wf[k % 4][:], in_=_w1src(k)),
                      f"w{k % 4}", 16, label=f"s2_wfA_{k}")
            for k in range(32):
                S.wait(g, f"s2_w1t_{k - 4}" if k >= 4
                       else f"s2_absred_{28 + k}")
                S.inc(g.dma_start(out=wf[k % 4][:], in_=_w1src(k)),
                      f"w{k % 4}", 16, label=f"s2_wfB_{k}")
            for dc in range(32):
                if dc >= 4:
                    S.wait(g, f"s2_w2red_{dc - 4}")
                else:
                    S.wait(g, f"s2_w1t_{28 + dc}")
                S.inc(g.dma_start(out=wf[dc % 4][:],
                                  in_=W2T[dc * 128:(dc + 1) * 128, :]),
                      f"w{dc % 4}", 16, label=f"s2_w2A_{dc}")
            for dc in range(32):
                S.wait(g, f"s2_w2t_{dc - 4}" if dc >= 4 else "s2_scale2")
                S.inc(g.dma_start(out=wf[dc % 4][:],
                                  in_=W2T[dc * 128:(dc + 1) * 128, :]),
                      f"w{dc % 4}", 16, label=f"s2_w2B_{dc}")
                if dc >= 2:
                    S.wait(g, f"s2_w2t_{dc - 2}")
                    S.inc(g.dma_start(
                        out=W2t_d[(dc - 2) * 128:(dc - 1) * 128, :],
                        in_=w2s[dc % 2][:]), f"o2{dc % 2}", 16,
                        label=f"s2_w2st_{dc - 2}")
            for dc in (30, 31):
                S.wait(g, f"s2_w2t_{dc}")
                S.inc(g.dma_start(out=W2t_d[dc * 128:(dc + 1) * 128, :],
                                  in_=w2s[dc % 2][:]), f"o2{dc % 2}", 16,
                      label=f"s2_w2st_{dc}")

        @block.sync
        def _(sy):
            for b in range(NBLK):
                if b >= 2:
                    S.wait(sy, f"s2_mmdone_{b - 2}")
                S.inc(sy.dma_start(
                    out=xnb[b % 2].rearrange("p (k c) -> p k c", c=512),
                    in_=xnbf_d.rearrange("(k p) t -> p k t", p=128)[:, :, b * 512:(b + 1) * 512]),
                    f"d{b % 2}", 16, label=f"s2_xnb_{b}")
                if b >= 1:
                    S.wait(sy, f"s2_h1done_{b - 1}")
                    S.inc(sy.dma_start(
                        out=h1_d.rearrange("(f p) t -> p f t", p=128)[:, :, (b - 1) * 512:b * 512],
                        in_=h1s.rearrange("p (f c) -> p f c", c=512)),
                        "o1", 16, label=f"s2_h1st_{b - 1}")
            S.wait(sy, f"s2_h1done_{NBLK - 1}")
            S.inc(sy.dma_start(
                out=h1_d.rearrange("(f p) t -> p f t", p=128)[:, :, (NBLK - 1) * 512:NBLK * 512],
                in_=h1s.rearrange("p (f c) -> p f c", c=512)),
                "o1", 16, label=f"s2_h1st_{NBLK - 1}")
            sy.wait_ge(S.sem("o1"), 16 * NBLK)
            sy.wait_ge(S.sem("o20"), 16 * 16)
            sy.wait_ge(S.sem("o21"), 16 * 16)

        @block.vector
        def _(v):
            S.chain(v.memset(acc[:], 0.0), "v")
            S.chain(v.memset(onec[:], 1.0), "v")
            S.chain(v.memset(onerow[:], 1.0), "v", label="s2_ones")
            for k in range(32):
                S.wait(v, f"s2_wfA_{k}")
                S.chain(v.tensor_reduce(red[:], wf[k % 4][:], axis=AX.X,
                                        op=ALU.add, apply_absolute_value=True), "v")
                S.chain(v.tensor_add(acc[:], acc[:], red[:]), "v",
                        label=f"s2_absred_{k}")
            S.wait(v, "s2_total")
            S.chain(v.tensor_copy(sc2[0:1, 0:1], ps_s[:]), "v")
            S.chain(v.tensor_scalar(sc2[0:1, 0:1], sc2[0:1, 0:1],
                                    0.5 / (4096 * 1024), 0.5e-8,
                                    op0=ALU.mult, op1=ALU.max), "v")
            S.chain(v.tensor_scalar_mul(sc2[0:1, 1:2], sc2[0:1, 0:1],
                                        -1.0), "v", label="s2_inv")
            S.wait(v, "s2_invb")
            S.chain(v.tensor_copy(invc2[:], ps_c[:, 0:2]), "v",
                    label="s2_scale")
            for k in range(32):
                fg, dc = k // 8, k % 8
                S.wait(v, f"s2_wfB_{k}")
                S.chain(v.tensor_scalar(wi[:], wf[k % 4][:],
                                        invc2[:, 1:2], None,
                                        op0=ALU.is_lt), "v")
                S.chain(v.tensor_scalar(wf[k % 4][:], wf[k % 4][:],
                                        invc2[:, 0:1], None,
                                        op0=ALU.is_gt), "v")
                S.chain(v.tensor_sub(wT[dc][:, fg * 1024:(fg + 1) * 1024],
                                     wf[k % 4][:], wi[:]), "v",
                        label=f"s2_w1t_{k}")
            S.chain(v.memset(acc[:], 0.0), "v")
            for dc in range(32):
                S.wait(v, f"s2_w2A_{dc}")
                S.chain(v.tensor_reduce(red[:], wf[dc % 4][:], axis=AX.X,
                                        op=ALU.add, apply_absolute_value=True), "v")
                S.chain(v.tensor_add(acc[:], acc[:], red[:]), "v",
                        label=f"s2_w2red_{dc}")
            S.wait(v, "s2_total2")
            S.chain(v.tensor_copy(sc1[:], ps_s[:]), "v")
            S.chain(v.tensor_scalar(sc1[:], sc1[:], 1.0 / (4096 * 1024), 1e-8,
                                    op0=ALU.mult, op1=ALU.max), "v")
            S.chain(v.reciprocal(sc1[:], sc1[:]), "v", label="s2_inv2")
            S.wait(v, "s2_invb2")
            S.chain(v.tensor_copy(invc[:], ps_c[:, 0:1]), "v",
                    label="s2_scale2")
            for dc in range(32):
                S.wait(v, f"s2_w2B_{dc}")
                if dc >= 2:
                    S.wait(v, f"s2_w2st_{dc - 2}")
                S.chain(v.tensor_scalar_mul(wf[dc % 4][:],
                                            wf[dc % 4][:],
                                            invc[:, 0:1]), "v")
                S.chain(v.tensor_scalar(wi[:], wf[dc % 4][:],
                                        -0.5, None, op0=ALU.is_lt), "v")
                S.chain(v.tensor_scalar(wf[dc % 4][:],
                                        wf[dc % 4][:], 0.5, None,
                                        op0=ALU.is_gt), "v")
                S.chain(v.tensor_sub(w2s[dc % 2][:], wf[dc % 4][:],
                                     wi[:]), "v",
                        label=f"s2_w2t_{dc}")

        @block.tensor
        def _(t):
            S.wait(t, "s2_absred_31")
            S.inc(t.matmul(ps_s[:], onec[:], acc[:], start=True, stop=True),
                  "t", label="s2_total")
            S.wait(t, "s2_inv")
            S.wait(t, "s2_ones")
            S.inc(t.matmul(ps_c[:, 0:2], onerow[:], sc2[:], start=True,
                           stop=True), "t", label="s2_invb")
            S.wait_cnt(t, "w", 16)
            for b in range(NBLK):
                if b == 3:
                    S.wait(t, "s2_w2red_31")
                    S.wait(t, "s2_scale")
                    S.inc(t.matmul(ps_s[:], onec[:], acc[:], start=True,
                                   stop=True), "t", label="s2_total2")
                if b == 4:
                    S.wait(t, "s2_inv2")
                    S.inc(t.matmul(ps_c[:, 0:1], onerow[:], sc1[:], start=True,
                                   stop=True), "t", label="s2_invb2")
                S.wait(t, f"s2_xnb_{b}")
                for ft in range(32):
                    if b == 0 and ft % 8 == 0:
                        S.wait(t, f"s2_w1t_{(ft // 8) * 8 + 7}")
                    ps = ps_r[ft % 3]
                    if ft >= 3:
                        S.wait(t, f"s2_gelu_{b}_{ft - 3}")
                    elif b > 0:
                        prevu = max(x for x in range(32) if x % 3 == ft % 3)
                        S.wait(t, f"s2_gelu_{b - 1}_{prevu}")
                    for dc in range(8):
                        S.inc(t.matmul(ps[:], wT[dc][:, ft * 128:(ft + 1) * 128],
                                       xnb[b % 2][:, dc * 512:(dc + 1) * 512],
                                       start=(dc == 0), stop=(dc == 7)),
                              "t", label=(f"s2_mm_{b}_{ft}" if dc == 7 else None))
                S.label(f"s2_mmdone_{b}", "t")

        @block.scalar
        def _(s):
            for b in range(NBLK):
                if b > 0:
                    S.wait(s, f"s2_h1st_{b - 1}")
                for ft in range(32):
                    ps = ps_r[ft % 3]
                    S.wait(s, f"s2_mm_{b}_{ft}")
                    S.chain(s.activation(h1s[:, ft * 512:(ft + 1) * 512], ps[:],
                                         AF.Gelu, bias=b1s[:, ft:ft + 1]),
                            "a", label=f"s2_gelu_{b}_{ft}")
                S.label(f"s2_h1done_{b}", "a")
    return nc


def build_s2():
    S = Sched()
    nc = None
    for p2 in (False, True):
        S.begin(p2)
        with ExitStack() as outer:
            env = _standalone_env("s2", outer)
            emit_s2(S, env)
            nc = env.nc
    return nc


def emit_s3(S: Sched, env):
    nc = env.nc
    W2t_d = env.io["W2t"]
    b2c = env.io["b2c"]
    h1_d = env.io["h1"]
    xT_d = env.io["xT"]
    s0_d = env.io["s0T"]
    x1_d = env.io["x1T"]

    with ExitStack() as ctx:
        sb = lambda n, s_, d_: ctx.enter_context(nc.sbuf_tensor(env.pfx + n, s_, d_))
        psx = lambda n, s_, d_: ctx.enter_context(nc.psum_tensor(env.pfx + n, s_, d_))
        sem = lambda n: env.outer.enter_context(nc.semaphore(env.pfx + n))

        wT = [sb(f"wT{i}", [128, 1024], BF16) for i in range(32)]
        h1b = [sb("h1b0", [128, 32 * 512], BF16), sb("h1b1", [128, 32 * 512], BF16)]
        xb = [sb("xb0", [128, 8 * 512], F32), sb("xb1", [128, 8 * 512], F32)]
        s0b = [sb("s0b0", [128, 8 * 512], F32), sb("s0b1", [128, 8 * 512], F32)]
        gb = sb("gb", [128, 2 * 512], F32)
        b2s = sb("b2s", [128, 8], F32)
        ps_r = [psx("ps_r0", [128, 512], F32), psx("ps_r1", [128, 512], F32),
                psx("ps_r2", [128, 512], F32), psx("ps_r3", [128, 512], F32)]

        for k in ["d0", "d1", "d20", "d21", "w", "w0", "w1", "w2", "w3", "t",
                  "v", "a", "p", "o10", "o11"]:
            S.reg(k, sem("sem_" + k))
        block = ctx.enter_context(nc.Block())

        @block.gpsimd
        def _(g):
            for dc in range(32):
                S.inc(g.dma_start(out=wT[dc][:], in_=W2t_d[dc * 128:(dc + 1) * 128, :]),
                      "w0", 16)
            S.inc(g.dma_start(out=b2s[:], in_=b2c), "w", 16)
            for b in range(NBLK):
                if b >= 2:
                    S.wait(g, f"s3_comb_{b - 2}")
                    S.wait(g, f"s3_x1st_{b - 2}")
                S.inc(g.dma_start(
                    out=xb[b % 2].rearrange("p (k c) -> p k c", c=512),
                    in_=xT_d.rearrange("(k p) t -> p k t", p=128)[:, :, b * 512:(b + 1) * 512]),
                    f"d2{b % 2}", 16)
                S.inc(g.dma_start(
                    out=s0b[b % 2].rearrange("p (k c) -> p k c", c=512),
                    in_=s0_d.rearrange("(k p) t -> p k t", p=128)[:, :, b * 512:(b + 1) * 512]),
                    f"d2{b % 2}", 16, label=f"s3_xs_{b}")

        @block.sync
        def _(sy):
            for b in range(NBLK):
                if b >= 2:
                    S.wait(sy, f"s3_mmdone_{b - 2}")
                S.inc(sy.dma_start(
                    out=h1b[b % 2].rearrange("p (f c) -> p f c", c=512),
                    in_=h1_d.rearrange("(f p) t -> p f t", p=128)[:, :, b * 512:(b + 1) * 512]),
                    f"d{b % 2}", 16, label=f"s3_h1in_{b}")
                if b >= 1:
                    S.wait(sy, f"s3_comb_{b - 1}")
                    S.inc(sy.dma_start(
                        out=x1_d.rearrange("(k p) t -> p k t", p=128)[:, :, (b - 1) * 512:b * 512],
                        in_=xb[(b - 1) % 2].rearrange("p (k c) -> p k c", c=512)),
                        f"o1{(b - 1) % 2}", 16, label=f"s3_x1st_{b - 1}")
            S.wait(sy, f"s3_comb_{NBLK - 1}")
            S.inc(sy.dma_start(
                out=x1_d.rearrange("(k p) t -> p k t", p=128)[:, :, (NBLK - 1) * 512:NBLK * 512],
                in_=xb[(NBLK - 1) % 2].rearrange("p (k c) -> p k c", c=512)),
                f"o1{(NBLK - 1) % 2}", 16, label=f"s3_x1st_{NBLK - 1}")
            sy.wait_ge(S.sem("o10"), 16 * (NBLK // 2))
            sy.wait_ge(S.sem("o11"), 16 * (NBLK // 2))

        @block.vector
        def _(v):
            for b in range(NBLK):
                S.wait(v, f"s3_xs_{b}")
                for do in range(8):
                    S.wait(v, f"s3_silu_{b}_{do}")
                    sl = slice(do * 512, (do + 1) * 512)
                    gsl = slice((do % 2) * 512, (do % 2 + 1) * 512)
                    S.chain(v.tensor_mul(gb[:, gsl], s0b[b % 2][:, sl],
                                         gb[:, gsl]), "v")
                    S.chain(v.tensor_add(xb[b % 2][:, sl], xb[b % 2][:, sl],
                                         gb[:, gsl]), "v",
                            label=(f"s3_comb_{b}" if do == 7 else
                                   f"s3_combp_{b}_{do}"))

        @block.tensor
        def _(t):
            S.wait_cnt(t, "w0", 16 * 32)
            for b in range(NBLK):
                S.wait(t, f"s3_h1in_{b}")
                for do in range(8):
                    ps = ps_r[do % 4]
                    prev = do - 4 if do >= 4 else (4 + do if b > 0 else -1)
                    if prev >= 0:
                        S.wait(t, f"s3_silu_{b if do >= 4 else b - 1}_{prev}")
                    for fc in range(32):
                        S.inc(t.matmul(ps[:], wT[fc][:, do * 128:(do + 1) * 128],
                                       h1b[b % 2][:, fc * 512:(fc + 1) * 512],
                                       start=(fc == 0), stop=(fc == 31)),
                              "t", label=(f"s3_mm_{b}_{do}" if fc == 31 else None))
                S.label(f"s3_mmdone_{b}", "t")

        @block.scalar
        def _(s):
            S.wait_cnt(s, "w", 16)
            for b in range(NBLK):
                if b > 0:
                    S.wait(s, f"s3_comb_{b - 1}")
                for do in range(8):
                    ps = ps_r[do % 4]
                    S.wait(s, f"s3_mm_{b}_{do}")
                    if do >= 2:
                        S.wait(s, f"s3_combp_{b}_{do - 2}")
                    elif b > 0:
                        S.wait(s, f"s3_combp_{b - 1}_{6 + do}" if do == 0
                               else f"s3_comb_{b - 1}")
                    S.chain(s.activation(gb[:, (do % 2) * 512:(do % 2 + 1) * 512],
                                         ps[:], AF.Silu, bias=b2s[:, do:do + 1]),
                            "a", label=f"s3_silu_{b}_{do}")
    return nc


def build_s3():
    S = Sched()
    nc = None
    for p2 in (False, True):
        S.begin(p2)
        with ExitStack() as outer:
            env = _standalone_env("s3", outer)
            emit_s3(S, env)
            nc = env.nc
    return nc


def emit_s4(S: Sched, env):
    nc = env.nc
    x1_d = env.io["x1T"]
    WinT = env.io["WinT"]
    bz_c = env.io["bz_c"]
    bxs_r = env.io["bxs_r"]
    bbc_c = env.io["bbc_c"]
    bbc_r = env.io["bbc_r"]
    bdt_c = env.io["bdt_c"]
    dtb_c = env.io["dtb_c"]
    Alog_c = env.io["Alog_c"]
    Dsk_r = env.io["Dsk_r"]
    gam_c = env.io["gam_c"]
    bet_c = env.io["bet_c"]
    xn1_d = env.io["xn1T"]
    yf_d = env.io["yfinT"]

    with ExitStack() as ctx:
        sb = lambda n, s_, d_: ctx.enter_context(nc.sbuf_tensor(env.pfx + n, s_, d_))
        psx = lambda n, s_, d_: ctx.enter_context(nc.psum_tensor(env.pfx + n, s_, d_))
        sem = lambda n: env.outer.enter_context(nc.semaphore(env.pfx + n))

        winb = [sb(f"winb{i}", [128, 4232], BF16) for i in range(8)]
        _x1b0 = sb("x1b0", [128, 4096], F32)
        x1b = [_x1b0, _x1b0]
        xnT = _x1b0
        ubf = sb("ubf", [128, 4096], BF16)
        zsil = sb("zsil", [128, 16 * 512], BF16)
        onec_bf = sb("onec_bf", [128, 1], BF16)
        stg_x = sb("stg_x", [128, 1024], BF16)
        stg_q = sb("stg_q", [128, 1024], BF16)
        xst = sb("xst", [128, 4 * 2048], BF16)
        dtxt = sb("dtxt", [128, 4 * 2048], BF16)
        yfin = sb("yfin", [128, 16 * 512], BF16)
        Bf = sb("Bf", [64, 512], BF16)
        Cf = sb("Cf", [64, 512], BF16)
        Cx8 = sb("Cx8", [64, 8 * 128], BF16)
        Btok = sb("Btok", [128, 4 * 64], BF16)
        Bw8 = sb("Bw8", [128, 8 * 64], BF16)
        Msb8 = sb("Msb8", [128, 8 * 128], BF16)
        Mt = sb("Mt", [128, 128], F32)
        BCm = sb("BCm", [128, 128], BF16)
        sct8 = sb("sct8", [128, 8 * 128], BF16)
        dtr = sb("dtr", [8, 512], F32)
        t8a = sb("t8a", [8, 512], F32)
        eLe = t8a
        Lrow = sb("Lrow", [8, 512], F32)
        eLp = sb("eLp", [8, 512], F32)
        smallTT = sb("smallTT", [128, 4 * 24], F32)
        erow = sb("erow", [1, 4 * 8], F32)
        dsk_s = sb("dsk_s", [1, 8], F32)
        dcol = sb("dcol", [64, 4 * 8], F32)
        stf = sb("stf", [64, 2048], F32)
        stb = sb("stb", [64, 2048], BF16)
        IDh = sb("IDh", [128, 8 * 128], BF16)
        idf = sb("idf", [128, 128], F32)
        idb = sb("idb", [128, 128], BF16)
        trib = sb("trib", [128, 128], BF16)
        oneh8 = sb("oneh8", [8, 8 * 128], F32)
        onec = sb("onec", [128, 1], F32)
        onerow = sb("onerow", [1, 128], F32)
        onerow_bf = sb("onerow_bf", [1, 128], BF16)
        bxs_bf = sb("bxs_bf", [1, 2048], BF16)
        bbc_rb = sb("bbc_rb", [1, 128], BF16)
        murow = sb("murow", [1, 512], F32)
        vrow = sb("vrow", [1, 512], F32)
        mrow2 = sb("mrow2", [1, 512], F32)
        bsum = sb("bsum", [8, 1], F32)
        dtbs = sb("dtbs", [8, 1], F32)
        negA = sb("negA", [8, 1], F32)
        Dcols = sb("Dcols", [128, 8], F32)
        bzs = sb("bzs", [128, 16], F32)
        bbcs = sb("bbcs", [128, 1], F32)
        gams = sb("gams", [128, 8], F32)
        bets = sb("bets", [128, 8], F32)

        ps_io = [psx("ps_io0", [128, 512], F32), psx("ps_io1", [128, 512], F32)]
        ps_stat = psx("ps_stat", [128, 512], F32)
        ps_bc2 = psx("ps_bc2", [128, 512], F32)
        ps_sc = psx("ps_sc", [128, 512], F32)
        ps_y = psx("ps_y", [128, 512], F32)
        ps_stu = psx("ps_stu", [64, 512], F32)
        ps_mis = psx("ps_mis", [128, 512], F32)

        for k in ["dx", "w", "wc", "t", "v", "a", "p", "o1", "o2"]:
            S.reg(k, sem("sem_" + k))
        block = ctx.enter_context(nc.Block())
        NCH = 4

        @block.gpsimd
        def _(g):
            for dc in range(8):
                S.inc(g.dma_start(out=winb[dc][:],
                                  in_=WinT[dc * 128:(dc + 1) * 128, :]), "w", 16)
            S.inc(g.dma_start(out=bzs[:], in_=bz_c), "wc", 16)
            S.inc(g.dma_start(out=bxs_bf[:], in_=bxs_r), "wc", 16)
            S.inc(g.dma_start(out=bbcs[:], in_=bbc_c), "wc", 16)
            S.inc(g.dma_start(out=bbc_rb[:], in_=bbc_r), "wc", 16)
            S.inc(g.dma_start(out=bsum[:], in_=bdt_c), "wc", 16)
            S.inc(g.dma_start(out=dtbs[:], in_=dtb_c), "wc", 16)
            S.inc(g.dma_start(out=negA[:], in_=Alog_c), "wc", 16)
            S.inc(g.dma_start(out=dsk_s[:], in_=Dsk_r), "wc", 16)
            S.inc(g.dma_start(out=gams[:], in_=gam_c), "wc", 16)
            S.inc(g.dma_start(out=bets[:], in_=bet_c), "wc", 16)
            g.wait_ge(S.sem("wc"), 16 * 10)
            S.chain(g.memset(idf[:], 1.0), "p")
            S.chain(g.affine_select(idf[:], idf[:], pattern=[[1, 128]], base=0,
                                    channel_multiplier=-1,
                                    compare_op=ALU.is_equal, fill=0.0), "p")
            S.chain(g.tensor_copy(idb[:], idf[:]), "p")
            S.chain(g.memset(trib[:], 1.0), "p")
            S.chain(g.affine_select(trib[:], trib[:], pattern=[[1, 128]], base=0,
                                    channel_multiplier=-1,
                                    compare_op=ALU.is_ge, fill=0.0), "p")
            S.chain(g.memset(oneh8[:], 1.0), "p")
            S.chain(g.affine_select(oneh8[:], oneh8[:], pattern=[[1, 8], [0, 128]],
                                    base=0, channel_multiplier=-1,
                                    compare_op=ALU.is_equal, fill=0.0), "p")
            S.chain(g.memset(onec[:], 1.0), "p")
            S.chain(g.memset(onec_bf[:], 1.0), "p")
            S.chain(g.memset(onerow[:], 1.0), "p")
            S.chain(g.memset(onerow_bf[:], 1.0), "p")
            S.chain(g.memset(stf[:], 0.0), "p")
            S.chain(g.memset(stb[:], 0.0), "p", label="s4_const")

        @block.sync
        def _(sy):
            for b in range(NBLK):
                if b >= 1:
                    S.wait(sy, f"s4_xnst_{b - 1}")
                    S.wait(sy, f"s4_uc_{b - 1}_{NT - 1}")
                    S.wait(sy, f"s4_sq2_{b - 1}_{NT - 1}")
                S.inc(sy.dma_start(
                    out=x1b[b % 2].rearrange("p (k c) -> p k c", c=512),
                    in_=x1_d.rearrange("(k p) t -> p k t", p=128)[:, :, b * 512:(b + 1) * 512]),
                    "dx", 16, label=f"s4_x1_{b}")
                if b >= 1:
                    S.wait(sy, f"s4_yf_done_{b - 1}")
                    S.inc(sy.dma_start(
                        out=yf_d.rearrange("(f p) t -> p f t", p=128)[:, :, (b - 1) * 512:b * 512],
                        in_=yfin.rearrange("p (f c) -> p f c", c=512)),
                        "o2", 16, label=f"s4_yfst_{b - 1}")
                S.wait(sy, f"s4_xn_done_{b}")
                S.inc(sy.dma_start(
                    out=xn1_d.rearrange("(k p) t -> p k t", p=128)[:, :, b * 512:(b + 1) * 512],
                    in_=xnT.rearrange("p (k c) -> p k c", c=512)),
                    "o1", 16, label=f"s4_xnst_{b}")
            S.wait(sy, f"s4_yf_done_{NBLK - 1}")
            S.inc(sy.dma_start(
                out=yf_d.rearrange("(f p) t -> p f t", p=128)[:, :, (NBLK - 1) * 512:NBLK * 512],
                in_=yfin.rearrange("p (f c) -> p f c", c=512)),
                "o2", 16, label=f"s4_yfst_{NBLK - 1}")
            sy.wait_ge(S.sem("o1"), 16 * NBLK)
            sy.wait_ge(S.sem("o2"), 16 * NBLK)

        @block.tensor
        def _(t):
            S.wait(t, "s4_const")
            S.wait_cnt(t, "w", 16 * 8)
            S.inc(t.matmul(ps_mis[:, 0:8], onerow[:], dsk_s[:],
                           start=True, stop=True), "t", label="s4_dskb")
            for b in range(NBLK):
                S.wait(t, f"s4_x1_{b}")
                if b > 0:
                    S.wait(t, f"s4_uc_{b - 1}_{NT - 1}")
                    S.wait(t, f"s4_m_{b - 1}_{NCH - 1}_7")
                for dc in range(NT):
                    S.wait(t, f"s4_cp1_{b}_{dc}")
                    S.inc(t.matmul(ps_stat[0:1, :], onec_bf[:],
                                   stg_x[:, (dc % 2) * 512:(dc % 2 + 1) * 512],
                                   start=(dc == 0), stop=(dc == NT - 1)),
                          "t", label=f"s4_summm1_{b}_{dc}")
                for dc in range(NT):
                    S.wait(t, f"s4_sq1_{b}_{dc}")
                    S.inc(t.matmul(ps_stat[32:33, :], onec_bf[:],
                                   stg_q[:, (dc % 2) * 512:(dc % 2 + 1) * 512],
                                   start=(dc == 0), stop=(dc == NT - 1)),
                          "t", label=f"s4_sqmm1_{b}_{dc}")
                S.wait(t, f"s4_rows1v_{b}")
                S.inc(t.matmul(ps_stat[:, :], onerow[:], murow[:],
                               start=True, stop=True), "t", label=f"s4_mub_{b}")
                S.wait(t, f"s4_rows1a_{b}")
                S.inc(t.matmul(ps_bc2[:, :], onerow[:], mrow2[:],
                               start=True, stop=True), "t", label=f"s4_rb_{b}")
                S.wait(t, f"s4_xn_done_{b}")
                for dc in range(NT):
                    S.wait(t, f"s4_cp2_{b}_{dc}")
                    S.inc(t.matmul(ps_stat[0:1, :], onec_bf[:],
                                   stg_x[:, (dc % 2) * 512:(dc % 2 + 1) * 512],
                                   start=(dc == 0), stop=(dc == NT - 1)),
                          "t", label=f"s4_summm2_{b}_{dc}")
                for dc in range(NT):
                    S.wait(t, f"s4_sq2_{b}_{dc}")
                    S.inc(t.matmul(ps_stat[32:33, :], onec_bf[:],
                                   stg_q[:, (dc % 2) * 512:(dc % 2 + 1) * 512],
                                   start=(dc == 0), stop=(dc == NT - 1)),
                          "t", label=f"s4_sqmm2_{b}_{dc}")
                S.label(f"s4_sum2_{b}", "t")
                S.wait(t, f"s4_rows2v_{b}")
                S.inc(t.matmul(ps_stat[:, :], onerow[:], murow[:],
                               start=True, stop=True), "t", label=f"s4_mub2_{b}")
                S.wait(t, f"s4_rows2a_{b}")
                S.inc(t.matmul(ps_bc2[:, :], onerow[:], mrow2[:],
                               start=True, stop=True), "t", label=f"s4_rb2_{b}")
                if b > 0:
                    S.wait(t, f"s4_stup_{b - 1}_{NCH - 1}_7")
                    S.wait(t, f"s4_ye_{b - 1}_{NCH - 1}_7")
                    S.wait(t, f"s4_dte_{b - 1}")
                    S.wait(t, f"s4_bctok_{b - 1}_3")
                    S.wait(t, f"s4_sct8_{b - 1}_3")
                for dc in range(NT):
                    S.wait(t, f"s4_ubf_{b}_{dc}")
                    S.inc(t.matmul(ps_io[1][0:8, 0:512],
                                   winb[dc][:, 4224:4232],
                                   ubf[:, dc * 512:(dc + 1) * 512],
                                   start=(dc == 0), stop=(dc == NT - 1)),
                          "t", label=(f"s4_dtmm_{b}" if dc == NT - 1 else None))
                for dc in range(NT):
                    S.inc(t.matmul(ps_sc[:, 0:512],
                                   winb[dc][:, 4096:4224],
                                   ubf[:, dc * 512:(dc + 1) * 512],
                                   start=(dc == 0), stop=(dc == NT - 1)),
                          "t", label=(f"s4_bcmm_{b}" if dc == NT - 1 else None))
                for tt in range(4):
                    if tt >= 2:
                        S.wait(t, f"s4_bctok_{b}_{tt - 2}")
                    elif tt == 1:
                        S.wait(t, f"s4_dte_{b}")
                    for dc in range(NT):
                        S.inc(t.matmul(
                            ps_io[tt % 2][:, 0:128],
                            ubf[:, dc * 512 + tt * 128:dc * 512 + (tt + 1) * 128],
                            winb[dc][:, 4096:4224],
                            start=(dc == 0), stop=False), "t")
                    S.inc(t.matmul(ps_io[tt % 2][:, 0:128], onerow_bf[:],
                                   bbc_rb[:], start=False, stop=True), "t",
                          label=f"s4_bctokmm_{b}_{tt}")
                for ft in range(16):
                    if ft >= 2:
                        S.wait(t, f"s4_zsil_{b}_{ft - 2}")
                    elif ft == 0:
                        S.wait(t, f"s4_bctok_{b}_2")
                    else:
                        S.wait(t, f"s4_bctok_{b}_3")
                    for dc in range(NT):
                        S.inc(t.matmul(ps_io[ft % 2][:],
                                       winb[dc][:, ft * 128:(ft + 1) * 128],
                                       ubf[:, dc * 512:(dc + 1) * 512],
                                       start=(dc == 0), stop=(dc == NT - 1)),
                              "t", label=(f"s4_zmm_{b}_{ft}" if dc == NT - 1
                                          else None))
                for tt in range(4):
                    for fb in range(4):
                        par = (tt * 4 + fb) % 2
                        pidx = tt * 4 + fb
                        if pidx >= 2:
                            pt, pf = divmod(pidx - 2, 4)
                            S.wait(t, f"s4_xse_{b}_{pt}_{pf}")
                        else:
                            S.wait(t, f"s4_zsil_{b}_{14 + pidx}")
                        for dc in range(NT):
                            S.inc(t.matmul(
                                ps_io[par][:, 0:512],
                                ubf[:, dc * 512 + tt * 128:dc * 512 + (tt + 1) * 128],
                                winb[dc][:, 2048 + fb * 512:2048 + (fb + 1) * 512],
                                start=(dc == 0), stop=False), "t")
                        S.inc(t.matmul(ps_io[par][:, 0:512], onerow_bf[:],
                                       bxs_bf[0:1, fb * 512:(fb + 1) * 512],
                                       start=False, stop=True), "t",
                              label=(f"s4_xsmm_{b}_{tt}_{fb}"))
                S.label(f"s4_inproj_{b}", "t")
                S.wait(t, f"s4_eL_{b}")
                for ch in range(NCH):
                    if ch > 0:
                        S.wait(t, f"s4_smallevac_{b}_{ch - 1}")
                    elif b > 0:
                        S.wait(t, f"s4_smallevac_{b - 1}_{NCH - 1}")
                        S.wait(t, f"s4_cxg_{b - 1}_{NCH - 1}_1")
                    else:
                        S.wait(t, "s4_idh")
                    S.inc(t.transpose(ps_mis[:, 0:8],
                                      dtr[:, ch * 128:(ch + 1) * 128],
                                      idf[0:8, 0:8]), "t")
                    S.inc(t.transpose(ps_mis[:, 8:16],
                                      Lrow[:, ch * 128:(ch + 1) * 128],
                                      idf[0:8, 0:8]), "t")
                    S.inc(t.transpose(ps_mis[:, 16:24],
                                      eLe[:, ch * 128:(ch + 1) * 128],
                                      idf[0:8, 0:8]), "t")
                    S.inc(t.transpose(ps_mis[0:1, 256:264],
                                      eLp[:, ch * 128 + 127:ch * 128 + 128],
                                      idf[0:8, 0:8]),
                          "t", label=f"s4_str_{b}_{ch}")
                for ch in range(NCH):
                    S.wait(t, f"s4_smallevac_{b}_{ch}")
                    if ch > 0:
                        S.wait(t, f"s4_dcole_{b}_{ch - 1}")
                    elif b > 0:
                        S.wait(t, f"s4_stup_{b - 1}_{NCH - 1}_7")
                    S.inc(t.matmul(ps_stu[:, 256:264], onerow[0:1, 0:64],
                                   erow[0:1, ch * 8:(ch + 1) * 8],
                                   start=True, stop=True), "t",
                          label=f"s4_dcb_{b}_{ch}")
                for ch in range(NCH):
                    S.wait(t, f"s4_bce_{b}")
                    if ch > 0:
                        S.wait(t, f"s4_bcmsk_{b}_{ch - 1}")
                    S.inc(t.matmul(ps_sc[:, 0:128], Bf[:, ch * 128:(ch + 1) * 128],
                                   Cf[:, ch * 128:(ch + 1) * 128],
                                   start=True, stop=True), "t",
                          label=f"s4_scmm_{b}_{ch}")
                    for h in range(8):
                        bank = ps_bc2 if h % 2 == 0 else ps_stat
                        prevm = (f"s4_m_{b}_{ch}_{h - 2}" if h >= 2 else
                                 (f"s4_m_{b}_{ch - 1}_{6 + h}" if ch > 0 else
                                  (f"s4_m_{b - 1}_{NCH - 1}_{6 + h}" if b > 0
                                   else None)))
                        if prevm:
                            S.wait(t, prevm)
                        S.inc(t.matmul(bank[:, 0:128],
                                       oneh8[:, h * 128:(h + 1) * 128],
                                       Lrow[:, ch * 128:(ch + 1) * 128],
                                       start=True, stop=True), "t",
                              label=f"s4_lb_{b}_{ch}_{h}")
                    for grp in range(2):
                        if ch == 0 and grp == 0:
                            S.wait(t, f"s4_smallevac_{b}_{NCH - 1}")
                        prevcx = (f"s4_cxg_{b}_{ch}_{grp - 1}" if grp > 0 else
                                  (f"s4_cxg_{b}_{ch - 1}_1" if ch > 0 else
                                   (f"s4_cxg_{b - 1}_{NCH - 1}_1" if b > 0 else None)))
                        if prevcx:
                            S.wait(t, prevcx)
                        for hh in range(4):
                            h = grp * 4 + hh
                            S.inc(t.matmul(ps_mis[:, hh * 128:(hh + 1) * 128],
                                           oneh8[:, h * 128:(h + 1) * 128],
                                           eLp[:, ch * 128:(ch + 1) * 128],
                                           start=True, stop=True), "t",
                                  label=(f"s4_epbg_{b}_{ch}_{grp}" if hh == 3
                                         else None))
                    S.wait(t, f"s4_sct8_{b}_{ch}")
                    S.wait(t, f"s4_cxg_{b}_{ch}_1")
                    S.wait(t, f"s4_bw8_{b}_{ch}")
                    for h in range(8):
                        ybank = ps_y if h % 2 == 0 else ps_io[0]
                        prevye = (f"s4_ye_{b}_{ch}_{h - 2}" if h > 1 else
                                  (f"s4_ye_{b}_{ch - 1}_{6 + h}" if ch > 0 else
                                   (f"s4_ye_{b - 1}_{NCH - 1}_{6 + h}" if b > 0
                                    else None)))
                        if prevye:
                            S.wait(t, prevye)
                        if ch > 0:
                            S.wait(t, f"s4_stup_{b}_{ch - 1}_{h}")
                        elif b > 0:
                            S.wait(t, f"s4_stup_{b - 1}_{NCH - 1}_{h}")
                        for half in range(2):
                            off = h * 256 + half * 128
                            S.inc(t.matmul(ybank[:, half * 128:(half + 1) * 128],
                                           dtxt[:, ch * 2048 + off:ch * 2048 + off + 128],
                                           sct8[:, h * 128:(h + 1) * 128],
                                           start=True, stop=False), "t")
                            S.inc(t.matmul(ybank[:, half * 128:(half + 1) * 128],
                                           xst[:, ch * 2048 + off:ch * 2048 + off + 128],
                                           IDh[:, h * 128:(h + 1) * 128],
                                           start=False, stop=False), "t")
                            S.inc(t.matmul(ybank[:, half * 128:(half + 1) * 128],
                                           stb[0:64, off:off + 128],
                                           Cx8[:, h * 128:(h + 1) * 128],
                                           start=False, stop=True),
                                  "t", label=(f"s4_ymm_{b}_{ch}_{h}"
                                              if half == 1 else None))
                        stbank = ps_stu if h % 2 == 0 else ps_io[1]
                        prevst = (f"s4_stup_{b}_{ch}_{h - 2}" if h > 1 else
                                  (f"s4_stup_{b}_{ch - 1}_{6 + h}" if ch > 0 else
                                   (f"s4_stup_{b - 1}_{NCH - 1}_{6 + h}" if b > 0
                                    else None)))
                        if prevst:
                            S.wait(t, prevst)
                        S.inc(t.matmul(stbank[0:64, 0:256],
                                       Bw8[:, h * 64:(h + 1) * 64],
                                       dtxt[:, ch * 2048 + h * 256:
                                            ch * 2048 + (h + 1) * 256],
                                       start=True, stop=True), "t",
                              label=f"s4_stmm_{b}_{ch}_{h}")

        @block.scalar
        def _(s):
            S.wait(s, "s4_const")
            S.chain(s.activation(negA[:], negA[:], AF.Exp), "a",
                    label="s4_negA1")
            for b in range(NBLK):
                def _stage1(dc):
                    if dc == 0:
                        S.wait(s, f"s4_x1_{b}")
                    if dc > 1:
                        S.wait(s, f"s4_summm1_{b}_{dc - 2}")
                    S.chain(s.activation(stg_x[:, (dc % 2) * 512:(dc % 2 + 1) * 512],
                                         x1b[b % 2][:, dc * 512:(dc + 1) * 512],
                                         AF.Copy), "a", label=f"s4_cp1_{b}_{dc}")
                def _stage1q(dc):
                    if dc > 1:
                        S.wait(s, f"s4_sqmm1_{b}_{dc - 2}")
                    S.chain(s.activation(stg_q[:, (dc % 2) * 512:(dc % 2 + 1) * 512],
                                         x1b[b % 2][:, dc * 512:(dc + 1) * 512],
                                         AF.Square), "a", label=f"s4_sq1_{b}_{dc}")
                def _mbuild(pb, ch, h):
                    S.wait(s, f"s4_lb_{pb}_{ch}_{h}")
                    S.wait(s, f"s4_smallevac_{pb}_{ch}")
                    if h == 0:
                        prevsct8 = (f"s4_sct8_{pb}_{ch - 1}" if ch > 0 else
                                    (f"s4_sct8_{pb - 1}_{NCH - 1}" if pb > 0
                                     else None))
                        if prevsct8:
                            S.wait(s, prevsct8)
                    bank = ps_bc2 if h % 2 == 0 else ps_stat
                    S.chain(s.activation(
                        Mt[:], bank[:, 0:128], AF.Relu, scale=-1.0,
                        bias=smallTT[:, ch * 24 + 8 + h:ch * 24 + 8 + h + 1]),
                        "a")
                    S.chain(s.activation(Msb8[:, h * 128:(h + 1) * 128],
                                         Mt[:], AF.Exp, scale=-1.0), "a",
                            label=f"s4_m_{pb}_{ch}_{h}")
                if b > 0:
                    free_ops = [("x", 0), ("x", 1), ("q", 0), ("q", 1)]
                    for ch in range(NCH):
                        for h in range(8):
                            _mbuild(b - 1, ch, h)
                        if free_ops:
                            kind, dc = free_ops.pop(0)
                            (_stage1 if kind == "x" else _stage1q)(dc)
                    for dc in range(2, NT):
                        _stage1(dc)
                    for dc in range(2, NT):
                        _stage1q(dc)
                else:
                    for dc in range(NT):
                        _stage1(dc)
                    for dc in range(NT):
                        _stage1q(dc)
                S.wait(s, f"s4_rows1q_{b}")
                S.chain(s.activation(mrow2[:], mrow2[:], AF.Ln), "a")
                S.chain(s.activation(mrow2[:], mrow2[:], AF.Exp, scale=-0.5), "a",
                        label=f"s4_rows1a_{b}")
                for dc in range(NT):
                    S.wait(s, f"s4_xnc_{b}_{dc}")
                    if dc > 1:
                        S.wait(s, f"s4_summm2_{b}_{dc - 2}")
                    else:
                        S.wait(s, f"s4_summm1_{b}_{NT - 1}")
                    S.chain(s.activation(stg_x[:, (dc % 2) * 512:(dc % 2 + 1) * 512],
                                         xnT[:, dc * 512:(dc + 1) * 512],
                                         AF.Copy), "a", label=f"s4_cp2_{b}_{dc}")
                for dc in range(NT):
                    if dc > 1:
                        S.wait(s, f"s4_sqmm2_{b}_{dc - 2}")
                    else:
                        S.wait(s, f"s4_sqmm1_{b}_{NT - 1}")
                    S.chain(s.activation(stg_q[:, (dc % 2) * 512:(dc % 2 + 1) * 512],
                                         xnT[:, dc * 512:(dc + 1) * 512],
                                         AF.Square), "a", label=f"s4_sq2_{b}_{dc}")
                S.wait(s, f"s4_rows2q_{b}")
                S.chain(s.activation(mrow2[:], mrow2[:], AF.Ln), "a")
                S.chain(s.activation(mrow2[:], mrow2[:], AF.Exp, scale=-0.5), "a",
                        label=f"s4_rows2a_{b}")
                for dc in range(NT):
                    S.wait(s, f"s4_uc_{b}_{dc}")
                    S.chain(s.activation(ubf[:, dc * 512:(dc + 1) * 512],
                                         ubf[:, dc * 512:(dc + 1) * 512],
                                         AF.Identity, bias=bets[:, dc:dc + 1],
                                         scale=gams[:, dc:dc + 1]), "a",
                            label=f"s4_ubf_{b}_{dc}")
                S.wait(s, f"s4_dtmm_{b}")
                S.wait(s, "s4_bsum")
                if b > 0:
                    S.wait(s, f"s4_dtla_{b - 1}")
                S.chain(s.activation(dtr[:], ps_io[1][0:8, 0:512], AF.Identity,
                                     bias=bsum[:]), "a", label=f"s4_dte_{b}")
                S.chain(s.activation(t8a[:], dtr[:], AF.Abs), "a")
                S.chain(s.activation(t8a[:], t8a[:], AF.Exp, scale=-1.0), "a")
                S.chain(s.activation(t8a[:], t8a[:], AF.Ln, bias=1.0), "a",
                        label=f"s4_sp_{b}")
                S.wait(s, f"s4_bcmm_{b}")
                if b > 0:
                    S.wait(s, f"s4_scmm_{b - 1}_{NCH - 1}")
                    S.wait(s, f"s4_cxg_{b - 1}_{NCH - 1}_1")
                S.chain(s.activation(Bf[:], ps_sc[0:64, 0:512], AF.Identity,
                                     bias=bbcs[0:64, 0:1]), "a")
                S.chain(s.activation(Cf[:], ps_sc[64:128, 0:512], AF.Identity,
                                     bias=bbcs[64:128, 0:1]), "a",
                        label=f"s4_bce_{b}")
                for tt in range(4):
                    S.wait(s, f"s4_bctokmm_{b}_{tt}")
                    if b > 0 and tt == 0:
                        S.wait(s, f"s4_bw8_{b - 1}_{NCH - 1}")
                    S.chain(s.activation(Btok[:, tt * 64:(tt + 1) * 64],
                                         ps_io[tt % 2][:, 0:64], AF.Copy), "a",
                            label=f"s4_bctok_{b}_{tt}")
                for ft in range(16):
                    S.wait(s, f"s4_zmm_{b}_{ft}")
                    if b > 0 and ft == 0:
                        S.wait(s, f"s4_ye_{b - 1}_{NCH - 1}_7")
                    S.chain(s.activation(zsil[:, ft * 512:(ft + 1) * 512],
                                         ps_io[ft % 2][:], AF.Silu,
                                         bias=bzs[:, ft:ft + 1]), "a",
                            label=f"s4_zsil_{b}_{ft}")
                for tt in range(4):
                    for fb in range(4):
                        S.wait(s, f"s4_xsmm_{b}_{tt}_{fb}")
                        if b > 0 and tt == 0 and fb == 0:
                            S.wait(s, f"s4_stmm_{b - 1}_{NCH - 1}_7")
                            S.wait(s, f"s4_dtx_{b - 1}")
                        S.chain(s.activation(
                            xst[:, tt * 2048 + fb * 512:tt * 2048 + (fb + 1) * 512],
                            ps_io[(tt * 4 + fb) % 2][:], AF.Copy), "a",
                            label=f"s4_xse_{b}_{tt}_{fb}")
                S.wait(s, f"s4_L_{b}")
                for ch in range(NCH):
                    S.chain(s.activation(eLe[:, ch * 128:(ch + 1) * 128],
                                         Lrow[:, ch * 128:(ch + 1) * 128],
                                         AF.Exp, scale=-1.0,
                                         bias=Lrow[:, ch * 128 + 127:ch * 128 + 128]),
                            "a")
                S.chain(s.activation(eLp[:], Lrow[:], AF.Exp), "a",
                        label=f"s4_eL_{b}")
                if b == NBLK - 1:
                    for ch in range(NCH):
                        for h in range(8):
                            _mbuild(b, ch, h)
        @block.vector
        def _(v):
            S.wait_cnt(v, "wc", 160)
            S.chain(v.tensor_add(bsum[:], bsum[:], dtbs[:]), "v",
                    label="s4_bsum")
            S.wait(v, "s4_negA1")
            S.chain(v.tensor_scalar_mul(negA[:], negA[:], -1.0), "v",
                    label="s4_negA")
            S.wait(v, "s4_dskb")
            S.wait(v, "s4_const")
            S.chain(v.tensor_copy(Dcols[:, 0:8], ps_mis[:, 0:8]), "v")
            for h in range(8):
                S.chain(v.tensor_scalar_mul(IDh[:, h * 128:(h + 1) * 128],
                                            idb[:], Dcols[:, h:h + 1]), "v",
                        label=("s4_idh" if h == 7 else None))
            for b in range(NBLK):
                S.wait(v, f"s4_sqmm1_{b}_{NT - 1}")
                S.chain(v.tensor_scalar_mul(murow[:], ps_stat[0:1, :], 1.0 / D), "v")
                S.chain(v.tensor_scalar_mul(vrow[:], ps_stat[32:33, :], 1.0 / D), "v",
                        label=f"s4_rows1v_{b}")
                S.chain(v.tensor_mul(mrow2[:], murow[:], murow[:]), "v")
                S.chain(v.tensor_sub(mrow2[:], vrow[:], mrow2[:]), "v")
                S.chain(v.tensor_scalar_add(mrow2[:], mrow2[:], 1e-5), "v",
                        label=f"s4_rows1q_{b}")
                S.wait(v, f"s4_mub_{b}")
                S.wait(v, f"s4_rb_{b}")
                if b > 0:
                    S.wait(v, f"s4_xnst_{b - 1}")
                for dc in range(NT):
                    S.chain(v.tensor_sub(xnT[:, dc * 512:(dc + 1) * 512],
                                         x1b[b % 2][:, dc * 512:(dc + 1) * 512],
                                         ps_stat[:, :]), "v")
                    S.chain(v.tensor_mul(xnT[:, dc * 512:(dc + 1) * 512],
                                         xnT[:, dc * 512:(dc + 1) * 512],
                                         ps_bc2[:, :]), "v",
                            label=f"s4_xnc_{b}_{dc}")
                S.label(f"s4_xn_done_{b}", "v")
                S.wait(v, f"s4_sum2_{b}")
                S.wait(v, f"s4_sqmm2_{b}_{NT - 1}")
                S.chain(v.tensor_scalar_mul(murow[:], ps_stat[0:1, :], 1.0 / D), "v")
                S.chain(v.tensor_scalar_mul(vrow[:], ps_stat[32:33, :], 1.0 / D), "v",
                        label=f"s4_rows2v_{b}")
                S.chain(v.tensor_mul(mrow2[:], murow[:], murow[:]), "v")
                S.chain(v.tensor_sub(mrow2[:], vrow[:], mrow2[:]), "v")
                S.chain(v.tensor_scalar_add(mrow2[:], mrow2[:], 1e-5), "v",
                        label=f"s4_rows2q_{b}")
                S.wait(v, f"s4_mub2_{b}")
                S.wait(v, f"s4_rb2_{b}")
                S.wait(v, f"s4_sqmm2_{b}_{NT - 1}")
                for dc in range(NT):
                    S.chain(v.tensor_sub(ubf[:, dc * 512:(dc + 1) * 512],
                                         xnT[:, dc * 512:(dc + 1) * 512],
                                         ps_stat[:, :]), "v")
                    S.chain(v.tensor_mul(ubf[:, dc * 512:(dc + 1) * 512],
                                         ubf[:, dc * 512:(dc + 1) * 512],
                                         ps_bc2[:, :]), "v",
                            label=f"s4_uc_{b}_{dc}")
                S.wait(v, f"s4_sp_{b}")
                S.chain(v.tensor_scalar_max(dtr[:], dtr[:], 0.0), "v")
                S.chain(v.tensor_add(dtr[:], dtr[:], t8a[:]), "v")
                S.wait(v, "s4_negA")
                S.chain(v.tensor_scalar_mul(t8a[:], dtr[:], negA[:, 0:1]), "v",
                        label=f"s4_dtla_{b}")
                if b > 0:
                    S.wait(v, f"s4_lb_{b - 1}_{NCH - 1}_7")
                    S.wait(v, f"s4_epbg_{b - 1}_{NCH - 1}_1")
                S.chain(v.memset(Mt[0:8, 0:128], 0.0), "v")
                for ch in range(NCH):
                    S.chain(v.tensor_tensor_scan(
                        Lrow[:, ch * 128:(ch + 1) * 128],
                        t8a[:, ch * 128:(ch + 1) * 128],
                        Mt[0:8, 0:128], 0.0, op0=ALU.add, op1=ALU.add), "v",
                        label=(f"s4_L_{b}" if ch == NCH - 1 else None))
                for ch in range(NCH):
                    S.wait(v, f"s4_str_{b}_{ch}")
                    S.chain(v.tensor_copy(smallTT[:, ch * 24:(ch + 1) * 24],
                                          ps_mis[:, 0:24]), "v")
                    S.chain(v.tensor_copy(erow[0:1, ch * 8:(ch + 1) * 8],
                                          ps_mis[0:1, 256:264]), "v",
                            label=f"s4_smallevac_{b}_{ch}")
                for ch in range(NCH):
                    S.wait(v, f"s4_dcb_{b}_{ch}")
                    S.chain(v.tensor_copy(dcol[:, ch * 8:(ch + 1) * 8],
                                          ps_stu[:, 256:264]), "v",
                            label=f"s4_dcole_{b}_{ch}")
                S.wait(v, f"s4_xse_{b}_3_3")
                if b > 0:
                    S.wait(v, f"s4_stmm_{b - 1}_{NCH - 1}_7")
                for tt in range(4):
                    for h in range(8):
                        S.chain(v.tensor_scalar_mul(
                            dtxt[:, tt * 2048 + h * 256:tt * 2048 + (h + 1) * 256],
                            xst[:, tt * 2048 + h * 256:tt * 2048 + (h + 1) * 256],
                            smallTT[:, tt * 24 + h:tt * 24 + h + 1]), "v",
                            label=(f"s4_dtx_{b}" if tt == 3 and h == 7 else None))
                for ch in range(NCH):
                    S.wait(v, f"s4_scmm_{b}_{ch}")
                    S.chain(v.tensor_mul(BCm[:], ps_sc[:, 0:128], trib[:]), "v",
                            label=f"s4_bcmsk_{b}_{ch}")
                    for grp in range(2):
                        S.wait(v, f"s4_epbg_{b}_{ch}_{grp}")
                        if grp == 0 and (ch > 0 or b > 0):
                            pb, pch = (b, ch - 1) if ch > 0 else (b - 1, NCH - 1)
                            S.wait(v, f"s4_ymm_{pb}_{pch}_7")
                        cfb = bass.AP(Cf[:].tensor, ch * 128,
                                      [[512, 64], [0, 4], [1, 128]])
                        S.chain(v.tensor_mul(
                            Cx8.rearrange("p (h c) -> p h c", c=128)[
                                :, grp * 4:(grp + 1) * 4, :],
                            ps_mis.rearrange("p (h c) -> p h c", c=128)[
                                0:64, 0:4, :],
                            cfb), "v", label=f"s4_cxg_{b}_{ch}_{grp}")
                    S.wait(v, f"s4_m_{b}_{ch}_7")
                    bcb = bass.AP(BCm[:].tensor, 0, [[128, 128], [0, 8], [1, 128]])
                    S.chain(v.tensor_mul(
                        sct8.rearrange("p (h c) -> p h c", c=128),
                        bcb,
                        Msb8.rearrange("p (h c) -> p h c", c=128)), "v",
                        label=f"s4_sct8_{b}_{ch}")
                    if ch > 0:
                        S.wait(v, f"s4_stmm_{b}_{ch - 1}_7")
                    elif b > 0:
                        S.wait(v, f"s4_stmm_{b - 1}_{NCH - 1}_7")
                    for h in range(8):
                        S.chain(v.tensor_scalar_mul(
                            Bw8[:, h * 64:(h + 1) * 64],
                            Btok[:, ch * 64:(ch + 1) * 64],
                            smallTT[:, ch * 24 + 16 + h:ch * 24 + 16 + h + 1]),
                            "v", label=(f"s4_bw8_{b}_{ch}" if h == 7 else None))
                    for h in range(8):
                        ybank = ps_y if h % 2 == 0 else ps_io[0]
                        stbank = ps_stu if h % 2 == 0 else ps_io[1]
                        S.wait(v, f"s4_ymm_{b}_{ch}_{h}")
                        if b > 0 and ch == 0 and h == 0:
                            S.wait(v, f"s4_yfst_{b - 1}")
                        for half in range(2):
                            fcol = (2 * h + half) * 512 + ch * 128
                            S.chain(v.tensor_mul(
                                yfin[:, fcol:fcol + 128],
                                ybank[:, half * 128:(half + 1) * 128],
                                zsil[:, fcol:fcol + 128]), "v",
                                label=(f"s4_ye_{b}_{ch}_{h}" if half == 1
                                       else None))
                        S.wait(v, f"s4_stmm_{b}_{ch}_{h}")
                        if h == 0:
                            S.wait(v, f"s4_dcole_{b}_{ch}")
                        S.chain(v.scalar_tensor_tensor(
                            stf[:, h * 256:(h + 1) * 256],
                            stf[:, h * 256:(h + 1) * 256],
                            dcol[:, ch * 8 + h:ch * 8 + h + 1],
                            stbank[0:64, 0:256], op0=ALU.mult, op1=ALU.add), "v")
                        S.chain(v.tensor_copy(stb[:, h * 256:(h + 1) * 256],
                                              stf[:, h * 256:(h + 1) * 256]),
                                "v", label=f"s4_stup_{b}_{ch}_{h}")
                S.label(f"s4_yf_done_{b}", "v")
    return nc


def build_s4():
    S = Sched()
    nc = None
    for p2 in (False, True):
        S.begin(p2)
        with ExitStack() as outer:
            env = _standalone_env("s4", outer)
            emit_s4(S, env)
            nc = env.nc
    return nc


def emit_s5(S: Sched, env):
    nc = env.nc
    Wo = env.io["Wo"]
    bo_c = env.io["bo_c"]
    yf_d = env.io["yfinT"]
    xn1_d = env.io["xn1T"]
    s1_d = env.io["s1T"]

    with ExitStack() as ctx:
        sb = lambda n, s_, d_: ctx.enter_context(nc.sbuf_tensor(env.pfx + n, s_, d_))
        psx = lambda n, s_, d_: ctx.enter_context(nc.psum_tensor(env.pfx + n, s_, d_))
        sem = lambda n: env.outer.enter_context(nc.semaphore(env.pfx + n))
        wT = [sb(f"wT{i}", [128, 1024], BF16) for i in range(16)]
        yb = [sb("yb0", [128, 16 * 512], BF16), sb("yb1", [128, 16 * 512], BF16)]
        xnb = [sb("xnb0", [128, 8 * 512], F32), sb("xnb1", [128, 8 * 512], F32)]
        s1b = sb("s1b", [128, 8 * 512], F32)
        bos = sb("bos", [128, 8], F32)
        ps_r = [psx("ps_r0", [128, 512], F32), psx("ps_r1", [128, 512], F32),
                psx("ps_r2", [128, 512], F32), psx("ps_r3", [128, 512], F32)]
        for k in ["d0", "d1", "w", "t", "v", "a", "p", "o1"]:
            S.reg(k, sem("sem_" + k))
        block = ctx.enter_context(nc.Block())

        @block.gpsimd
        def _(g):
            for fc in range(16):
                S.inc(g.dma_start(out=wT[fc][:], in_=Wo[fc * 128:(fc + 1) * 128, :]),
                      "w", 16)
            S.inc(g.dma_start(out=bos[:], in_=bo_c), "w", 16)

        @block.sync
        def _(sy):
            for b in range(NBLK):
                if b >= 2:
                    S.wait(sy, f"s5_mmdone_{b - 2}")
                    S.wait(sy, f"s5_comb_{b - 2}")
                S.inc(sy.dma_start(
                    out=yb[b % 2].rearrange("p (f c) -> p f c", c=512),
                    in_=yf_d.rearrange("(f p) t -> p f t", p=128)[:, :, b * 512:(b + 1) * 512]),
                    f"d{b % 2}", 16)
                S.inc(sy.dma_start(
                    out=xnb[b % 2].rearrange("p (k c) -> p k c", c=512),
                    in_=xn1_d.rearrange("(k p) t -> p k t", p=128)[:, :, b * 512:(b + 1) * 512]),
                    f"d{b % 2}", 16, label=f"s5_in_{b}")
                if b >= 1:
                    S.wait(sy, f"s5_comb_{b - 1}")
                    S.inc(sy.dma_start(
                        out=s1_d.rearrange("(k p) t -> p k t", p=128)[:, :, (b - 1) * 512:b * 512],
                        in_=s1b.rearrange("p (k c) -> p k c", c=512)),
                        "o1", 16, label=f"s5_st_{b - 1}")
            S.wait(sy, f"s5_comb_{NBLK - 1}")
            S.inc(sy.dma_start(
                out=s1_d.rearrange("(k p) t -> p k t", p=128)[:, :, (NBLK - 1) * 512:NBLK * 512],
                in_=s1b.rearrange("p (k c) -> p k c", c=512)),
                "o1", 16, label=f"s5_st_{NBLK - 1}")
            sy.wait_ge(S.sem("o1"), 16 * NBLK)

        @block.tensor
        def _(t):
            S.wait_cnt(t, "w", 16 * 17)
            for b in range(NBLK):
                S.wait(t, f"s5_in_{b}")
                for do in range(8):
                    if do >= 4:
                        S.wait(t, f"s5_ev_{b}_{do - 4}")
                    elif b > 0:
                        S.wait(t, f"s5_ev_{b - 1}_{4 + do}")
                    ps = ps_r[do % 4]
                    for fc in range(16):
                        S.inc(t.matmul(ps[:], wT[fc][:, do * 128:(do + 1) * 128],
                                       yb[b % 2][:, fc * 512:(fc + 1) * 512],
                                       start=(fc == 0), stop=(fc == 15)),
                              "t", label=(f"s5_mm_{b}_{do}" if fc == 15 else None))
                S.label(f"s5_mmdone_{b}", "t")

        @block.vector
        def _(v):
            for b in range(NBLK):
                if b > 0:
                    S.wait(v, f"s5_st_{b - 1}")
                for do in range(8):
                    ps = ps_r[do % 4]
                    S.wait(v, f"s5_mm_{b}_{do}")
                    S.chain(v.scalar_tensor_tensor(
                        s1b[:, do * 512:(do + 1) * 512], ps[:],
                        bos[:, do:do + 1],
                        xnb[b % 2][:, do * 512:(do + 1) * 512],
                        op0=ALU.add, op1=ALU.add), "v",
                        label=f"s5_ev_{b}_{do}")
                S.label(f"s5_comb_{b}", "v")
    return nc


def build_s5():
    S = Sched()
    nc = None
    for p2 in (False, True):
        S.begin(p2)
        with ExitStack() as outer:
            env = _standalone_env("s5", outer)
            emit_s5(S, env)
            nc = env.nc
    return nc


def emit_s6(S: Sched, env):
    nc = env.nc
    wgT = env.io["wgT"]
    wuT = env.io["wuT"]
    xn1_d = env.io["xn1T"]
    hg_d = env.io["hgT"]

    with ExitStack() as ctx:
        sb = lambda n, s_, d_: ctx.enter_context(nc.sbuf_tensor(env.pfx + n, s_, d_))
        psx = lambda n, s_, d_: ctx.enter_context(nc.psum_tensor(env.pfx + n, s_, d_))
        sem = lambda n: env.outer.enter_context(nc.semaphore(env.pfx + n))
        wg = [sb(f"wg{i}", [128, 3072], BF16) for i in range(8)]
        wu = [sb(f"wu{i}", [128, 3072], BF16) for i in range(8)]
        xf = [sb("xf0", [128, 8 * 512], F32), sb("xf1", [128, 8 * 512], F32)]
        xb = sb("xb", [128, 8 * 512], BF16)
        hg = sb("hg", [128, 24 * 512], BF16)
        sg = [sb("sg0", [128, 512], BF16), sb("sg1", [128, 512], BF16)]
        ps_g = [psx(f"ps_g{i}", [128, 512], F32) for i in range(3)]
        ps_u = [psx(f"ps_u{i}", [128, 512], F32) for i in range(3)]
        for k in ["d0", "d1", "w", "t", "v", "a", "p", "o1"]:
            S.reg(k, sem("sem_" + k))
        block = ctx.enter_context(nc.Block())

        @block.gpsimd
        def _(g):
            for dc in range(8):
                S.inc(g.dma_start(out=wg[dc][:], in_=wgT[dc * 128:(dc + 1) * 128, :]),
                      "w", 16)
                S.inc(g.dma_start(out=wu[dc][:], in_=wuT[dc * 128:(dc + 1) * 128, :]),
                      "w", 16)

        @block.sync
        def _(sy):
            for b in range(NBLK):
                if b >= 2:
                    S.wait(sy, f"s6_xcast_{b - 2}")
                S.inc(sy.dma_start(
                    out=xf[b % 2].rearrange("p (k c) -> p k c", c=512),
                    in_=xn1_d.rearrange("(k p) t -> p k t", p=128)[:, :, b * 512:(b + 1) * 512]),
                    f"d{b % 2}", 16, label=f"s6_in_{b}")
                if b >= 1:
                    S.wait(sy, f"s6_hgdone_{b - 1}")
                    S.inc(sy.dma_start(
                        out=hg_d.rearrange("(f p) t -> p f t", p=128)[:, :, (b - 1) * 512:b * 512],
                        in_=hg.rearrange("p (f c) -> p f c", c=512)),
                        "o1", 16, label=f"s6_st_{b - 1}")
            S.wait(sy, f"s6_hgdone_{NBLK - 1}")
            S.inc(sy.dma_start(
                out=hg_d.rearrange("(f p) t -> p f t", p=128)[:, :, (NBLK - 1) * 512:NBLK * 512],
                in_=hg.rearrange("p (f c) -> p f c", c=512)),
                "o1", 16, label=f"s6_st_{NBLK - 1}")
            sy.wait_ge(S.sem("o1"), 16 * NBLK)

        @block.vector
        def _(v):
            for b in range(NBLK):
                S.wait(v, f"s6_in_{b}")
                if b > 0:
                    S.wait(v, f"s6_mmdone_{b - 1}")
                S.chain(v.tensor_scalar_mul(xb[:], xf[b % 2][:], 1.0 / 127.0),
                        "v", label=f"s6_xcast_{b}")
                if b > 0:
                    S.wait(v, f"s6_st_{b - 1}")
                for ft in range(24):
                    S.wait(v, f"s6_sg_{b}_{ft}")
                    S.wait(v, f"s6_umm_{b}_{ft}")
                    S.chain(v.tensor_mul(hg[:, ft * 512:(ft + 1) * 512],
                                         sg[ft % 2][:], ps_u[ft % 3][:]), "v",
                            label=f"s6_hge_{b}_{ft}")
                S.label(f"s6_hgdone_{b}", "v")

        @block.tensor
        def _(t):
            S.wait_cnt(t, "w", 16 * 16)
            for b in range(NBLK):
                S.wait(t, f"s6_xcast_{b}")
                for ft in range(24):
                    if ft >= 3:
                        S.wait(t, f"s6_sg_{b}_{ft - 3}")
                        S.wait(t, f"s6_hge_{b}_{ft - 3}")
                    elif b > 0:
                        S.wait(t, f"s6_sg_{b - 1}_{21 + ft}")
                        S.wait(t, f"s6_hge_{b - 1}_{21 + ft}")
                    for dc in range(8):
                        S.inc(t.matmul(ps_g[ft % 3][:],
                                       wg[dc][:, ft * 128:(ft + 1) * 128],
                                       xb[:, dc * 512:(dc + 1) * 512],
                                       start=(dc == 0), stop=(dc == 7)),
                              "t", label=(f"s6_gmm_{b}_{ft}" if dc == 7 else None))
                    for dc in range(8):
                        S.inc(t.matmul(ps_u[ft % 3][:],
                                       wu[dc][:, ft * 128:(ft + 1) * 128],
                                       xb[:, dc * 512:(dc + 1) * 512],
                                       start=(dc == 0), stop=(dc == 7)),
                              "t", label=(f"s6_umm_{b}_{ft}" if dc == 7 else None))
                S.label(f"s6_mmdone_{b}", "t")

        @block.scalar
        def _(s):
            for b in range(NBLK):
                for ft in range(24):
                    S.wait(s, f"s6_gmm_{b}_{ft}")
                    if ft > 1:
                        S.wait(s, f"s6_hge_{b}_{ft - 2}")
                    elif b > 0:
                        S.wait(s, f"s6_hge_{b - 1}_{22 + ft}")
                    S.chain(s.activation(sg[ft % 2][:], ps_g[ft % 3][:], AF.Silu),
                            "a", label=f"s6_sg_{b}_{ft}")
    return nc


def build_s6():
    S = Sched()
    nc = None
    for p2 in (False, True):
        S.begin(p2)
        with ExitStack() as outer:
            env = _standalone_env("s6", outer)
            emit_s6(S, env)
            nc = env.nc
    return nc


def emit_s7(S: Sched, env):
    nc = env.nc
    wdT = env.io["wdT"]
    hg_d = env.io["hgT"]
    x1_d = env.io["x1T"]
    s1_d = env.io["s1T"]
    out_d = env.io["out"]

    with ExitStack() as ctx:
        sb = lambda n, s_, d_: ctx.enter_context(nc.sbuf_tensor(env.pfx + n, s_, d_))
        psx = lambda n, s_, d_: ctx.enter_context(nc.psum_tensor(env.pfx + n, s_, d_))
        sem = lambda n: env.outer.enter_context(nc.semaphore(env.pfx + n))
        wT = [sb(f"wT{i}", [128, 1024], BF16) for i in range(24)]
        hgb = [sb("hgb0", [128, 24 * 512], BF16), sb("hgb1", [128, 24 * 512], BF16)]
        x1b = [sb("x1b0", [128, 8 * 512], F32), sb("x1b1", [128, 8 * 512], F32)]
        s1b = [sb("s1b0", [128, 8 * 512], F32), sb("s1b1", [128, 8 * 512], F32)]
        gb = sb("gb", [128, 8 * 512], F32)
        otok = sb("otok", [128, 4 * 1024], F32)
        idf = sb("idf", [128, 128], F32)
        ps_r = [psx(f"ps_r{i}", [128, 512], F32) for i in range(4)]
        ps_t = [psx("ps_t0", [128, 512], F32), psx("ps_t1", [128, 512], F32)]
        for k in ["d0", "d1", "d20", "d21", "w", "t", "v", "a", "p", "o1"]:
            S.reg(k, sem("sem_" + k))
        block = ctx.enter_context(nc.Block())

        @block.gpsimd
        def _(g):
            for fc in range(24):
                S.inc(g.dma_start(out=wT[fc][:], in_=wdT[fc * 128:(fc + 1) * 128, :]),
                      "w", 16)
            S.chain(g.memset(idf[:], 1.0), "p")
            S.chain(g.affine_select(idf[:], idf[:], pattern=[[1, 128]], base=0,
                                    channel_multiplier=-1,
                                    compare_op=ALU.is_equal, fill=0.0), "p",
                    label="s7_const")
            for b in range(NBLK):
                if b >= 2:
                    S.wait(g, f"s7_comb_{b - 2}")
                S.inc(g.dma_start(
                    out=x1b[b % 2].rearrange("p (k c) -> p k c", c=512),
                    in_=x1_d.rearrange("(k p) t -> p k t", p=128)[:, :, b * 512:(b + 1) * 512]),
                    f"d2{b % 2}", 16)
                S.inc(g.dma_start(
                    out=s1b[b % 2].rearrange("p (k c) -> p k c", c=512),
                    in_=s1_d.rearrange("(k p) t -> p k t", p=128)[:, :, b * 512:(b + 1) * 512]),
                    f"d2{b % 2}", 16, label=f"s7_xs_{b}")

        @block.sync
        def _(sy):
            for b in range(NBLK):
                if b >= 2:
                    S.wait(sy, f"s7_mmdone_{b - 2}")
                S.inc(sy.dma_start(
                    out=hgb[b % 2].rearrange("p (f c) -> p f c", c=512),
                    in_=hg_d.rearrange("(f p) t -> p f t", p=128)[:, :, b * 512:(b + 1) * 512]),
                    f"d{b % 2}", 16, label=f"s7_in_{b}")
                if b >= 1:
                    S.wait(sy, f"s7_otok_{b - 1}")
                    for tt in range(4):
                        S.inc(sy.dma_start(
                            out=out_d[((b - 1) * 4 + tt) * 128:
                                      ((b - 1) * 4 + tt + 1) * 128, :],
                            in_=otok[:, tt * 1024:(tt + 1) * 1024]),
                            "o1", 16, label=(f"s7_st_{b - 1}" if tt == 3 else None))
            S.wait(sy, f"s7_otok_{NBLK - 1}")
            for tt in range(4):
                S.inc(sy.dma_start(
                    out=out_d[((NBLK - 1) * 4 + tt) * 128:
                              ((NBLK - 1) * 4 + tt + 1) * 128, :],
                    in_=otok[:, tt * 1024:(tt + 1) * 1024]),
                    "o1", 16, label=(f"s7_st_{NBLK - 1}" if tt == 3 else None))
            sy.wait_ge(S.sem("o1"), 16 * 4 * NBLK)

        @block.tensor
        def _(t):
            S.wait_cnt(t, "w", 16 * 24)
            S.wait(t, "s7_const")
            for b in range(NBLK):
                S.wait(t, f"s7_in_{b}")
                for do in range(8):
                    if do >= 4:
                        S.wait(t, f"s7_g_{b}_{do - 4}")
                    elif b > 0:
                        S.wait(t, f"s7_g_{b - 1}_{4 + do}")
                    ps = ps_r[do % 4]
                    for fc in range(24):
                        S.inc(t.matmul(ps[:], wT[fc][:, do * 128:(do + 1) * 128],
                                       hgb[b % 2][:, fc * 512:(fc + 1) * 512],
                                       start=(fc == 0), stop=(fc == 23)),
                              "t", label=(f"s7_mm_{b}_{do}" if fc == 23 else None))
                S.label(f"s7_mmdone_{b}", "t")
                S.wait(t, f"s7_comb_{b}")
                for tt in range(4):
                    for half in range(2):
                        if tt > 0 or b > 0:
                            ptt, phalf = (tt, half) if tt > 0 else (4, half)
                            S.wait(t, f"s7_te_{b if tt > 0 else b - 1}_"
                                      f"{tt - 1 if tt > 0 else 3}_{half}")
                        for j in range(4):
                            jj = half * 4 + j
                            S.inc(t.transpose(
                                ps_t[half][:, j * 128:(j + 1) * 128],
                                gb[:, jj * 512 + tt * 128:jj * 512 + (tt + 1) * 128],
                                idf[:]), "t",
                                label=(f"s7_tr_{b}_{tt}_{half}" if j == 3
                                       else None))

        @block.scalar
        def _(s):
            for b in range(NBLK):
                for do in range(8):
                    ps = ps_r[do % 4]
                    S.wait(s, f"s7_mm_{b}_{do}")
                    if b > 0 and do == 0:
                        S.wait(s, f"s7_tr_{b - 1}_3_1")
                    S.chain(s.activation(gb[:, do * 512:(do + 1) * 512], ps[:],
                                         AF.Silu, scale=1.0 / 127.0), "a",
                            label=f"s7_g_{b}_{do}")

        @block.vector
        def _(v):
            for b in range(NBLK):
                S.wait(v, f"s7_xs_{b}")
                for do in range(8):
                    S.wait(v, f"s7_g_{b}_{do}")
                    sl = slice(do * 512, (do + 1) * 512)
                    S.chain(v.tensor_mul(gb[:, sl], s1b[b % 2][:, sl],
                                         gb[:, sl]), "v")
                    S.chain(v.tensor_add(gb[:, sl], gb[:, sl],
                                         x1b[b % 2][:, sl]), "v",
                            label=(f"s7_comb_{b}" if do == 7 else None))
                if b > 0:
                    S.wait(v, f"s7_st_{b - 1}")
                for tt in range(4):
                    for half in range(2):
                        S.wait(v, f"s7_tr_{b}_{tt}_{half}")
                        dst = otok.rearrange("p (k c) -> p k c", c=1024)
                        S.chain(v.tensor_copy(
                            dst[:, tt:tt + 1, half * 512:(half + 1) * 512].rearrange(
                                "p a (j c) -> p (a j) c", c=128),
                            ps_t[half].rearrange("p (j c) -> p j c", c=128)),
                            "v", label=f"s7_te_{b}_{tt}_{half}")
                S.label(f"s7_otok_{b}", "v")
    return nc


def build_s7():
    S = Sched()
    nc = None
    for p2 in (False, True):
        S.begin(p2)
        with ExitStack() as outer:
            env = _standalone_env("s7", outer)
            emit_s7(S, env)
            nc = env.nc
    return nc


def build_s1():
    S = Sched()
    nc = None
    for p2 in (False, True):
        S.begin(p2)
        with ExitStack() as outer:
            env = _standalone_env("s1", outer)
            emit_s1(S, env)
            nc = env.nc
    return nc


def _shard_layouts(inputs):
    i = inputs
    c = {}
    c["s1"] = {
        "w_in": i["L0_attn_w_in"],
        "w_a": i["L0_attn_w_a"],
        "b_inT": np.ascontiguousarray(i["L0_attn_b_in"].T),
        "b_aT": np.ascontiguousarray(i["L0_attn_b_a"].T),
        "a_paramT": np.ascontiguousarray(i["L0_attn_a_param"].reshape(8, 128).T),
    }
    c["s2"] = {
        "W1T": np.ascontiguousarray(i["L0_ffn_W1"].T),
        "W2T": np.ascontiguousarray(i["L0_ffn_W2"].T),
        "b1c": np.ascontiguousarray(i["L0_ffn_b1"].reshape(32, 128).T),
    }
    c["s3"] = {
        "b2c": np.ascontiguousarray(i["L0_ffn_b2"].reshape(8, 128).T),
    }
    b_in = i["L1_attn_b_in"]
    c["s4"] = {
        "WinT": i["L1_attn_W_in"],
        "bz_c": np.ascontiguousarray(b_in[0:2048].reshape(16, 128).T),
        "bxs_r": np.ascontiguousarray(b_in[2048:4096].reshape(1, 2048)),
        "bbc_c": np.ascontiguousarray(b_in[4096:4224].reshape(128, 1)),
        "bbc_r": np.ascontiguousarray(b_in[4096:4224].reshape(1, 128)),
        "bdt_c": np.ascontiguousarray(b_in[4224:4232].reshape(8, 1)),
        "dtb_c": np.ascontiguousarray(i["L1_attn_dt_bias"].reshape(8, 1)),
        "Alog_c": np.ascontiguousarray(i["L1_attn_A_log"].reshape(8, 1)),
        "Dsk_r": np.ascontiguousarray(i["L1_attn_D_skip"].reshape(1, 8)),
        "gam_c": np.ascontiguousarray(i["L1_attn_ln_gamma"].reshape(8, 128).T),
        "bet_c": np.ascontiguousarray(i["L1_attn_ln_beta"].reshape(8, 128).T),
    }
    c["s5"] = {
        "Wo": i["L1_attn_W_out"],
        "bo_c": np.ascontiguousarray(i["L1_attn_b_out"].reshape(8, 128).T),
    }
    c["s6"] = {
        "wgT": np.ascontiguousarray(i["L1_ffn_w_gate"].T),
        "wuT": np.ascontiguousarray(i["L1_ffn_w_up"].T),
    }
    c["s7"] = {
        "wdT": np.ascontiguousarray(i["L1_ffn_w_down"].T),
    }
    return c


def build_fused():
    emitters = {"s1": emit_s1, "s2": emit_s2, "s3": emit_s3, "s4": emit_s4,
                "s5": emit_s5, "s6": emit_s6, "s7": emit_s7}
    S = Sched()
    nc = None
    for p2 in (False, True):
        S.begin(p2)
        with ExitStack() as outer:
            nc = bass.Bass("TRN2", target_bir_lowering=False)
            outs_of = set()
            for st in _STAGES:
                for name, _sh, _dt in IO_SPECS[st]["out"]:
                    outs_of.add(name)
            kinds = {}
            for st in _STAGES:
                for name, sh, dt_ in IO_SPECS[st]["in"]:
                    kinds.setdefault(name, ("Internal" if name in outs_of
                                            else "ExternalInput", sh, dt_))
            for st in _STAGES:
                for name, sh, dt_ in IO_SPECS[st]["out"]:
                    kinds.setdefault(name, ("ExternalOutput" if name == "out"
                                            else "Internal", sh, dt_))
            io = {}
            for name, (kind, sh, dt_) in kinds.items():
                io[name] = _ios(nc, name, sh, dt_, kind)
            bar = outer.enter_context(nc.semaphore("stage_bar"))
            for k, st in enumerate(_STAGES):
                if k > 0:
                    _barrier_block(nc, bar, 5 * k, True)
                S.pfx = st + "_"
                env = Env(nc, io, outer, pfx=st + "_")
                emitters[st](S, env)
            S.pfx = ""
    return nc


def kernel(**inputs):
    NC = 8
    x = inputs["L0_attn_x"]
    c = _shard_layouts(inputs)
    common = {}
    for d in c.values():
        common.update(d)
    nc = build_fused()
    in_maps = [{**common, "x": np.ascontiguousarray(x[b])} for b in range(NC)]
    res = run_bass_kernel_spmd(nc, in_maps, core_ids=list(range(NC)))
    return np.stack([res.results[b]["out"] for b in range(NC)], axis=0)


def kernel_unfused(**inputs):
    NC = 8
    cores = list(range(NC))
    x = inputs["L0_attn_x"]
    c = _shard_layouts(inputs)

    r1 = run_bass_kernel_spmd(build_s1(), [
        {**c["s1"], "x": np.ascontiguousarray(x[b])} for b in range(NC)],
        core_ids=cores)
    r2 = run_bass_kernel_spmd(build_s2(), [
        {**c["s2"], "xnbf": r1.results[b]["xnbf"]} for b in range(NC)],
        core_ids=cores)
    r3 = run_bass_kernel_spmd(build_s3(), [
        {**c["s3"], "h1": r2.results[b]["h1"], "W2t": r2.results[b]["W2t"],
         "xT": r1.results[b]["xT"], "s0T": r1.results[b]["s0T"]}
        for b in range(NC)], core_ids=cores)
    r4 = run_bass_kernel_spmd(build_s4(), [
        {**c["s4"], "x1T": r3.results[b]["x1T"]} for b in range(NC)],
        core_ids=cores)
    r5 = run_bass_kernel_spmd(build_s5(), [
        {**c["s5"], "yfinT": r4.results[b]["yfinT"],
         "xn1T": r4.results[b]["xn1T"]} for b in range(NC)],
        core_ids=cores)
    r6 = run_bass_kernel_spmd(build_s6(), [
        {**c["s6"], "xn1T": r4.results[b]["xn1T"]} for b in range(NC)],
        core_ids=cores)
    r7 = run_bass_kernel_spmd(build_s7(), [
        {**c["s7"], "hgT": r6.results[b]["hgT"], "x1T": r3.results[b]["x1T"],
         "s1T": r5.results[b]["s1T"]} for b in range(NC)],
        core_ids=cores)
    return np.stack([r7.results[b]["out"] for b in range(NC)], axis=0)


# version 22
# speedup vs baseline: 1.0490x; 1.0008x over previous
import numpy as np
import ml_dtypes
from contextlib import ExitStack

import concourse.bass as bass
import concourse.mybir as mybir
from concourse.bass_utils import run_bass_kernel_spmd

F32 = mybir.dt.float32
BF16 = mybir.dt.bfloat16
I32 = mybir.dt.int32
AF = mybir.ActivationFunctionType
ALU = mybir.AluOpType
AX = mybir.AxisListType

T = 4096
D = 1024
NT = 8
BLK = 512
NBLK = T // BLK
LN7 = float(np.log(np.float32(1e-7)))
L1M = float(np.log1p(np.float64(-1e-5)))


class Sched:

    def __init__(self):
        self.rec = {}
        self.pass2 = False

    def begin(self, pass2):
        self.pass2 = pass2
        self.cnt = {}
        self.sems = {}
        self.pfx = ""

    def reg(self, key, handle):
        self.sems[self.pfx + key] = handle

    def inc(self, instr, key, n=1, label=None):
        key = self.pfx + key
        self.cnt[key] = self.cnt.get(key, 0) + n
        if instr is not None:
            instr.then_inc(self.sems[key], n)
        if label is not None:
            if self.pass2:
                assert self.rec.get(label) == (key, self.cnt[key]), \
                    f"schedule drift at {label}"
            else:
                self.rec[label] = (key, self.cnt[key])
        return self.cnt[key]

    def chain(self, instr, key, label=None):
        key = self.pfx + key
        prev = self.cnt.get(key, 0)
        if self.pass2 and prev > 0:
            instr._wait_ge(self.sems[key], prev)
        self.cnt[key] = prev + 1
        instr.then_inc(self.sems[key], 1)
        if label is not None:
            if self.pass2:
                assert self.rec.get(label) == (key, self.cnt[key]), \
                    f"schedule drift at {label}"
            else:
                self.rec[label] = (key, self.cnt[key])
        return self.cnt[key]

    def label(self, label, key):
        key = self.pfx + key
        if self.pass2:
            assert self.rec.get(label) == (key, self.cnt.get(key, 0)), \
                f"schedule drift at {label}"
        else:
            self.rec[label] = (key, self.cnt.get(key, 0))

    def wait(self, eng, label):
        if not self.pass2:
            return
        key, val = self.rec[label]
        if val > 0:
            eng.wait_ge(self.sems[key], val)

    def wait_cnt(self, eng, key, val):
        if self.pass2 and val > 0:
            eng.wait_ge(self.sems[self.pfx + key], val)

    def sem(self, key):
        return self.sems[self.pfx + key]


def _ios(nc, name, shape, dtype, kind):
    return nc.dram_tensor(name, shape, dtype, kind=kind).ap()


IO_SPECS = {
    "s1": {"in": [("x", [T, D], F32), ("w_in", [8, 128, 128], F32),
                  ("w_a", [8, 128, 128], F32), ("b_inT", [128, 8], F32),
                  ("b_aT", [128, 8], F32), ("a_paramT", [128, 8], F32)],
           "out": [("xT", [D, T], F32), ("xnbf", [D, T], BF16),
                   ("s0T", [D, T], F32)]},
    "s2": {"in": [("W1T", [1024, 4096], F32), ("W2T", [4096, 1024], F32),
                  ("b1c", [128, 32], F32), ("xnbf", [D, T], BF16)],
           "out": [("h1", [4096, T], BF16), ("W2t", [4096, 1024], BF16)]},
    "s3": {"in": [("W2t", [4096, 1024], BF16), ("b2c", [128, 8], F32),
                  ("h1", [4096, T], BF16), ("xT", [D, T], F32),
                  ("s0T", [D, T], F32)],
           "out": [("x1T", [D, T], F32)]},
    "s4": {"in": [("x1T", [D, T], F32), ("WinT", [1024, 4232], F32),
                  ("bz_c", [128, 16], F32), ("bxs_r", [1, 2048], F32),
                  ("bbc_c", [128, 1], F32), ("bbc_r", [1, 128], F32),
                  ("bdt_c", [8, 1], F32), ("dtb_c", [8, 1], F32),
                  ("Alog_c", [8, 1], F32), ("Dsk_r", [1, 8], F32),
                  ("gam_c", [128, 8], F32), ("bet_c", [128, 8], F32)],
           "out": [("xn1T", [D, T], F32), ("yfinT", [2048, T], BF16)]},
    "s5": {"in": [("Wo", [2048, 1024], F32), ("bo_c", [128, 8], F32),
                  ("yfinT", [2048, T], BF16), ("xn1T", [D, T], F32)],
           "out": [("s1T", [D, T], F32)]},
    "s6": {"in": [("wgT", [1024, 3072], I32), ("wuT", [1024, 3072], I32),
                  ("xn1T", [D, T], F32)],
           "out": [("hgT", [3072, T], BF16)]},
    "s7": {"in": [("wdT", [3072, 1024], I32), ("hgT", [3072, T], BF16),
                  ("x1T", [D, T], F32), ("s1T", [D, T], F32)],
           "out": [("out", [T, D], F32)]},
}


class Env:
    def __init__(self, nc, io, outer, pfx=""):
        self.nc = nc
        self.io = io
        self.outer = outer
        self.pfx = pfx


def _standalone_env(stage, outer):
    nc = bass.Bass("TRN2", target_bir_lowering=False)
    io = {}
    for name, shape, dt_ in IO_SPECS[stage]["in"]:
        io[name] = _ios(nc, name, shape, dt_, "ExternalInput")
    for name, shape, dt_ in IO_SPECS[stage]["out"]:
        io[name] = _ios(nc, name, shape, dt_, "ExternalOutput")
    return Env(nc, io, outer)


_STAGES = ["s1", "s2", "s3", "s4", "s5", "s6", "s7"]
_ENGS = ["gpsimd", "sync", "tensor", "vector", "scalar"]


def _barrier_block(nc, bar, wait_val, do_inc):
    with nc.Block() as bb:
        def mk():
            def body(eng):
                if do_inc:
                    eng.nop().then_inc(bar, 1)
                if wait_val > 0:
                    eng.wait_ge(bar, wait_val)
            return body
        for engname in _ENGS:
            getattr(bb, engname)(mk())


def emit_s1(S: Sched, env):
    nc = env.nc
    x = env.io["x"]
    w_in = env.io["w_in"]
    w_a = env.io["w_a"]
    b_inT = env.io["b_inT"]
    b_aT = env.io["b_aT"]
    a_paramT = env.io["a_paramT"]
    xT_d = env.io["xT"]
    xnbf_d = env.io["xnbf"]
    s0_d = env.io["s0T"]

    with ExitStack() as ctx:
        sb = lambda n, s, d: ctx.enter_context(nc.sbuf_tensor(env.pfx + n, s, d))
        psx = lambda n, s, d: ctx.enter_context(nc.psum_tensor(env.pfx + n, s, d))
        sem = lambda n: env.outer.enter_context(nc.semaphore(env.pfx + n))

        xtok = [sb("xtok0", [128, 4096], F32), sb("xtok1", [128, 4096], F32)]
        cN = sb("cN", [128, 4096], F32)
        dX = sb("dX", [128, 4096], F32)
        eG = sb("eG", [128, 4096], F32)
        fA = sb("fA", [128, 4096], F32)
        gM = sb("gM", [128, 4096], F32)
        laX = sb("laX", [128, 8 * 513], F32)
        iL = sb("iL", [128, 4096], F32)
        jS = sb("jS", [128, 4096], F32)
        xnT = sb("xnT", [128, 4096], F32)
        xnbfT = sb("xnbfT", [128, 4096], BF16)
        sqt = sb("sqt", [128, 1024], F32)
        mu_c = sb("mu_c", [128, 4], F32)
        sq_c = sb("sq_c", [128, 4], F32)
        rs_c = sb("rs_c", [128, 4], F32)
        Lcar = sb("Lcar", [128, 8], F32)
        Scar = sb("Scar", [128, 8], F32)
        wbf = sb("wbf", [128, 2048], BF16)
        bcols = sb("bcols", [128, 16], F32)
        negc = sb("negc", [128, 8], F32)
        apc = sb("apc", [128, 8], F32)
        tmp8 = sb("tmp8", [128, 8], F32)
        idf = sb("idf", [128, 128], F32)
        zeros = sb("zeros", [128, 512], F32)

        ps_tr = [psx("ps_tr0", [128, 512], F32), psx("ps_tr1", [128, 512], F32)]
        ps_g = [psx("ps_g0", [128, 512], F32), psx("ps_g1", [128, 512], F32)]

        for k in ["d", "w", "t", "v", "a", "p", "o1", "o2", "o3"]:
            S.reg(k, sem("sem_" + k))
        block = ctx.enter_context(nc.Block())

        @block.gpsimd
        def _(g):
            wbf3 = wbf.rearrange("p (g h j) -> p g h j", g=2, j=128)
            S.inc(g.dma_start(out=wbf3[:, 0], in_=w_in.rearrange("h i j -> i h j")),
                  "w", 16)
            S.inc(g.dma_start(out=wbf3[:, 1], in_=w_a.rearrange("h i j -> i h j")),
                  "w", 16)
            S.inc(g.dma_start(out=bcols[:, 0:8], in_=b_inT), "w", 16)
            S.inc(g.dma_start(out=bcols[:, 8:16], in_=b_aT), "w", 16)
            S.inc(g.dma_start(out=apc[:], in_=a_paramT), "w", 16)
            g.wait_ge(S.sem("w"), 80)
            S.chain(g.memset(idf[:], 1.0), "p")
            S.chain(g.affine_select(idf[:], idf[:], pattern=[[1, 128]], base=0,
                                    channel_multiplier=-1,
                                    compare_op=ALU.is_equal, fill=0.0), "p")
            S.chain(g.memset(zeros[:], 0.0), "p")
            S.chain(g.memset(Lcar[:], 0.0), "p")
            S.chain(g.memset(Scar[:], 0.0), "p")
            S.chain(g.memset(laX[:], 0.0), "p", label="s1_const")

        @block.sync
        def _(sy):
            def _stores(b):
                S.wait(sy, f"s1_xT_done_{b}")
                for k in range(NT):
                    S.inc(sy.dma_start(
                        out=xT_d[k * 128:(k + 1) * 128, b * 512:(b + 1) * 512],
                        in_=dX[:, k * 512:(k + 1) * 512]), "o1", 16,
                        label=(f"s1_xTst_{b}" if k == NT - 1 else None))
                S.wait(sy, f"s1_xnbf_done_{b}")
                for k in range(NT):
                    S.inc(sy.dma_start(
                        out=xnbf_d[k * 128:(k + 1) * 128, b * 512:(b + 1) * 512],
                        in_=xnbfT[:, k * 512:(k + 1) * 512]), "o2", 16,
                        label=(f"s1_xnst_{b}" if k == NT - 1 else None))
                S.wait(sy, f"s1_h_done_{b}")
                for k in range(NT):
                    S.inc(sy.dma_start(
                        out=s0_d[k * 128:(k + 1) * 128, b * 512:(b + 1) * 512],
                        in_=iL[:, k * 512:(k + 1) * 512]), "o3", 16,
                        label=(f"s1_out_{b}" if k == NT - 1 else None))
            for b in range(NBLK):
                if b >= 2:
                    S.wait(sy, f"s1_trx_{b - 2}_3_1")
                    S.wait(sy, f"s1_sq_{b - 2}")
                    S.wait(sy, f"s1_ln_{b - 2}_3")
                if b >= 1:
                    S.wait(sy, f"s1_xld_{b - 1}")
                for tt in range(4):
                    S.inc(sy.dma_start(
                        out=xtok[b % 2][:, tt * 1024:(tt + 1) * 1024],
                        in_=x[(b * 4 + tt) * 128:(b * 4 + tt + 1) * 128, :]),
                        "d", 16, label=(f"s1_xld_{b}" if tt == 3 else None))
                if b >= 1:
                    _stores(b - 1)
            _stores(NBLK - 1)
            sy.wait_ge(S.sem("o1"), 16 * 8 * NBLK)
            sy.wait_ge(S.sem("o2"), 16 * 8 * NBLK)
            sy.wait_ge(S.sem("o3"), 16 * 8 * NBLK)

        @block.tensor
        def _(t):
            S.wait(t, "s1_const")
            S.wait_cnt(t, "w", 80)
            for b in range(NBLK):
                for tt in range(4):
                    S.wait_cnt(t, "d", 64 * (b + 1))
                    for half in range(2):
                        if tt > 0:
                            S.wait(t, f"s1_trxe_{b}_{tt - 1}_{half}")
                        elif b > 0:
                            S.wait(t, f"s1_trne_{b - 1}_3_{half}")
                        for j in range(4):
                            jj = half * 4 + j
                            ti = t.transpose(ps_tr[half][:, j * 128:(j + 1) * 128],
                                             xtok[b % 2][:, tt * 1024 + jj * 128:
                                                         tt * 1024 + (jj + 1) * 128],
                                             idf[:])
                            S.inc(ti, "t",
                                  label=(f"s1_trx_{b}_{tt}_{half}" if j == 3
                                         else None))
                for tt in range(4):
                    S.wait(t, f"s1_ln_{b}_{tt}")
                    for half in range(2):
                        if tt > 0:
                            S.wait(t, f"s1_trne_{b}_{tt - 1}_{half}")
                        else:
                            S.wait(t, f"s1_trxe_{b}_3_{half}")
                        for j in range(4):
                            jj = half * 4 + j
                            ti = t.transpose(ps_tr[half][:, j * 128:(j + 1) * 128],
                                             cN[:, tt * 1024 + jj * 128:
                                                tt * 1024 + (jj + 1) * 128],
                                             idf[:])
                            S.inc(ti, "t",
                                  label=(f"s1_trn_{b}_{tt}_{half}" if j == 3
                                         else None))
                S.wait(t, f"s1_xnbf_done_{b}")
                for h in range(8):
                    for gi in range(2):
                        if h > 0:
                            S.wait(t, f"s1_sgm_{b}_{h - 1}_{gi}")
                        elif b > 0:
                            S.wait(t, f"s1_sgm_{b - 1}_7_{gi}")
                        S.inc(t.matmul(ps_g[gi][:],
                                       wbf[:, gi * 1024 + h * 128:
                                           gi * 1024 + (h + 1) * 128],
                                       xnbfT[:, h * 512:(h + 1) * 512],
                                       start=True, stop=True),
                              "t", label=f"s1_gate_{b}_{h}_{gi}")

        @block.scalar
        def _(s):
            S.wait_cnt(s, "w", 80)
            S.wait(s, "s1_const")
            S.chain(s.activation(tmp8[:], apc[:], AF.Abs), "a")
            S.chain(s.activation(tmp8[:], tmp8[:], AF.Exp, scale=-1.0), "a")
            S.chain(s.activation(tmp8[:], tmp8[:], AF.Ln, bias=1.0), "a",
                    label="s1_sp1")
            for b in range(NBLK):
                for tt in range(4):
                    S.wait_cnt(s, "d", 64 * (b + 1))
                    S.chain(s.activation(sqt[:],
                                         xtok[b % 2][:, tt * 1024:(tt + 1) * 1024],
                                         AF.Square, accum_out=sq_c[:, tt:tt + 1]),
                            "a", label=(f"s1_sq_{b}" if tt == 3 else None))
                S.wait(s, f"s1_var_{b}")
                S.chain(s.activation(rs_c[:], rs_c[:], AF.Ln), "a")
                S.chain(s.activation(rs_c[:], rs_c[:], AF.Exp, scale=-0.5), "a",
                        label=f"s1_rstd_{b}")
                if b > 0:
                    S.wait(s, f"s1_h_done_{b - 1}")
                for h in range(8):
                    for gi in range(2):
                        S.wait(s, f"s1_gate_{b}_{h}_{gi}")
                        dst = eG if gi == 0 else fA
                        S.chain(s.activation(dst[:, h * 512:(h + 1) * 512],
                                             ps_g[gi][:], AF.Sigmoid,
                                             bias=bcols[:, 8 * gi + h:
                                                        8 * gi + h + 1]),
                                "a", label=f"s1_sgm_{b}_{h}_{gi}")
                S.wait(s, f"s1_la_{b}")
                for k in range(NT):
                    S.chain(s.activation(fA[:, k * 512:(k + 1) * 512],
                                         laX[:, k * 513 + 1:k * 513 + 513],
                                         AF.Exp, scale=2.0), "a",
                            label=(f"s1_a2_{b}" if k == NT - 1 else None))
                S.wait(s, f"s1_m2_{b}")
                S.chain(s.activation(gM[:], gM[:], AF.Ln), "a")
                S.chain(s.activation(gM[:], gM[:], AF.Exp, scale=0.5), "a",
                        label=f"s1_mult_{b}")
                S.wait(s, f"s1_lc_{b}")
                S.wait(s, f"s1_xgm_{b}")
                S.chain(s.activation(gM[:], iL[:], AF.Exp, scale=-1.0), "a",
                        label=f"s1_pinv_{b}")
                S.wait(s, f"s1_w_{b}")
                S.chain(s.activation(eG[:], iL[:], AF.Exp), "a",
                        label=f"s1_p_{b}")

        @block.vector
        def _(v):
            S.wait(v, "s1_sp1")
            S.chain(v.tensor_scalar_max(negc[:], apc[:], 0.0), "v")
            S.chain(v.tensor_add(negc[:], negc[:], tmp8[:]), "v")
            S.chain(v.tensor_scalar(negc[:], negc[:], 1e-3, -1.0,
                                    op0=ALU.add, op1=ALU.mult), "v",
                    label="s1_negc")
            for b in range(NBLK):
                for tt in range(4):
                    S.wait_cnt(v, "d", 64 * (b + 1))
                    S.chain(v.tensor_reduce(mu_c[:, tt:tt + 1],
                                            xtok[b % 2][:, tt * 1024:(tt + 1) * 1024],
                                            axis=AX.X, op=ALU.add), "v")
                S.wait(v, f"s1_sq_{b}")
                S.chain(v.tensor_scalar_mul(mu_c[:], mu_c[:], 1.0 / D), "v")
                S.chain(v.tensor_mul(rs_c[:], mu_c[:], mu_c[:]), "v")
                S.chain(v.scalar_tensor_tensor(rs_c[:], sq_c[:], 1.0 / D, rs_c[:],
                                               op0=ALU.mult, op1=ALU.subtract), "v")
                S.chain(v.tensor_scalar_add(rs_c[:], rs_c[:], 1e-5), "v",
                        label=f"s1_var_{b}")
                S.wait(v, f"s1_rstd_{b}")
                for tt in range(4):
                    S.chain(v.tensor_scalar(cN[:, tt * 1024:(tt + 1) * 1024],
                                            xtok[b % 2][:, tt * 1024:(tt + 1) * 1024],
                                            mu_c[:, tt:tt + 1], rs_c[:, tt:tt + 1],
                                            op0=ALU.subtract, op1=ALU.mult),
                            "v", label=f"s1_ln_{b}_{tt}")
                if b > 0:
                    S.wait(v, f"s1_xTst_{b - 1}")
                for tt in range(4):
                    for half in range(2):
                        S.wait(v, f"s1_trx_{b}_{tt}_{half}")
                        dst = dX.rearrange("p (k c) -> p k c", c=512)
                        S.chain(v.tensor_copy(
                            dst[:, half * 4:half * 4 + 4, tt * 128:(tt + 1) * 128],
                            ps_tr[half].rearrange("p (k c) -> p k c", c=128)),
                            "v", label=f"s1_trxe_{b}_{tt}_{half}")
                S.label(f"s1_xT_done_{b}", "v")
                if b > 0:
                    S.wait(v, f"s1_xnst_{b - 1}")
                    S.wait(v, f"s1_gate_{b - 1}_7_1")
                for tt in range(4):
                    for half in range(2):
                        S.wait(v, f"s1_trn_{b}_{tt}_{half}")
                        dstn = xnT.rearrange("p (k c) -> p k c", c=512)
                        S.chain(v.tensor_copy(
                            dstn[:, half * 4:half * 4 + 4, tt * 128:(tt + 1) * 128],
                            ps_tr[half].rearrange("p (k c) -> p k c", c=128)),
                            "v", label=f"s1_trne_{b}_{tt}_{half}")
                S.chain(v.tensor_copy(xnbfT[:], xnT[:]), "v",
                        label=f"s1_xnbf_done_{b}")
                if b == 0:
                    S.wait(v, "s1_negc")
                else:
                    S.wait(v, f"s1_a2_{b - 1}")
                for k in range(NT):
                    S.wait(v, f"s1_sgm_{b}_{k}_1")
                    S.chain(v.tensor_scalar(laX[:, k * 513 + 1:k * 513 + 513],
                                            fA[:, k * 512:(k + 1) * 512],
                                            negc[:, k:k + 1], L1M,
                                            op0=ALU.mult, op1=ALU.min),
                            "v", label=(f"s1_la_{b}" if k == NT - 1 else None))
                if b > 0:
                    S.wait(v, f"s1_out_{b - 1}")
                for k in range(NT):
                    S.chain(v.tensor_tensor_scan(iL[:, k * 512:(k + 1) * 512],
                                                 laX[:, k * 513:k * 513 + 512],
                                                 zeros[:, 0:512],
                                                 Lcar[:, k:k + 1],
                                                 op0=ALU.add, op1=ALU.add), "v")
                S.chain(v.tensor_copy(
                    Lcar.rearrange("p (k o) -> p k o", o=1),
                    iL.rearrange("p (k c) -> p k c", c=512)[:, :, 511:512]), "v")
                S.chain(v.tensor_copy(
                    laX.rearrange("p (k c) -> p k c", c=513)[:, :, 0:1],
                    laX.rearrange("p (k c) -> p k c", c=513)[:, :, 512:513]), "v")
                S.chain(v.tensor_scalar_max(iL[:], iL[:], LN7), "v",
                        label=f"s1_lc_{b}")
                S.wait(v, f"s1_a2_{b}")
                S.chain(v.tensor_scalar(gM[:], fA[:], -1.0, 1.0,
                                        op0=ALU.mult, op1=ALU.add), "v")
                S.chain(v.tensor_scalar_max(gM[:], gM[:], 1e-5), "v",
                        label=f"s1_m2_{b}")
                S.wait(v, f"s1_sgm_{b}_7_0")
                S.chain(v.tensor_mul(cN[:], xnT[:], eG[:]), "v",
                        label=f"s1_xg_{b}")
                S.wait(v, f"s1_mult_{b}")
                S.chain(v.tensor_mul(eG[:], cN[:], gM[:]), "v",
                        label=f"s1_xgm_{b}")
                S.wait(v, f"s1_pinv_{b}")
                S.chain(v.scalar_tensor_tensor(cN[:], eG[:], 1e-7, gM[:],
                                               op0=ALU.add, op1=ALU.mult),
                        "v", label=f"s1_w_{b}")
                for k in range(NT):
                    S.chain(v.tensor_tensor_scan(jS[:, k * 512:(k + 1) * 512],
                                                 cN[:, k * 512:(k + 1) * 512],
                                                 zeros[:, 0:512],
                                                 Scar[:, k:k + 1],
                                                 op0=ALU.add, op1=ALU.add), "v")
                S.chain(v.tensor_copy(
                    Scar.rearrange("p (k o) -> p k o", o=1),
                    jS.rearrange("p (k c) -> p k c", c=512)[:, :, 511:512]), "v",
                    label=f"s1_scum_{b}")
                S.wait(v, f"s1_p_{b}")
                S.chain(v.tensor_mul(iL[:], eG[:], jS[:]), "v",
                        label=f"s1_h_done_{b}")
    return nc




def emit_s2(S: Sched, env):
    nc = env.nc
    W1T = env.io["W1T"]
    W2T = env.io["W2T"]
    b1c = env.io["b1c"]
    xnbf_d = env.io["xnbf"]
    h1_d = env.io["h1"]
    W2t_d = env.io["W2t"]

    with ExitStack() as ctx:
        sb = lambda n, s_, d_: ctx.enter_context(nc.sbuf_tensor(env.pfx + n, s_, d_))
        psx = lambda n, s_, d_: ctx.enter_context(nc.psum_tensor(env.pfx + n, s_, d_))
        sem = lambda n: env.outer.enter_context(nc.semaphore(env.pfx + n))

        wf = [sb(f"wf{i}", [128, 1024], F32) for i in range(4)]
        wi = sb("wi", [128, 1024], F32)
        sc2 = sb("sc2", [1, 2], F32)
        invc2 = sb("invc2", [128, 2], F32)
        w2s = [sb(f"w2s{i}", [128, 1024], BF16) for i in range(2)]
        wT = [sb(f"wT{i}", [128, 4096], BF16) for i in range(8)]
        xnb = [sb("xnb0", [128, 8 * 512], BF16), sb("xnb1", [128, 8 * 512], BF16)]
        h1s = sb("h1s", [128, 32 * 512], BF16)
        b1s = sb("b1s", [128, 32], F32)
        acc = sb("acc", [128, 1], F32)
        red = sb("red", [128, 1], F32)
        onec = sb("onec", [128, 1], F32)
        onerow = sb("onerow", [1, 128], F32)
        sc1 = sb("sc1", [1, 1], F32)
        invc = sb("invc", [128, 1], F32)
        ps_r = [psx("ps_r0", [128, 512], F32), psx("ps_r1", [128, 512], F32),
                psx("ps_r2", [128, 512], F32)]
        ps_s = psx("ps_s", [1, 1], F32)
        ps_c = psx("ps_c", [128, 2], F32)

        for k in ["d0", "d1", "w", "w0", "w1", "w2", "w3", "t", "v", "a", "p",
                  "o1", "o20", "o21"]:
            S.reg(k, sem("sem_" + k))
        block = ctx.enter_context(nc.Block())

        @block.gpsimd
        def _(g):
            S.inc(g.dma_start(out=b1s[:], in_=b1c), "w", 16)
            def _w1src(k):
                fg, dc = k // 8, k % 8
                return W1T[dc * 128:(dc + 1) * 128,
                           fg * 1024:(fg + 1) * 1024]
            for k in range(32):
                if k >= 4:
                    S.wait(g, f"s2_absred_{k - 4}")
                S.inc(g.dma_start(out=wf[k % 4][:], in_=_w1src(k)),
                      f"w{k % 4}", 16, label=f"s2_wfA_{k}")
            for k in range(32):
                S.wait(g, f"s2_w1t_{k - 4}" if k >= 4
                       else f"s2_absred_{28 + k}")
                S.inc(g.dma_start(out=wf[k % 4][:], in_=_w1src(k)),
                      f"w{k % 4}", 16, label=f"s2_wfB_{k}")
            for dc in range(32):
                if dc >= 4:
                    S.wait(g, f"s2_w2red_{dc - 4}")
                else:
                    S.wait(g, f"s2_w1t_{28 + dc}")
                S.inc(g.dma_start(out=wf[dc % 4][:],
                                  in_=W2T[dc * 128:(dc + 1) * 128, :]),
                      f"w{dc % 4}", 16, label=f"s2_w2A_{dc}")
            for dc in range(32):
                S.wait(g, f"s2_w2t_{dc - 4}" if dc >= 4 else "s2_scale2")
                S.inc(g.dma_start(out=wf[dc % 4][:],
                                  in_=W2T[dc * 128:(dc + 1) * 128, :]),
                      f"w{dc % 4}", 16, label=f"s2_w2B_{dc}")
                if dc >= 2:
                    S.wait(g, f"s2_w2t_{dc - 2}")
                    S.inc(g.dma_start(
                        out=W2t_d[(dc - 2) * 128:(dc - 1) * 128, :],
                        in_=w2s[dc % 2][:]), f"o2{dc % 2}", 16,
                        label=f"s2_w2st_{dc - 2}")
            for dc in (30, 31):
                S.wait(g, f"s2_w2t_{dc}")
                S.inc(g.dma_start(out=W2t_d[dc * 128:(dc + 1) * 128, :],
                                  in_=w2s[dc % 2][:]), f"o2{dc % 2}", 16,
                      label=f"s2_w2st_{dc}")

        @block.sync
        def _(sy):
            for b in range(NBLK):
                if b >= 2:
                    S.wait(sy, f"s2_mmdone_{b - 2}")
                S.inc(sy.dma_start(
                    out=xnb[b % 2].rearrange("p (k c) -> p k c", c=512),
                    in_=xnbf_d.rearrange("(k p) t -> p k t", p=128)[:, :, b * 512:(b + 1) * 512]),
                    f"d{b % 2}", 16, label=f"s2_xnb_{b}")
                if b >= 1:
                    S.wait(sy, f"s2_h1done_{b - 1}")
                    S.inc(sy.dma_start(
                        out=h1_d.rearrange("(f p) t -> p f t", p=128)[:, :, (b - 1) * 512:b * 512],
                        in_=h1s.rearrange("p (f c) -> p f c", c=512)),
                        "o1", 16, label=f"s2_h1st_{b - 1}")
            S.wait(sy, f"s2_h1done_{NBLK - 1}")
            S.inc(sy.dma_start(
                out=h1_d.rearrange("(f p) t -> p f t", p=128)[:, :, (NBLK - 1) * 512:NBLK * 512],
                in_=h1s.rearrange("p (f c) -> p f c", c=512)),
                "o1", 16, label=f"s2_h1st_{NBLK - 1}")
            sy.wait_ge(S.sem("o1"), 16 * NBLK)
            sy.wait_ge(S.sem("o20"), 16 * 16)
            sy.wait_ge(S.sem("o21"), 16 * 16)

        @block.vector
        def _(v):
            S.chain(v.memset(acc[:], 0.0), "v")
            S.chain(v.memset(onec[:], 1.0), "v")
            S.chain(v.memset(onerow[:], 1.0), "v", label="s2_ones")
            for k in range(32):
                S.wait(v, f"s2_wfA_{k}")
                S.chain(v.tensor_reduce(red[:], wf[k % 4][:], axis=AX.X,
                                        op=ALU.add, apply_absolute_value=True), "v")
                S.chain(v.tensor_add(acc[:], acc[:], red[:]), "v",
                        label=f"s2_absred_{k}")
            S.wait(v, "s2_total")
            S.chain(v.tensor_copy(sc2[0:1, 0:1], ps_s[:]), "v")
            S.chain(v.tensor_scalar(sc2[0:1, 0:1], sc2[0:1, 0:1],
                                    0.5 / (4096 * 1024), 0.5e-8,
                                    op0=ALU.mult, op1=ALU.max), "v")
            S.chain(v.tensor_scalar_mul(sc2[0:1, 1:2], sc2[0:1, 0:1],
                                        -1.0), "v", label="s2_inv")
            S.wait(v, "s2_invb")
            S.chain(v.tensor_copy(invc2[:], ps_c[:, 0:2]), "v",
                    label="s2_scale")
            for k in range(32):
                fg, dc = k // 8, k % 8
                S.wait(v, f"s2_wfB_{k}")
                S.chain(v.tensor_scalar(wi[:], wf[k % 4][:],
                                        invc2[:, 1:2], None,
                                        op0=ALU.is_lt), "v")
                S.chain(v.tensor_scalar(wf[k % 4][:], wf[k % 4][:],
                                        invc2[:, 0:1], None,
                                        op0=ALU.is_gt), "v")
                S.chain(v.tensor_sub(wT[dc][:, fg * 1024:(fg + 1) * 1024],
                                     wf[k % 4][:], wi[:]), "v",
                        label=f"s2_w1t_{k}")
            S.chain(v.memset(acc[:], 0.0), "v")
            for dc in range(32):
                S.wait(v, f"s2_w2A_{dc}")
                S.chain(v.tensor_reduce(red[:], wf[dc % 4][:], axis=AX.X,
                                        op=ALU.add, apply_absolute_value=True), "v")
                S.chain(v.tensor_add(acc[:], acc[:], red[:]), "v",
                        label=f"s2_w2red_{dc}")
            S.wait(v, "s2_total2")
            S.chain(v.tensor_copy(sc1[:], ps_s[:]), "v")
            S.chain(v.tensor_scalar(sc1[:], sc1[:], 1.0 / (4096 * 1024), 1e-8,
                                    op0=ALU.mult, op1=ALU.max), "v")
            S.chain(v.reciprocal(sc1[:], sc1[:]), "v", label="s2_inv2")
            S.wait(v, "s2_invb2")
            S.chain(v.tensor_copy(invc[:], ps_c[:, 0:1]), "v",
                    label="s2_scale2")
            for dc in range(32):
                S.wait(v, f"s2_w2B_{dc}")
                if dc >= 2:
                    S.wait(v, f"s2_w2st_{dc - 2}")
                S.chain(v.tensor_scalar_mul(wf[dc % 4][:],
                                            wf[dc % 4][:],
                                            invc[:, 0:1]), "v")
                S.chain(v.tensor_scalar(wi[:], wf[dc % 4][:],
                                        -0.5, None, op0=ALU.is_lt), "v")
                S.chain(v.tensor_scalar(wf[dc % 4][:],
                                        wf[dc % 4][:], 0.5, None,
                                        op0=ALU.is_gt), "v")
                S.chain(v.tensor_sub(w2s[dc % 2][:], wf[dc % 4][:],
                                     wi[:]), "v",
                        label=f"s2_w2t_{dc}")

        @block.tensor
        def _(t):
            S.wait(t, "s2_absred_31")
            S.inc(t.matmul(ps_s[:], onec[:], acc[:], start=True, stop=True),
                  "t", label="s2_total")
            S.wait(t, "s2_inv")
            S.wait(t, "s2_ones")
            S.inc(t.matmul(ps_c[:, 0:2], onerow[:], sc2[:], start=True,
                           stop=True), "t", label="s2_invb")
            S.wait_cnt(t, "w", 16)
            for b in range(NBLK):
                if b == 3:
                    S.wait(t, "s2_w2red_31")
                    S.wait(t, "s2_scale")
                    S.inc(t.matmul(ps_s[:], onec[:], acc[:], start=True,
                                   stop=True), "t", label="s2_total2")
                if b == 4:
                    S.wait(t, "s2_inv2")
                    S.inc(t.matmul(ps_c[:, 0:1], onerow[:], sc1[:], start=True,
                                   stop=True), "t", label="s2_invb2")
                S.wait(t, f"s2_xnb_{b}")
                for ft in range(32):
                    if b == 0 and ft % 8 == 0:
                        S.wait(t, f"s2_w1t_{(ft // 8) * 8 + 7}")
                    ps = ps_r[ft % 3]
                    if ft >= 3:
                        S.wait(t, f"s2_gelu_{b}_{ft - 3}")
                    elif b > 0:
                        prevu = max(x for x in range(32) if x % 3 == ft % 3)
                        S.wait(t, f"s2_gelu_{b - 1}_{prevu}")
                    for dc in range(8):
                        S.inc(t.matmul(ps[:], wT[dc][:, ft * 128:(ft + 1) * 128],
                                       xnb[b % 2][:, dc * 512:(dc + 1) * 512],
                                       start=(dc == 0), stop=(dc == 7)),
                              "t", label=(f"s2_mm_{b}_{ft}" if dc == 7 else None))
                S.label(f"s2_mmdone_{b}", "t")

        @block.scalar
        def _(s):
            for b in range(NBLK):
                if b > 0:
                    S.wait(s, f"s2_h1st_{b - 1}")
                for ft in range(32):
                    ps = ps_r[ft % 3]
                    S.wait(s, f"s2_mm_{b}_{ft}")
                    S.chain(s.activation(h1s[:, ft * 512:(ft + 1) * 512], ps[:],
                                         AF.Gelu, bias=b1s[:, ft:ft + 1]),
                            "a", label=f"s2_gelu_{b}_{ft}")
                S.label(f"s2_h1done_{b}", "a")
    return nc


def build_s2():
    S = Sched()
    nc = None
    for p2 in (False, True):
        S.begin(p2)
        with ExitStack() as outer:
            env = _standalone_env("s2", outer)
            emit_s2(S, env)
            nc = env.nc
    return nc


def emit_s3(S: Sched, env):
    nc = env.nc
    W2t_d = env.io["W2t"]
    b2c = env.io["b2c"]
    h1_d = env.io["h1"]
    xT_d = env.io["xT"]
    s0_d = env.io["s0T"]
    x1_d = env.io["x1T"]

    with ExitStack() as ctx:
        sb = lambda n, s_, d_: ctx.enter_context(nc.sbuf_tensor(env.pfx + n, s_, d_))
        psx = lambda n, s_, d_: ctx.enter_context(nc.psum_tensor(env.pfx + n, s_, d_))
        sem = lambda n: env.outer.enter_context(nc.semaphore(env.pfx + n))

        wT = [sb(f"wT{i}", [128, 1024], BF16) for i in range(32)]
        h1b = [sb("h1b0", [128, 32 * 512], BF16), sb("h1b1", [128, 32 * 512], BF16)]
        xb = [sb("xb0", [128, 8 * 512], F32), sb("xb1", [128, 8 * 512], F32)]
        s0b = [sb("s0b0", [128, 8 * 512], F32), sb("s0b1", [128, 8 * 512], F32)]
        gb = sb("gb", [128, 2 * 512], F32)
        b2s = sb("b2s", [128, 8], F32)
        ps_r = [psx("ps_r0", [128, 512], F32), psx("ps_r1", [128, 512], F32),
                psx("ps_r2", [128, 512], F32), psx("ps_r3", [128, 512], F32)]

        for k in ["d0", "d1", "d20", "d21", "w", "w0", "w1", "w2", "w3", "t",
                  "v", "a", "p", "o10", "o11"]:
            S.reg(k, sem("sem_" + k))
        block = ctx.enter_context(nc.Block())

        @block.gpsimd
        def _(g):
            for dc in range(32):
                S.inc(g.dma_start(out=wT[dc][:], in_=W2t_d[dc * 128:(dc + 1) * 128, :]),
                      "w0", 16)
            S.inc(g.dma_start(out=b2s[:], in_=b2c), "w", 16)
            for b in range(NBLK):
                if b >= 2:
                    S.wait(g, f"s3_comb_{b - 2}")
                    S.wait(g, f"s3_x1st_{b - 2}")
                S.inc(g.dma_start(
                    out=xb[b % 2].rearrange("p (k c) -> p k c", c=512),
                    in_=xT_d.rearrange("(k p) t -> p k t", p=128)[:, :, b * 512:(b + 1) * 512]),
                    f"d2{b % 2}", 16)
                S.inc(g.dma_start(
                    out=s0b[b % 2].rearrange("p (k c) -> p k c", c=512),
                    in_=s0_d.rearrange("(k p) t -> p k t", p=128)[:, :, b * 512:(b + 1) * 512]),
                    f"d2{b % 2}", 16, label=f"s3_xs_{b}")

        @block.sync
        def _(sy):
            for b in range(NBLK):
                if b >= 2:
                    S.wait(sy, f"s3_mmdone_{b - 2}")
                S.inc(sy.dma_start(
                    out=h1b[b % 2].rearrange("p (f c) -> p f c", c=512),
                    in_=h1_d.rearrange("(f p) t -> p f t", p=128)[:, :, b * 512:(b + 1) * 512]),
                    f"d{b % 2}", 16, label=f"s3_h1in_{b}")
                if b >= 1:
                    S.wait(sy, f"s3_comb_{b - 1}")
                    S.inc(sy.dma_start(
                        out=x1_d.rearrange("(k p) t -> p k t", p=128)[:, :, (b - 1) * 512:b * 512],
                        in_=xb[(b - 1) % 2].rearrange("p (k c) -> p k c", c=512)),
                        f"o1{(b - 1) % 2}", 16, label=f"s3_x1st_{b - 1}")
            S.wait(sy, f"s3_comb_{NBLK - 1}")
            S.inc(sy.dma_start(
                out=x1_d.rearrange("(k p) t -> p k t", p=128)[:, :, (NBLK - 1) * 512:NBLK * 512],
                in_=xb[(NBLK - 1) % 2].rearrange("p (k c) -> p k c", c=512)),
                f"o1{(NBLK - 1) % 2}", 16, label=f"s3_x1st_{NBLK - 1}")
            sy.wait_ge(S.sem("o10"), 16 * (NBLK // 2))
            sy.wait_ge(S.sem("o11"), 16 * (NBLK // 2))

        @block.vector
        def _(v):
            for b in range(NBLK):
                S.wait(v, f"s3_xs_{b}")
                for do in range(8):
                    S.wait(v, f"s3_silu_{b}_{do}")
                    sl = slice(do * 512, (do + 1) * 512)
                    gsl = slice((do % 2) * 512, (do % 2 + 1) * 512)
                    S.chain(v.tensor_mul(gb[:, gsl], s0b[b % 2][:, sl],
                                         gb[:, gsl]), "v")
                    S.chain(v.tensor_add(xb[b % 2][:, sl], xb[b % 2][:, sl],
                                         gb[:, gsl]), "v",
                            label=(f"s3_comb_{b}" if do == 7 else
                                   f"s3_combp_{b}_{do}"))

        @block.tensor
        def _(t):
            S.wait_cnt(t, "w0", 16 * 32)
            for b in range(NBLK):
                S.wait(t, f"s3_h1in_{b}")
                for do in range(8):
                    ps = ps_r[do % 4]
                    prev = do - 4 if do >= 4 else (4 + do if b > 0 else -1)
                    if prev >= 0:
                        S.wait(t, f"s3_silu_{b if do >= 4 else b - 1}_{prev}")
                    for fc in range(32):
                        S.inc(t.matmul(ps[:], wT[fc][:, do * 128:(do + 1) * 128],
                                       h1b[b % 2][:, fc * 512:(fc + 1) * 512],
                                       start=(fc == 0), stop=(fc == 31)),
                              "t", label=(f"s3_mm_{b}_{do}" if fc == 31 else None))
                S.label(f"s3_mmdone_{b}", "t")

        @block.scalar
        def _(s):
            S.wait_cnt(s, "w", 16)
            for b in range(NBLK):
                if b > 0:
                    S.wait(s, f"s3_comb_{b - 1}")
                for do in range(8):
                    ps = ps_r[do % 4]
                    S.wait(s, f"s3_mm_{b}_{do}")
                    if do >= 2:
                        S.wait(s, f"s3_combp_{b}_{do - 2}")
                    elif b > 0:
                        S.wait(s, f"s3_combp_{b - 1}_{6 + do}" if do == 0
                               else f"s3_comb_{b - 1}")
                    S.chain(s.activation(gb[:, (do % 2) * 512:(do % 2 + 1) * 512],
                                         ps[:], AF.Silu, bias=b2s[:, do:do + 1]),
                            "a", label=f"s3_silu_{b}_{do}")
    return nc


def build_s3():
    S = Sched()
    nc = None
    for p2 in (False, True):
        S.begin(p2)
        with ExitStack() as outer:
            env = _standalone_env("s3", outer)
            emit_s3(S, env)
            nc = env.nc
    return nc


def emit_s4(S: Sched, env):
    nc = env.nc
    x1_d = env.io["x1T"]
    WinT = env.io["WinT"]
    bz_c = env.io["bz_c"]
    bxs_r = env.io["bxs_r"]
    bbc_c = env.io["bbc_c"]
    bbc_r = env.io["bbc_r"]
    bdt_c = env.io["bdt_c"]
    dtb_c = env.io["dtb_c"]
    Alog_c = env.io["Alog_c"]
    Dsk_r = env.io["Dsk_r"]
    gam_c = env.io["gam_c"]
    bet_c = env.io["bet_c"]
    xn1_d = env.io["xn1T"]
    yf_d = env.io["yfinT"]

    with ExitStack() as ctx:
        sb = lambda n, s_, d_: ctx.enter_context(nc.sbuf_tensor(env.pfx + n, s_, d_))
        psx = lambda n, s_, d_: ctx.enter_context(nc.psum_tensor(env.pfx + n, s_, d_))
        sem = lambda n: env.outer.enter_context(nc.semaphore(env.pfx + n))

        winb = [sb(f"winb{i}", [128, 4232], BF16) for i in range(8)]
        _x1b0 = sb("x1b0", [128, 4096], F32)
        x1b = [_x1b0, _x1b0]
        xnT = _x1b0
        ubf = sb("ubf", [128, 4096], BF16)
        zsil = sb("zsil", [128, 16 * 512], BF16)
        onec_bf = sb("onec_bf", [128, 1], BF16)
        stg_x = sb("stg_x", [128, 1024], BF16)
        stg_q = sb("stg_q", [128, 1024], BF16)
        xst = sb("xst", [128, 4 * 2048], BF16)
        dtxt = sb("dtxt", [128, 4 * 2048], BF16)
        yfin = sb("yfin", [128, 16 * 512], BF16)
        Bf = sb("Bf", [64, 512], BF16)
        Cf = sb("Cf", [64, 512], BF16)
        Cx8 = sb("Cx8", [64, 8 * 128], BF16)
        Btok = sb("Btok", [128, 4 * 64], BF16)
        Bw8 = sb("Bw8", [128, 8 * 64], BF16)
        Msb8 = sb("Msb8", [128, 8 * 128], BF16)
        Mt = sb("Mt", [128, 128], F32)
        BCm = sb("BCm", [128, 128], BF16)
        sct8 = sb("sct8", [128, 8 * 128], BF16)
        dtr = sb("dtr", [8, 512], F32)
        t8a = sb("t8a", [8, 512], F32)
        eLe = t8a
        Lrow = sb("Lrow", [8, 512], F32)
        eLp = sb("eLp", [8, 512], F32)
        smallTT = sb("smallTT", [128, 4 * 24], F32)
        erow = sb("erow", [1, 4 * 8], F32)
        dsk_s = sb("dsk_s", [1, 8], F32)
        dcol = sb("dcol", [64, 4 * 8], F32)
        stf = sb("stf", [64, 2048], F32)
        stb = sb("stb", [64, 2048], BF16)
        IDh = sb("IDh", [128, 8 * 128], BF16)
        idf = sb("idf", [128, 128], F32)
        idb = sb("idb", [128, 128], BF16)
        trib = sb("trib", [128, 128], BF16)
        oneh8 = sb("oneh8", [8, 8 * 128], F32)
        onec = sb("onec", [128, 1], F32)
        onerow = sb("onerow", [1, 128], F32)
        onerow_bf = sb("onerow_bf", [1, 128], BF16)
        bxs_bf = sb("bxs_bf", [1, 2048], BF16)
        bbc_rb = sb("bbc_rb", [1, 128], BF16)
        murow = sb("murow", [1, 512], F32)
        vrow = sb("vrow", [1, 512], F32)
        mrow2 = sb("mrow2", [1, 512], F32)
        bsum = sb("bsum", [8, 1], F32)
        dtbs = sb("dtbs", [8, 1], F32)
        negA = sb("negA", [8, 1], F32)
        Dcols = sb("Dcols", [128, 8], F32)
        bzs = sb("bzs", [128, 16], F32)
        bbcs = sb("bbcs", [128, 1], F32)
        gams = sb("gams", [128, 8], F32)
        bets = sb("bets", [128, 8], F32)

        ps_io = [psx("ps_io0", [128, 512], F32), psx("ps_io1", [128, 512], F32)]
        ps_stat = psx("ps_stat", [128, 512], F32)
        ps_bc2 = psx("ps_bc2", [128, 512], F32)
        ps_sc = psx("ps_sc", [128, 512], F32)
        ps_y = psx("ps_y", [128, 512], F32)
        ps_stu = psx("ps_stu", [64, 512], F32)
        ps_mis = psx("ps_mis", [128, 512], F32)

        for k in ["dx", "w", "wc", "t", "v", "a", "p", "o1", "o2"]:
            S.reg(k, sem("sem_" + k))
        block = ctx.enter_context(nc.Block())
        NCH = 4

        @block.gpsimd
        def _(g):
            for dc in range(8):
                S.inc(g.dma_start(out=winb[dc][:],
                                  in_=WinT[dc * 128:(dc + 1) * 128, :]), "w", 16)
            S.inc(g.dma_start(out=bzs[:], in_=bz_c), "wc", 16)
            S.inc(g.dma_start(out=bxs_bf[:], in_=bxs_r), "wc", 16)
            S.inc(g.dma_start(out=bbcs[:], in_=bbc_c), "wc", 16)
            S.inc(g.dma_start(out=bbc_rb[:], in_=bbc_r), "wc", 16)
            S.inc(g.dma_start(out=bsum[:], in_=bdt_c), "wc", 16)
            S.inc(g.dma_start(out=dtbs[:], in_=dtb_c), "wc", 16)
            S.inc(g.dma_start(out=negA[:], in_=Alog_c), "wc", 16)
            S.inc(g.dma_start(out=dsk_s[:], in_=Dsk_r), "wc", 16)
            S.inc(g.dma_start(out=gams[:], in_=gam_c), "wc", 16)
            S.inc(g.dma_start(out=bets[:], in_=bet_c), "wc", 16)
            g.wait_ge(S.sem("wc"), 16 * 10)
            S.chain(g.memset(idf[:], 1.0), "p")
            S.chain(g.affine_select(idf[:], idf[:], pattern=[[1, 128]], base=0,
                                    channel_multiplier=-1,
                                    compare_op=ALU.is_equal, fill=0.0), "p")
            S.chain(g.tensor_copy(idb[:], idf[:]), "p")
            S.chain(g.memset(trib[:], 1.0), "p")
            S.chain(g.affine_select(trib[:], trib[:], pattern=[[1, 128]], base=0,
                                    channel_multiplier=-1,
                                    compare_op=ALU.is_ge, fill=0.0), "p")
            S.chain(g.memset(oneh8[:], 1.0), "p")
            S.chain(g.affine_select(oneh8[:], oneh8[:], pattern=[[1, 8], [0, 128]],
                                    base=0, channel_multiplier=-1,
                                    compare_op=ALU.is_equal, fill=0.0), "p")
            S.chain(g.memset(onec[:], 1.0), "p")
            S.chain(g.memset(onec_bf[:], 1.0), "p")
            S.chain(g.memset(onerow[:], 1.0), "p")
            S.chain(g.memset(onerow_bf[:], 1.0), "p")
            S.chain(g.memset(stf[:], 0.0), "p")
            S.chain(g.memset(stb[:], 0.0), "p", label="s4_const")

        @block.sync
        def _(sy):
            for b in range(NBLK):
                if b >= 1:
                    S.wait(sy, f"s4_xnst_{b - 1}")
                    S.wait(sy, f"s4_uc_{b - 1}_{NT - 1}")
                    S.wait(sy, f"s4_sq2_{b - 1}_{NT - 1}")
                S.inc(sy.dma_start(
                    out=x1b[b % 2].rearrange("p (k c) -> p k c", c=512),
                    in_=x1_d.rearrange("(k p) t -> p k t", p=128)[:, :, b * 512:(b + 1) * 512]),
                    "dx", 16, label=f"s4_x1_{b}")
                if b >= 1:
                    S.wait(sy, f"s4_yf_done_{b - 1}")
                    S.inc(sy.dma_start(
                        out=yf_d.rearrange("(f p) t -> p f t", p=128)[:, :, (b - 1) * 512:b * 512],
                        in_=yfin.rearrange("p (f c) -> p f c", c=512)),
                        "o2", 16, label=f"s4_yfst_{b - 1}")
                S.wait(sy, f"s4_xn_done_{b}")
                S.inc(sy.dma_start(
                    out=xn1_d.rearrange("(k p) t -> p k t", p=128)[:, :, b * 512:(b + 1) * 512],
                    in_=xnT.rearrange("p (k c) -> p k c", c=512)),
                    "o1", 16, label=f"s4_xnst_{b}")
            S.wait(sy, f"s4_yf_done_{NBLK - 1}")
            S.inc(sy.dma_start(
                out=yf_d.rearrange("(f p) t -> p f t", p=128)[:, :, (NBLK - 1) * 512:NBLK * 512],
                in_=yfin.rearrange("p (f c) -> p f c", c=512)),
                "o2", 16, label=f"s4_yfst_{NBLK - 1}")
            sy.wait_ge(S.sem("o1"), 16 * NBLK)
            sy.wait_ge(S.sem("o2"), 16 * NBLK)

        @block.tensor
        def _(t):
            S.wait(t, "s4_const")
            S.wait_cnt(t, "w", 16 * 8)
            S.inc(t.matmul(ps_mis[:, 0:8], onerow[:], dsk_s[:],
                           start=True, stop=True), "t", label="s4_dskb")
            for b in range(NBLK):
                S.wait(t, f"s4_x1_{b}")
                if b > 0:
                    S.wait(t, f"s4_uc_{b - 1}_{NT - 1}")
                    S.wait(t, f"s4_m_{b - 1}_{NCH - 1}_7")
                for dc in range(NT):
                    S.wait(t, f"s4_cp1_{b}_{dc}")
                    S.inc(t.matmul(ps_stat[0:1, :], onec_bf[:],
                                   stg_x[:, (dc % 2) * 512:(dc % 2 + 1) * 512],
                                   start=(dc == 0), stop=(dc == NT - 1)),
                          "t", label=f"s4_summm1_{b}_{dc}")
                for dc in range(NT):
                    S.wait(t, f"s4_sq1_{b}_{dc}")
                    S.inc(t.matmul(ps_stat[32:33, :], onec_bf[:],
                                   stg_q[:, (dc % 2) * 512:(dc % 2 + 1) * 512],
                                   start=(dc == 0), stop=(dc == NT - 1)),
                          "t", label=f"s4_sqmm1_{b}_{dc}")
                S.wait(t, f"s4_rows1v_{b}")
                S.inc(t.matmul(ps_stat[:, :], onerow[:], murow[:],
                               start=True, stop=True), "t", label=f"s4_mub_{b}")
                S.wait(t, f"s4_rows1a_{b}")
                S.inc(t.matmul(ps_bc2[:, :], onerow[:], mrow2[:],
                               start=True, stop=True), "t", label=f"s4_rb_{b}")
                S.wait(t, f"s4_xn_done_{b}")
                for dc in range(NT):
                    S.wait(t, f"s4_cp2_{b}_{dc}")
                    S.inc(t.matmul(ps_stat[0:1, :], onec_bf[:],
                                   stg_x[:, (dc % 2) * 512:(dc % 2 + 1) * 512],
                                   start=(dc == 0), stop=(dc == NT - 1)),
                          "t", label=f"s4_summm2_{b}_{dc}")
                for dc in range(NT):
                    S.wait(t, f"s4_sq2_{b}_{dc}")
                    S.inc(t.matmul(ps_stat[32:33, :], onec_bf[:],
                                   stg_q[:, (dc % 2) * 512:(dc % 2 + 1) * 512],
                                   start=(dc == 0), stop=(dc == NT - 1)),
                          "t", label=f"s4_sqmm2_{b}_{dc}")
                S.label(f"s4_sum2_{b}", "t")
                S.wait(t, f"s4_rows2v_{b}")
                S.inc(t.matmul(ps_stat[:, :], onerow[:], murow[:],
                               start=True, stop=True), "t", label=f"s4_mub2_{b}")
                S.wait(t, f"s4_rows2a_{b}")
                S.inc(t.matmul(ps_bc2[:, :], onerow[:], mrow2[:],
                               start=True, stop=True), "t", label=f"s4_rb2_{b}")
                if b > 0:
                    S.wait(t, f"s4_stup_{b - 1}_{NCH - 1}_7")
                    S.wait(t, f"s4_ye_{b - 1}_{NCH - 1}_7")
                    S.wait(t, f"s4_dte_{b - 1}")
                    S.wait(t, f"s4_bctok_{b - 1}_3")
                    S.wait(t, f"s4_sct8_{b - 1}_3")
                for dc in range(NT):
                    S.wait(t, f"s4_ubf_{b}_{dc}")
                    S.inc(t.matmul(ps_io[1][0:8, 0:512],
                                   winb[dc][:, 4224:4232],
                                   ubf[:, dc * 512:(dc + 1) * 512],
                                   start=(dc == 0), stop=(dc == NT - 1)),
                          "t", label=(f"s4_dtmm_{b}" if dc == NT - 1 else None))
                for dc in range(NT):
                    S.inc(t.matmul(ps_sc[:, 0:512],
                                   winb[dc][:, 4096:4224],
                                   ubf[:, dc * 512:(dc + 1) * 512],
                                   start=(dc == 0), stop=(dc == NT - 1)),
                          "t", label=(f"s4_bcmm_{b}" if dc == NT - 1 else None))
                for tt in range(4):
                    if tt >= 2:
                        S.wait(t, f"s4_bctok_{b}_{tt - 2}")
                    elif tt == 1:
                        S.wait(t, f"s4_dte_{b}")
                    for dc in range(NT):
                        S.inc(t.matmul(
                            ps_io[tt % 2][:, 0:128],
                            ubf[:, dc * 512 + tt * 128:dc * 512 + (tt + 1) * 128],
                            winb[dc][:, 4096:4224],
                            start=(dc == 0), stop=False), "t")
                    S.inc(t.matmul(ps_io[tt % 2][:, 0:128], onerow_bf[:],
                                   bbc_rb[:], start=False, stop=True), "t",
                          label=f"s4_bctokmm_{b}_{tt}")
                for ft in range(16):
                    if ft >= 2:
                        S.wait(t, f"s4_zsil_{b}_{ft - 2}")
                    elif ft == 0:
                        S.wait(t, f"s4_bctok_{b}_2")
                    else:
                        S.wait(t, f"s4_bctok_{b}_3")
                    for dc in range(NT):
                        S.inc(t.matmul(ps_io[ft % 2][:],
                                       winb[dc][:, ft * 128:(ft + 1) * 128],
                                       ubf[:, dc * 512:(dc + 1) * 512],
                                       start=(dc == 0), stop=(dc == NT - 1)),
                              "t", label=(f"s4_zmm_{b}_{ft}" if dc == NT - 1
                                          else None))
                for tt in range(4):
                    for fb in range(4):
                        par = (tt * 4 + fb) % 2
                        pidx = tt * 4 + fb
                        if pidx >= 2:
                            pt, pf = divmod(pidx - 2, 4)
                            S.wait(t, f"s4_xse_{b}_{pt}_{pf}")
                        else:
                            S.wait(t, f"s4_zsil_{b}_{14 + pidx}")
                        for dc in range(NT):
                            S.inc(t.matmul(
                                ps_io[par][:, 0:512],
                                ubf[:, dc * 512 + tt * 128:dc * 512 + (tt + 1) * 128],
                                winb[dc][:, 2048 + fb * 512:2048 + (fb + 1) * 512],
                                start=(dc == 0), stop=False), "t")
                        S.inc(t.matmul(ps_io[par][:, 0:512], onerow_bf[:],
                                       bxs_bf[0:1, fb * 512:(fb + 1) * 512],
                                       start=False, stop=True), "t",
                              label=(f"s4_xsmm_{b}_{tt}_{fb}"))
                S.label(f"s4_inproj_{b}", "t")
                S.wait(t, f"s4_eL_{b}")
                for ch in range(NCH):
                    if ch > 0:
                        S.wait(t, f"s4_smallevac_{b}_{ch - 1}")
                    elif b > 0:
                        S.wait(t, f"s4_smallevac_{b - 1}_{NCH - 1}")
                        S.wait(t, f"s4_cxg_{b - 1}_{NCH - 1}_1")
                    else:
                        S.wait(t, "s4_idh")
                    S.inc(t.transpose(ps_mis[:, 0:8],
                                      dtr[:, ch * 128:(ch + 1) * 128],
                                      idf[0:8, 0:8]), "t")
                    S.inc(t.transpose(ps_mis[:, 8:16],
                                      Lrow[:, ch * 128:(ch + 1) * 128],
                                      idf[0:8, 0:8]), "t")
                    S.inc(t.transpose(ps_mis[:, 16:24],
                                      eLe[:, ch * 128:(ch + 1) * 128],
                                      idf[0:8, 0:8]), "t")
                    S.inc(t.transpose(ps_mis[0:1, 256:264],
                                      eLp[:, ch * 128 + 127:ch * 128 + 128],
                                      idf[0:8, 0:8]),
                          "t", label=f"s4_str_{b}_{ch}")
                for ch in range(NCH):
                    S.wait(t, f"s4_smallevac_{b}_{ch}")
                    if ch > 0:
                        S.wait(t, f"s4_dcole_{b}_{ch - 1}")
                    elif b > 0:
                        S.wait(t, f"s4_stup_{b - 1}_{NCH - 1}_7")
                    S.inc(t.matmul(ps_stu[:, 256:264], onerow[0:1, 0:64],
                                   erow[0:1, ch * 8:(ch + 1) * 8],
                                   start=True, stop=True), "t",
                          label=f"s4_dcb_{b}_{ch}")
                for ch in range(NCH):
                    S.wait(t, f"s4_bce_{b}")
                    if ch > 0:
                        S.wait(t, f"s4_bcmsk_{b}_{ch - 1}")
                    S.inc(t.matmul(ps_sc[:, 0:128], Bf[:, ch * 128:(ch + 1) * 128],
                                   Cf[:, ch * 128:(ch + 1) * 128],
                                   start=True, stop=True), "t",
                          label=f"s4_scmm_{b}_{ch}")
                    for h in range(8):
                        bank = ps_bc2 if h % 2 == 0 else ps_stat
                        prevm = (f"s4_m_{b}_{ch}_{h - 2}" if h >= 2 else
                                 (f"s4_m_{b}_{ch - 1}_{6 + h}" if ch > 0 else
                                  (f"s4_m_{b - 1}_{NCH - 1}_{6 + h}" if b > 0
                                   else None)))
                        if prevm:
                            S.wait(t, prevm)
                        S.inc(t.matmul(bank[:, 0:128],
                                       oneh8[:, h * 128:(h + 1) * 128],
                                       Lrow[:, ch * 128:(ch + 1) * 128],
                                       start=True, stop=True), "t",
                              label=f"s4_lb_{b}_{ch}_{h}")
                    for grp in range(2):
                        if ch == 0 and grp == 0:
                            S.wait(t, f"s4_smallevac_{b}_{NCH - 1}")
                        prevcx = (f"s4_cxg_{b}_{ch}_{grp - 1}" if grp > 0 else
                                  (f"s4_cxg_{b}_{ch - 1}_1" if ch > 0 else
                                   (f"s4_cxg_{b - 1}_{NCH - 1}_1" if b > 0 else None)))
                        if prevcx:
                            S.wait(t, prevcx)
                        for hh in range(4):
                            h = grp * 4 + hh
                            S.inc(t.matmul(ps_mis[:, hh * 128:(hh + 1) * 128],
                                           oneh8[:, h * 128:(h + 1) * 128],
                                           eLp[:, ch * 128:(ch + 1) * 128],
                                           start=True, stop=True), "t",
                                  label=(f"s4_epbg_{b}_{ch}_{grp}" if hh == 3
                                         else None))
                    for h in range(8):
                        ybank = ps_y if h % 2 == 0 else ps_io[0]
                        prevye = (f"s4_ye_{b}_{ch}_{h - 2}" if h > 1 else
                                  (f"s4_ye_{b}_{ch - 1}_{6 + h}" if ch > 0 else
                                   (f"s4_ye_{b - 1}_{NCH - 1}_{6 + h}" if b > 0
                                    else None)))
                        if prevye:
                            S.wait(t, prevye)
                        if ch > 0:
                            S.wait(t, f"s4_stup_{b}_{ch - 1}_{h}")
                        elif b > 0:
                            S.wait(t, f"s4_stup_{b - 1}_{NCH - 1}_{h}")
                        for half in range(2):
                            off = h * 256 + half * 128
                            S.inc(t.matmul(ybank[:, half * 128:(half + 1) * 128],
                                           xst[:, ch * 2048 + off:ch * 2048 + off + 128],
                                           IDh[:, h * 128:(h + 1) * 128],
                                           start=True, stop=False), "t")
                            if h == 0 and half == 0:
                                S.wait(t, f"s4_sct8_{b}_{ch}")
                            S.inc(t.matmul(ybank[:, half * 128:(half + 1) * 128],
                                           dtxt[:, ch * 2048 + off:ch * 2048 + off + 128],
                                           sct8[:, h * 128:(h + 1) * 128],
                                           start=False, stop=False), "t")
                            if h == 0 and half == 0:
                                S.wait(t, f"s4_cxg_{b}_{ch}_1")
                            S.inc(t.matmul(ybank[:, half * 128:(half + 1) * 128],
                                           stb[0:64, off:off + 128],
                                           Cx8[:, h * 128:(h + 1) * 128],
                                           start=False, stop=True),
                                  "t", label=(f"s4_ymm_{b}_{ch}_{h}"
                                              if half == 1 else None))
                        stbank = ps_stu if h % 2 == 0 else ps_io[1]
                        prevst = (f"s4_stup_{b}_{ch}_{h - 2}" if h > 1 else
                                  (f"s4_stup_{b}_{ch - 1}_{6 + h}" if ch > 0 else
                                   (f"s4_stup_{b - 1}_{NCH - 1}_{6 + h}" if b > 0
                                    else None)))
                        if prevst:
                            S.wait(t, prevst)
                        if h == 0:
                            S.wait(t, f"s4_bw8_{b}_{ch}")
                        S.inc(t.matmul(stbank[0:64, 0:256],
                                       Bw8[:, h * 64:(h + 1) * 64],
                                       dtxt[:, ch * 2048 + h * 256:
                                            ch * 2048 + (h + 1) * 256],
                                       start=True, stop=True), "t",
                              label=f"s4_stmm_{b}_{ch}_{h}")

        @block.scalar
        def _(s):
            S.wait(s, "s4_const")
            S.chain(s.activation(negA[:], negA[:], AF.Exp), "a",
                    label="s4_negA1")
            for b in range(NBLK):
                def _stage1(dc):
                    if dc == 0:
                        S.wait(s, f"s4_x1_{b}")
                    if dc > 1:
                        S.wait(s, f"s4_summm1_{b}_{dc - 2}")
                    S.chain(s.activation(stg_x[:, (dc % 2) * 512:(dc % 2 + 1) * 512],
                                         x1b[b % 2][:, dc * 512:(dc + 1) * 512],
                                         AF.Copy), "a", label=f"s4_cp1_{b}_{dc}")
                def _stage1q(dc):
                    if dc > 1:
                        S.wait(s, f"s4_sqmm1_{b}_{dc - 2}")
                    S.chain(s.activation(stg_q[:, (dc % 2) * 512:(dc % 2 + 1) * 512],
                                         x1b[b % 2][:, dc * 512:(dc + 1) * 512],
                                         AF.Square), "a", label=f"s4_sq1_{b}_{dc}")
                def _mbuild(pb, ch, h):
                    S.wait(s, f"s4_lb_{pb}_{ch}_{h}")
                    S.wait(s, f"s4_smallevac_{pb}_{ch}")
                    if h == 0:
                        prevsct8 = (f"s4_sct8_{pb}_{ch - 1}" if ch > 0 else
                                    (f"s4_sct8_{pb - 1}_{NCH - 1}" if pb > 0
                                     else None))
                        if prevsct8:
                            S.wait(s, prevsct8)
                    bank = ps_bc2 if h % 2 == 0 else ps_stat
                    S.chain(s.activation(
                        Mt[:], bank[:, 0:128], AF.Relu, scale=-1.0,
                        bias=smallTT[:, ch * 24 + 8 + h:ch * 24 + 8 + h + 1]),
                        "a")
                    S.chain(s.activation(Msb8[:, h * 128:(h + 1) * 128],
                                         Mt[:], AF.Exp, scale=-1.0), "a",
                            label=f"s4_m_{pb}_{ch}_{h}")
                if b > 0:
                    free_ops = [("x", 0), ("x", 1), ("q", 0), ("q", 1)]
                    for ch in range(NCH):
                        for h in range(8):
                            _mbuild(b - 1, ch, h)
                        if free_ops:
                            kind, dc = free_ops.pop(0)
                            (_stage1 if kind == "x" else _stage1q)(dc)
                    for dc in range(2, NT):
                        _stage1(dc)
                    for dc in range(2, NT):
                        _stage1q(dc)
                else:
                    for dc in range(NT):
                        _stage1(dc)
                    for dc in range(NT):
                        _stage1q(dc)
                S.wait(s, f"s4_rows1q_{b}")
                S.chain(s.activation(mrow2[:], mrow2[:], AF.Ln), "a")
                S.chain(s.activation(mrow2[:], mrow2[:], AF.Exp, scale=-0.5), "a",
                        label=f"s4_rows1a_{b}")
                for dc in range(NT):
                    S.wait(s, f"s4_xnc_{b}_{dc}")
                    if dc > 1:
                        S.wait(s, f"s4_summm2_{b}_{dc - 2}")
                    else:
                        S.wait(s, f"s4_summm1_{b}_{NT - 1}")
                    S.chain(s.activation(stg_x[:, (dc % 2) * 512:(dc % 2 + 1) * 512],
                                         xnT[:, dc * 512:(dc + 1) * 512],
                                         AF.Copy), "a", label=f"s4_cp2_{b}_{dc}")
                for dc in range(NT):
                    if dc > 1:
                        S.wait(s, f"s4_sqmm2_{b}_{dc - 2}")
                    else:
                        S.wait(s, f"s4_sqmm1_{b}_{NT - 1}")
                    S.chain(s.activation(stg_q[:, (dc % 2) * 512:(dc % 2 + 1) * 512],
                                         xnT[:, dc * 512:(dc + 1) * 512],
                                         AF.Square), "a", label=f"s4_sq2_{b}_{dc}")
                S.wait(s, f"s4_rows2q_{b}")
                S.chain(s.activation(mrow2[:], mrow2[:], AF.Ln), "a")
                S.chain(s.activation(mrow2[:], mrow2[:], AF.Exp, scale=-0.5), "a",
                        label=f"s4_rows2a_{b}")
                for dc in range(NT):
                    S.wait(s, f"s4_uc_{b}_{dc}")
                    S.chain(s.activation(ubf[:, dc * 512:(dc + 1) * 512],
                                         ubf[:, dc * 512:(dc + 1) * 512],
                                         AF.Identity, bias=bets[:, dc:dc + 1],
                                         scale=gams[:, dc:dc + 1]), "a",
                            label=f"s4_ubf_{b}_{dc}")
                S.wait(s, f"s4_dtmm_{b}")
                S.wait(s, "s4_bsum")
                if b > 0:
                    S.wait(s, f"s4_dtla_{b - 1}")
                S.chain(s.activation(dtr[:], ps_io[1][0:8, 0:512], AF.Identity,
                                     bias=bsum[:]), "a", label=f"s4_dte_{b}")
                S.chain(s.activation(t8a[:], dtr[:], AF.Abs), "a")
                S.chain(s.activation(t8a[:], t8a[:], AF.Exp, scale=-1.0), "a")
                S.chain(s.activation(t8a[:], t8a[:], AF.Ln, bias=1.0), "a",
                        label=f"s4_sp_{b}")
                S.wait(s, f"s4_bcmm_{b}")
                if b > 0:
                    S.wait(s, f"s4_scmm_{b - 1}_{NCH - 1}")
                    S.wait(s, f"s4_cxg_{b - 1}_{NCH - 1}_1")
                S.chain(s.activation(Bf[:], ps_sc[0:64, 0:512], AF.Identity,
                                     bias=bbcs[0:64, 0:1]), "a")
                S.chain(s.activation(Cf[:], ps_sc[64:128, 0:512], AF.Identity,
                                     bias=bbcs[64:128, 0:1]), "a",
                        label=f"s4_bce_{b}")
                for tt in range(4):
                    S.wait(s, f"s4_bctokmm_{b}_{tt}")
                    if b > 0 and tt == 0:
                        S.wait(s, f"s4_bw8_{b - 1}_{NCH - 1}")
                    S.chain(s.activation(Btok[:, tt * 64:(tt + 1) * 64],
                                         ps_io[tt % 2][:, 0:64], AF.Copy), "a",
                            label=f"s4_bctok_{b}_{tt}")
                for ft in range(16):
                    S.wait(s, f"s4_zmm_{b}_{ft}")
                    if b > 0 and ft == 0:
                        S.wait(s, f"s4_ye_{b - 1}_{NCH - 1}_7")
                    S.chain(s.activation(zsil[:, ft * 512:(ft + 1) * 512],
                                         ps_io[ft % 2][:], AF.Silu,
                                         bias=bzs[:, ft:ft + 1]), "a",
                            label=f"s4_zsil_{b}_{ft}")
                for tt in range(4):
                    for fb in range(4):
                        S.wait(s, f"s4_xsmm_{b}_{tt}_{fb}")
                        if b > 0 and tt == 0 and fb == 0:
                            S.wait(s, f"s4_stmm_{b - 1}_{NCH - 1}_7")
                            S.wait(s, f"s4_dtx_{b - 1}")
                        S.chain(s.activation(
                            xst[:, tt * 2048 + fb * 512:tt * 2048 + (fb + 1) * 512],
                            ps_io[(tt * 4 + fb) % 2][:], AF.Copy), "a",
                            label=f"s4_xse_{b}_{tt}_{fb}")
                S.wait(s, f"s4_L_{b}")
                for ch in range(NCH):
                    S.chain(s.activation(eLe[:, ch * 128:(ch + 1) * 128],
                                         Lrow[:, ch * 128:(ch + 1) * 128],
                                         AF.Exp, scale=-1.0,
                                         bias=Lrow[:, ch * 128 + 127:ch * 128 + 128]),
                            "a")
                S.chain(s.activation(eLp[:], Lrow[:], AF.Exp), "a",
                        label=f"s4_eL_{b}")
                if b == NBLK - 1:
                    for ch in range(NCH):
                        for h in range(8):
                            _mbuild(b, ch, h)
        @block.vector
        def _(v):
            S.wait_cnt(v, "wc", 160)
            S.chain(v.tensor_add(bsum[:], bsum[:], dtbs[:]), "v",
                    label="s4_bsum")
            S.wait(v, "s4_negA1")
            S.chain(v.tensor_scalar_mul(negA[:], negA[:], -1.0), "v",
                    label="s4_negA")
            S.wait(v, "s4_dskb")
            S.wait(v, "s4_const")
            S.chain(v.tensor_copy(Dcols[:, 0:8], ps_mis[:, 0:8]), "v")
            for h in range(8):
                S.chain(v.tensor_scalar_mul(IDh[:, h * 128:(h + 1) * 128],
                                            idb[:], Dcols[:, h:h + 1]), "v",
                        label=("s4_idh" if h == 7 else None))
            for b in range(NBLK):
                S.wait(v, f"s4_sqmm1_{b}_{NT - 1}")
                S.chain(v.tensor_scalar_mul(murow[:], ps_stat[0:1, :], 1.0 / D), "v")
                S.chain(v.tensor_scalar_mul(vrow[:], ps_stat[32:33, :], 1.0 / D), "v",
                        label=f"s4_rows1v_{b}")
                S.chain(v.tensor_mul(mrow2[:], murow[:], murow[:]), "v")
                S.chain(v.tensor_sub(mrow2[:], vrow[:], mrow2[:]), "v")
                S.chain(v.tensor_scalar_add(mrow2[:], mrow2[:], 1e-5), "v",
                        label=f"s4_rows1q_{b}")
                S.wait(v, f"s4_mub_{b}")
                S.wait(v, f"s4_rb_{b}")
                if b > 0:
                    S.wait(v, f"s4_xnst_{b - 1}")
                for dc in range(NT):
                    S.chain(v.tensor_sub(xnT[:, dc * 512:(dc + 1) * 512],
                                         x1b[b % 2][:, dc * 512:(dc + 1) * 512],
                                         ps_stat[:, :]), "v")
                    S.chain(v.tensor_mul(xnT[:, dc * 512:(dc + 1) * 512],
                                         xnT[:, dc * 512:(dc + 1) * 512],
                                         ps_bc2[:, :]), "v",
                            label=f"s4_xnc_{b}_{dc}")
                S.label(f"s4_xn_done_{b}", "v")
                S.wait(v, f"s4_sum2_{b}")
                S.wait(v, f"s4_sqmm2_{b}_{NT - 1}")
                S.chain(v.tensor_scalar_mul(murow[:], ps_stat[0:1, :], 1.0 / D), "v")
                S.chain(v.tensor_scalar_mul(vrow[:], ps_stat[32:33, :], 1.0 / D), "v",
                        label=f"s4_rows2v_{b}")
                S.chain(v.tensor_mul(mrow2[:], murow[:], murow[:]), "v")
                S.chain(v.tensor_sub(mrow2[:], vrow[:], mrow2[:]), "v")
                S.chain(v.tensor_scalar_add(mrow2[:], mrow2[:], 1e-5), "v",
                        label=f"s4_rows2q_{b}")
                S.wait(v, f"s4_mub2_{b}")
                S.wait(v, f"s4_rb2_{b}")
                S.wait(v, f"s4_sqmm2_{b}_{NT - 1}")
                for dc in range(NT):
                    S.chain(v.tensor_sub(ubf[:, dc * 512:(dc + 1) * 512],
                                         xnT[:, dc * 512:(dc + 1) * 512],
                                         ps_stat[:, :]), "v")
                    S.chain(v.tensor_mul(ubf[:, dc * 512:(dc + 1) * 512],
                                         ubf[:, dc * 512:(dc + 1) * 512],
                                         ps_bc2[:, :]), "v",
                            label=f"s4_uc_{b}_{dc}")
                S.wait(v, f"s4_sp_{b}")
                S.chain(v.tensor_scalar_max(dtr[:], dtr[:], 0.0), "v")
                S.chain(v.tensor_add(dtr[:], dtr[:], t8a[:]), "v")
                S.wait(v, "s4_negA")
                S.chain(v.tensor_scalar_mul(t8a[:], dtr[:], negA[:, 0:1]), "v",
                        label=f"s4_dtla_{b}")
                if b > 0:
                    S.wait(v, f"s4_lb_{b - 1}_{NCH - 1}_7")
                    S.wait(v, f"s4_epbg_{b - 1}_{NCH - 1}_1")
                S.chain(v.memset(Mt[0:8, 0:128], 0.0), "v")
                for ch in range(NCH):
                    S.chain(v.tensor_tensor_scan(
                        Lrow[:, ch * 128:(ch + 1) * 128],
                        t8a[:, ch * 128:(ch + 1) * 128],
                        Mt[0:8, 0:128], 0.0, op0=ALU.add, op1=ALU.add), "v",
                        label=(f"s4_L_{b}" if ch == NCH - 1 else None))
                for ch in range(NCH):
                    S.wait(v, f"s4_str_{b}_{ch}")
                    S.chain(v.tensor_copy(smallTT[:, ch * 24:(ch + 1) * 24],
                                          ps_mis[:, 0:24]), "v")
                    S.chain(v.tensor_copy(erow[0:1, ch * 8:(ch + 1) * 8],
                                          ps_mis[0:1, 256:264]), "v",
                            label=f"s4_smallevac_{b}_{ch}")
                for ch in range(NCH):
                    S.wait(v, f"s4_dcb_{b}_{ch}")
                    S.chain(v.tensor_copy(dcol[:, ch * 8:(ch + 1) * 8],
                                          ps_stu[:, 256:264]), "v",
                            label=f"s4_dcole_{b}_{ch}")
                S.wait(v, f"s4_xse_{b}_3_3")
                if b > 0:
                    S.wait(v, f"s4_stmm_{b - 1}_{NCH - 1}_7")
                for tt in range(4):
                    for h in range(8):
                        S.chain(v.tensor_scalar_mul(
                            dtxt[:, tt * 2048 + h * 256:tt * 2048 + (h + 1) * 256],
                            xst[:, tt * 2048 + h * 256:tt * 2048 + (h + 1) * 256],
                            smallTT[:, tt * 24 + h:tt * 24 + h + 1]), "v",
                            label=(f"s4_dtx_{b}" if tt == 3 and h == 7 else None))
                for ch in range(NCH):
                    S.wait(v, f"s4_scmm_{b}_{ch}")
                    S.chain(v.tensor_mul(BCm[:], ps_sc[:, 0:128], trib[:]), "v",
                            label=f"s4_bcmsk_{b}_{ch}")
                    for grp in range(2):
                        S.wait(v, f"s4_epbg_{b}_{ch}_{grp}")
                        if grp == 0 and (ch > 0 or b > 0):
                            pb, pch = (b, ch - 1) if ch > 0 else (b - 1, NCH - 1)
                            S.wait(v, f"s4_ymm_{pb}_{pch}_7")
                        cfb = bass.AP(Cf[:].tensor, ch * 128,
                                      [[512, 64], [0, 4], [1, 128]])
                        S.chain(v.tensor_mul(
                            Cx8.rearrange("p (h c) -> p h c", c=128)[
                                :, grp * 4:(grp + 1) * 4, :],
                            ps_mis.rearrange("p (h c) -> p h c", c=128)[
                                0:64, 0:4, :],
                            cfb), "v", label=f"s4_cxg_{b}_{ch}_{grp}")
                    S.wait(v, f"s4_m_{b}_{ch}_7")
                    bcb = bass.AP(BCm[:].tensor, 0, [[128, 128], [0, 8], [1, 128]])
                    S.chain(v.tensor_mul(
                        sct8.rearrange("p (h c) -> p h c", c=128),
                        bcb,
                        Msb8.rearrange("p (h c) -> p h c", c=128)), "v",
                        label=f"s4_sct8_{b}_{ch}")
                    if ch > 0:
                        S.wait(v, f"s4_stmm_{b}_{ch - 1}_7")
                    elif b > 0:
                        S.wait(v, f"s4_stmm_{b - 1}_{NCH - 1}_7")
                    for h in range(8):
                        S.chain(v.tensor_scalar_mul(
                            Bw8[:, h * 64:(h + 1) * 64],
                            Btok[:, ch * 64:(ch + 1) * 64],
                            smallTT[:, ch * 24 + 16 + h:ch * 24 + 16 + h + 1]),
                            "v", label=(f"s4_bw8_{b}_{ch}" if h == 7 else None))
                    for h in range(8):
                        ybank = ps_y if h % 2 == 0 else ps_io[0]
                        stbank = ps_stu if h % 2 == 0 else ps_io[1]
                        S.wait(v, f"s4_ymm_{b}_{ch}_{h}")
                        if b > 0 and ch == 0 and h == 0:
                            S.wait(v, f"s4_yfst_{b - 1}")
                        for half in range(2):
                            fcol = (2 * h + half) * 512 + ch * 128
                            S.chain(v.tensor_mul(
                                yfin[:, fcol:fcol + 128],
                                ybank[:, half * 128:(half + 1) * 128],
                                zsil[:, fcol:fcol + 128]), "v",
                                label=(f"s4_ye_{b}_{ch}_{h}" if half == 1
                                       else None))
                        S.wait(v, f"s4_stmm_{b}_{ch}_{h}")
                        if h == 0:
                            S.wait(v, f"s4_dcole_{b}_{ch}")
                        S.chain(v.scalar_tensor_tensor(
                            stf[:, h * 256:(h + 1) * 256],
                            stf[:, h * 256:(h + 1) * 256],
                            dcol[:, ch * 8 + h:ch * 8 + h + 1],
                            stbank[0:64, 0:256], op0=ALU.mult, op1=ALU.add), "v")
                        S.chain(v.tensor_copy(stb[:, h * 256:(h + 1) * 256],
                                              stf[:, h * 256:(h + 1) * 256]),
                                "v", label=f"s4_stup_{b}_{ch}_{h}")
                S.label(f"s4_yf_done_{b}", "v")
    return nc


def build_s4():
    S = Sched()
    nc = None
    for p2 in (False, True):
        S.begin(p2)
        with ExitStack() as outer:
            env = _standalone_env("s4", outer)
            emit_s4(S, env)
            nc = env.nc
    return nc


def emit_s5(S: Sched, env):
    nc = env.nc
    Wo = env.io["Wo"]
    bo_c = env.io["bo_c"]
    yf_d = env.io["yfinT"]
    xn1_d = env.io["xn1T"]
    s1_d = env.io["s1T"]

    with ExitStack() as ctx:
        sb = lambda n, s_, d_: ctx.enter_context(nc.sbuf_tensor(env.pfx + n, s_, d_))
        psx = lambda n, s_, d_: ctx.enter_context(nc.psum_tensor(env.pfx + n, s_, d_))
        sem = lambda n: env.outer.enter_context(nc.semaphore(env.pfx + n))
        wT = [sb(f"wT{i}", [128, 1024], BF16) for i in range(16)]
        yb = [sb("yb0", [128, 16 * 512], BF16), sb("yb1", [128, 16 * 512], BF16)]
        xnb = [sb("xnb0", [128, 8 * 512], F32), sb("xnb1", [128, 8 * 512], F32)]
        s1b = sb("s1b", [128, 8 * 512], F32)
        bos = sb("bos", [128, 8], F32)
        ps_r = [psx("ps_r0", [128, 512], F32), psx("ps_r1", [128, 512], F32),
                psx("ps_r2", [128, 512], F32), psx("ps_r3", [128, 512], F32)]
        for k in ["d0", "d1", "w", "t", "v", "a", "p", "o1"]:
            S.reg(k, sem("sem_" + k))
        block = ctx.enter_context(nc.Block())

        @block.gpsimd
        def _(g):
            for fc in range(16):
                S.inc(g.dma_start(out=wT[fc][:], in_=Wo[fc * 128:(fc + 1) * 128, :]),
                      "w", 16)
            S.inc(g.dma_start(out=bos[:], in_=bo_c), "w", 16)

        @block.sync
        def _(sy):
            for b in range(NBLK):
                if b >= 2:
                    S.wait(sy, f"s5_mmdone_{b - 2}")
                    S.wait(sy, f"s5_comb_{b - 2}")
                S.inc(sy.dma_start(
                    out=yb[b % 2].rearrange("p (f c) -> p f c", c=512),
                    in_=yf_d.rearrange("(f p) t -> p f t", p=128)[:, :, b * 512:(b + 1) * 512]),
                    f"d{b % 2}", 16)
                S.inc(sy.dma_start(
                    out=xnb[b % 2].rearrange("p (k c) -> p k c", c=512),
                    in_=xn1_d.rearrange("(k p) t -> p k t", p=128)[:, :, b * 512:(b + 1) * 512]),
                    f"d{b % 2}", 16, label=f"s5_in_{b}")
                if b >= 1:
                    S.wait(sy, f"s5_comb_{b - 1}")
                    S.inc(sy.dma_start(
                        out=s1_d.rearrange("(k p) t -> p k t", p=128)[:, :, (b - 1) * 512:b * 512],
                        in_=s1b.rearrange("p (k c) -> p k c", c=512)),
                        "o1", 16, label=f"s5_st_{b - 1}")
            S.wait(sy, f"s5_comb_{NBLK - 1}")
            S.inc(sy.dma_start(
                out=s1_d.rearrange("(k p) t -> p k t", p=128)[:, :, (NBLK - 1) * 512:NBLK * 512],
                in_=s1b.rearrange("p (k c) -> p k c", c=512)),
                "o1", 16, label=f"s5_st_{NBLK - 1}")
            sy.wait_ge(S.sem("o1"), 16 * NBLK)

        @block.tensor
        def _(t):
            S.wait_cnt(t, "w", 16 * 17)
            for b in range(NBLK):
                S.wait(t, f"s5_in_{b}")
                for do in range(8):
                    if do >= 4:
                        S.wait(t, f"s5_ev_{b}_{do - 4}")
                    elif b > 0:
                        S.wait(t, f"s5_ev_{b - 1}_{4 + do}")
                    ps = ps_r[do % 4]
                    for fc in range(16):
                        S.inc(t.matmul(ps[:], wT[fc][:, do * 128:(do + 1) * 128],
                                       yb[b % 2][:, fc * 512:(fc + 1) * 512],
                                       start=(fc == 0), stop=(fc == 15)),
                              "t", label=(f"s5_mm_{b}_{do}" if fc == 15 else None))
                S.label(f"s5_mmdone_{b}", "t")

        @block.vector
        def _(v):
            for b in range(NBLK):
                if b > 0:
                    S.wait(v, f"s5_st_{b - 1}")
                for do in range(8):
                    ps = ps_r[do % 4]
                    S.wait(v, f"s5_mm_{b}_{do}")
                    S.chain(v.scalar_tensor_tensor(
                        s1b[:, do * 512:(do + 1) * 512], ps[:],
                        bos[:, do:do + 1],
                        xnb[b % 2][:, do * 512:(do + 1) * 512],
                        op0=ALU.add, op1=ALU.add), "v",
                        label=f"s5_ev_{b}_{do}")
                S.label(f"s5_comb_{b}", "v")
    return nc


def build_s5():
    S = Sched()
    nc = None
    for p2 in (False, True):
        S.begin(p2)
        with ExitStack() as outer:
            env = _standalone_env("s5", outer)
            emit_s5(S, env)
            nc = env.nc
    return nc


def emit_s6(S: Sched, env):
    nc = env.nc
    wgT = env.io["wgT"]
    wuT = env.io["wuT"]
    xn1_d = env.io["xn1T"]
    hg_d = env.io["hgT"]

    with ExitStack() as ctx:
        sb = lambda n, s_, d_: ctx.enter_context(nc.sbuf_tensor(env.pfx + n, s_, d_))
        psx = lambda n, s_, d_: ctx.enter_context(nc.psum_tensor(env.pfx + n, s_, d_))
        sem = lambda n: env.outer.enter_context(nc.semaphore(env.pfx + n))
        wg = [sb(f"wg{i}", [128, 3072], BF16) for i in range(8)]
        wu = [sb(f"wu{i}", [128, 3072], BF16) for i in range(8)]
        xf = [sb("xf0", [128, 8 * 512], F32), sb("xf1", [128, 8 * 512], F32)]
        xb = sb("xb", [128, 8 * 512], BF16)
        hg = sb("hg", [128, 24 * 512], BF16)
        sg = [sb("sg0", [128, 512], BF16), sb("sg1", [128, 512], BF16)]
        ps_g = [psx(f"ps_g{i}", [128, 512], F32) for i in range(3)]
        ps_u = [psx(f"ps_u{i}", [128, 512], F32) for i in range(3)]
        for k in ["d0", "d1", "w", "t", "v", "a", "p", "o1"]:
            S.reg(k, sem("sem_" + k))
        block = ctx.enter_context(nc.Block())

        @block.gpsimd
        def _(g):
            for dc in range(8):
                S.inc(g.dma_start(out=wg[dc][:], in_=wgT[dc * 128:(dc + 1) * 128, :]),
                      "w", 16)
                S.inc(g.dma_start(out=wu[dc][:], in_=wuT[dc * 128:(dc + 1) * 128, :]),
                      "w", 16)

        @block.sync
        def _(sy):
            for b in range(NBLK):
                if b >= 2:
                    S.wait(sy, f"s6_xcast_{b - 2}")
                S.inc(sy.dma_start(
                    out=xf[b % 2].rearrange("p (k c) -> p k c", c=512),
                    in_=xn1_d.rearrange("(k p) t -> p k t", p=128)[:, :, b * 512:(b + 1) * 512]),
                    f"d{b % 2}", 16, label=f"s6_in_{b}")
                if b >= 1:
                    S.wait(sy, f"s6_hgdone_{b - 1}")
                    S.inc(sy.dma_start(
                        out=hg_d.rearrange("(f p) t -> p f t", p=128)[:, :, (b - 1) * 512:b * 512],
                        in_=hg.rearrange("p (f c) -> p f c", c=512)),
                        "o1", 16, label=f"s6_st_{b - 1}")
            S.wait(sy, f"s6_hgdone_{NBLK - 1}")
            S.inc(sy.dma_start(
                out=hg_d.rearrange("(f p) t -> p f t", p=128)[:, :, (NBLK - 1) * 512:NBLK * 512],
                in_=hg.rearrange("p (f c) -> p f c", c=512)),
                "o1", 16, label=f"s6_st_{NBLK - 1}")
            sy.wait_ge(S.sem("o1"), 16 * NBLK)

        @block.vector
        def _(v):
            for b in range(NBLK):
                S.wait(v, f"s6_in_{b}")
                if b > 0:
                    S.wait(v, f"s6_mmdone_{b - 1}")
                S.chain(v.tensor_scalar_mul(xb[:], xf[b % 2][:], 1.0 / 127.0),
                        "v", label=f"s6_xcast_{b}")
                if b > 0:
                    S.wait(v, f"s6_st_{b - 1}")
                for ft in range(24):
                    S.wait(v, f"s6_sg_{b}_{ft}")
                    S.wait(v, f"s6_umm_{b}_{ft}")
                    S.chain(v.tensor_mul(hg[:, ft * 512:(ft + 1) * 512],
                                         sg[ft % 2][:], ps_u[ft % 3][:]), "v",
                            label=f"s6_hge_{b}_{ft}")
                S.label(f"s6_hgdone_{b}", "v")

        @block.tensor
        def _(t):
            S.wait_cnt(t, "w", 16 * 16)
            for b in range(NBLK):
                S.wait(t, f"s6_xcast_{b}")
                for ft in range(24):
                    if ft >= 3:
                        S.wait(t, f"s6_sg_{b}_{ft - 3}")
                        S.wait(t, f"s6_hge_{b}_{ft - 3}")
                    elif b > 0:
                        S.wait(t, f"s6_sg_{b - 1}_{21 + ft}")
                        S.wait(t, f"s6_hge_{b - 1}_{21 + ft}")
                    for dc in range(8):
                        S.inc(t.matmul(ps_g[ft % 3][:],
                                       wg[dc][:, ft * 128:(ft + 1) * 128],
                                       xb[:, dc * 512:(dc + 1) * 512],
                                       start=(dc == 0), stop=(dc == 7)),
                              "t", label=(f"s6_gmm_{b}_{ft}" if dc == 7 else None))
                    for dc in range(8):
                        S.inc(t.matmul(ps_u[ft % 3][:],
                                       wu[dc][:, ft * 128:(ft + 1) * 128],
                                       xb[:, dc * 512:(dc + 1) * 512],
                                       start=(dc == 0), stop=(dc == 7)),
                              "t", label=(f"s6_umm_{b}_{ft}" if dc == 7 else None))
                S.label(f"s6_mmdone_{b}", "t")

        @block.scalar
        def _(s):
            for b in range(NBLK):
                for ft in range(24):
                    S.wait(s, f"s6_gmm_{b}_{ft}")
                    if ft > 1:
                        S.wait(s, f"s6_hge_{b}_{ft - 2}")
                    elif b > 0:
                        S.wait(s, f"s6_hge_{b - 1}_{22 + ft}")
                    S.chain(s.activation(sg[ft % 2][:], ps_g[ft % 3][:], AF.Silu),
                            "a", label=f"s6_sg_{b}_{ft}")
    return nc


def build_s6():
    S = Sched()
    nc = None
    for p2 in (False, True):
        S.begin(p2)
        with ExitStack() as outer:
            env = _standalone_env("s6", outer)
            emit_s6(S, env)
            nc = env.nc
    return nc


def emit_s7(S: Sched, env):
    nc = env.nc
    wdT = env.io["wdT"]
    hg_d = env.io["hgT"]
    x1_d = env.io["x1T"]
    s1_d = env.io["s1T"]
    out_d = env.io["out"]

    with ExitStack() as ctx:
        sb = lambda n, s_, d_: ctx.enter_context(nc.sbuf_tensor(env.pfx + n, s_, d_))
        psx = lambda n, s_, d_: ctx.enter_context(nc.psum_tensor(env.pfx + n, s_, d_))
        sem = lambda n: env.outer.enter_context(nc.semaphore(env.pfx + n))
        wT = [sb(f"wT{i}", [128, 1024], BF16) for i in range(24)]
        hgb = [sb("hgb0", [128, 24 * 512], BF16), sb("hgb1", [128, 24 * 512], BF16)]
        x1b = [sb("x1b0", [128, 8 * 512], F32), sb("x1b1", [128, 8 * 512], F32)]
        s1b = [sb("s1b0", [128, 8 * 512], F32), sb("s1b1", [128, 8 * 512], F32)]
        gb = sb("gb", [128, 8 * 512], F32)
        otok = sb("otok", [128, 4 * 1024], F32)
        idf = sb("idf", [128, 128], F32)
        ps_r = [psx(f"ps_r{i}", [128, 512], F32) for i in range(4)]
        ps_t = [psx("ps_t0", [128, 512], F32), psx("ps_t1", [128, 512], F32)]
        for k in ["d0", "d1", "d20", "d21", "w", "t", "v", "a", "p", "o1"]:
            S.reg(k, sem("sem_" + k))
        block = ctx.enter_context(nc.Block())

        @block.gpsimd
        def _(g):
            for fc in range(24):
                S.inc(g.dma_start(out=wT[fc][:], in_=wdT[fc * 128:(fc + 1) * 128, :]),
                      "w", 16)
            S.chain(g.memset(idf[:], 1.0), "p")
            S.chain(g.affine_select(idf[:], idf[:], pattern=[[1, 128]], base=0,
                                    channel_multiplier=-1,
                                    compare_op=ALU.is_equal, fill=0.0), "p",
                    label="s7_const")
            for b in range(NBLK):
                if b >= 2:
                    S.wait(g, f"s7_comb_{b - 2}")
                S.inc(g.dma_start(
                    out=x1b[b % 2].rearrange("p (k c) -> p k c", c=512),
                    in_=x1_d.rearrange("(k p) t -> p k t", p=128)[:, :, b * 512:(b + 1) * 512]),
                    f"d2{b % 2}", 16)
                S.inc(g.dma_start(
                    out=s1b[b % 2].rearrange("p (k c) -> p k c", c=512),
                    in_=s1_d.rearrange("(k p) t -> p k t", p=128)[:, :, b * 512:(b + 1) * 512]),
                    f"d2{b % 2}", 16, label=f"s7_xs_{b}")

        @block.sync
        def _(sy):
            for b in range(NBLK):
                if b >= 2:
                    S.wait(sy, f"s7_mmdone_{b - 2}")
                S.inc(sy.dma_start(
                    out=hgb[b % 2].rearrange("p (f c) -> p f c", c=512),
                    in_=hg_d.rearrange("(f p) t -> p f t", p=128)[:, :, b * 512:(b + 1) * 512]),
                    f"d{b % 2}", 16, label=f"s7_in_{b}")
                if b >= 1:
                    S.wait(sy, f"s7_otok_{b - 1}")
                    for tt in range(4):
                        S.inc(sy.dma_start(
                            out=out_d[((b - 1) * 4 + tt) * 128:
                                      ((b - 1) * 4 + tt + 1) * 128, :],
                            in_=otok[:, tt * 1024:(tt + 1) * 1024]),
                            "o1", 16, label=(f"s7_st_{b - 1}" if tt == 3 else None))
            S.wait(sy, f"s7_otok_{NBLK - 1}")
            for tt in range(4):
                S.inc(sy.dma_start(
                    out=out_d[((NBLK - 1) * 4 + tt) * 128:
                              ((NBLK - 1) * 4 + tt + 1) * 128, :],
                    in_=otok[:, tt * 1024:(tt + 1) * 1024]),
                    "o1", 16, label=(f"s7_st_{NBLK - 1}" if tt == 3 else None))
            sy.wait_ge(S.sem("o1"), 16 * 4 * NBLK)

        @block.tensor
        def _(t):
            S.wait_cnt(t, "w", 16 * 24)
            S.wait(t, "s7_const")
            for b in range(NBLK):
                S.wait(t, f"s7_in_{b}")
                for do in range(8):
                    if do >= 4:
                        S.wait(t, f"s7_g_{b}_{do - 4}")
                    elif b > 0:
                        S.wait(t, f"s7_g_{b - 1}_{4 + do}")
                    ps = ps_r[do % 4]
                    for fc in range(24):
                        S.inc(t.matmul(ps[:], wT[fc][:, do * 128:(do + 1) * 128],
                                       hgb[b % 2][:, fc * 512:(fc + 1) * 512],
                                       start=(fc == 0), stop=(fc == 23)),
                              "t", label=(f"s7_mm_{b}_{do}" if fc == 23 else None))
                S.label(f"s7_mmdone_{b}", "t")
                S.wait(t, f"s7_comb_{b}")
                for tt in range(4):
                    for half in range(2):
                        if tt > 0 or b > 0:
                            ptt, phalf = (tt, half) if tt > 0 else (4, half)
                            S.wait(t, f"s7_te_{b if tt > 0 else b - 1}_"
                                      f"{tt - 1 if tt > 0 else 3}_{half}")
                        for j in range(4):
                            jj = half * 4 + j
                            S.inc(t.transpose(
                                ps_t[half][:, j * 128:(j + 1) * 128],
                                gb[:, jj * 512 + tt * 128:jj * 512 + (tt + 1) * 128],
                                idf[:]), "t",
                                label=(f"s7_tr_{b}_{tt}_{half}" if j == 3
                                       else None))

        @block.scalar
        def _(s):
            for b in range(NBLK):
                for do in range(8):
                    ps = ps_r[do % 4]
                    S.wait(s, f"s7_mm_{b}_{do}")
                    if b > 0 and do == 0:
                        S.wait(s, f"s7_tr_{b - 1}_3_1")
                    S.chain(s.activation(gb[:, do * 512:(do + 1) * 512], ps[:],
                                         AF.Silu, scale=1.0 / 127.0), "a",
                            label=f"s7_g_{b}_{do}")

        @block.vector
        def _(v):
            for b in range(NBLK):
                S.wait(v, f"s7_xs_{b}")
                for do in range(8):
                    S.wait(v, f"s7_g_{b}_{do}")
                    sl = slice(do * 512, (do + 1) * 512)
                    S.chain(v.tensor_mul(gb[:, sl], s1b[b % 2][:, sl],
                                         gb[:, sl]), "v")
                    S.chain(v.tensor_add(gb[:, sl], gb[:, sl],
                                         x1b[b % 2][:, sl]), "v",
                            label=(f"s7_comb_{b}" if do == 7 else None))
                if b > 0:
                    S.wait(v, f"s7_st_{b - 1}")
                for tt in range(4):
                    for half in range(2):
                        S.wait(v, f"s7_tr_{b}_{tt}_{half}")
                        dst = otok.rearrange("p (k c) -> p k c", c=1024)
                        S.chain(v.tensor_copy(
                            dst[:, tt:tt + 1, half * 512:(half + 1) * 512].rearrange(
                                "p a (j c) -> p (a j) c", c=128),
                            ps_t[half].rearrange("p (j c) -> p j c", c=128)),
                            "v", label=f"s7_te_{b}_{tt}_{half}")
                S.label(f"s7_otok_{b}", "v")
    return nc


def build_s7():
    S = Sched()
    nc = None
    for p2 in (False, True):
        S.begin(p2)
        with ExitStack() as outer:
            env = _standalone_env("s7", outer)
            emit_s7(S, env)
            nc = env.nc
    return nc


def build_s1():
    S = Sched()
    nc = None
    for p2 in (False, True):
        S.begin(p2)
        with ExitStack() as outer:
            env = _standalone_env("s1", outer)
            emit_s1(S, env)
            nc = env.nc
    return nc


def _shard_layouts(inputs):
    i = inputs
    c = {}
    c["s1"] = {
        "w_in": i["L0_attn_w_in"],
        "w_a": i["L0_attn_w_a"],
        "b_inT": np.ascontiguousarray(i["L0_attn_b_in"].T),
        "b_aT": np.ascontiguousarray(i["L0_attn_b_a"].T),
        "a_paramT": np.ascontiguousarray(i["L0_attn_a_param"].reshape(8, 128).T),
    }
    c["s2"] = {
        "W1T": np.ascontiguousarray(i["L0_ffn_W1"].T),
        "W2T": np.ascontiguousarray(i["L0_ffn_W2"].T),
        "b1c": np.ascontiguousarray(i["L0_ffn_b1"].reshape(32, 128).T),
    }
    c["s3"] = {
        "b2c": np.ascontiguousarray(i["L0_ffn_b2"].reshape(8, 128).T),
    }
    b_in = i["L1_attn_b_in"]
    c["s4"] = {
        "WinT": i["L1_attn_W_in"],
        "bz_c": np.ascontiguousarray(b_in[0:2048].reshape(16, 128).T),
        "bxs_r": np.ascontiguousarray(b_in[2048:4096].reshape(1, 2048)),
        "bbc_c": np.ascontiguousarray(b_in[4096:4224].reshape(128, 1)),
        "bbc_r": np.ascontiguousarray(b_in[4096:4224].reshape(1, 128)),
        "bdt_c": np.ascontiguousarray(b_in[4224:4232].reshape(8, 1)),
        "dtb_c": np.ascontiguousarray(i["L1_attn_dt_bias"].reshape(8, 1)),
        "Alog_c": np.ascontiguousarray(i["L1_attn_A_log"].reshape(8, 1)),
        "Dsk_r": np.ascontiguousarray(i["L1_attn_D_skip"].reshape(1, 8)),
        "gam_c": np.ascontiguousarray(i["L1_attn_ln_gamma"].reshape(8, 128).T),
        "bet_c": np.ascontiguousarray(i["L1_attn_ln_beta"].reshape(8, 128).T),
    }
    c["s5"] = {
        "Wo": i["L1_attn_W_out"],
        "bo_c": np.ascontiguousarray(i["L1_attn_b_out"].reshape(8, 128).T),
    }
    c["s6"] = {
        "wgT": np.ascontiguousarray(i["L1_ffn_w_gate"].T),
        "wuT": np.ascontiguousarray(i["L1_ffn_w_up"].T),
    }
    c["s7"] = {
        "wdT": np.ascontiguousarray(i["L1_ffn_w_down"].T),
    }
    return c


def build_fused():
    emitters = {"s1": emit_s1, "s2": emit_s2, "s3": emit_s3, "s4": emit_s4,
                "s5": emit_s5, "s6": emit_s6, "s7": emit_s7}
    S = Sched()
    nc = None
    for p2 in (False, True):
        S.begin(p2)
        with ExitStack() as outer:
            nc = bass.Bass("TRN2", target_bir_lowering=False)
            outs_of = set()
            for st in _STAGES:
                for name, _sh, _dt in IO_SPECS[st]["out"]:
                    outs_of.add(name)
            kinds = {}
            for st in _STAGES:
                for name, sh, dt_ in IO_SPECS[st]["in"]:
                    kinds.setdefault(name, ("Internal" if name in outs_of
                                            else "ExternalInput", sh, dt_))
            for st in _STAGES:
                for name, sh, dt_ in IO_SPECS[st]["out"]:
                    kinds.setdefault(name, ("ExternalOutput" if name == "out"
                                            else "Internal", sh, dt_))
            io = {}
            for name, (kind, sh, dt_) in kinds.items():
                io[name] = _ios(nc, name, sh, dt_, kind)
            bar = outer.enter_context(nc.semaphore("stage_bar"))
            for k, st in enumerate(_STAGES):
                if k > 0:
                    _barrier_block(nc, bar, 5 * k, True)
                S.pfx = st + "_"
                env = Env(nc, io, outer, pfx=st + "_")
                emitters[st](S, env)
            S.pfx = ""
    return nc


def kernel(**inputs):
    NC = 8
    x = inputs["L0_attn_x"]
    c = _shard_layouts(inputs)
    common = {}
    for d in c.values():
        common.update(d)
    nc = build_fused()
    in_maps = [{**common, "x": np.ascontiguousarray(x[b])} for b in range(NC)]
    res = run_bass_kernel_spmd(nc, in_maps, core_ids=list(range(NC)))
    return np.stack([res.results[b]["out"] for b in range(NC)], axis=0)


def kernel_unfused(**inputs):
    NC = 8
    cores = list(range(NC))
    x = inputs["L0_attn_x"]
    c = _shard_layouts(inputs)

    r1 = run_bass_kernel_spmd(build_s1(), [
        {**c["s1"], "x": np.ascontiguousarray(x[b])} for b in range(NC)],
        core_ids=cores)
    r2 = run_bass_kernel_spmd(build_s2(), [
        {**c["s2"], "xnbf": r1.results[b]["xnbf"]} for b in range(NC)],
        core_ids=cores)
    r3 = run_bass_kernel_spmd(build_s3(), [
        {**c["s3"], "h1": r2.results[b]["h1"], "W2t": r2.results[b]["W2t"],
         "xT": r1.results[b]["xT"], "s0T": r1.results[b]["s0T"]}
        for b in range(NC)], core_ids=cores)
    r4 = run_bass_kernel_spmd(build_s4(), [
        {**c["s4"], "x1T": r3.results[b]["x1T"]} for b in range(NC)],
        core_ids=cores)
    r5 = run_bass_kernel_spmd(build_s5(), [
        {**c["s5"], "yfinT": r4.results[b]["yfinT"],
         "xn1T": r4.results[b]["xn1T"]} for b in range(NC)],
        core_ids=cores)
    r6 = run_bass_kernel_spmd(build_s6(), [
        {**c["s6"], "xn1T": r4.results[b]["xn1T"]} for b in range(NC)],
        core_ids=cores)
    r7 = run_bass_kernel_spmd(build_s7(), [
        {**c["s7"], "hgT": r6.results[b]["hgT"], "x1T": r3.results[b]["x1T"],
         "s1T": r5.results[b]["s1T"]} for b in range(NC)],
        core_ids=cores)
    return np.stack([r7.results[b]["out"] for b in range(NC)], axis=0)


# version 23
# speedup vs baseline: 1.0522x; 1.0030x over previous
import numpy as np
import ml_dtypes
from contextlib import ExitStack

import concourse.bass as bass
import concourse.mybir as mybir
from concourse.bass_utils import run_bass_kernel_spmd

F32 = mybir.dt.float32
BF16 = mybir.dt.bfloat16
I32 = mybir.dt.int32
AF = mybir.ActivationFunctionType
ALU = mybir.AluOpType
AX = mybir.AxisListType

T = 4096
D = 1024
NT = 8
BLK = 512
NBLK = T // BLK
LN7 = float(np.log(np.float32(1e-7)))
L1M = float(np.log1p(np.float64(-1e-5)))


class Sched:

    def __init__(self):
        self.rec = {}
        self.pass2 = False

    def begin(self, pass2):
        self.pass2 = pass2
        self.cnt = {}
        self.sems = {}
        self.pfx = ""

    def reg(self, key, handle):
        self.sems[self.pfx + key] = handle

    def inc(self, instr, key, n=1, label=None):
        key = self.pfx + key
        self.cnt[key] = self.cnt.get(key, 0) + n
        if instr is not None:
            instr.then_inc(self.sems[key], n)
        if label is not None:
            if self.pass2:
                assert self.rec.get(label) == (key, self.cnt[key]), \
                    f"schedule drift at {label}"
            else:
                self.rec[label] = (key, self.cnt[key])
        return self.cnt[key]

    def chain(self, instr, key, label=None):
        key = self.pfx + key
        prev = self.cnt.get(key, 0)
        if self.pass2 and prev > 0:
            instr._wait_ge(self.sems[key], prev)
        self.cnt[key] = prev + 1
        instr.then_inc(self.sems[key], 1)
        if label is not None:
            if self.pass2:
                assert self.rec.get(label) == (key, self.cnt[key]), \
                    f"schedule drift at {label}"
            else:
                self.rec[label] = (key, self.cnt[key])
        return self.cnt[key]

    def label(self, label, key):
        key = self.pfx + key
        if self.pass2:
            assert self.rec.get(label) == (key, self.cnt.get(key, 0)), \
                f"schedule drift at {label}"
        else:
            self.rec[label] = (key, self.cnt.get(key, 0))

    def wait(self, eng, label):
        if not self.pass2:
            return
        key, val = self.rec[label]
        if val > 0:
            eng.wait_ge(self.sems[key], val)

    def wait_cnt(self, eng, key, val):
        if self.pass2 and val > 0:
            eng.wait_ge(self.sems[self.pfx + key], val)

    def sem(self, key):
        return self.sems[self.pfx + key]


def _ios(nc, name, shape, dtype, kind):
    return nc.dram_tensor(name, shape, dtype, kind=kind).ap()


IO_SPECS = {
    "s1": {"in": [("x", [T, D], F32), ("w_in", [8, 128, 128], F32),
                  ("w_a", [8, 128, 128], F32), ("b_inT", [128, 8], F32),
                  ("b_aT", [128, 8], F32), ("a_paramT", [128, 8], F32)],
           "out": [("xT", [D, T], F32), ("xnbf", [D, T], BF16),
                   ("s0T", [D, T], F32)]},
    "s2": {"in": [("W1T", [1024, 4096], F32), ("W2T", [4096, 1024], F32),
                  ("b1c", [128, 32], F32), ("xnbf", [D, T], BF16)],
           "out": [("h1", [4096, T], BF16), ("W2t", [4096, 1024], BF16)]},
    "s3": {"in": [("W2t", [4096, 1024], BF16), ("b2c", [128, 8], F32),
                  ("h1", [4096, T], BF16), ("xT", [D, T], F32),
                  ("s0T", [D, T], F32)],
           "out": [("x1T", [D, T], F32)]},
    "s4": {"in": [("x1T", [D, T], F32), ("WinT", [1024, 4232], F32),
                  ("bz_c", [128, 16], F32), ("bxs_r", [1, 2048], F32),
                  ("bbc_c", [128, 1], F32), ("bbc_r", [1, 128], F32),
                  ("bdt_c", [8, 1], F32), ("dtb_c", [8, 1], F32),
                  ("Alog_c", [8, 1], F32), ("Dsk_r", [1, 8], F32),
                  ("gam_c", [128, 8], F32), ("bet_c", [128, 8], F32)],
           "out": [("xn1T", [D, T], F32), ("yfinT", [2048, T], BF16)]},
    "s5": {"in": [("Wo", [2048, 1024], F32), ("bo_c", [128, 8], F32),
                  ("yfinT", [2048, T], BF16), ("xn1T", [D, T], F32)],
           "out": [("s1T", [D, T], F32)]},
    "s6": {"in": [("wgT", [1024, 3072], I32), ("wuT", [1024, 3072], I32),
                  ("xn1T", [D, T], F32)],
           "out": [("hgT", [3072, T], BF16)]},
    "s7": {"in": [("wdT", [3072, 1024], I32), ("hgT", [3072, T], BF16),
                  ("x1T", [D, T], F32), ("s1T", [D, T], F32)],
           "out": [("out", [T, D], F32)]},
}


class Env:
    def __init__(self, nc, io, outer, pfx=""):
        self.nc = nc
        self.io = io
        self.outer = outer
        self.pfx = pfx


def _standalone_env(stage, outer):
    nc = bass.Bass("TRN2", target_bir_lowering=False)
    io = {}
    for name, shape, dt_ in IO_SPECS[stage]["in"]:
        io[name] = _ios(nc, name, shape, dt_, "ExternalInput")
    for name, shape, dt_ in IO_SPECS[stage]["out"]:
        io[name] = _ios(nc, name, shape, dt_, "ExternalOutput")
    return Env(nc, io, outer)


_STAGES = ["s1", "s2", "s3", "s4", "s5", "s6", "s7"]
_ENGS = ["gpsimd", "sync", "tensor", "vector", "scalar"]


def _barrier_block(nc, bar, wait_val, do_inc):
    with nc.Block() as bb:
        def mk():
            def body(eng):
                if do_inc:
                    eng.nop().then_inc(bar, 1)
                if wait_val > 0:
                    eng.wait_ge(bar, wait_val)
            return body
        for engname in _ENGS:
            getattr(bb, engname)(mk())


def emit_s1(S: Sched, env):
    nc = env.nc
    x = env.io["x"]
    w_in = env.io["w_in"]
    w_a = env.io["w_a"]
    b_inT = env.io["b_inT"]
    b_aT = env.io["b_aT"]
    a_paramT = env.io["a_paramT"]
    xT_d = env.io["xT"]
    xnbf_d = env.io["xnbf"]
    s0_d = env.io["s0T"]

    with ExitStack() as ctx:
        sb = lambda n, s, d: ctx.enter_context(nc.sbuf_tensor(env.pfx + n, s, d))
        psx = lambda n, s, d: ctx.enter_context(nc.psum_tensor(env.pfx + n, s, d))
        sem = lambda n: env.outer.enter_context(nc.semaphore(env.pfx + n))

        xtok = [sb("xtok0", [128, 4096], F32), sb("xtok1", [128, 4096], F32)]
        cN = sb("cN", [128, 4096], F32)
        dX = sb("dX", [128, 4096], F32)
        eG = sb("eG", [128, 4096], F32)
        fA = sb("fA", [128, 4096], F32)
        gM = sb("gM", [128, 4096], F32)
        laX = sb("laX", [128, 8 * 513], F32)
        iL = sb("iL", [128, 4096], F32)
        jS = sb("jS", [128, 4096], F32)
        xnT = sb("xnT", [128, 4096], F32)
        xnbfT = sb("xnbfT", [128, 4096], BF16)
        sqt = sb("sqt", [128, 1024], F32)
        mu_c = sb("mu_c", [128, 4], F32)
        sq_c = sb("sq_c", [128, 4], F32)
        rs_c = sb("rs_c", [128, 4], F32)
        Lcar = sb("Lcar", [128, 8], F32)
        Scar = sb("Scar", [128, 8], F32)
        wbf = sb("wbf", [128, 2048], BF16)
        bcols = sb("bcols", [128, 16], F32)
        negc = sb("negc", [128, 8], F32)
        apc = sb("apc", [128, 8], F32)
        tmp8 = sb("tmp8", [128, 8], F32)
        idf = sb("idf", [128, 128], F32)
        zeros = sb("zeros", [128, 512], F32)

        ps_tr = [psx("ps_tr0", [128, 512], F32), psx("ps_tr1", [128, 512], F32)]
        ps_g = [psx("ps_g0", [128, 512], F32), psx("ps_g1", [128, 512], F32)]

        for k in ["d", "w", "t", "v", "a", "p", "o1", "o2", "o3"]:
            S.reg(k, sem("sem_" + k))
        block = ctx.enter_context(nc.Block())

        @block.gpsimd
        def _(g):
            wbf3 = wbf.rearrange("p (g h j) -> p g h j", g=2, j=128)
            S.inc(g.dma_start(out=wbf3[:, 0], in_=w_in.rearrange("h i j -> i h j")),
                  "w", 16)
            S.inc(g.dma_start(out=wbf3[:, 1], in_=w_a.rearrange("h i j -> i h j")),
                  "w", 16)
            S.inc(g.dma_start(out=bcols[:, 0:8], in_=b_inT), "w", 16)
            S.inc(g.dma_start(out=bcols[:, 8:16], in_=b_aT), "w", 16)
            S.inc(g.dma_start(out=apc[:], in_=a_paramT), "w", 16)
            g.wait_ge(S.sem("w"), 80)
            S.chain(g.memset(idf[:], 1.0), "p")
            S.chain(g.affine_select(idf[:], idf[:], pattern=[[1, 128]], base=0,
                                    channel_multiplier=-1,
                                    compare_op=ALU.is_equal, fill=0.0), "p")
            S.chain(g.memset(zeros[:], 0.0), "p")
            S.chain(g.memset(Lcar[:], 0.0), "p")
            S.chain(g.memset(Scar[:], 0.0), "p")
            S.chain(g.memset(laX[:], 0.0), "p", label="s1_const")

        @block.sync
        def _(sy):
            def _stores(b):
                S.wait(sy, f"s1_xT_done_{b}")
                for k in range(NT):
                    S.inc(sy.dma_start(
                        out=xT_d[k * 128:(k + 1) * 128, b * 512:(b + 1) * 512],
                        in_=dX[:, k * 512:(k + 1) * 512]), "o1", 16,
                        label=(f"s1_xTst_{b}" if k == NT - 1 else None))
                S.wait(sy, f"s1_xnbf_done_{b}")
                for k in range(NT):
                    S.inc(sy.dma_start(
                        out=xnbf_d[k * 128:(k + 1) * 128, b * 512:(b + 1) * 512],
                        in_=xnbfT[:, k * 512:(k + 1) * 512]), "o2", 16,
                        label=(f"s1_xnst_{b}" if k == NT - 1 else None))
                S.wait(sy, f"s1_h_done_{b}")
                for k in range(NT):
                    S.inc(sy.dma_start(
                        out=s0_d[k * 128:(k + 1) * 128, b * 512:(b + 1) * 512],
                        in_=iL[:, k * 512:(k + 1) * 512]), "o3", 16,
                        label=(f"s1_out_{b}" if k == NT - 1 else None))
            for b in range(NBLK):
                if b >= 2:
                    S.wait(sy, f"s1_trx_{b - 2}_3_1")
                    S.wait(sy, f"s1_sq_{b - 2}")
                    S.wait(sy, f"s1_ln_{b - 2}_3")
                if b >= 1:
                    S.wait(sy, f"s1_xld_{b - 1}")
                for tt in range(4):
                    S.inc(sy.dma_start(
                        out=xtok[b % 2][:, tt * 1024:(tt + 1) * 1024],
                        in_=x[(b * 4 + tt) * 128:(b * 4 + tt + 1) * 128, :]),
                        "d", 16, label=(f"s1_xld_{b}" if tt == 3 else None))
                if b >= 1:
                    _stores(b - 1)
            _stores(NBLK - 1)
            sy.wait_ge(S.sem("o1"), 16 * 8 * NBLK)
            sy.wait_ge(S.sem("o2"), 16 * 8 * NBLK)
            sy.wait_ge(S.sem("o3"), 16 * 8 * NBLK)

        @block.tensor
        def _(t):
            S.wait(t, "s1_const")
            S.wait_cnt(t, "w", 80)
            for b in range(NBLK):
                for tt in range(4):
                    S.wait_cnt(t, "d", 64 * (b + 1))
                    for half in range(2):
                        if tt > 0:
                            S.wait(t, f"s1_trxe_{b}_{tt - 1}_{half}")
                        elif b > 0:
                            S.wait(t, f"s1_trne_{b - 1}_3_{half}")
                        for j in range(4):
                            jj = half * 4 + j
                            ti = t.transpose(ps_tr[half][:, j * 128:(j + 1) * 128],
                                             xtok[b % 2][:, tt * 1024 + jj * 128:
                                                         tt * 1024 + (jj + 1) * 128],
                                             idf[:])
                            S.inc(ti, "t",
                                  label=(f"s1_trx_{b}_{tt}_{half}" if j == 3
                                         else None))
                for tt in range(4):
                    S.wait(t, f"s1_ln_{b}_{tt}")
                    for half in range(2):
                        if tt > 0:
                            S.wait(t, f"s1_trne_{b}_{tt - 1}_{half}")
                        else:
                            S.wait(t, f"s1_trxe_{b}_3_{half}")
                        for j in range(4):
                            jj = half * 4 + j
                            ti = t.transpose(ps_tr[half][:, j * 128:(j + 1) * 128],
                                             cN[:, tt * 1024 + jj * 128:
                                                tt * 1024 + (jj + 1) * 128],
                                             idf[:])
                            S.inc(ti, "t",
                                  label=(f"s1_trn_{b}_{tt}_{half}" if j == 3
                                         else None))
                S.wait(t, f"s1_xnbf_done_{b}")
                for h in range(8):
                    for gi in range(2):
                        if h > 0:
                            S.wait(t, f"s1_sgm_{b}_{h - 1}_{gi}")
                        elif b > 0:
                            S.wait(t, f"s1_sgm_{b - 1}_7_{gi}")
                        S.inc(t.matmul(ps_g[gi][:],
                                       wbf[:, gi * 1024 + h * 128:
                                           gi * 1024 + (h + 1) * 128],
                                       xnbfT[:, h * 512:(h + 1) * 512],
                                       start=True, stop=True),
                              "t", label=f"s1_gate_{b}_{h}_{gi}")

        @block.scalar
        def _(s):
            S.wait_cnt(s, "w", 80)
            S.wait(s, "s1_const")
            S.chain(s.activation(tmp8[:], apc[:], AF.Abs), "a")
            S.chain(s.activation(tmp8[:], tmp8[:], AF.Exp, scale=-1.0), "a")
            S.chain(s.activation(tmp8[:], tmp8[:], AF.Ln, bias=1.0), "a",
                    label="s1_sp1")
            for b in range(NBLK):
                for tt in range(4):
                    S.wait_cnt(s, "d", 64 * (b + 1))
                    S.chain(s.activation(sqt[:],
                                         xtok[b % 2][:, tt * 1024:(tt + 1) * 1024],
                                         AF.Square, accum_out=sq_c[:, tt:tt + 1]),
                            "a", label=(f"s1_sq_{b}" if tt == 3 else None))
                S.wait(s, f"s1_var_{b}")
                S.chain(s.activation(rs_c[:], rs_c[:], AF.Ln), "a")
                S.chain(s.activation(rs_c[:], rs_c[:], AF.Exp, scale=-0.5), "a",
                        label=f"s1_rstd_{b}")
                if b > 0:
                    S.wait(s, f"s1_h_done_{b - 1}")
                for h in range(8):
                    for gi in range(2):
                        S.wait(s, f"s1_gate_{b}_{h}_{gi}")
                        dst = eG if gi == 0 else fA
                        S.chain(s.activation(dst[:, h * 512:(h + 1) * 512],
                                             ps_g[gi][:], AF.Sigmoid,
                                             bias=bcols[:, 8 * gi + h:
                                                        8 * gi + h + 1]),
                                "a", label=f"s1_sgm_{b}_{h}_{gi}")
                S.wait(s, f"s1_la_{b}")
                for k in range(NT):
                    S.chain(s.activation(fA[:, k * 512:(k + 1) * 512],
                                         laX[:, k * 513 + 1:k * 513 + 513],
                                         AF.Exp, scale=2.0), "a",
                            label=(f"s1_a2_{b}" if k == NT - 1 else None))
                S.wait(s, f"s1_m2_{b}")
                S.chain(s.activation(gM[:], gM[:], AF.Ln), "a")
                S.chain(s.activation(gM[:], gM[:], AF.Exp, scale=0.5), "a",
                        label=f"s1_mult_{b}")
                S.wait(s, f"s1_lc_{b}")
                S.wait(s, f"s1_xgm_{b}")
                S.chain(s.activation(gM[:], iL[:], AF.Exp, scale=-1.0), "a",
                        label=f"s1_pinv_{b}")
                S.wait(s, f"s1_w_{b}")
                S.chain(s.activation(eG[:], iL[:], AF.Exp), "a",
                        label=f"s1_p_{b}")

        @block.vector
        def _(v):
            S.wait(v, "s1_sp1")
            S.chain(v.tensor_scalar_max(negc[:], apc[:], 0.0), "v")
            S.chain(v.tensor_add(negc[:], negc[:], tmp8[:]), "v")
            S.chain(v.tensor_scalar(negc[:], negc[:], 1e-3, -1.0,
                                    op0=ALU.add, op1=ALU.mult), "v",
                    label="s1_negc")
            for b in range(NBLK):
                for tt in range(4):
                    S.wait_cnt(v, "d", 64 * (b + 1))
                    S.chain(v.tensor_reduce(mu_c[:, tt:tt + 1],
                                            xtok[b % 2][:, tt * 1024:(tt + 1) * 1024],
                                            axis=AX.X, op=ALU.add), "v")
                S.wait(v, f"s1_sq_{b}")
                S.chain(v.tensor_scalar_mul(mu_c[:], mu_c[:], 1.0 / D), "v")
                S.chain(v.tensor_mul(rs_c[:], mu_c[:], mu_c[:]), "v")
                S.chain(v.scalar_tensor_tensor(rs_c[:], sq_c[:], 1.0 / D, rs_c[:],
                                               op0=ALU.mult, op1=ALU.subtract), "v")
                S.chain(v.tensor_scalar_add(rs_c[:], rs_c[:], 1e-5), "v",
                        label=f"s1_var_{b}")
                S.wait(v, f"s1_rstd_{b}")
                for tt in range(4):
                    S.chain(v.tensor_scalar(cN[:, tt * 1024:(tt + 1) * 1024],
                                            xtok[b % 2][:, tt * 1024:(tt + 1) * 1024],
                                            mu_c[:, tt:tt + 1], rs_c[:, tt:tt + 1],
                                            op0=ALU.subtract, op1=ALU.mult),
                            "v", label=f"s1_ln_{b}_{tt}")
                if b > 0:
                    S.wait(v, f"s1_xTst_{b - 1}")
                for tt in range(4):
                    for half in range(2):
                        S.wait(v, f"s1_trx_{b}_{tt}_{half}")
                        dst = dX.rearrange("p (k c) -> p k c", c=512)
                        S.chain(v.tensor_copy(
                            dst[:, half * 4:half * 4 + 4, tt * 128:(tt + 1) * 128],
                            ps_tr[half].rearrange("p (k c) -> p k c", c=128)),
                            "v", label=f"s1_trxe_{b}_{tt}_{half}")
                S.label(f"s1_xT_done_{b}", "v")
                if b > 0:
                    S.wait(v, f"s1_xnst_{b - 1}")
                    S.wait(v, f"s1_gate_{b - 1}_7_1")
                for tt in range(4):
                    for half in range(2):
                        S.wait(v, f"s1_trn_{b}_{tt}_{half}")
                        dstn = xnT.rearrange("p (k c) -> p k c", c=512)
                        S.chain(v.tensor_copy(
                            dstn[:, half * 4:half * 4 + 4, tt * 128:(tt + 1) * 128],
                            ps_tr[half].rearrange("p (k c) -> p k c", c=128)),
                            "v", label=f"s1_trne_{b}_{tt}_{half}")
                S.chain(v.tensor_copy(xnbfT[:], xnT[:]), "v",
                        label=f"s1_xnbf_done_{b}")
                if b == 0:
                    S.wait(v, "s1_negc")
                else:
                    S.wait(v, f"s1_a2_{b - 1}")
                for k in range(NT):
                    S.wait(v, f"s1_sgm_{b}_{k}_1")
                    S.chain(v.tensor_scalar(laX[:, k * 513 + 1:k * 513 + 513],
                                            fA[:, k * 512:(k + 1) * 512],
                                            negc[:, k:k + 1], L1M,
                                            op0=ALU.mult, op1=ALU.min),
                            "v", label=(f"s1_la_{b}" if k == NT - 1 else None))
                if b > 0:
                    S.wait(v, f"s1_out_{b - 1}")
                for k in range(NT):
                    S.chain(v.tensor_tensor_scan(iL[:, k * 512:(k + 1) * 512],
                                                 laX[:, k * 513:k * 513 + 512],
                                                 zeros[:, 0:512],
                                                 Lcar[:, k:k + 1],
                                                 op0=ALU.add, op1=ALU.add), "v")
                S.chain(v.tensor_copy(
                    Lcar.rearrange("p (k o) -> p k o", o=1),
                    iL.rearrange("p (k c) -> p k c", c=512)[:, :, 511:512]), "v")
                S.chain(v.tensor_copy(
                    laX.rearrange("p (k c) -> p k c", c=513)[:, :, 0:1],
                    laX.rearrange("p (k c) -> p k c", c=513)[:, :, 512:513]), "v")
                S.chain(v.tensor_scalar_max(iL[:], iL[:], LN7), "v",
                        label=f"s1_lc_{b}")
                S.wait(v, f"s1_a2_{b}")
                S.chain(v.tensor_scalar(gM[:], fA[:], -1.0, 1.0,
                                        op0=ALU.mult, op1=ALU.add), "v")
                S.chain(v.tensor_scalar_max(gM[:], gM[:], 1e-5), "v",
                        label=f"s1_m2_{b}")
                S.wait(v, f"s1_sgm_{b}_7_0")
                S.chain(v.tensor_mul(cN[:], xnT[:], eG[:]), "v",
                        label=f"s1_xg_{b}")
                S.wait(v, f"s1_mult_{b}")
                S.chain(v.tensor_mul(eG[:], cN[:], gM[:]), "v",
                        label=f"s1_xgm_{b}")
                S.wait(v, f"s1_pinv_{b}")
                S.chain(v.scalar_tensor_tensor(cN[:], eG[:], 1e-7, gM[:],
                                               op0=ALU.add, op1=ALU.mult),
                        "v", label=f"s1_w_{b}")
                for k in range(NT):
                    S.chain(v.tensor_tensor_scan(jS[:, k * 512:(k + 1) * 512],
                                                 cN[:, k * 512:(k + 1) * 512],
                                                 zeros[:, 0:512],
                                                 Scar[:, k:k + 1],
                                                 op0=ALU.add, op1=ALU.add), "v")
                S.chain(v.tensor_copy(
                    Scar.rearrange("p (k o) -> p k o", o=1),
                    jS.rearrange("p (k c) -> p k c", c=512)[:, :, 511:512]), "v",
                    label=f"s1_scum_{b}")
                S.wait(v, f"s1_p_{b}")
                S.chain(v.tensor_mul(iL[:], eG[:], jS[:]), "v",
                        label=f"s1_h_done_{b}")
    return nc




def emit_s2(S: Sched, env):
    nc = env.nc
    W1T = env.io["W1T"]
    W2T = env.io["W2T"]
    b1c = env.io["b1c"]
    xnbf_d = env.io["xnbf"]
    h1_d = env.io["h1"]
    W2t_d = env.io["W2t"]

    with ExitStack() as ctx:
        sb = lambda n, s_, d_: ctx.enter_context(nc.sbuf_tensor(env.pfx + n, s_, d_))
        psx = lambda n, s_, d_: ctx.enter_context(nc.psum_tensor(env.pfx + n, s_, d_))
        sem = lambda n: env.outer.enter_context(nc.semaphore(env.pfx + n))

        wf = [sb(f"wf{i}", [128, 1024], F32) for i in range(4)]
        wi = sb("wi", [128, 1024], F32)
        sc2 = sb("sc2", [1, 2], F32)
        invc2 = sb("invc2", [128, 2], F32)
        w2s = [sb(f"w2s{i}", [128, 1024], BF16) for i in range(2)]
        wT = [sb(f"wT{i}", [128, 4096], BF16) for i in range(8)]
        xnb = [sb("xnb0", [128, 8 * 512], BF16), sb("xnb1", [128, 8 * 512], BF16)]
        h1s = sb("h1s", [128, 32 * 512], BF16)
        b1s = sb("b1s", [128, 32], F32)
        acc = sb("acc", [128, 1], F32)
        red = sb("red", [128, 1], F32)
        onec = sb("onec", [128, 1], F32)
        onerow = sb("onerow", [1, 128], F32)
        sc1 = sb("sc1", [1, 1], F32)
        invc = sb("invc", [128, 1], F32)
        ps_r = [psx("ps_r0", [128, 512], F32), psx("ps_r1", [128, 512], F32),
                psx("ps_r2", [128, 512], F32)]
        ps_s = psx("ps_s", [1, 1], F32)
        ps_c = psx("ps_c", [128, 2], F32)

        for k in ["d0", "d1", "w", "w0", "w1", "w2", "w3", "t", "v", "a", "p",
                  "o1", "o20", "o21", "wA0", "wA1"]:
            S.reg(k, sem("sem_" + k))
        block = ctx.enter_context(nc.Block())

        def _w1src(k):
            fg, dc = k // 8, k % 8
            return W1T[dc * 128:(dc + 1) * 128,
                       fg * 1024:(fg + 1) * 1024]

        @block.gpsimd
        def _(g):
            S.inc(g.dma_start(out=b1s[:], in_=b1c), "w", 16)
            for k in range(0, 32, 2):
                if k >= 4:
                    S.wait(g, f"s2_absred_{k - 4}")
                S.inc(g.dma_start(out=wf[k % 4][:], in_=_w1src(k)),
                      f"w{k % 4}", 16, label=f"s2_wfA_{k}")
            for k in range(32):
                S.wait(g, f"s2_w1t_{k - 4}" if k >= 4
                       else f"s2_absred_{28 + k}")
                S.inc(g.dma_start(out=wf[k % 4][:], in_=_w1src(k)),
                      f"w{k % 4}", 16, label=f"s2_wfB_{k}")
            for dc in range(32):
                if dc >= 4:
                    S.wait(g, f"s2_w2red_{dc - 4}")
                else:
                    S.wait(g, f"s2_w1t_{28 + dc}")
                S.inc(g.dma_start(out=wf[dc % 4][:],
                                  in_=W2T[dc * 128:(dc + 1) * 128, :]),
                      f"w{dc % 4}", 16, label=f"s2_w2A_{dc}")
            for dc in range(32):
                S.wait(g, f"s2_w2t_{dc - 4}" if dc >= 4 else "s2_scale2")
                S.inc(g.dma_start(out=wf[dc % 4][:],
                                  in_=W2T[dc * 128:(dc + 1) * 128, :]),
                      f"w{dc % 4}", 16, label=f"s2_w2B_{dc}")
                if dc >= 2:
                    S.wait(g, f"s2_w2t_{dc - 2}")
                    S.inc(g.dma_start(
                        out=W2t_d[(dc - 2) * 128:(dc - 1) * 128, :],
                        in_=w2s[dc % 2][:]), f"o2{dc % 2}", 16,
                        label=f"s2_w2st_{dc - 2}")
            for dc in (30, 31):
                S.wait(g, f"s2_w2t_{dc}")
                S.inc(g.dma_start(out=W2t_d[dc * 128:(dc + 1) * 128, :],
                                  in_=w2s[dc % 2][:]), f"o2{dc % 2}", 16,
                      label=f"s2_w2st_{dc}")

        @block.sync
        def _(sy):
            def _w1odd():
                for k in range(1, 32, 2):
                    if k >= 4:
                        S.wait(sy, f"s2_absred_{k - 4}")
                    S.inc(sy.dma_start(out=wf[k % 4][:], in_=_w1src(k)),
                          f"wA{(k % 4) // 2}", 16, label=f"s2_wfA_{k}")
            for b in range(NBLK):
                if b == 1:
                    _w1odd()
                if b >= 2:
                    S.wait(sy, f"s2_mmdone_{b - 2}")
                S.inc(sy.dma_start(
                    out=xnb[b % 2].rearrange("p (k c) -> p k c", c=512),
                    in_=xnbf_d.rearrange("(k p) t -> p k t", p=128)[:, :, b * 512:(b + 1) * 512]),
                    f"d{b % 2}", 16, label=f"s2_xnb_{b}")
                if b >= 1:
                    S.wait(sy, f"s2_h1done_{b - 1}")
                    S.inc(sy.dma_start(
                        out=h1_d.rearrange("(f p) t -> p f t", p=128)[:, :, (b - 1) * 512:b * 512],
                        in_=h1s.rearrange("p (f c) -> p f c", c=512)),
                        "o1", 16, label=f"s2_h1st_{b - 1}")
            S.wait(sy, f"s2_h1done_{NBLK - 1}")
            S.inc(sy.dma_start(
                out=h1_d.rearrange("(f p) t -> p f t", p=128)[:, :, (NBLK - 1) * 512:NBLK * 512],
                in_=h1s.rearrange("p (f c) -> p f c", c=512)),
                "o1", 16, label=f"s2_h1st_{NBLK - 1}")
            sy.wait_ge(S.sem("o1"), 16 * NBLK)
            sy.wait_ge(S.sem("o20"), 16 * 16)
            sy.wait_ge(S.sem("o21"), 16 * 16)

        @block.vector
        def _(v):
            S.chain(v.memset(acc[:], 0.0), "v")
            S.chain(v.memset(onec[:], 1.0), "v")
            S.chain(v.memset(onerow[:], 1.0), "v", label="s2_ones")
            for k in range(32):
                S.wait(v, f"s2_wfA_{k}")
                S.chain(v.tensor_reduce(red[:], wf[k % 4][:], axis=AX.X,
                                        op=ALU.add, apply_absolute_value=True), "v")
                S.chain(v.tensor_add(acc[:], acc[:], red[:]), "v",
                        label=f"s2_absred_{k}")
            S.wait(v, "s2_total")
            S.chain(v.tensor_copy(sc2[0:1, 0:1], ps_s[:]), "v")
            S.chain(v.tensor_scalar(sc2[0:1, 0:1], sc2[0:1, 0:1],
                                    0.5 / (4096 * 1024), 0.5e-8,
                                    op0=ALU.mult, op1=ALU.max), "v")
            S.chain(v.tensor_scalar_mul(sc2[0:1, 1:2], sc2[0:1, 0:1],
                                        -1.0), "v", label="s2_inv")
            S.wait(v, "s2_invb")
            S.chain(v.tensor_copy(invc2[:], ps_c[:, 0:2]), "v",
                    label="s2_scale")
            for k in range(32):
                fg, dc = k // 8, k % 8
                S.wait(v, f"s2_wfB_{k}")
                S.chain(v.tensor_scalar(wi[:], wf[k % 4][:],
                                        invc2[:, 1:2], None,
                                        op0=ALU.is_lt), "v")
                S.chain(v.tensor_scalar(wf[k % 4][:], wf[k % 4][:],
                                        invc2[:, 0:1], None,
                                        op0=ALU.is_gt), "v")
                S.chain(v.tensor_sub(wT[dc][:, fg * 1024:(fg + 1) * 1024],
                                     wf[k % 4][:], wi[:]), "v",
                        label=f"s2_w1t_{k}")
            S.chain(v.memset(acc[:], 0.0), "v")
            for dc in range(32):
                S.wait(v, f"s2_w2A_{dc}")
                S.chain(v.tensor_reduce(red[:], wf[dc % 4][:], axis=AX.X,
                                        op=ALU.add, apply_absolute_value=True), "v")
                S.chain(v.tensor_add(acc[:], acc[:], red[:]), "v",
                        label=f"s2_w2red_{dc}")
            S.wait(v, "s2_total2")
            S.chain(v.tensor_copy(sc1[:], ps_s[:]), "v")
            S.chain(v.tensor_scalar(sc1[:], sc1[:], 1.0 / (4096 * 1024), 1e-8,
                                    op0=ALU.mult, op1=ALU.max), "v")
            S.chain(v.reciprocal(sc1[:], sc1[:]), "v", label="s2_inv2")
            S.wait(v, "s2_invb2")
            S.chain(v.tensor_copy(invc[:], ps_c[:, 0:1]), "v",
                    label="s2_scale2")
            for dc in range(32):
                S.wait(v, f"s2_w2B_{dc}")
                if dc >= 2:
                    S.wait(v, f"s2_w2st_{dc - 2}")
                S.chain(v.tensor_scalar_mul(wf[dc % 4][:],
                                            wf[dc % 4][:],
                                            invc[:, 0:1]), "v")
                S.chain(v.tensor_scalar(wi[:], wf[dc % 4][:],
                                        -0.5, None, op0=ALU.is_lt), "v")
                S.chain(v.tensor_scalar(wf[dc % 4][:],
                                        wf[dc % 4][:], 0.5, None,
                                        op0=ALU.is_gt), "v")
                S.chain(v.tensor_sub(w2s[dc % 2][:], wf[dc % 4][:],
                                     wi[:]), "v",
                        label=f"s2_w2t_{dc}")

        @block.tensor
        def _(t):
            S.wait(t, "s2_absred_31")
            S.inc(t.matmul(ps_s[:], onec[:], acc[:], start=True, stop=True),
                  "t", label="s2_total")
            S.wait(t, "s2_inv")
            S.wait(t, "s2_ones")
            S.inc(t.matmul(ps_c[:, 0:2], onerow[:], sc2[:], start=True,
                           stop=True), "t", label="s2_invb")
            S.wait_cnt(t, "w", 16)
            for b in range(NBLK):
                if b == 3:
                    S.wait(t, "s2_w2red_31")
                    S.wait(t, "s2_scale")
                    S.inc(t.matmul(ps_s[:], onec[:], acc[:], start=True,
                                   stop=True), "t", label="s2_total2")
                if b == 4:
                    S.wait(t, "s2_inv2")
                    S.inc(t.matmul(ps_c[:, 0:1], onerow[:], sc1[:], start=True,
                                   stop=True), "t", label="s2_invb2")
                S.wait(t, f"s2_xnb_{b}")
                for ft in range(32):
                    if b == 0 and ft % 8 == 0:
                        S.wait(t, f"s2_w1t_{(ft // 8) * 8 + 7}")
                    ps = ps_r[ft % 3]
                    if ft >= 3:
                        S.wait(t, f"s2_gelu_{b}_{ft - 3}")
                    elif b > 0:
                        prevu = max(x for x in range(32) if x % 3 == ft % 3)
                        S.wait(t, f"s2_gelu_{b - 1}_{prevu}")
                    for dc in range(8):
                        S.inc(t.matmul(ps[:], wT[dc][:, ft * 128:(ft + 1) * 128],
                                       xnb[b % 2][:, dc * 512:(dc + 1) * 512],
                                       start=(dc == 0), stop=(dc == 7)),
                              "t", label=(f"s2_mm_{b}_{ft}" if dc == 7 else None))
                S.label(f"s2_mmdone_{b}", "t")

        @block.scalar
        def _(s):
            for b in range(NBLK):
                if b > 0:
                    S.wait(s, f"s2_h1st_{b - 1}")
                for ft in range(32):
                    ps = ps_r[ft % 3]
                    S.wait(s, f"s2_mm_{b}_{ft}")
                    S.chain(s.activation(h1s[:, ft * 512:(ft + 1) * 512], ps[:],
                                         AF.Gelu, bias=b1s[:, ft:ft + 1]),
                            "a", label=f"s2_gelu_{b}_{ft}")
                S.label(f"s2_h1done_{b}", "a")
    return nc


def build_s2():
    S = Sched()
    nc = None
    for p2 in (False, True):
        S.begin(p2)
        with ExitStack() as outer:
            env = _standalone_env("s2", outer)
            emit_s2(S, env)
            nc = env.nc
    return nc


def emit_s3(S: Sched, env):
    nc = env.nc
    W2t_d = env.io["W2t"]
    b2c = env.io["b2c"]
    h1_d = env.io["h1"]
    xT_d = env.io["xT"]
    s0_d = env.io["s0T"]
    x1_d = env.io["x1T"]

    with ExitStack() as ctx:
        sb = lambda n, s_, d_: ctx.enter_context(nc.sbuf_tensor(env.pfx + n, s_, d_))
        psx = lambda n, s_, d_: ctx.enter_context(nc.psum_tensor(env.pfx + n, s_, d_))
        sem = lambda n: env.outer.enter_context(nc.semaphore(env.pfx + n))

        wT = [sb(f"wT{i}", [128, 1024], BF16) for i in range(32)]
        h1b = [sb("h1b0", [128, 32 * 512], BF16), sb("h1b1", [128, 32 * 512], BF16)]
        xb = [sb("xb0", [128, 8 * 512], F32), sb("xb1", [128, 8 * 512], F32)]
        s0b = [sb("s0b0", [128, 8 * 512], F32), sb("s0b1", [128, 8 * 512], F32)]
        gb = sb("gb", [128, 2 * 512], F32)
        b2s = sb("b2s", [128, 8], F32)
        ps_r = [psx("ps_r0", [128, 512], F32), psx("ps_r1", [128, 512], F32),
                psx("ps_r2", [128, 512], F32), psx("ps_r3", [128, 512], F32)]

        for k in ["d0", "d1", "d20", "d21", "w", "w0", "w1", "w2", "w3", "t",
                  "v", "a", "p", "o10", "o11"]:
            S.reg(k, sem("sem_" + k))
        block = ctx.enter_context(nc.Block())

        @block.gpsimd
        def _(g):
            for dc in range(32):
                S.inc(g.dma_start(out=wT[dc][:], in_=W2t_d[dc * 128:(dc + 1) * 128, :]),
                      "w0", 16)
            S.inc(g.dma_start(out=b2s[:], in_=b2c), "w", 16)
            for b in range(NBLK):
                if b >= 2:
                    S.wait(g, f"s3_comb_{b - 2}")
                    S.wait(g, f"s3_x1st_{b - 2}")
                S.inc(g.dma_start(
                    out=xb[b % 2].rearrange("p (k c) -> p k c", c=512),
                    in_=xT_d.rearrange("(k p) t -> p k t", p=128)[:, :, b * 512:(b + 1) * 512]),
                    f"d2{b % 2}", 16)
                S.inc(g.dma_start(
                    out=s0b[b % 2].rearrange("p (k c) -> p k c", c=512),
                    in_=s0_d.rearrange("(k p) t -> p k t", p=128)[:, :, b * 512:(b + 1) * 512]),
                    f"d2{b % 2}", 16, label=f"s3_xs_{b}")

        @block.sync
        def _(sy):
            for b in range(NBLK):
                if b >= 2:
                    S.wait(sy, f"s3_mmdone_{b - 2}")
                S.inc(sy.dma_start(
                    out=h1b[b % 2].rearrange("p (f c) -> p f c", c=512),
                    in_=h1_d.rearrange("(f p) t -> p f t", p=128)[:, :, b * 512:(b + 1) * 512]),
                    f"d{b % 2}", 16, label=f"s3_h1in_{b}")
                if b >= 1:
                    S.wait(sy, f"s3_comb_{b - 1}")
                    S.inc(sy.dma_start(
                        out=x1_d.rearrange("(k p) t -> p k t", p=128)[:, :, (b - 1) * 512:b * 512],
                        in_=xb[(b - 1) % 2].rearrange("p (k c) -> p k c", c=512)),
                        f"o1{(b - 1) % 2}", 16, label=f"s3_x1st_{b - 1}")
            S.wait(sy, f"s3_comb_{NBLK - 1}")
            S.inc(sy.dma_start(
                out=x1_d.rearrange("(k p) t -> p k t", p=128)[:, :, (NBLK - 1) * 512:NBLK * 512],
                in_=xb[(NBLK - 1) % 2].rearrange("p (k c) -> p k c", c=512)),
                f"o1{(NBLK - 1) % 2}", 16, label=f"s3_x1st_{NBLK - 1}")
            sy.wait_ge(S.sem("o10"), 16 * (NBLK // 2))
            sy.wait_ge(S.sem("o11"), 16 * (NBLK // 2))

        @block.vector
        def _(v):
            for b in range(NBLK):
                S.wait(v, f"s3_xs_{b}")
                for do in range(8):
                    S.wait(v, f"s3_silu_{b}_{do}")
                    sl = slice(do * 512, (do + 1) * 512)
                    gsl = slice((do % 2) * 512, (do % 2 + 1) * 512)
                    S.chain(v.tensor_mul(gb[:, gsl], s0b[b % 2][:, sl],
                                         gb[:, gsl]), "v")
                    S.chain(v.tensor_add(xb[b % 2][:, sl], xb[b % 2][:, sl],
                                         gb[:, gsl]), "v",
                            label=(f"s3_comb_{b}" if do == 7 else
                                   f"s3_combp_{b}_{do}"))

        @block.tensor
        def _(t):
            S.wait_cnt(t, "w0", 16 * 32)
            for b in range(NBLK):
                S.wait(t, f"s3_h1in_{b}")
                for do in range(8):
                    ps = ps_r[do % 4]
                    prev = do - 4 if do >= 4 else (4 + do if b > 0 else -1)
                    if prev >= 0:
                        S.wait(t, f"s3_silu_{b if do >= 4 else b - 1}_{prev}")
                    for fc in range(32):
                        S.inc(t.matmul(ps[:], wT[fc][:, do * 128:(do + 1) * 128],
                                       h1b[b % 2][:, fc * 512:(fc + 1) * 512],
                                       start=(fc == 0), stop=(fc == 31)),
                              "t", label=(f"s3_mm_{b}_{do}" if fc == 31 else None))
                S.label(f"s3_mmdone_{b}", "t")

        @block.scalar
        def _(s):
            S.wait_cnt(s, "w", 16)
            for b in range(NBLK):
                if b > 0:
                    S.wait(s, f"s3_comb_{b - 1}")
                for do in range(8):
                    ps = ps_r[do % 4]
                    S.wait(s, f"s3_mm_{b}_{do}")
                    if do >= 2:
                        S.wait(s, f"s3_combp_{b}_{do - 2}")
                    elif b > 0:
                        S.wait(s, f"s3_combp_{b - 1}_{6 + do}" if do == 0
                               else f"s3_comb_{b - 1}")
                    S.chain(s.activation(gb[:, (do % 2) * 512:(do % 2 + 1) * 512],
                                         ps[:], AF.Silu, bias=b2s[:, do:do + 1]),
                            "a", label=f"s3_silu_{b}_{do}")
    return nc


def build_s3():
    S = Sched()
    nc = None
    for p2 in (False, True):
        S.begin(p2)
        with ExitStack() as outer:
            env = _standalone_env("s3", outer)
            emit_s3(S, env)
            nc = env.nc
    return nc


def emit_s4(S: Sched, env):
    nc = env.nc
    x1_d = env.io["x1T"]
    WinT = env.io["WinT"]
    bz_c = env.io["bz_c"]
    bxs_r = env.io["bxs_r"]
    bbc_c = env.io["bbc_c"]
    bbc_r = env.io["bbc_r"]
    bdt_c = env.io["bdt_c"]
    dtb_c = env.io["dtb_c"]
    Alog_c = env.io["Alog_c"]
    Dsk_r = env.io["Dsk_r"]
    gam_c = env.io["gam_c"]
    bet_c = env.io["bet_c"]
    xn1_d = env.io["xn1T"]
    yf_d = env.io["yfinT"]

    with ExitStack() as ctx:
        sb = lambda n, s_, d_: ctx.enter_context(nc.sbuf_tensor(env.pfx + n, s_, d_))
        psx = lambda n, s_, d_: ctx.enter_context(nc.psum_tensor(env.pfx + n, s_, d_))
        sem = lambda n: env.outer.enter_context(nc.semaphore(env.pfx + n))

        winb = [sb(f"winb{i}", [128, 4232], BF16) for i in range(8)]
        _x1b0 = sb("x1b0", [128, 4096], F32)
        x1b = [_x1b0, _x1b0]
        xnT = _x1b0
        ubf = sb("ubf", [128, 4096], BF16)
        zsil = sb("zsil", [128, 16 * 512], BF16)
        onec_bf = sb("onec_bf", [128, 1], BF16)
        stg_x = sb("stg_x", [128, 1024], BF16)
        stg_q = sb("stg_q", [128, 1024], BF16)
        xst = sb("xst", [128, 4 * 2048], BF16)
        dtxt = sb("dtxt", [128, 4 * 2048], BF16)
        yfin = sb("yfin", [128, 16 * 512], BF16)
        Bf = sb("Bf", [64, 512], BF16)
        Cf = sb("Cf", [64, 512], BF16)
        Cx8 = sb("Cx8", [64, 8 * 128], BF16)
        Btok = sb("Btok", [128, 4 * 64], BF16)
        Bw8 = sb("Bw8", [128, 8 * 64], BF16)
        Msb8 = sb("Msb8", [128, 8 * 128], BF16)
        Mt = sb("Mt", [128, 128], F32)
        BCm = sb("BCm", [128, 128], BF16)
        sct8 = sb("sct8", [128, 8 * 128], BF16)
        dtr = sb("dtr", [8, 512], F32)
        t8a = sb("t8a", [8, 512], F32)
        eLe = t8a
        Lrow = sb("Lrow", [8, 512], F32)
        eLp = sb("eLp", [8, 512], F32)
        smallTT = sb("smallTT", [128, 4 * 24], F32)
        erow = sb("erow", [1, 4 * 8], F32)
        dsk_s = sb("dsk_s", [1, 8], F32)
        dcol = sb("dcol", [64, 4 * 8], F32)
        stf = sb("stf", [64, 2048], F32)
        stb = sb("stb", [64, 2048], BF16)
        IDh = sb("IDh", [128, 8 * 128], BF16)
        idf = sb("idf", [128, 128], F32)
        idb = sb("idb", [128, 128], BF16)
        trib = sb("trib", [128, 128], BF16)
        oneh8 = sb("oneh8", [8, 8 * 128], F32)
        onec = sb("onec", [128, 1], F32)
        onerow = sb("onerow", [1, 128], F32)
        onerow_bf = sb("onerow_bf", [1, 128], BF16)
        bxs_bf = sb("bxs_bf", [1, 2048], BF16)
        bbc_rb = sb("bbc_rb", [1, 128], BF16)
        murow = sb("murow", [1, 512], F32)
        vrow = sb("vrow", [1, 512], F32)
        mrow2 = sb("mrow2", [1, 512], F32)
        bsum = sb("bsum", [8, 1], F32)
        dtbs = sb("dtbs", [8, 1], F32)
        negA = sb("negA", [8, 1], F32)
        Dcols = sb("Dcols", [128, 8], F32)
        bzs = sb("bzs", [128, 16], F32)
        bbcs = sb("bbcs", [128, 1], F32)
        gams = sb("gams", [128, 8], F32)
        bets = sb("bets", [128, 8], F32)

        ps_io = [psx("ps_io0", [128, 512], F32), psx("ps_io1", [128, 512], F32)]
        ps_stat = psx("ps_stat", [128, 512], F32)
        ps_bc2 = psx("ps_bc2", [128, 512], F32)
        ps_sc = psx("ps_sc", [128, 512], F32)
        ps_y = psx("ps_y", [128, 512], F32)
        ps_stu = psx("ps_stu", [64, 512], F32)
        ps_mis = psx("ps_mis", [128, 512], F32)

        for k in ["dx", "w", "wc", "t", "v", "a", "p", "o1", "o2"]:
            S.reg(k, sem("sem_" + k))
        block = ctx.enter_context(nc.Block())
        NCH = 4

        @block.gpsimd
        def _(g):
            for dc in range(8):
                S.inc(g.dma_start(out=winb[dc][:],
                                  in_=WinT[dc * 128:(dc + 1) * 128, :]), "w", 16)
            S.inc(g.dma_start(out=bzs[:], in_=bz_c), "wc", 16)
            S.inc(g.dma_start(out=bxs_bf[:], in_=bxs_r), "wc", 16)
            S.inc(g.dma_start(out=bbcs[:], in_=bbc_c), "wc", 16)
            S.inc(g.dma_start(out=bbc_rb[:], in_=bbc_r), "wc", 16)
            S.inc(g.dma_start(out=bsum[:], in_=bdt_c), "wc", 16)
            S.inc(g.dma_start(out=dtbs[:], in_=dtb_c), "wc", 16)
            S.inc(g.dma_start(out=negA[:], in_=Alog_c), "wc", 16)
            S.inc(g.dma_start(out=dsk_s[:], in_=Dsk_r), "wc", 16)
            S.inc(g.dma_start(out=gams[:], in_=gam_c), "wc", 16)
            S.inc(g.dma_start(out=bets[:], in_=bet_c), "wc", 16)
            g.wait_ge(S.sem("wc"), 16 * 10)
            S.chain(g.memset(idf[:], 1.0), "p")
            S.chain(g.affine_select(idf[:], idf[:], pattern=[[1, 128]], base=0,
                                    channel_multiplier=-1,
                                    compare_op=ALU.is_equal, fill=0.0), "p")
            S.chain(g.tensor_copy(idb[:], idf[:]), "p")
            S.chain(g.memset(trib[:], 1.0), "p")
            S.chain(g.affine_select(trib[:], trib[:], pattern=[[1, 128]], base=0,
                                    channel_multiplier=-1,
                                    compare_op=ALU.is_ge, fill=0.0), "p")
            S.chain(g.memset(oneh8[:], 1.0), "p")
            S.chain(g.affine_select(oneh8[:], oneh8[:], pattern=[[1, 8], [0, 128]],
                                    base=0, channel_multiplier=-1,
                                    compare_op=ALU.is_equal, fill=0.0), "p")
            S.chain(g.memset(onec[:], 1.0), "p")
            S.chain(g.memset(onec_bf[:], 1.0), "p")
            S.chain(g.memset(onerow[:], 1.0), "p")
            S.chain(g.memset(onerow_bf[:], 1.0), "p")
            S.chain(g.memset(stf[:], 0.0), "p")
            S.chain(g.memset(stb[:], 0.0), "p", label="s4_const")

        @block.sync
        def _(sy):
            for b in range(NBLK):
                if b >= 1:
                    S.wait(sy, f"s4_xnst_{b - 1}")
                    S.wait(sy, f"s4_uc_{b - 1}_{NT - 1}")
                    S.wait(sy, f"s4_sq2_{b - 1}_{NT - 1}")
                S.inc(sy.dma_start(
                    out=x1b[b % 2].rearrange("p (k c) -> p k c", c=512),
                    in_=x1_d.rearrange("(k p) t -> p k t", p=128)[:, :, b * 512:(b + 1) * 512]),
                    "dx", 16, label=f"s4_x1_{b}")
                if b >= 1:
                    S.wait(sy, f"s4_yf_done_{b - 1}")
                    S.inc(sy.dma_start(
                        out=yf_d.rearrange("(f p) t -> p f t", p=128)[:, :, (b - 1) * 512:b * 512],
                        in_=yfin.rearrange("p (f c) -> p f c", c=512)),
                        "o2", 16, label=f"s4_yfst_{b - 1}")
                S.wait(sy, f"s4_xn_done_{b}")
                S.inc(sy.dma_start(
                    out=xn1_d.rearrange("(k p) t -> p k t", p=128)[:, :, b * 512:(b + 1) * 512],
                    in_=xnT.rearrange("p (k c) -> p k c", c=512)),
                    "o1", 16, label=f"s4_xnst_{b}")
            S.wait(sy, f"s4_yf_done_{NBLK - 1}")
            S.inc(sy.dma_start(
                out=yf_d.rearrange("(f p) t -> p f t", p=128)[:, :, (NBLK - 1) * 512:NBLK * 512],
                in_=yfin.rearrange("p (f c) -> p f c", c=512)),
                "o2", 16, label=f"s4_yfst_{NBLK - 1}")
            sy.wait_ge(S.sem("o1"), 16 * NBLK)
            sy.wait_ge(S.sem("o2"), 16 * NBLK)

        @block.tensor
        def _(t):
            S.wait(t, "s4_const")
            S.wait_cnt(t, "w", 16 * 8)
            S.inc(t.matmul(ps_mis[:, 0:8], onerow[:], dsk_s[:],
                           start=True, stop=True), "t", label="s4_dskb")
            for b in range(NBLK):
                S.wait(t, f"s4_x1_{b}")
                if b > 0:
                    S.wait(t, f"s4_uc_{b - 1}_{NT - 1}")
                    S.wait(t, f"s4_m_{b - 1}_{NCH - 1}_7")
                for dc in range(NT):
                    S.wait(t, f"s4_cp1_{b}_{dc}")
                    S.inc(t.matmul(ps_stat[0:1, :], onec_bf[:],
                                   stg_x[:, (dc % 2) * 512:(dc % 2 + 1) * 512],
                                   start=(dc == 0), stop=(dc == NT - 1)),
                          "t", label=f"s4_summm1_{b}_{dc}")
                for dc in range(NT):
                    S.wait(t, f"s4_sq1_{b}_{dc}")
                    S.inc(t.matmul(ps_stat[32:33, :], onec_bf[:],
                                   stg_q[:, (dc % 2) * 512:(dc % 2 + 1) * 512],
                                   start=(dc == 0), stop=(dc == NT - 1)),
                          "t", label=f"s4_sqmm1_{b}_{dc}")
                S.wait(t, f"s4_rows1v_{b}")
                S.inc(t.matmul(ps_stat[:, :], onerow[:], murow[:],
                               start=True, stop=True), "t", label=f"s4_mub_{b}")
                S.wait(t, f"s4_rows1a_{b}")
                S.inc(t.matmul(ps_bc2[:, :], onerow[:], mrow2[:],
                               start=True, stop=True), "t", label=f"s4_rb_{b}")
                S.wait(t, f"s4_xn_done_{b}")
                for dc in range(NT):
                    S.wait(t, f"s4_cp2_{b}_{dc}")
                    S.inc(t.matmul(ps_stat[0:1, :], onec_bf[:],
                                   stg_x[:, (dc % 2) * 512:(dc % 2 + 1) * 512],
                                   start=(dc == 0), stop=(dc == NT - 1)),
                          "t", label=f"s4_summm2_{b}_{dc}")
                for dc in range(NT):
                    S.wait(t, f"s4_sq2_{b}_{dc}")
                    S.inc(t.matmul(ps_stat[32:33, :], onec_bf[:],
                                   stg_q[:, (dc % 2) * 512:(dc % 2 + 1) * 512],
                                   start=(dc == 0), stop=(dc == NT - 1)),
                          "t", label=f"s4_sqmm2_{b}_{dc}")
                S.label(f"s4_sum2_{b}", "t")
                S.wait(t, f"s4_rows2v_{b}")
                S.inc(t.matmul(ps_stat[:, :], onerow[:], murow[:],
                               start=True, stop=True), "t", label=f"s4_mub2_{b}")
                S.wait(t, f"s4_rows2a_{b}")
                S.inc(t.matmul(ps_bc2[:, :], onerow[:], mrow2[:],
                               start=True, stop=True), "t", label=f"s4_rb2_{b}")
                if b > 0:
                    S.wait(t, f"s4_stup_{b - 1}_{NCH - 1}_7")
                    S.wait(t, f"s4_ye_{b - 1}_{NCH - 1}_7")
                    S.wait(t, f"s4_dte_{b - 1}")
                    S.wait(t, f"s4_bctok_{b - 1}_3")
                    S.wait(t, f"s4_sct8_{b - 1}_3")
                for dc in range(NT):
                    S.wait(t, f"s4_ubf_{b}_{dc}")
                    S.inc(t.matmul(ps_io[1][0:8, 0:512],
                                   winb[dc][:, 4224:4232],
                                   ubf[:, dc * 512:(dc + 1) * 512],
                                   start=(dc == 0), stop=(dc == NT - 1)),
                          "t", label=(f"s4_dtmm_{b}" if dc == NT - 1 else None))
                for dc in range(NT):
                    S.inc(t.matmul(ps_sc[:, 0:512],
                                   winb[dc][:, 4096:4224],
                                   ubf[:, dc * 512:(dc + 1) * 512],
                                   start=(dc == 0), stop=(dc == NT - 1)),
                          "t", label=(f"s4_bcmm_{b}" if dc == NT - 1 else None))
                for tt in range(4):
                    if tt >= 2:
                        S.wait(t, f"s4_bctok_{b}_{tt - 2}")
                    elif tt == 1:
                        S.wait(t, f"s4_dte_{b}")
                    for dc in range(NT):
                        S.inc(t.matmul(
                            ps_io[tt % 2][:, 0:128],
                            ubf[:, dc * 512 + tt * 128:dc * 512 + (tt + 1) * 128],
                            winb[dc][:, 4096:4224],
                            start=(dc == 0), stop=False), "t")
                    S.inc(t.matmul(ps_io[tt % 2][:, 0:128], onerow_bf[:],
                                   bbc_rb[:], start=False, stop=True), "t",
                          label=f"s4_bctokmm_{b}_{tt}")
                for ft in range(16):
                    if ft >= 2:
                        S.wait(t, f"s4_zsil_{b}_{ft - 2}")
                    elif ft == 0:
                        S.wait(t, f"s4_bctok_{b}_2")
                    else:
                        S.wait(t, f"s4_bctok_{b}_3")
                    for dc in range(NT):
                        S.inc(t.matmul(ps_io[ft % 2][:],
                                       winb[dc][:, ft * 128:(ft + 1) * 128],
                                       ubf[:, dc * 512:(dc + 1) * 512],
                                       start=(dc == 0), stop=(dc == NT - 1)),
                              "t", label=(f"s4_zmm_{b}_{ft}" if dc == NT - 1
                                          else None))
                for tt in range(4):
                    for fb in range(4):
                        par = (tt * 4 + fb) % 2
                        pidx = tt * 4 + fb
                        if pidx >= 2:
                            pt, pf = divmod(pidx - 2, 4)
                            S.wait(t, f"s4_xse_{b}_{pt}_{pf}")
                        else:
                            S.wait(t, f"s4_zsil_{b}_{14 + pidx}")
                        for dc in range(NT):
                            S.inc(t.matmul(
                                ps_io[par][:, 0:512],
                                ubf[:, dc * 512 + tt * 128:dc * 512 + (tt + 1) * 128],
                                winb[dc][:, 2048 + fb * 512:2048 + (fb + 1) * 512],
                                start=(dc == 0), stop=False), "t")
                        S.inc(t.matmul(ps_io[par][:, 0:512], onerow_bf[:],
                                       bxs_bf[0:1, fb * 512:(fb + 1) * 512],
                                       start=False, stop=True), "t",
                              label=(f"s4_xsmm_{b}_{tt}_{fb}"))
                S.label(f"s4_inproj_{b}", "t")
                S.wait(t, f"s4_eL_{b}")
                for ch in range(NCH):
                    if ch > 0:
                        S.wait(t, f"s4_smallevac_{b}_{ch - 1}")
                    elif b > 0:
                        S.wait(t, f"s4_smallevac_{b - 1}_{NCH - 1}")
                        S.wait(t, f"s4_cxg_{b - 1}_{NCH - 1}_1")
                    else:
                        S.wait(t, "s4_idh")
                    S.inc(t.transpose(ps_mis[:, 0:8],
                                      dtr[:, ch * 128:(ch + 1) * 128],
                                      idf[0:8, 0:8]), "t")
                    S.inc(t.transpose(ps_mis[:, 8:16],
                                      Lrow[:, ch * 128:(ch + 1) * 128],
                                      idf[0:8, 0:8]), "t")
                    S.inc(t.transpose(ps_mis[:, 16:24],
                                      eLe[:, ch * 128:(ch + 1) * 128],
                                      idf[0:8, 0:8]), "t")
                    S.inc(t.transpose(ps_mis[0:1, 256:264],
                                      eLp[:, ch * 128 + 127:ch * 128 + 128],
                                      idf[0:8, 0:8]),
                          "t", label=f"s4_str_{b}_{ch}")
                for ch in range(NCH):
                    S.wait(t, f"s4_smallevac_{b}_{ch}")
                    if ch > 0:
                        S.wait(t, f"s4_dcole_{b}_{ch - 1}")
                    elif b > 0:
                        S.wait(t, f"s4_stup_{b - 1}_{NCH - 1}_7")
                    S.inc(t.matmul(ps_stu[:, 256:264], onerow[0:1, 0:64],
                                   erow[0:1, ch * 8:(ch + 1) * 8],
                                   start=True, stop=True), "t",
                          label=f"s4_dcb_{b}_{ch}")
                for ch in range(NCH):
                    S.wait(t, f"s4_bce_{b}")
                    if ch > 0:
                        S.wait(t, f"s4_bcmsk_{b}_{ch - 1}")
                    S.inc(t.matmul(ps_sc[:, 0:128], Bf[:, ch * 128:(ch + 1) * 128],
                                   Cf[:, ch * 128:(ch + 1) * 128],
                                   start=True, stop=True), "t",
                          label=f"s4_scmm_{b}_{ch}")
                    for h in range(8):
                        bank = ps_bc2 if h % 2 == 0 else ps_stat
                        prevm = (f"s4_m_{b}_{ch}_{h - 2}" if h >= 2 else
                                 (f"s4_m_{b}_{ch - 1}_{6 + h}" if ch > 0 else
                                  (f"s4_m_{b - 1}_{NCH - 1}_{6 + h}" if b > 0
                                   else None)))
                        if prevm:
                            S.wait(t, prevm)
                        S.inc(t.matmul(bank[:, 0:128],
                                       oneh8[:, h * 128:(h + 1) * 128],
                                       Lrow[:, ch * 128:(ch + 1) * 128],
                                       start=True, stop=True), "t",
                              label=f"s4_lb_{b}_{ch}_{h}")
                    for grp in range(2):
                        if ch == 0 and grp == 0:
                            S.wait(t, f"s4_smallevac_{b}_{NCH - 1}")
                        prevcx = (f"s4_cxg_{b}_{ch}_{grp - 1}" if grp > 0 else
                                  (f"s4_cxg_{b}_{ch - 1}_1" if ch > 0 else
                                   (f"s4_cxg_{b - 1}_{NCH - 1}_1" if b > 0 else None)))
                        if prevcx:
                            S.wait(t, prevcx)
                        for hh in range(4):
                            h = grp * 4 + hh
                            S.inc(t.matmul(ps_mis[:, hh * 128:(hh + 1) * 128],
                                           oneh8[:, h * 128:(h + 1) * 128],
                                           eLp[:, ch * 128:(ch + 1) * 128],
                                           start=True, stop=True), "t",
                                  label=(f"s4_epbg_{b}_{ch}_{grp}" if hh == 3
                                         else None))
                    for h in range(8):
                        ybank = ps_y if h % 2 == 0 else ps_io[0]
                        prevye = (f"s4_ye_{b}_{ch}_{h - 2}" if h > 1 else
                                  (f"s4_ye_{b}_{ch - 1}_{6 + h}" if ch > 0 else
                                   (f"s4_ye_{b - 1}_{NCH - 1}_{6 + h}" if b > 0
                                    else None)))
                        if prevye:
                            S.wait(t, prevye)
                        if ch > 0:
                            S.wait(t, f"s4_stup_{b}_{ch - 1}_{h}")
                        elif b > 0:
                            S.wait(t, f"s4_stup_{b - 1}_{NCH - 1}_{h}")
                        for half in range(2):
                            off = h * 256 + half * 128
                            S.inc(t.matmul(ybank[:, half * 128:(half + 1) * 128],
                                           xst[:, ch * 2048 + off:ch * 2048 + off + 128],
                                           IDh[:, h * 128:(h + 1) * 128],
                                           start=True, stop=False), "t")
                            if h == 0 and half == 0:
                                S.wait(t, f"s4_sct8_{b}_{ch}")
                            S.inc(t.matmul(ybank[:, half * 128:(half + 1) * 128],
                                           dtxt[:, ch * 2048 + off:ch * 2048 + off + 128],
                                           sct8[:, h * 128:(h + 1) * 128],
                                           start=False, stop=False), "t")
                            if h == 0 and half == 0:
                                S.wait(t, f"s4_cxg_{b}_{ch}_1")
                            S.inc(t.matmul(ybank[:, half * 128:(half + 1) * 128],
                                           stb[0:64, off:off + 128],
                                           Cx8[:, h * 128:(h + 1) * 128],
                                           start=False, stop=True),
                                  "t", label=(f"s4_ymm_{b}_{ch}_{h}"
                                              if half == 1 else None))
                        stbank = ps_stu if h % 2 == 0 else ps_io[1]
                        prevst = (f"s4_stup_{b}_{ch}_{h - 2}" if h > 1 else
                                  (f"s4_stup_{b}_{ch - 1}_{6 + h}" if ch > 0 else
                                   (f"s4_stup_{b - 1}_{NCH - 1}_{6 + h}" if b > 0
                                    else None)))
                        if prevst:
                            S.wait(t, prevst)
                        if h == 0:
                            S.wait(t, f"s4_bw8_{b}_{ch}")
                        S.inc(t.matmul(stbank[0:64, 0:256],
                                       Bw8[:, h * 64:(h + 1) * 64],
                                       dtxt[:, ch * 2048 + h * 256:
                                            ch * 2048 + (h + 1) * 256],
                                       start=True, stop=True), "t",
                              label=f"s4_stmm_{b}_{ch}_{h}")

        @block.scalar
        def _(s):
            S.wait(s, "s4_const")
            S.chain(s.activation(negA[:], negA[:], AF.Exp), "a",
                    label="s4_negA1")
            for b in range(NBLK):
                def _stage1(dc):
                    if dc == 0:
                        S.wait(s, f"s4_x1_{b}")
                    if dc > 1:
                        S.wait(s, f"s4_summm1_{b}_{dc - 2}")
                    S.chain(s.activation(stg_x[:, (dc % 2) * 512:(dc % 2 + 1) * 512],
                                         x1b[b % 2][:, dc * 512:(dc + 1) * 512],
                                         AF.Copy), "a", label=f"s4_cp1_{b}_{dc}")
                def _stage1q(dc):
                    if dc > 1:
                        S.wait(s, f"s4_sqmm1_{b}_{dc - 2}")
                    S.chain(s.activation(stg_q[:, (dc % 2) * 512:(dc % 2 + 1) * 512],
                                         x1b[b % 2][:, dc * 512:(dc + 1) * 512],
                                         AF.Square), "a", label=f"s4_sq1_{b}_{dc}")
                def _mbuild(pb, ch, h):
                    S.wait(s, f"s4_lb_{pb}_{ch}_{h}")
                    S.wait(s, f"s4_smallevac_{pb}_{ch}")
                    if h == 0:
                        prevsct8 = (f"s4_sct8_{pb}_{ch - 1}" if ch > 0 else
                                    (f"s4_sct8_{pb - 1}_{NCH - 1}" if pb > 0
                                     else None))
                        if prevsct8:
                            S.wait(s, prevsct8)
                    bank = ps_bc2 if h % 2 == 0 else ps_stat
                    S.chain(s.activation(
                        Mt[:], bank[:, 0:128], AF.Relu, scale=-1.0,
                        bias=smallTT[:, ch * 24 + 8 + h:ch * 24 + 8 + h + 1]),
                        "a")
                    S.chain(s.activation(Msb8[:, h * 128:(h + 1) * 128],
                                         Mt[:], AF.Exp, scale=-1.0), "a",
                            label=f"s4_m_{pb}_{ch}_{h}")
                if b > 0:
                    free_ops = [("x", 0), ("x", 1), ("q", 0), ("q", 1)]
                    for ch in range(NCH):
                        for h in range(8):
                            _mbuild(b - 1, ch, h)
                        if free_ops:
                            kind, dc = free_ops.pop(0)
                            (_stage1 if kind == "x" else _stage1q)(dc)
                    for dc in range(2, NT):
                        _stage1(dc)
                    for dc in range(2, NT):
                        _stage1q(dc)
                else:
                    for dc in range(NT):
                        _stage1(dc)
                    for dc in range(NT):
                        _stage1q(dc)
                S.wait(s, f"s4_rows1q_{b}")
                S.chain(s.activation(mrow2[:], mrow2[:], AF.Ln), "a")
                S.chain(s.activation(mrow2[:], mrow2[:], AF.Exp, scale=-0.5), "a",
                        label=f"s4_rows1a_{b}")
                for dc in range(NT):
                    S.wait(s, f"s4_xnc_{b}_{dc}")
                    if dc > 1:
                        S.wait(s, f"s4_summm2_{b}_{dc - 2}")
                    else:
                        S.wait(s, f"s4_summm1_{b}_{NT - 1}")
                    S.chain(s.activation(stg_x[:, (dc % 2) * 512:(dc % 2 + 1) * 512],
                                         xnT[:, dc * 512:(dc + 1) * 512],
                                         AF.Copy), "a", label=f"s4_cp2_{b}_{dc}")
                for dc in range(NT):
                    if dc > 1:
                        S.wait(s, f"s4_sqmm2_{b}_{dc - 2}")
                    else:
                        S.wait(s, f"s4_sqmm1_{b}_{NT - 1}")
                    S.chain(s.activation(stg_q[:, (dc % 2) * 512:(dc % 2 + 1) * 512],
                                         xnT[:, dc * 512:(dc + 1) * 512],
                                         AF.Square), "a", label=f"s4_sq2_{b}_{dc}")
                S.wait(s, f"s4_rows2q_{b}")
                S.chain(s.activation(mrow2[:], mrow2[:], AF.Ln), "a")
                S.chain(s.activation(mrow2[:], mrow2[:], AF.Exp, scale=-0.5), "a",
                        label=f"s4_rows2a_{b}")
                for dc in range(NT):
                    S.wait(s, f"s4_uc_{b}_{dc}")
                    S.chain(s.activation(ubf[:, dc * 512:(dc + 1) * 512],
                                         ubf[:, dc * 512:(dc + 1) * 512],
                                         AF.Identity, bias=bets[:, dc:dc + 1],
                                         scale=gams[:, dc:dc + 1]), "a",
                            label=f"s4_ubf_{b}_{dc}")
                S.wait(s, f"s4_dtmm_{b}")
                S.wait(s, "s4_bsum")
                if b > 0:
                    S.wait(s, f"s4_dtla_{b - 1}")
                S.chain(s.activation(dtr[:], ps_io[1][0:8, 0:512], AF.Identity,
                                     bias=bsum[:]), "a", label=f"s4_dte_{b}")
                S.chain(s.activation(t8a[:], dtr[:], AF.Abs), "a")
                S.chain(s.activation(t8a[:], t8a[:], AF.Exp, scale=-1.0), "a")
                S.chain(s.activation(t8a[:], t8a[:], AF.Ln, bias=1.0), "a",
                        label=f"s4_sp_{b}")
                S.wait(s, f"s4_bcmm_{b}")
                if b > 0:
                    S.wait(s, f"s4_scmm_{b - 1}_{NCH - 1}")
                    S.wait(s, f"s4_cxg_{b - 1}_{NCH - 1}_1")
                S.chain(s.activation(Bf[:], ps_sc[0:64, 0:512], AF.Identity,
                                     bias=bbcs[0:64, 0:1]), "a")
                S.chain(s.activation(Cf[:], ps_sc[64:128, 0:512], AF.Identity,
                                     bias=bbcs[64:128, 0:1]), "a",
                        label=f"s4_bce_{b}")
                for tt in range(4):
                    S.wait(s, f"s4_bctokmm_{b}_{tt}")
                    if b > 0 and tt == 0:
                        S.wait(s, f"s4_bw8_{b - 1}_{NCH - 1}")
                    S.chain(s.activation(Btok[:, tt * 64:(tt + 1) * 64],
                                         ps_io[tt % 2][:, 0:64], AF.Copy), "a",
                            label=f"s4_bctok_{b}_{tt}")
                for ft in range(16):
                    S.wait(s, f"s4_zmm_{b}_{ft}")
                    if b > 0 and ft == 0:
                        S.wait(s, f"s4_ye_{b - 1}_{NCH - 1}_7")
                    S.chain(s.activation(zsil[:, ft * 512:(ft + 1) * 512],
                                         ps_io[ft % 2][:], AF.Silu,
                                         bias=bzs[:, ft:ft + 1]), "a",
                            label=f"s4_zsil_{b}_{ft}")
                for tt in range(4):
                    for fb in range(4):
                        S.wait(s, f"s4_xsmm_{b}_{tt}_{fb}")
                        if b > 0 and tt == 0 and fb == 0:
                            S.wait(s, f"s4_stmm_{b - 1}_{NCH - 1}_7")
                            S.wait(s, f"s4_dtx_{b - 1}")
                        S.chain(s.activation(
                            xst[:, tt * 2048 + fb * 512:tt * 2048 + (fb + 1) * 512],
                            ps_io[(tt * 4 + fb) % 2][:], AF.Copy), "a",
                            label=f"s4_xse_{b}_{tt}_{fb}")
                S.wait(s, f"s4_L_{b}")
                for ch in range(NCH):
                    S.chain(s.activation(eLe[:, ch * 128:(ch + 1) * 128],
                                         Lrow[:, ch * 128:(ch + 1) * 128],
                                         AF.Exp, scale=-1.0,
                                         bias=Lrow[:, ch * 128 + 127:ch * 128 + 128]),
                            "a")
                S.chain(s.activation(eLp[:], Lrow[:], AF.Exp), "a",
                        label=f"s4_eL_{b}")
                if b == NBLK - 1:
                    for ch in range(NCH):
                        for h in range(8):
                            _mbuild(b, ch, h)
        @block.vector
        def _(v):
            S.wait_cnt(v, "wc", 160)
            S.chain(v.tensor_add(bsum[:], bsum[:], dtbs[:]), "v",
                    label="s4_bsum")
            S.wait(v, "s4_negA1")
            S.chain(v.tensor_scalar_mul(negA[:], negA[:], -1.0), "v",
                    label="s4_negA")
            S.wait(v, "s4_dskb")
            S.wait(v, "s4_const")
            S.chain(v.tensor_copy(Dcols[:, 0:8], ps_mis[:, 0:8]), "v")
            for h in range(8):
                S.chain(v.tensor_scalar_mul(IDh[:, h * 128:(h + 1) * 128],
                                            idb[:], Dcols[:, h:h + 1]), "v",
                        label=("s4_idh" if h == 7 else None))
            for b in range(NBLK):
                S.wait(v, f"s4_sqmm1_{b}_{NT - 1}")
                S.chain(v.tensor_scalar_mul(murow[:], ps_stat[0:1, :], 1.0 / D), "v")
                S.chain(v.tensor_scalar_mul(vrow[:], ps_stat[32:33, :], 1.0 / D), "v",
                        label=f"s4_rows1v_{b}")
                S.chain(v.tensor_mul(mrow2[:], murow[:], murow[:]), "v")
                S.chain(v.tensor_sub(mrow2[:], vrow[:], mrow2[:]), "v")
                S.chain(v.tensor_scalar_add(mrow2[:], mrow2[:], 1e-5), "v",
                        label=f"s4_rows1q_{b}")
                S.wait(v, f"s4_mub_{b}")
                S.wait(v, f"s4_rb_{b}")
                if b > 0:
                    S.wait(v, f"s4_xnst_{b - 1}")
                for dc in range(NT):
                    S.chain(v.tensor_sub(xnT[:, dc * 512:(dc + 1) * 512],
                                         x1b[b % 2][:, dc * 512:(dc + 1) * 512],
                                         ps_stat[:, :]), "v")
                    S.chain(v.tensor_mul(xnT[:, dc * 512:(dc + 1) * 512],
                                         xnT[:, dc * 512:(dc + 1) * 512],
                                         ps_bc2[:, :]), "v",
                            label=f"s4_xnc_{b}_{dc}")
                S.label(f"s4_xn_done_{b}", "v")
                S.wait(v, f"s4_sum2_{b}")
                S.wait(v, f"s4_sqmm2_{b}_{NT - 1}")
                S.chain(v.tensor_scalar_mul(murow[:], ps_stat[0:1, :], 1.0 / D), "v")
                S.chain(v.tensor_scalar_mul(vrow[:], ps_stat[32:33, :], 1.0 / D), "v",
                        label=f"s4_rows2v_{b}")
                S.chain(v.tensor_mul(mrow2[:], murow[:], murow[:]), "v")
                S.chain(v.tensor_sub(mrow2[:], vrow[:], mrow2[:]), "v")
                S.chain(v.tensor_scalar_add(mrow2[:], mrow2[:], 1e-5), "v",
                        label=f"s4_rows2q_{b}")
                S.wait(v, f"s4_mub2_{b}")
                S.wait(v, f"s4_rb2_{b}")
                S.wait(v, f"s4_sqmm2_{b}_{NT - 1}")
                for dc in range(NT):
                    S.chain(v.tensor_sub(ubf[:, dc * 512:(dc + 1) * 512],
                                         xnT[:, dc * 512:(dc + 1) * 512],
                                         ps_stat[:, :]), "v")
                    S.chain(v.tensor_mul(ubf[:, dc * 512:(dc + 1) * 512],
                                         ubf[:, dc * 512:(dc + 1) * 512],
                                         ps_bc2[:, :]), "v",
                            label=f"s4_uc_{b}_{dc}")
                S.wait(v, f"s4_sp_{b}")
                S.chain(v.tensor_scalar_max(dtr[:], dtr[:], 0.0), "v")
                S.chain(v.tensor_add(dtr[:], dtr[:], t8a[:]), "v")
                S.wait(v, "s4_negA")
                S.chain(v.tensor_scalar_mul(t8a[:], dtr[:], negA[:, 0:1]), "v",
                        label=f"s4_dtla_{b}")
                if b > 0:
                    S.wait(v, f"s4_lb_{b - 1}_{NCH - 1}_7")
                    S.wait(v, f"s4_epbg_{b - 1}_{NCH - 1}_1")
                S.chain(v.memset(Mt[0:8, 0:128], 0.0), "v")
                for ch in range(NCH):
                    S.chain(v.tensor_tensor_scan(
                        Lrow[:, ch * 128:(ch + 1) * 128],
                        t8a[:, ch * 128:(ch + 1) * 128],
                        Mt[0:8, 0:128], 0.0, op0=ALU.add, op1=ALU.add), "v",
                        label=(f"s4_L_{b}" if ch == NCH - 1 else None))
                for ch in range(NCH):
                    S.wait(v, f"s4_str_{b}_{ch}")
                    S.chain(v.tensor_copy(smallTT[:, ch * 24:(ch + 1) * 24],
                                          ps_mis[:, 0:24]), "v")
                    S.chain(v.tensor_copy(erow[0:1, ch * 8:(ch + 1) * 8],
                                          ps_mis[0:1, 256:264]), "v",
                            label=f"s4_smallevac_{b}_{ch}")
                for ch in range(NCH):
                    S.wait(v, f"s4_dcb_{b}_{ch}")
                    S.chain(v.tensor_copy(dcol[:, ch * 8:(ch + 1) * 8],
                                          ps_stu[:, 256:264]), "v",
                            label=f"s4_dcole_{b}_{ch}")
                S.wait(v, f"s4_xse_{b}_3_3")
                if b > 0:
                    S.wait(v, f"s4_stmm_{b - 1}_{NCH - 1}_7")
                for tt in range(4):
                    for h in range(8):
                        S.chain(v.tensor_scalar_mul(
                            dtxt[:, tt * 2048 + h * 256:tt * 2048 + (h + 1) * 256],
                            xst[:, tt * 2048 + h * 256:tt * 2048 + (h + 1) * 256],
                            smallTT[:, tt * 24 + h:tt * 24 + h + 1]), "v",
                            label=(f"s4_dtx_{b}" if tt == 3 and h == 7 else None))
                for ch in range(NCH):
                    S.wait(v, f"s4_scmm_{b}_{ch}")
                    S.chain(v.tensor_mul(BCm[:], ps_sc[:, 0:128], trib[:]), "v",
                            label=f"s4_bcmsk_{b}_{ch}")
                    for grp in range(2):
                        S.wait(v, f"s4_epbg_{b}_{ch}_{grp}")
                        if grp == 0 and (ch > 0 or b > 0):
                            pb, pch = (b, ch - 1) if ch > 0 else (b - 1, NCH - 1)
                            S.wait(v, f"s4_ymm_{pb}_{pch}_7")
                        cfb = bass.AP(Cf[:].tensor, ch * 128,
                                      [[512, 64], [0, 4], [1, 128]])
                        S.chain(v.tensor_mul(
                            Cx8.rearrange("p (h c) -> p h c", c=128)[
                                :, grp * 4:(grp + 1) * 4, :],
                            ps_mis.rearrange("p (h c) -> p h c", c=128)[
                                0:64, 0:4, :],
                            cfb), "v", label=f"s4_cxg_{b}_{ch}_{grp}")
                    S.wait(v, f"s4_m_{b}_{ch}_7")
                    bcb = bass.AP(BCm[:].tensor, 0, [[128, 128], [0, 8], [1, 128]])
                    S.chain(v.tensor_mul(
                        sct8.rearrange("p (h c) -> p h c", c=128),
                        bcb,
                        Msb8.rearrange("p (h c) -> p h c", c=128)), "v",
                        label=f"s4_sct8_{b}_{ch}")
                    if ch > 0:
                        S.wait(v, f"s4_stmm_{b}_{ch - 1}_7")
                    elif b > 0:
                        S.wait(v, f"s4_stmm_{b - 1}_{NCH - 1}_7")
                    for h in range(8):
                        S.chain(v.tensor_scalar_mul(
                            Bw8[:, h * 64:(h + 1) * 64],
                            Btok[:, ch * 64:(ch + 1) * 64],
                            smallTT[:, ch * 24 + 16 + h:ch * 24 + 16 + h + 1]),
                            "v", label=(f"s4_bw8_{b}_{ch}" if h == 7 else None))
                    for h in range(8):
                        ybank = ps_y if h % 2 == 0 else ps_io[0]
                        stbank = ps_stu if h % 2 == 0 else ps_io[1]
                        S.wait(v, f"s4_ymm_{b}_{ch}_{h}")
                        if b > 0 and ch == 0 and h == 0:
                            S.wait(v, f"s4_yfst_{b - 1}")
                        for half in range(2):
                            fcol = (2 * h + half) * 512 + ch * 128
                            S.chain(v.tensor_mul(
                                yfin[:, fcol:fcol + 128],
                                ybank[:, half * 128:(half + 1) * 128],
                                zsil[:, fcol:fcol + 128]), "v",
                                label=(f"s4_ye_{b}_{ch}_{h}" if half == 1
                                       else None))
                        S.wait(v, f"s4_stmm_{b}_{ch}_{h}")
                        if h == 0:
                            S.wait(v, f"s4_dcole_{b}_{ch}")
                        S.chain(v.scalar_tensor_tensor(
                            stf[:, h * 256:(h + 1) * 256],
                            stf[:, h * 256:(h + 1) * 256],
                            dcol[:, ch * 8 + h:ch * 8 + h + 1],
                            stbank[0:64, 0:256], op0=ALU.mult, op1=ALU.add), "v")
                        S.chain(v.tensor_copy(stb[:, h * 256:(h + 1) * 256],
                                              stf[:, h * 256:(h + 1) * 256]),
                                "v", label=f"s4_stup_{b}_{ch}_{h}")
                S.label(f"s4_yf_done_{b}", "v")
    return nc


def build_s4():
    S = Sched()
    nc = None
    for p2 in (False, True):
        S.begin(p2)
        with ExitStack() as outer:
            env = _standalone_env("s4", outer)
            emit_s4(S, env)
            nc = env.nc
    return nc


def emit_s5(S: Sched, env):
    nc = env.nc
    Wo = env.io["Wo"]
    bo_c = env.io["bo_c"]
    yf_d = env.io["yfinT"]
    xn1_d = env.io["xn1T"]
    s1_d = env.io["s1T"]

    with ExitStack() as ctx:
        sb = lambda n, s_, d_: ctx.enter_context(nc.sbuf_tensor(env.pfx + n, s_, d_))
        psx = lambda n, s_, d_: ctx.enter_context(nc.psum_tensor(env.pfx + n, s_, d_))
        sem = lambda n: env.outer.enter_context(nc.semaphore(env.pfx + n))
        wT = [sb(f"wT{i}", [128, 1024], BF16) for i in range(16)]
        yb = [sb("yb0", [128, 16 * 512], BF16), sb("yb1", [128, 16 * 512], BF16)]
        xnb = [sb("xnb0", [128, 8 * 512], F32), sb("xnb1", [128, 8 * 512], F32)]
        s1b = sb("s1b", [128, 8 * 512], F32)
        bos = sb("bos", [128, 8], F32)
        ps_r = [psx("ps_r0", [128, 512], F32), psx("ps_r1", [128, 512], F32),
                psx("ps_r2", [128, 512], F32), psx("ps_r3", [128, 512], F32)]
        for k in ["d0", "d1", "w", "t", "v", "a", "p", "o1"]:
            S.reg(k, sem("sem_" + k))
        block = ctx.enter_context(nc.Block())

        @block.gpsimd
        def _(g):
            for fc in range(16):
                S.inc(g.dma_start(out=wT[fc][:], in_=Wo[fc * 128:(fc + 1) * 128, :]),
                      "w", 16)
            S.inc(g.dma_start(out=bos[:], in_=bo_c), "w", 16)

        @block.sync
        def _(sy):
            for b in range(NBLK):
                if b >= 2:
                    S.wait(sy, f"s5_mmdone_{b - 2}")
                    S.wait(sy, f"s5_comb_{b - 2}")
                S.inc(sy.dma_start(
                    out=yb[b % 2].rearrange("p (f c) -> p f c", c=512),
                    in_=yf_d.rearrange("(f p) t -> p f t", p=128)[:, :, b * 512:(b + 1) * 512]),
                    f"d{b % 2}", 16)
                S.inc(sy.dma_start(
                    out=xnb[b % 2].rearrange("p (k c) -> p k c", c=512),
                    in_=xn1_d.rearrange("(k p) t -> p k t", p=128)[:, :, b * 512:(b + 1) * 512]),
                    f"d{b % 2}", 16, label=f"s5_in_{b}")
                if b >= 1:
                    S.wait(sy, f"s5_comb_{b - 1}")
                    S.inc(sy.dma_start(
                        out=s1_d.rearrange("(k p) t -> p k t", p=128)[:, :, (b - 1) * 512:b * 512],
                        in_=s1b.rearrange("p (k c) -> p k c", c=512)),
                        "o1", 16, label=f"s5_st_{b - 1}")
            S.wait(sy, f"s5_comb_{NBLK - 1}")
            S.inc(sy.dma_start(
                out=s1_d.rearrange("(k p) t -> p k t", p=128)[:, :, (NBLK - 1) * 512:NBLK * 512],
                in_=s1b.rearrange("p (k c) -> p k c", c=512)),
                "o1", 16, label=f"s5_st_{NBLK - 1}")
            sy.wait_ge(S.sem("o1"), 16 * NBLK)

        @block.tensor
        def _(t):
            S.wait_cnt(t, "w", 16 * 17)
            for b in range(NBLK):
                S.wait(t, f"s5_in_{b}")
                for do in range(8):
                    if do >= 4:
                        S.wait(t, f"s5_ev_{b}_{do - 4}")
                    elif b > 0:
                        S.wait(t, f"s5_ev_{b - 1}_{4 + do}")
                    ps = ps_r[do % 4]
                    for fc in range(16):
                        S.inc(t.matmul(ps[:], wT[fc][:, do * 128:(do + 1) * 128],
                                       yb[b % 2][:, fc * 512:(fc + 1) * 512],
                                       start=(fc == 0), stop=(fc == 15)),
                              "t", label=(f"s5_mm_{b}_{do}" if fc == 15 else None))
                S.label(f"s5_mmdone_{b}", "t")

        @block.vector
        def _(v):
            for b in range(NBLK):
                if b > 0:
                    S.wait(v, f"s5_st_{b - 1}")
                for do in range(8):
                    ps = ps_r[do % 4]
                    S.wait(v, f"s5_mm_{b}_{do}")
                    S.chain(v.scalar_tensor_tensor(
                        s1b[:, do * 512:(do + 1) * 512], ps[:],
                        bos[:, do:do + 1],
                        xnb[b % 2][:, do * 512:(do + 1) * 512],
                        op0=ALU.add, op1=ALU.add), "v",
                        label=f"s5_ev_{b}_{do}")
                S.label(f"s5_comb_{b}", "v")
    return nc


def build_s5():
    S = Sched()
    nc = None
    for p2 in (False, True):
        S.begin(p2)
        with ExitStack() as outer:
            env = _standalone_env("s5", outer)
            emit_s5(S, env)
            nc = env.nc
    return nc


def emit_s6(S: Sched, env):
    nc = env.nc
    wgT = env.io["wgT"]
    wuT = env.io["wuT"]
    xn1_d = env.io["xn1T"]
    hg_d = env.io["hgT"]

    with ExitStack() as ctx:
        sb = lambda n, s_, d_: ctx.enter_context(nc.sbuf_tensor(env.pfx + n, s_, d_))
        psx = lambda n, s_, d_: ctx.enter_context(nc.psum_tensor(env.pfx + n, s_, d_))
        sem = lambda n: env.outer.enter_context(nc.semaphore(env.pfx + n))
        wg = [sb(f"wg{i}", [128, 3072], BF16) for i in range(8)]
        wu = [sb(f"wu{i}", [128, 3072], BF16) for i in range(8)]
        xf = [sb("xf0", [128, 8 * 512], F32), sb("xf1", [128, 8 * 512], F32)]
        xb = sb("xb", [128, 8 * 512], BF16)
        hg = sb("hg", [128, 24 * 512], BF16)
        sg = [sb("sg0", [128, 512], BF16), sb("sg1", [128, 512], BF16)]
        ps_g = [psx(f"ps_g{i}", [128, 512], F32) for i in range(3)]
        ps_u = [psx(f"ps_u{i}", [128, 512], F32) for i in range(3)]
        for k in ["d0", "d1", "w", "t", "v", "a", "p", "o1"]:
            S.reg(k, sem("sem_" + k))
        block = ctx.enter_context(nc.Block())

        @block.gpsimd
        def _(g):
            for dc in range(8):
                S.inc(g.dma_start(out=wg[dc][:], in_=wgT[dc * 128:(dc + 1) * 128, :]),
                      "w", 16)
                S.inc(g.dma_start(out=wu[dc][:], in_=wuT[dc * 128:(dc + 1) * 128, :]),
                      "w", 16)

        @block.sync
        def _(sy):
            for b in range(NBLK):
                if b >= 2:
                    S.wait(sy, f"s6_xcast_{b - 2}")
                S.inc(sy.dma_start(
                    out=xf[b % 2].rearrange("p (k c) -> p k c", c=512),
                    in_=xn1_d.rearrange("(k p) t -> p k t", p=128)[:, :, b * 512:(b + 1) * 512]),
                    f"d{b % 2}", 16, label=f"s6_in_{b}")
                if b >= 1:
                    S.wait(sy, f"s6_hgdone_{b - 1}")
                    S.inc(sy.dma_start(
                        out=hg_d.rearrange("(f p) t -> p f t", p=128)[:, :, (b - 1) * 512:b * 512],
                        in_=hg.rearrange("p (f c) -> p f c", c=512)),
                        "o1", 16, label=f"s6_st_{b - 1}")
            S.wait(sy, f"s6_hgdone_{NBLK - 1}")
            S.inc(sy.dma_start(
                out=hg_d.rearrange("(f p) t -> p f t", p=128)[:, :, (NBLK - 1) * 512:NBLK * 512],
                in_=hg.rearrange("p (f c) -> p f c", c=512)),
                "o1", 16, label=f"s6_st_{NBLK - 1}")
            sy.wait_ge(S.sem("o1"), 16 * NBLK)

        @block.vector
        def _(v):
            for b in range(NBLK):
                S.wait(v, f"s6_in_{b}")
                if b > 0:
                    S.wait(v, f"s6_mmdone_{b - 1}")
                S.chain(v.tensor_scalar_mul(xb[:], xf[b % 2][:], 1.0 / 127.0),
                        "v", label=f"s6_xcast_{b}")
                if b > 0:
                    S.wait(v, f"s6_st_{b - 1}")
                for ft in range(24):
                    S.wait(v, f"s6_sg_{b}_{ft}")
                    S.wait(v, f"s6_umm_{b}_{ft}")
                    S.chain(v.tensor_mul(hg[:, ft * 512:(ft + 1) * 512],
                                         sg[ft % 2][:], ps_u[ft % 3][:]), "v",
                            label=f"s6_hge_{b}_{ft}")
                S.label(f"s6_hgdone_{b}", "v")

        @block.tensor
        def _(t):
            S.wait_cnt(t, "w", 16 * 16)
            for b in range(NBLK):
                S.wait(t, f"s6_xcast_{b}")
                for ft in range(24):
                    if ft >= 3:
                        S.wait(t, f"s6_sg_{b}_{ft - 3}")
                        S.wait(t, f"s6_hge_{b}_{ft - 3}")
                    elif b > 0:
                        S.wait(t, f"s6_sg_{b - 1}_{21 + ft}")
                        S.wait(t, f"s6_hge_{b - 1}_{21 + ft}")
                    for dc in range(8):
                        S.inc(t.matmul(ps_g[ft % 3][:],
                                       wg[dc][:, ft * 128:(ft + 1) * 128],
                                       xb[:, dc * 512:(dc + 1) * 512],
                                       start=(dc == 0), stop=(dc == 7)),
                              "t", label=(f"s6_gmm_{b}_{ft}" if dc == 7 else None))
                    for dc in range(8):
                        S.inc(t.matmul(ps_u[ft % 3][:],
                                       wu[dc][:, ft * 128:(ft + 1) * 128],
                                       xb[:, dc * 512:(dc + 1) * 512],
                                       start=(dc == 0), stop=(dc == 7)),
                              "t", label=(f"s6_umm_{b}_{ft}" if dc == 7 else None))
                S.label(f"s6_mmdone_{b}", "t")

        @block.scalar
        def _(s):
            for b in range(NBLK):
                for ft in range(24):
                    S.wait(s, f"s6_gmm_{b}_{ft}")
                    if ft > 1:
                        S.wait(s, f"s6_hge_{b}_{ft - 2}")
                    elif b > 0:
                        S.wait(s, f"s6_hge_{b - 1}_{22 + ft}")
                    S.chain(s.activation(sg[ft % 2][:], ps_g[ft % 3][:], AF.Silu),
                            "a", label=f"s6_sg_{b}_{ft}")
    return nc


def build_s6():
    S = Sched()
    nc = None
    for p2 in (False, True):
        S.begin(p2)
        with ExitStack() as outer:
            env = _standalone_env("s6", outer)
            emit_s6(S, env)
            nc = env.nc
    return nc


def emit_s7(S: Sched, env):
    nc = env.nc
    wdT = env.io["wdT"]
    hg_d = env.io["hgT"]
    x1_d = env.io["x1T"]
    s1_d = env.io["s1T"]
    out_d = env.io["out"]

    with ExitStack() as ctx:
        sb = lambda n, s_, d_: ctx.enter_context(nc.sbuf_tensor(env.pfx + n, s_, d_))
        psx = lambda n, s_, d_: ctx.enter_context(nc.psum_tensor(env.pfx + n, s_, d_))
        sem = lambda n: env.outer.enter_context(nc.semaphore(env.pfx + n))
        wT = [sb(f"wT{i}", [128, 1024], BF16) for i in range(24)]
        hgb = [sb("hgb0", [128, 24 * 512], BF16), sb("hgb1", [128, 24 * 512], BF16)]
        x1b = [sb("x1b0", [128, 8 * 512], F32), sb("x1b1", [128, 8 * 512], F32)]
        s1b = [sb("s1b0", [128, 8 * 512], F32), sb("s1b1", [128, 8 * 512], F32)]
        gb = sb("gb", [128, 8 * 512], F32)
        otok = sb("otok", [128, 4 * 1024], F32)
        idf = sb("idf", [128, 128], F32)
        ps_r = [psx(f"ps_r{i}", [128, 512], F32) for i in range(4)]
        ps_t = [psx("ps_t0", [128, 512], F32), psx("ps_t1", [128, 512], F32)]
        for k in ["d0", "d1", "d20", "d21", "w", "t", "v", "a", "p", "o1"]:
            S.reg(k, sem("sem_" + k))
        block = ctx.enter_context(nc.Block())

        @block.gpsimd
        def _(g):
            for fc in range(24):
                S.inc(g.dma_start(out=wT[fc][:], in_=wdT[fc * 128:(fc + 1) * 128, :]),
                      "w", 16)
            S.chain(g.memset(idf[:], 1.0), "p")
            S.chain(g.affine_select(idf[:], idf[:], pattern=[[1, 128]], base=0,
                                    channel_multiplier=-1,
                                    compare_op=ALU.is_equal, fill=0.0), "p",
                    label="s7_const")
            for b in range(NBLK):
                if b >= 2:
                    S.wait(g, f"s7_comb_{b - 2}")
                S.inc(g.dma_start(
                    out=x1b[b % 2].rearrange("p (k c) -> p k c", c=512),
                    in_=x1_d.rearrange("(k p) t -> p k t", p=128)[:, :, b * 512:(b + 1) * 512]),
                    f"d2{b % 2}", 16)
                S.inc(g.dma_start(
                    out=s1b[b % 2].rearrange("p (k c) -> p k c", c=512),
                    in_=s1_d.rearrange("(k p) t -> p k t", p=128)[:, :, b * 512:(b + 1) * 512]),
                    f"d2{b % 2}", 16, label=f"s7_xs_{b}")

        @block.sync
        def _(sy):
            for b in range(NBLK):
                if b >= 2:
                    S.wait(sy, f"s7_mmdone_{b - 2}")
                S.inc(sy.dma_start(
                    out=hgb[b % 2].rearrange("p (f c) -> p f c", c=512),
                    in_=hg_d.rearrange("(f p) t -> p f t", p=128)[:, :, b * 512:(b + 1) * 512]),
                    f"d{b % 2}", 16, label=f"s7_in_{b}")
                if b >= 1:
                    S.wait(sy, f"s7_otok_{b - 1}")
                    for tt in range(4):
                        S.inc(sy.dma_start(
                            out=out_d[((b - 1) * 4 + tt) * 128:
                                      ((b - 1) * 4 + tt + 1) * 128, :],
                            in_=otok[:, tt * 1024:(tt + 1) * 1024]),
                            "o1", 16, label=(f"s7_st_{b - 1}" if tt == 3 else None))
            S.wait(sy, f"s7_otok_{NBLK - 1}")
            for tt in range(4):
                S.inc(sy.dma_start(
                    out=out_d[((NBLK - 1) * 4 + tt) * 128:
                              ((NBLK - 1) * 4 + tt + 1) * 128, :],
                    in_=otok[:, tt * 1024:(tt + 1) * 1024]),
                    "o1", 16, label=(f"s7_st_{NBLK - 1}" if tt == 3 else None))
            sy.wait_ge(S.sem("o1"), 16 * 4 * NBLK)

        @block.tensor
        def _(t):
            S.wait_cnt(t, "w", 16 * 24)
            S.wait(t, "s7_const")
            for b in range(NBLK):
                S.wait(t, f"s7_in_{b}")
                for do in range(8):
                    if do >= 4:
                        S.wait(t, f"s7_g_{b}_{do - 4}")
                    elif b > 0:
                        S.wait(t, f"s7_g_{b - 1}_{4 + do}")
                    ps = ps_r[do % 4]
                    for fc in range(24):
                        S.inc(t.matmul(ps[:], wT[fc][:, do * 128:(do + 1) * 128],
                                       hgb[b % 2][:, fc * 512:(fc + 1) * 512],
                                       start=(fc == 0), stop=(fc == 23)),
                              "t", label=(f"s7_mm_{b}_{do}" if fc == 23 else None))
                S.label(f"s7_mmdone_{b}", "t")
                S.wait(t, f"s7_comb_{b}")
                for tt in range(4):
                    for half in range(2):
                        if tt > 0 or b > 0:
                            ptt, phalf = (tt, half) if tt > 0 else (4, half)
                            S.wait(t, f"s7_te_{b if tt > 0 else b - 1}_"
                                      f"{tt - 1 if tt > 0 else 3}_{half}")
                        for j in range(4):
                            jj = half * 4 + j
                            S.inc(t.transpose(
                                ps_t[half][:, j * 128:(j + 1) * 128],
                                gb[:, jj * 512 + tt * 128:jj * 512 + (tt + 1) * 128],
                                idf[:]), "t",
                                label=(f"s7_tr_{b}_{tt}_{half}" if j == 3
                                       else None))

        @block.scalar
        def _(s):
            for b in range(NBLK):
                for do in range(8):
                    ps = ps_r[do % 4]
                    S.wait(s, f"s7_mm_{b}_{do}")
                    if b > 0 and do == 0:
                        S.wait(s, f"s7_tr_{b - 1}_3_1")
                    S.chain(s.activation(gb[:, do * 512:(do + 1) * 512], ps[:],
                                         AF.Silu, scale=1.0 / 127.0), "a",
                            label=f"s7_g_{b}_{do}")

        @block.vector
        def _(v):
            for b in range(NBLK):
                S.wait(v, f"s7_xs_{b}")
                for do in range(8):
                    S.wait(v, f"s7_g_{b}_{do}")
                    sl = slice(do * 512, (do + 1) * 512)
                    S.chain(v.tensor_mul(gb[:, sl], s1b[b % 2][:, sl],
                                         gb[:, sl]), "v")
                    S.chain(v.tensor_add(gb[:, sl], gb[:, sl],
                                         x1b[b % 2][:, sl]), "v",
                            label=(f"s7_comb_{b}" if do == 7 else None))
                if b > 0:
                    S.wait(v, f"s7_st_{b - 1}")
                for tt in range(4):
                    for half in range(2):
                        S.wait(v, f"s7_tr_{b}_{tt}_{half}")
                        dst = otok.rearrange("p (k c) -> p k c", c=1024)
                        S.chain(v.tensor_copy(
                            dst[:, tt:tt + 1, half * 512:(half + 1) * 512].rearrange(
                                "p a (j c) -> p (a j) c", c=128),
                            ps_t[half].rearrange("p (j c) -> p j c", c=128)),
                            "v", label=f"s7_te_{b}_{tt}_{half}")
                S.label(f"s7_otok_{b}", "v")
    return nc


def build_s7():
    S = Sched()
    nc = None
    for p2 in (False, True):
        S.begin(p2)
        with ExitStack() as outer:
            env = _standalone_env("s7", outer)
            emit_s7(S, env)
            nc = env.nc
    return nc


def build_s1():
    S = Sched()
    nc = None
    for p2 in (False, True):
        S.begin(p2)
        with ExitStack() as outer:
            env = _standalone_env("s1", outer)
            emit_s1(S, env)
            nc = env.nc
    return nc


def _shard_layouts(inputs):
    i = inputs
    c = {}
    c["s1"] = {
        "w_in": i["L0_attn_w_in"],
        "w_a": i["L0_attn_w_a"],
        "b_inT": np.ascontiguousarray(i["L0_attn_b_in"].T),
        "b_aT": np.ascontiguousarray(i["L0_attn_b_a"].T),
        "a_paramT": np.ascontiguousarray(i["L0_attn_a_param"].reshape(8, 128).T),
    }
    c["s2"] = {
        "W1T": np.ascontiguousarray(i["L0_ffn_W1"].T),
        "W2T": np.ascontiguousarray(i["L0_ffn_W2"].T),
        "b1c": np.ascontiguousarray(i["L0_ffn_b1"].reshape(32, 128).T),
    }
    c["s3"] = {
        "b2c": np.ascontiguousarray(i["L0_ffn_b2"].reshape(8, 128).T),
    }
    b_in = i["L1_attn_b_in"]
    c["s4"] = {
        "WinT": i["L1_attn_W_in"],
        "bz_c": np.ascontiguousarray(b_in[0:2048].reshape(16, 128).T),
        "bxs_r": np.ascontiguousarray(b_in[2048:4096].reshape(1, 2048)),
        "bbc_c": np.ascontiguousarray(b_in[4096:4224].reshape(128, 1)),
        "bbc_r": np.ascontiguousarray(b_in[4096:4224].reshape(1, 128)),
        "bdt_c": np.ascontiguousarray(b_in[4224:4232].reshape(8, 1)),
        "dtb_c": np.ascontiguousarray(i["L1_attn_dt_bias"].reshape(8, 1)),
        "Alog_c": np.ascontiguousarray(i["L1_attn_A_log"].reshape(8, 1)),
        "Dsk_r": np.ascontiguousarray(i["L1_attn_D_skip"].reshape(1, 8)),
        "gam_c": np.ascontiguousarray(i["L1_attn_ln_gamma"].reshape(8, 128).T),
        "bet_c": np.ascontiguousarray(i["L1_attn_ln_beta"].reshape(8, 128).T),
    }
    c["s5"] = {
        "Wo": i["L1_attn_W_out"],
        "bo_c": np.ascontiguousarray(i["L1_attn_b_out"].reshape(8, 128).T),
    }
    c["s6"] = {
        "wgT": np.ascontiguousarray(i["L1_ffn_w_gate"].T),
        "wuT": np.ascontiguousarray(i["L1_ffn_w_up"].T),
    }
    c["s7"] = {
        "wdT": np.ascontiguousarray(i["L1_ffn_w_down"].T),
    }
    return c


def build_fused():
    emitters = {"s1": emit_s1, "s2": emit_s2, "s3": emit_s3, "s4": emit_s4,
                "s5": emit_s5, "s6": emit_s6, "s7": emit_s7}
    S = Sched()
    nc = None
    for p2 in (False, True):
        S.begin(p2)
        with ExitStack() as outer:
            nc = bass.Bass("TRN2", target_bir_lowering=False)
            outs_of = set()
            for st in _STAGES:
                for name, _sh, _dt in IO_SPECS[st]["out"]:
                    outs_of.add(name)
            kinds = {}
            for st in _STAGES:
                for name, sh, dt_ in IO_SPECS[st]["in"]:
                    kinds.setdefault(name, ("Internal" if name in outs_of
                                            else "ExternalInput", sh, dt_))
            for st in _STAGES:
                for name, sh, dt_ in IO_SPECS[st]["out"]:
                    kinds.setdefault(name, ("ExternalOutput" if name == "out"
                                            else "Internal", sh, dt_))
            io = {}
            for name, (kind, sh, dt_) in kinds.items():
                io[name] = _ios(nc, name, sh, dt_, kind)
            bar = outer.enter_context(nc.semaphore("stage_bar"))
            for k, st in enumerate(_STAGES):
                if k > 0:
                    _barrier_block(nc, bar, 5 * k, True)
                S.pfx = st + "_"
                env = Env(nc, io, outer, pfx=st + "_")
                emitters[st](S, env)
            S.pfx = ""
    return nc


def kernel(**inputs):
    NC = 8
    x = inputs["L0_attn_x"]
    c = _shard_layouts(inputs)
    common = {}
    for d in c.values():
        common.update(d)
    nc = build_fused()
    in_maps = [{**common, "x": np.ascontiguousarray(x[b])} for b in range(NC)]
    res = run_bass_kernel_spmd(nc, in_maps, core_ids=list(range(NC)))
    return np.stack([res.results[b]["out"] for b in range(NC)], axis=0)


def kernel_unfused(**inputs):
    NC = 8
    cores = list(range(NC))
    x = inputs["L0_attn_x"]
    c = _shard_layouts(inputs)

    r1 = run_bass_kernel_spmd(build_s1(), [
        {**c["s1"], "x": np.ascontiguousarray(x[b])} for b in range(NC)],
        core_ids=cores)
    r2 = run_bass_kernel_spmd(build_s2(), [
        {**c["s2"], "xnbf": r1.results[b]["xnbf"]} for b in range(NC)],
        core_ids=cores)
    r3 = run_bass_kernel_spmd(build_s3(), [
        {**c["s3"], "h1": r2.results[b]["h1"], "W2t": r2.results[b]["W2t"],
         "xT": r1.results[b]["xT"], "s0T": r1.results[b]["s0T"]}
        for b in range(NC)], core_ids=cores)
    r4 = run_bass_kernel_spmd(build_s4(), [
        {**c["s4"], "x1T": r3.results[b]["x1T"]} for b in range(NC)],
        core_ids=cores)
    r5 = run_bass_kernel_spmd(build_s5(), [
        {**c["s5"], "yfinT": r4.results[b]["yfinT"],
         "xn1T": r4.results[b]["xn1T"]} for b in range(NC)],
        core_ids=cores)
    r6 = run_bass_kernel_spmd(build_s6(), [
        {**c["s6"], "xn1T": r4.results[b]["xn1T"]} for b in range(NC)],
        core_ids=cores)
    r7 = run_bass_kernel_spmd(build_s7(), [
        {**c["s7"], "hgT": r6.results[b]["hgT"], "x1T": r3.results[b]["x1T"],
         "s1T": r5.results[b]["s1T"]} for b in range(NC)],
        core_ids=cores)
    return np.stack([r7.results[b]["out"] for b in range(NC)], axis=0)
